# version 15
# speedup vs baseline: 1.0406x; 1.0177x over previous
import math
import numpy as np
import concourse.bass as bass
import concourse.mybir as mybir
from concourse.bass_utils import run_bass_kernel_spmd

F32 = mybir.dt.float32
BF16 = mybir.dt.bfloat16
AF = mybir.ActivationFunctionType
ALU = mybir.AluOpType
AX = mybir.AxisListType

T = 4096
D = 1024
NT = T // 128
NP = T // 512

ENGINES = ("sync", "tensor", "vector", "scalar", "gpsimd")


NDMA_SEMS = 8


class Op:
    __slots__ = ("eng", "fn", "deps", "is_dma", "ordinal", "waits", "tag",
                 "sem_key", "sem_val")

    def __init__(self, eng, fn, deps, is_dma, tag):
        self.eng = eng
        self.fn = fn
        self.deps = deps
        self.is_dma = is_dma
        self.ordinal = None
        self.waits = []
        self.tag = tag
        self.sem_key = None
        self.sem_val = None


class Rec:

    def __init__(self):
        self.streams = {e: [] for e in ENGINES}
        self.bufs = {}

    def add(self, eng, fn, deps=(), is_dma=False, tag="", r=(), w=()):
        deps = [d for d in deps if d is not None]
        for b in r:
            ent = self.bufs.get(b)
            if ent is not None and ent[0] is not None:
                deps.append(ent[0])
        for b in w:
            ent = self.bufs.get(b)
            if ent is not None:
                if ent[0] is not None:
                    deps.append(ent[0])
                deps.extend(ent[1].values())
                deps.extend(ent[2])
        if eng == "tensor" and not is_dma:
            deps = [d for d in deps
                    if not (d.eng == "tensor" and not d.is_dma)]
        seen = set()
        uniq = []
        for d in deps:
            if id(d) not in seen:
                seen.add(id(d))
                uniq.append(d)
        op = Op(eng, fn, uniq, is_dma, tag)
        self.streams[eng].append(op)
        for b in r:
            ent = self.bufs.setdefault(b, [None, {}, []])
            if is_dma:
                ent[2].append(op)
            else:
                ent[1][eng] = op
        for b in w:
            self.bufs[b] = [op, {}, []]
        return op

    def dma(self, eng, out, in_, deps=(), tag="", r=(), w=()):
        return self.add(
            eng, lambda e, o=out, i=in_: e.dma_start(out=o, in_=i),
            deps, is_dma=True, tag=tag, r=r, w=w)

    def fence(self, tag="fence"):
        lasts = []
        for e in ENGINES:
            if self.streams[e]:
                lasts.append(self.streams[e][-1])
            per_slot = {}
            di = 0
            for op in self.streams[e]:
                if op.is_dma:
                    per_slot[di % NDMA_SEMS] = op
                    di += 1
            lasts.extend(per_slot.values())
        for e in ENGINES:
            self.add(e, lambda en: en.nop(), deps=list(lasts), tag=tag)

    def emit(self, nc, sems):
        pos = {}
        for e in ENGINES:
            for i, op in enumerate(self.streams[e]):
                pos[id(op)] = i
        slot_prev = {}
        for e in ENGINES:
            dma_idx = 0
            dcount = [0] * NDMA_SEMS
            for op in self.streams[e]:
                if not op.is_dma:
                    continue
                j = dma_idx % NDMA_SEMS
                dma_idx += 1
                dcount[j] += 16
                op.ordinal = -1
                op.sem_key = (e, "d", j)
                op.sem_val = dcount[j]
                key = (e, j)
                if key in slot_prev:
                    op.deps = op.deps + [slot_prev[key]]
                slot_prev[key] = op
        for e in ENGINES:
            for op in self.streams[e]:
                for d in op.deps:
                    if d.ordinal is None:
                        d.ordinal = -1
        for e in ENGINES:
            cc = 0
            for op in self.streams[e]:
                if op.ordinal == -1 and not op.is_dma:
                    cc += 1
                    op.sem_key = (e, "c")
                    op.sem_val = cc
        for e in ENGINES:
            water = {}
            for op in self.streams[e]:
                for d in op.deps:
                    if d.sem_key is None:
                        continue
                    if water.get(d.sem_key, 0) < d.sem_val:
                        water[d.sem_key] = d.sem_val
                        op.waits.append((d.sem_key, d.sem_val))
        with nc.Block() as blk:
            for e in ENGINES:
                stream = self.streams[e]
                if not stream:
                    continue
                dec = getattr(blk, e)

                def make(stream=stream):
                    def _(eng):
                        for op in stream:
                            for (key, val) in op.waits:
                                eng.wait_ge(sems[key], val)
                            instr = op.fn(eng)
                            if op.sem_key is not None:
                                instr.then_inc(sems[op.sem_key],
                                               16 if op.is_dma else 1)
                    return _

                dec(make())


class Alloc:

    def __init__(self, nc):
        self.nc = nc
        self.ctx = []

    def sb(self, name, shape, dt):
        cm = self.nc.sbuf_tensor(name, shape, dt)
        h = cm.__enter__()
        self.ctx.append(cm)
        self.named = getattr(self, "named", {})
        self.named[name] = cm
        return h

    def mark(self):
        return len(self.ctx)

    def free_suffix(self, mark):
        for cm in reversed(self.ctx[mark:]):
            cm.__exit__(None, None, None)
        dead = set(id(cm) for cm in self.ctx[mark:])
        del self.ctx[mark:]
        self.named = {k: v for k, v in getattr(self, "named", {}).items()
                      if id(v) not in dead}

    def free(self, names):
        cms = [self.named.pop(n) for n in names]
        cms.sort(key=lambda cm: self.ctx.index(cm), reverse=True)
        for cm in cms:
            self.ctx.remove(cm)
            cm.__exit__(None, None, None)

    def ps(self, name, shape, dt=F32):
        cm = self.nc.psum_tensor(name, shape, dt)
        h = cm.__enter__()
        self.ctx.append(cm)
        return h

    def sem(self, name):
        cm = self.nc.semaphore(name)
        h = cm.__enter__()
        self.ctx.append(cm)
        return h

    def close(self):
        for cm in reversed(self.ctx):
            cm.__exit__(None, None, None)
        self.ctx = []


def finish(nc, rec, al):
    sems = {}
    for e in ENGINES:
        sems[(e, "c")] = al.sem(f"s_{e}_c")
        for j in range(NDMA_SEMS):
            sems[(e, "d", j)] = al.sem(f"s_{e}_d{j}")
    tail_deps = []
    for e in ENGINES:
        if rec.streams[e]:
            tail_deps.append(rec.streams[e][-1])
        last_per_slot = {}
        di = 0
        for op in rec.streams[e]:
            if op.is_dma:
                last_per_slot[di % NDMA_SEMS] = op
                di += 1
        tail_deps.extend(last_per_slot.values())
    rec.add("sync", lambda e: e.nop(), deps=tail_deps, tag="final_wait")
    rec.emit(nc, sems)



def _c(a):
    return np.ascontiguousarray(np.asarray(a), dtype=np.float32)


def _col_chunks(v, nchunk):
    return _c(np.asarray(v).reshape(nchunk, 128).T)


def _row(v):
    return _c(np.asarray(v).reshape(1, -1))



class ConstPool:

    def __init__(self, rec, tile):
        self.rec = rec
        self.tile = tile
        self.vals = {}
        self.n = 0
        self.op = None

    def get(self, v):
        v = float(v)
        if v not in self.vals:
            assert self.n < self.tile.shape[1]
            j = self.n
            self.n += 1
            self.vals[v] = j
            o = self.rec.add("gpsimd", lambda e, j=j, v=v: e.memset(
                self.tile[:, j:j + 1], v), deps=[self.op], tag="const")
            self.op = o
        return self.tile[:, self.vals[v]:self.vals[v] + 1]


def make_identity_ops(rec, ident):
    sq = ident.shape[1]
    o1 = rec.add("gpsimd", lambda e: e.memset(ident[:], 0.0), tag="id0")
    return rec.add("gpsimd", lambda e: e.affine_select(
        out=ident[:], in_=ident[:], compare_op=ALU.not_equal, fill=1.0,
        base=0, pattern=[[-1, sq]], channel_multiplier=1), deps=[o1], tag="id1")


class LNHelper:

    def __init__(self, rec, scratch_sq, stats, name="ln", sq_tok=None):
        self.rec = rec
        self.sq = scratch_sq
        self.st = stats
        self.nm = name
        self.sq_tok = sq_tok or (name + "sq")

    def emit(self, x_ap, x_tok, xln_out, xln_tok, eps_ap, deps=(), const_dep=None):
        rec, sq, st = self.rec, self.sq, self.st
        nm = self.nm
        w = x_ap.shape[1]
        d = float(w)
        o_sum = rec.add("vector", lambda e: e.reduce_sum(st[:, 0:1], x_ap, axis=AX.X),
                        deps=deps, tag="ln_sum", r=[x_tok], w=[nm + "a"])
        o_sq = rec.add("scalar", lambda e: e.activation(
            sq[:, 0:w], x_ap, AF.Square, accum_out=st[:, 1:2]), deps=deps,
            tag="ln_sq", r=[x_tok], w=[nm + "b", self.sq_tok])
        rec.add("vector", lambda e: e.tensor_scalar(
            st[:, 2:3], st[:, 0:1], -1.0 / d, None, ALU.mult),
            tag="ln_nm", r=[nm + "a"], w=[nm + "c"])
        rec.add("vector", lambda e: e.tensor_scalar(
            st[:, 3:4], st[:, 1:2], 1.0 / d, None, ALU.mult),
            tag="ln_ms", r=[nm + "b"], w=[nm + "d"])
        rec.add("vector", lambda e: e.tensor_tensor(
            st[:, 4:5], st[:, 2:3], st[:, 2:3], ALU.mult),
            tag="ln_m2", r=[nm + "c"], w=[nm + "e"])
        rec.add("vector", lambda e: e.tensor_tensor(
            st[:, 5:6], st[:, 3:4], st[:, 4:5], ALU.subtract),
            tag="ln_var", r=[nm + "d", nm + "e"], w=[nm + "f"])
        rec.add("scalar", lambda e: e.activation(
            st[:, 6:7], st[:, 5:6], AF.Ln, bias=eps_ap), deps=[const_dep],
            tag="ln_ln", r=[nm + "f"], w=[nm + "g"])
        rec.add("scalar", lambda e: e.activation(
            st[:, 7:8], st[:, 6:7], AF.Exp, scale=-0.5),
            tag="ln_rstd", r=[nm + "g"], w=[nm + "h"])
        o_out = rec.add("vector", lambda e: e.tensor_scalar(
            xln_out, x_ap, st[:, 2:3], st[:, 7:8], ALU.add, ALU.mult),
            tag="ln_out", r=[x_tok, nm + "c", nm + "h"], w=[xln_tok])
        return o_out


def transpose_to_fm(rec, ident, id_op, xln16, xln_tok, p_tr, ptr_tok,
                    xf, xf_tok, tl, nchunk=8, col_stride=512):
    o_trs = []
    for c in range(nchunk):
        o_t = rec.add("tensor", lambda e, c=c: e.transpose(
            p_tr[:, 128 * c:128 * (c + 1)], xln16[:, 128 * c:128 * (c + 1)],
            ident[:]), deps=[id_op], tag=f"tr{c}",
            r=[xln_tok], w=[ptr_tok])
        o_trs.append(o_t)
    evs = []
    for c in range(nchunk):
        o_ev = rec.add("vector", lambda e, c=c, tl=tl: e.tensor_copy(
            xf[:, col_stride * c + 128 * tl: col_stride * c + 128 * (tl + 1)],
            p_tr[:, 128 * c:128 * (c + 1)]),
            tag=f"trev{c}", r=[ptr_tok], w=[xf_tok])
        evs.append(o_ev)
    return evs



def build_l1_ffn(nc, al, rec, x, y, W1, W2, b1c, thc, wrc, wic, b2row, r_hbm,
                 pfx="f4_"):

    S, P = al.sb, al.ps

    ident = S(pfx + "ident", [128, 128], BF16)
    ones1 = S(pfx + "ones1", [1, 128], F32)
    w_s = S(pfx + "w_s", [128, 32768], BF16)
    wstage = [S(pfx + f"wstage{i}", [128, 4096], F32) for i in range(2)]
    b1_s = S(pfx + "b1", [128, 32], F32)
    th_s = S(pfx + "th", [128, 32], F32)
    wr_s = S(pfx + "wr", [128, 32], F32)
    wi_s = S(pfx + "wi", [128, 32], F32)
    vec_s = S(pfx + "vec", [128, 32 * 8], F32)
    rr_int = S(pfx + "rrint", [128, 32], mybir.dt.int32)
    b2r_s = S(pfx + "b2r", [1, D], F32)
    b2bc = S(pfx + "b2bc", [128, D], F32)
    xt = [S(pfx + f"xt{i}", [128, D], F32) for i in range(2)]
    sq_scr = S(pfx + "sq", [128, D], F32)
    stats = S(pfx + "stats", [128, 8], F32)
    xln16 = S(pfx + "xln16", [128, D], BF16)
    xf = S(pfx + "xf", [128, T], BF16)
    c1 = S(pfx + "c1", [128, 512], F32)
    s1 = S(pfx + "s1", [128, 512], F32)
    hid = S(pfx + "hid", [128, 512], F32)
    t12 = S(pfx + "t12", [128, 512], F32)
    r16 = [S(pfx + f"r16{i}", [128, 512], BF16) for i in range(2)]
    rbuf = S(pfx + "rbuf", [128, 32 * 512], BF16)
    outt = [S(pfx + f"outt{i}", [128, D], F32) for i in range(2)]
    cpool = S(pfx + "cpool", [128, 8], F32)

    p_tr = P(pfx + "ptr", [128, 1024], BF16)
    p_h = [P(pfx + f"ph{i}", [128, 512]) for i in range(2)]
    p_o = [P(pfx + f"po{i}", [128, 512]) for i in range(2)]
    p_bc = P(pfx + "pbc", [128, 512])

    ln = LNHelper(rec, sq_scr, stats, name=pfx + "ln")
    cp = ConstPool(rec, cpool)
    eps5 = cp.get(1e-5)
    cp_op = cp.op

    o_id = make_identity_ops(rec, ident)
    rec.add("gpsimd", lambda e: e.memset(ones1[:], 1.0), tag="ones", w=["ones1"])
    rec.dma("sync", b1_s[:], b1c[:], w=["b1"])
    rec.dma("sync", th_s[:], thc[:], w=["th"])
    rec.dma("sync", wr_s[:], wrc[:], w=["wr"])
    rec.dma("sync", wi_s[:], wic[:], w=["wi"])
    rec.dma("sync", b2r_s[:], b2row[:], w=["b2r"])

    for j in range(2):
        rec.add("tensor", lambda e, j=j: e.matmul(
            p_bc[:], ones1[0:1, :], b2r_s[0:1, 512 * j:512 * (j + 1)],
            start=True, stop=True), tag="b2bc",
            r=["ones1", "b2r"], w=["pbc"])
        rec.add("scalar", lambda e, j=j: e.copy(
            b2bc[:, 512 * j:512 * (j + 1)], p_bc[:]), tag="b2cp",
            r=["pbc"], w=["b2bc"])

    V = lambda i: vec_s[:, 32 * i:32 * (i + 1)]
    TWO_PI = 2.0 * math.pi

    def sin_reduced(dst_i, src_ap, src_tok, add_const, tmp_i=7):
        t_a = V(tmp_i)
        rec.add("vector", lambda e: e.tensor_scalar(
            t_a, src_ap, float(add_const), 1.0 / TWO_PI, ALU.add, ALU.mult),
            tag="rr_scale", r=[src_tok], w=["vec"])
        rec.add("vector", lambda e: e.tensor_copy(rr_int[:], t_a),
                tag="rr_toint", r=["vec"], w=["rrint"])
        rec.add("vector", lambda e: e.tensor_copy(V(dst_i), rr_int[:]),
                tag="rr_tofl", r=["rrint"], w=["vec"])
        rec.add("vector", lambda e: e.tensor_scalar(
            t_a, src_ap, float(add_const), None, ALU.add),
            tag="rr_add", r=[src_tok], w=["vec"])
        rec.add("vector", lambda e: e.scalar_tensor_tensor(
            t_a, V(dst_i), -TWO_PI, t_a, ALU.mult, ALU.add),
            tag="rr_fold", r=["vec"], w=["vec"])
        rec.add("vector", lambda e: e.tensor_scalar(
            V(dst_i), t_a, float(math.pi), None, ALU.is_gt),
            tag="rr_gt", r=["vec"], w=["vec"])
        rec.add("vector", lambda e: e.scalar_tensor_tensor(
            t_a, V(dst_i), -TWO_PI, t_a, ALU.mult, ALU.add),
            tag="rr_f2", r=["vec"], w=["vec"])
        rec.add("vector", lambda e: e.tensor_scalar(
            V(dst_i), t_a, float(-math.pi), None, ALU.is_lt),
            tag="rr_lt", r=["vec"], w=["vec"])
        rec.add("vector", lambda e: e.scalar_tensor_tensor(
            t_a, V(dst_i), TWO_PI, t_a, ALU.mult, ALU.add),
            tag="rr_f3", r=["vec"], w=["vec"])
        return rec.add("scalar", lambda e: e.activation(V(dst_i), t_a, AF.Sin),
                       tag="rr_sin", r=["vec"], w=["vec"])

    sin_reduced(0, th_s[:], "th", 0.0)
    sin_reduced(1, th_s[:], "th", math.pi / 2.0)
    rec.add("vector", lambda e: e.tensor_tensor(V(2), V(1), wr_s[:], ALU.mult),
            tag="qp1", r=["vec", "wr"], w=["vec"])
    rec.add("vector", lambda e: e.tensor_tensor(V(7), V(0), wi_s[:], ALU.mult),
            tag="qp2", r=["vec", "wi"], w=["vec"])
    rec.add("vector", lambda e: e.tensor_tensor(V(2), V(2), V(7), ALU.add),
            tag="qp", r=["vec"], w=["vec"])
    rec.add("vector", lambda e: e.tensor_tensor(V(2), V(2), wi_s[:], ALU.mult),
            tag="A1", r=["vec", "wi"], w=["vec"])
    rec.add("vector", lambda e: e.tensor_scalar(V(2), V(2), 0.1, None, ALU.mult),
            tag="A", r=["vec"], w=["vec"])
    rec.add("vector", lambda e: e.tensor_tensor(V(3), V(2), V(0), ALU.mult),
            tag="Sp", r=["vec"], w=["vec"])
    rec.add("vector", lambda e: e.tensor_tensor(V(4), V(2), V(1), ALU.mult),
            tag="Cp", r=["vec"], w=["vec"])
    rec.add("vector", lambda e: e.tensor_scalar(
        V(5), b1_s[:], 0.1, math.pi / 2.0, ALU.mult, ALU.add),
        tag="b5", r=["b1"], w=["vec"])
    o_vec_done = rec.add("vector", lambda e: e.tensor_scalar(
        V(6), b1_s[:], 0.1, None, ALU.mult), tag="b6", r=["b1"], w=["vec"])

    for k in range(8):
        stg = wstage[k % 2]
        rec.dma("sync", stg[:], W1[128 * k:128 * (k + 1), :],
                tag=f"w1ld{k}", w=[f"wstage{k % 2}"])
        rec.add("vector", lambda e, k=k, stg=stg: e.tensor_copy(
            w_s[:, 4096 * k:4096 * (k + 1)], stg[:]),
            tag=f"w1cast{k}", r=[f"wstage{k % 2}"], w=["w_s"])

    for p in range(NP):
        for tl in range(4):
            ti = 4 * p + tl
            q = ti % 2
            rec.dma("sync", xt[q][:], x[128 * ti:128 * (ti + 1), :],
                    tag=f"x{ti}", w=[f"xt{q}"])
            ln.emit(xt[q][:], f"xt{q}", xln16[:], "xln16", eps5,
                    const_dep=cp_op)
            transpose_to_fm(rec, ident, o_id, xln16, "xln16", p_tr, "ptr",
                            xf, "xf", tl)
        for m in range(32):
            bank = p_h[m % 2]
            bt = f"ph{m % 2}"
            for k in range(8):
                rec.add("tensor", lambda e, k=k, m=m, bank=bank: e.matmul(
                    bank[:], w_s[:, 4096 * k + 128 * m:4096 * k + 128 * (m + 1)],
                    xf[:, 512 * k:512 * (k + 1)],
                    start=(k == 0), stop=(k == 7)),
                    tag=f"hmm{p}_{m}", r=["w_s", "xf"], w=[bt])
            rec.add("scalar", lambda e, m=m, bank=bank: e.activation(
                c1[:], bank[:], AF.Sin, bias=V(5)[:, m:m + 1], scale=0.1),
                tag="qc1", r=[bt, "vec"], w=["c1"])
            rec.add("scalar", lambda e, m=m, bank=bank: e.activation(
                s1[:], bank[:], AF.Sin, bias=V(6)[:, m:m + 1], scale=0.1),
                tag="qs1", r=[bt, "vec"], w=["s1"])
            rec.add("scalar", lambda e, m=m, bank=bank: e.add(
                hid[:], bank[:], b1_s[:, m:m + 1]),
                tag="qh", r=[bt, "b1"], w=["hid"])
            rec.add("vector", lambda e, m=m: e.scalar_tensor_tensor(
                t12[:], c1[:], V(3)[:, m:m + 1], hid[:], ALU.mult, ALU.add),
                tag="qt", r=["c1", "hid", "vec"], w=["t12"])
            rec.add("vector", lambda e, m=m: e.scalar_tensor_tensor(
                c1[:], s1[:], V(4)[:, m:m + 1], t12[:], ALU.mult, ALU.add),
                tag="qq", r=["s1", "t12", "vec", "c1"], w=["c1"])
            rb = r16[m % 2]
            rec.add("scalar", lambda e, rb=rb: e.activation(
                rb[:], c1[:], AF.Relu), tag="qr",
                r=["c1"], w=[f"r16{m % 2}"])
            rec.dma("gpsimd", r_hbm[m, :, 512 * p:512 * (p + 1)], rb[:],
                    tag=f"rs{p}_{m}", r=[f"r16{m % 2}"], w=[f"rhbm{p}_{m}"])

    for m in range(32):
        stg = wstage[m % 2][:, 0:1024]
        rec.dma("sync", stg, W2[128 * m:128 * (m + 1), :],
                tag=f"w2ld{m}", w=[f"wstage{m % 2}"])
        rec.add("vector", lambda e, m=m, stg=stg: e.tensor_copy(
            w_s[:, 1024 * m:1024 * (m + 1)], stg),
            tag=f"w2cast{m}", r=[f"wstage{m % 2}"], w=["w_s"])

    y_done = []
    for p in range(NP):
        for m in range(32):
            rec.dma("sync", rbuf[:, 512 * m:512 * (m + 1)],
                    r_hbm[m, :, 512 * p:512 * (p + 1)],
                    tag=f"rl{p}_{m}", r=[f"rhbm{p}_{m}"], w=[f"rbuf{m}"])
        for tl in range(4):
            ti = 4 * p + tl
            q = tl % 2
            obuf = outt[q]
            rec.dma("scalar", xt[q][:], x[128 * ti:128 * (ti + 1), :],
                    tag=f"xb{ti}", w=[f"xt{q}"])
            for nh in range(2):
                bank = p_o[nh]
                bt = f"po{nh}"
                for m in range(32):
                    rec.add("tensor", lambda e, m=m, tl=tl, nh=nh, bank=bank:
                            e.matmul(
                        bank[:],
                        rbuf[:, 512 * m + 128 * tl:512 * m + 128 * (tl + 1)],
                        w_s[:, 1024 * m + 512 * nh:1024 * m + 512 * (nh + 1)],
                        start=(m == 0), stop=(m == 31)),
                        tag=f"omm{ti}_{nh}", r=[f"rbuf{m}", "w_s"], w=[bt])
                rec.add("vector", lambda e, nh=nh, obuf=obuf, bank=bank:
                        e.tensor_tensor(
                    obuf[:, 512 * nh:512 * (nh + 1)], bank[:],
                    b2bc[:, 512 * nh:512 * (nh + 1)], ALU.add),
                    tag="oadd1", r=[bt, "b2bc"], w=[f"outt{q}"])
                rec.add("vector", lambda e, nh=nh, obuf=obuf, q=q:
                        e.tensor_tensor(
                    obuf[:, 512 * nh:512 * (nh + 1)],
                    obuf[:, 512 * nh:512 * (nh + 1)],
                    xt[q][:, 512 * nh:512 * (nh + 1)], ALU.add),
                    tag="oadd2", r=[f"outt{q}", f"xt{q}"], w=[f"outt{q}"])
            d_y = rec.dma("sync", y[128 * ti:128 * (ti + 1), :], obuf[:],
                          tag=f"y{ti}", r=[f"outt{q}"])
            y_done.append(d_y)
    return y_done



def build_l0_attn(nc, al, rec, x, y, wqk, wv, wfc, bqk_r, bv_c, bfc_r,
                  gam_r, bet_r, pfx="f1_", filler=None):
    S, P = al.sb, al.ps

    ident = S(pfx + "ident", [128, 128], BF16)
    ones1 = S(pfx + "ones1", [1, 128], F32)
    wqk_s = S(pfx + "wqk", [128, 8192], BF16)
    wv_s = S(pfx + "wv", [128, 4096], BF16)
    wfc_s = S(pfx + "wfc", [128, 4096], BF16)
    wstage = [S(pfx + f"wstage{i}", [128, 1024], F32) for i in range(2)]
    brow = S(pfx + "brow", [1, 4096], F32)
    bbc = S(pfx + "bbc", [128, 4096], F32)
    bv_s = S(pfx + "bv", [128, 4], F32)
    xt = [S(pfx + f"xt{i}", [128, D], F32) for i in range(2)]
    sq_scr = S(pfx + "sq", [128, D], F32)
    stats = S(pfx + "stats", [128, 8], F32)
    xln16 = [S(pfx + f"xln16{i}", [128, D], BF16) for i in range(2)]
    xf = [S(pfx + f"xf{i}", [128, D], BF16) for i in range(2)]
    qk16 = S(pfx + "qk16", [128, 1024], BF16)
    vfm = [S(pfx + f"vfm{c}", [128, T], BF16) for c in range(4)]
    mu_all = S(pfx + "mu", [128, 32], F32)
    rs_all = S(pfx + "rs", [128, 32], F32)
    sc32 = S(pfx + "sc32", [64, 512], F32)
    smax = S(pfx + "smax", [64, 8], F32)
    ssum = S(pfx + "ssum", [64, 8], F32)
    srec = S(pfx + "srec", [64, 8], F32)
    attn16a = S(pfx + "attn16a", [64, 512], BF16)
    attn16 = S(pfx + "attn16", [128, 256], BF16)
    o16 = S(pfx + "o16", [128, 2048], BF16)
    ev1 = S(pfx + "ev1", [128, D], F32)
    outt = [S(pfx + f"outt{i}", [128, D], F32) for i in range(2)]
    st2 = S(pfx + "st2", [128, 8], F32)
    cpool = S(pfx + "cpool", [128, 8], F32)

    p_tr = P(pfx + "ptr", [128, 1024], BF16)
    p_qk = P(pfx + "pqk", [128, 1024])
    p_sc = P(pfx + "psc", [64, 512])
    p_v = P(pfx + "pv", [128, 512])
    p_o = P(pfx + "po", [128, 512])
    p_fc = P(pfx + "pfc", [128, 1024])
    p_bc = p_fc

    ln = LNHelper(rec, sq_scr, stats, name=pfx + "ln")
    ln2 = LNHelper(rec, sq_scr, st2, name=pfx + "lm", sq_tok=pfx + "lnsq")
    cp = ConstPool(rec, cpool)
    eps5 = cp.get(1e-5)
    eps6 = cp.get(1e-6)
    cp_op = cp.op

    o_id = make_identity_ops(rec, ident)
    rec.add("gpsimd", lambda e: e.memset(ones1[:], 1.0), tag="ones", w=["ones1"])

    rec.dma("sync", brow[0:1, 0:1024], bqk_r[:], w=["brow"])
    rec.dma("sync", brow[0:1, 1024:2048], bfc_r[:], w=["brow"])
    rec.dma("sync", brow[0:1, 2048:3072], gam_r[:], w=["brow"])
    rec.dma("sync", brow[0:1, 3072:4096], bet_r[:], w=["brow"])
    rec.dma("sync", bv_s[:], bv_c[:], w=["bv"])
    for j in range(8):
        rec.add("tensor", lambda e, j=j: e.matmul(
            p_bc[:, 0:512], ones1[0:1, :], brow[0:1, 512 * j:512 * (j + 1)],
            start=True, stop=True), tag="bbc",
            r=["ones1", "brow"], w=["pfc"])
        rec.add("scalar", lambda e, j=j: e.copy(
            bbc[:, 512 * j:512 * (j + 1)], p_bc[:, 0:512]), tag="bcp",
            r=["pfc"], w=["bbc"])
    BQK = bbc[:, 0:1024]
    BFC = bbc[:, 1024:2048]
    GAM = bbc[:, 2048:3072]
    BET = bbc[:, 3072:4096]

    for k in range(8):
        rec.dma("sync", wstage[k % 2][:], wqk[128 * k:128 * (k + 1), :],
                tag=f"wqkld{k}", w=[f"wstage{k % 2}"])
        rec.add("vector", lambda e, k=k: e.tensor_copy(
            wqk_s[:, 1024 * k:1024 * (k + 1)], wstage[k % 2][:]),
            tag=f"wqkcast{k}", r=[f"wstage{k % 2}"], w=["wqk_s"])
    for k in range(8):
        rec.dma("sync", wstage[k % 2][:, 0:512], wv[128 * k:128 * (k + 1), :],
                tag=f"wvld{k}", w=[f"wstage{k % 2}"])
        rec.add("vector", lambda e, k=k: e.tensor_copy(
            wv_s[:, 512 * k:512 * (k + 1)], wstage[k % 2][:, 0:512]),
            tag=f"wvcast{k}", r=[f"wstage{k % 2}"], w=["wv_s"])
    for j in range(4):
        rec.dma("sync", wstage[j % 2][:], wfc[128 * j:128 * (j + 1), :],
                tag=f"wfcld{j}", w=[f"wstage{j % 2}"])
        rec.add("vector", lambda e, j=j: e.tensor_copy(
            wfc_s[:, 1024 * j:1024 * (j + 1)], wstage[j % 2][:]),
            tag=f"wfccast{j}", r=[f"wstage{j % 2}"], w=["wfc_s"])

    def l0a_front(ti):
        q = ti % 2
        rec.dma("sync", xt[q][:], x[128 * ti:128 * (ti + 1), :],
                tag=f"x{ti}", w=[f"xt{q}"])
        ln.emit(xt[q][:], f"xt{q}", xln16[q][:], f"xln16{q}", eps5,
                const_dep=cp_op)
        rec.add("vector", lambda e, ti=ti: e.tensor_copy(
            mu_all[:, ti:ti + 1], stats[:, 2:3]), tag="stash_mu",
            r=[pfx + "lnc"], w=["mu_all"])
        rec.add("vector", lambda e, ti=ti: e.tensor_copy(
            rs_all[:, ti:ti + 1], stats[:, 7:8]), tag="stash_rs",
            r=[pfx + "lnh"], w=["rs_all"])

    def l0a_mid(ti):
        q = ti % 2
        transpose_to_fm(rec, ident, o_id, xln16[q], f"xln16{q}", p_tr, "ptr",
                        xf[q], f"xf{q}", 0, col_stride=128)
        for n in range(2):
            for c in range(8):
                rec.add("tensor", lambda e, n=n, c=c, q=q: e.matmul(
                    p_qk[:, 512 * n:512 * (n + 1)],
                    xf[q][:, 128 * c:128 * (c + 1)],
                    wqk_s[:, 1024 * c + 512 * n:1024 * c + 512 * (n + 1)],
                    start=(c == 0), stop=(c == 7)),
                    tag=f"qk{ti}_{n}", r=[f"xf{q}", "wqk_s"], w=["pqk"])
        for n in range(2):
            rec.add("vector", lambda e, n=n: e.tensor_tensor(
                qk16[:, 512 * n:512 * (n + 1)], p_qk[:, 512 * n:512 * (n + 1)],
                BQK[:, 512 * n:512 * (n + 1)], ALU.add),
                tag="qkbias", r=["pqk", "bbc"], w=["qk16"])
        for h in range(8):
            rec.add("tensor", lambda e, h=h, ti=ti: e.matmul(
                p_sc[:, 64 * h:64 * (h + 1)],
                qk16[:, 64 * h:64 * (h + 1)],
                qk16[:, 512 + 64 * h:512 + 64 * (h + 1)],
                start=(ti == 0 and h == 0), stop=(ti == NT - 1 and h == 7)),
                tag=f"sc{ti}_{h}", r=["qk16"], w=["psc"])
        if ti + 1 < NT:
            l0a_front(ti + 1)
        for c in range(4):
            for k in range(8):
                rec.add("tensor", lambda e, c=c, k=k, q=q: e.matmul(
                    p_v[:, 128 * c:128 * (c + 1)],
                    wv_s[:, 512 * k + 128 * c:512 * k + 128 * (c + 1)],
                    xf[q][:, 128 * k:128 * (k + 1)],
                    start=(k == 0), stop=(k == 7)),
                    tag=f"v{ti}_{c}", r=[f"xf{q}", "wv_s"], w=["pv"])
        for c in range(4):
            rec.add("vector", lambda e, c=c, ti=ti: e.tensor_scalar(
                vfm[c][:, 128 * ti:128 * (ti + 1)], p_v[:, 128 * c:128 * (c + 1)],
                bv_s[:, c:c + 1], None, ALU.add),
                tag=f"vbias{ti}_{c}", r=["pv", "bv"], w=[f"vfm{c}"])
        if filler is not None:
            filler(5)

    l0a_front(0)
    for ti in range(NT):
        l0a_mid(ti)

    rec.add("vector", lambda e: e.tensor_copy(sc32[:], p_sc[:]),
            tag="scev", r=["psc"], w=["sc32"])
    for h in range(8):
        cs = slice(64 * h, 64 * (h + 1))
        rec.add("vector", lambda e, cs=cs, h=h: e.reduce_max(
            smax[:, h:h + 1], sc32[:, cs], axis=AX.X),
            tag="smx", r=["sc32"], w=["smax"])
        rec.add("vector", lambda e, h=h: e.tensor_scalar(
            smax[:, h:h + 1], smax[:, h:h + 1], -0.125, None, ALU.mult),
            tag="smxn", r=["smax"], w=["smax"])
        rec.add("scalar", lambda e, cs=cs, h=h: e.activation(
            sc32[:, cs], sc32[:, cs], AF.Exp, bias=smax[:, h:h + 1],
            scale=0.125, accum_out=ssum[:, h:h + 1]),
            tag="sexp", r=["sc32", "smax"], w=["sc32", "ssum"])
        rec.add("vector", lambda e, h=h: e.reciprocal(
            srec[:, h:h + 1], ssum[:, h:h + 1]),
            tag="srec", r=["ssum"], w=["srec"])
        rec.add("vector", lambda e, cs=cs, h=h: e.tensor_scalar(
            attn16a[:, cs], sc32[:, cs], srec[:, h:h + 1], None, ALU.mult),
            tag="snorm", r=["sc32", "srec"], w=["attn16a"])
    for c in range(4):
        rec.add("vector", lambda e, c=c: e.tensor_copy(
            attn16[0:64, 64 * c:64 * (c + 1)],
            attn16a[:, 128 * c:128 * c + 64]),
            tag="apack_e", r=["attn16a"], w=["attn16"])
        rec.dma("sync", attn16[64:128, 64 * c:64 * (c + 1)],
                attn16a[:, 128 * c + 64:128 * c + 128],
                tag="apack_o", r=["attn16a"], w=["attn16"])

    y_done = []
    for g in range(NP):
        for c in range(4):
            rec.add("tensor", lambda e, c=c, g=g: e.matmul(
                p_o[0:64, :], attn16[0:64, 64 * c:64 * (c + 1)],
                vfm[c][0:64, 512 * g:512 * (g + 1)],
                start=True, stop=True, tile_position=(0, 0)),
                tag=f"o{g}_{c}a", r=["attn16", f"vfm{c}"], w=["po"])
            rec.add("tensor", lambda e, c=c, g=g: e.matmul(
                p_o[64:128, :], attn16[64:128, 64 * c:64 * (c + 1)],
                vfm[c][64:128, 512 * g:512 * (g + 1)],
                start=True, stop=True, tile_position=(64, 64)),
                tag=f"o{g}_{c}b", r=["attn16", f"vfm{c}"], w=["po"])
            rec.add("vector", lambda e, c=c: e.tensor_copy(
                o16[:, 512 * c:512 * (c + 1)], p_o[:]),
                tag=f"oev{g}_{c}", r=["po"], w=["o16"])
        for tl in range(4):
            ti = 4 * g + tl
            q = tl % 2
            rec.dma("sync", xt[q][:], x[128 * ti:128 * (ti + 1), :],
                    tag=f"xb{ti}", w=[f"xt{q}"])
            for n in range(2):
                for c in range(4):
                    rec.add("tensor", lambda e, n=n, c=c, tl=tl: e.matmul(
                        p_fc[:, 512 * n:512 * (n + 1)],
                        o16[:, 512 * c + 128 * tl:512 * c + 128 * (tl + 1)],
                        wfc_s[:, 1024 * c + 512 * n:1024 * c + 512 * (n + 1)],
                        start=(c == 0), stop=(c == 3)),
                        tag=f"fc{ti}_{n}", r=["o16", "wfc_s"], w=["pfc"])
            rec.add("vector", lambda e: e.tensor_tensor(
                ev1[:], p_fc[:], BFC[:], ALU.add),
                tag="fcb", r=["pfc", "bbc"], w=["ev1"])
            rec.add("vector", lambda e, ti=ti, q=q: e.tensor_scalar(
                sq_scr[:], xt[q][:], mu_all[:, ti:ti + 1], rs_all[:, ti:ti + 1],
                ALU.add, ALU.mult),
                tag="xlnr", r=[f"xt{q}", "mu_all", "rs_all"], w=[pfx + "lnsq"])
            rec.add("vector", lambda e: e.tensor_tensor(
                ev1[:], ev1[:], sq_scr[:], ALU.add),
                tag="fcxln", r=["ev1", pfx + "lnsq"], w=["ev1"])
            ln2.emit(ev1[:], "ev1", outt[q][:], f"outt{q}", eps6, const_dep=cp_op)
            rec.add("vector", lambda e, q=q: e.tensor_tensor(
                outt[q][:], outt[q][:], GAM[:], ALU.mult),
                tag="gmul", r=[f"outt{q}", "bbc"], w=[f"outt{q}"])
            rec.add("vector", lambda e, q=q: e.tensor_tensor(
                outt[q][:], outt[q][:], BET[:], ALU.add),
                tag="badd", r=[f"outt{q}", "bbc"], w=[f"outt{q}"])
            rec.add("vector", lambda e, q=q: e.tensor_tensor(
                outt[q][:], outt[q][:], xt[q][:], ALU.add),
                tag="resid", r=[f"outt{q}", f"xt{q}"], w=[f"outt{q}"])
            d_y = rec.dma("sync", y[128 * ti:128 * (ti + 1), :], outt[q][:],
                          tag=f"y{ti}", r=[f"outt{q}"])
            y_done.append(d_y)
            if filler is not None:
                filler(3)
    return y_done



NH0 = 16
HD0 = 64


def build_l0_mla(nc, al, rec, x, s1_out, xin_hbm, k_hbm, q_hbm, v_hbm,
                 kvdT, qdT, ropekT, kuT, qmixT, vuT, owT, ln1_r,
                 cos_t, sin_t, pmat, pfx="f2_"):
    S, P = al.sb, al.ps

    ident = S(pfx + "ident", [128, 128], BF16)
    ones1 = S(pfx + "ones1", [1, 128], F32)
    ident32 = S(pfx + "ident32", [128, 128], F32)
    kvq_s = S(pfx + "kvq", [128, 4096], BF16)
    ropek_s = S(pfx + "ropek", [128, 8192], BF16)
    kuT_s = S(pfx + "kuT", [128, 2048], BF16)
    qmix_s = S(pfx + "qmix", [128, 2048], BF16)
    vuT_s = S(pfx + "vuT", [128, 2048], BF16)
    owT_s = S(pfx + "owT", [64, 16384], BF16)
    p16 = S(pfx + "p16", [128, 128], BF16)
    cos_s = S(pfx + "cos", [128, T], F32)
    sin_s = S(pfx + "sin", [128, T], F32)
    sel_s = S(pfx + "sel", [128, 2], F32)
    wstage = [S(pfx + f"wstage{i}", [128, 2048], F32) for i in range(2)]
    ln1bc = S(pfx + "ln1bc", [128, D], F32)
    lrow = S(pfx + "lrow", [1, D], F32)
    xt = [S(pfx + f"xt{i}", [128, D], F32) for i in range(2)]
    sq_scr = S(pfx + "sq", [128, D], F32)
    stats = S(pfx + "stats", [128, 8], F32)
    st2 = S(pfx + "st2", [128, 8], F32)
    xin_t = S(pfx + "xin", [128, D], F32)
    ml16 = S(pfx + "ml16", [128, D], BF16)
    mlf = [S(pfx + f"mlf{i}", [128, D], BF16) for i in range(2)]
    lat16 = S(pfx + "lat16", [128, 512], BF16)
    kx16 = [S(pfx + f"kx16{i}", [128, 128], BF16) for i in range(2)]
    t1f = [S(pfx + f"t1f{i}", [128, 128], F32) for i in range(2)]
    t2f = [S(pfx + f"t2f{i}", [128, 128], F32) for i in range(2)]
    eff16 = [S(pfx + f"eff16{i}", [128, 128], BF16) for i in range(2)]
    q2f = S(pfx + "q2f", [128, 128], F32)
    nrm2 = S(pfx + "nrm2", [128, 2], F32)
    qmax = S(pfx + "qmax", [128, 16], F32)
    kmax = S(pfx + "kmax", [128, 16], F32)
    vst = S(pfx + "vst", [128, 1040], BF16)
    mrow = S(pfx + "mrow", [1, 32], F32)
    mh = S(pfx + "mh", [128, 16], F32)
    bias_t = S(pfx + "bias", [128, 16], F32)
    qbuf = S(pfx + "qbuf", [128, 4096], BF16)
    kbuf = [S(pfx + f"kbuf{i}", [128, 128], BF16) for i in range(4)]
    vbuf = [S(pfx + f"vbuf{i}", [128, 130], BF16) for i in range(4)]
    pa16 = [S(pfx + f"pa16{i}", [128, 512], BF16) for i in range(2)]
    pb16 = [S(pfx + f"pb16{i}", [128, 512], BF16) for i in range(2)]
    mask01 = S(pfx + "mask01", [128, 128], BF16)
    rs_t = S(pfx + "rs_t", [128, 1024], F32)
    bc_s = S(pfx + "bc_s", [64, 1024], F32)
    ones64 = S(pfx + "ones64", [128, 64], F32)
    h_s = S(pfx + "h_s", [64, 8192], BF16)
    s1t = [S(pfx + f"s1t{i}", [128, D], F32) for i in range(2)]
    cpool = S(pfx + "cpool", [128, 8], F32)

    p_tr = P(pfx + "ptr", [128, 1024], BF16)
    b0 = P(pfx + "b0", [128, 512])
    b1 = P(pfx + "b1", [128, 512])
    b2 = P(pfx + "b2", [128, 512])
    b3 = P(pfx + "b3", [128, 512])
    b4 = P(pfx + "b4", [128, 512])
    p_v2 = P(pfx + "pv2", [128, 1024])
    p_lat = b0
    p_k = b1
    p_px = b2
    p_n = b3
    p_sA = b0
    p_sB = b1
    p_oA = b2
    p_oB = b3
    p_bc2 = b4

    ln = LNHelper(rec, sq_scr, stats, name=pfx + "ln")
    lnb = LNHelper(rec, sq_scr, st2, name=pfx + "lm", sq_tok=pfx + "lnsq")
    cp = ConstPool(rec, cpool)
    eps5 = cp.get(1e-5)
    cp_op = cp.op

    o_id = make_identity_ops(rec, ident)
    rec.add("gpsimd", lambda e: e.memset(ones1[:], 1.0), tag="ones", w=["ones1"])
    rec.add("gpsimd", lambda e: e.memset(ones64[:], 1.0), tag="ones64", w=["ones64"])
    o_i32a = rec.add("gpsimd", lambda e: e.memset(ident32[:], 0.0), tag="i32a",
                     w=["id32"])
    rec.add("gpsimd", lambda e: e.affine_select(
        out=ident32[:], in_=ident32[:], compare_op=ALU.not_equal, fill=1.0,
        base=0, pattern=[[-1, 128]], channel_multiplier=1), tag="i32b",
        r=["id32"], w=["id32"])
    rec.add("gpsimd", lambda e: e.memset(sel_s[:, 0:1], 1.0), tag="sel0", w=["sel"])
    rec.add("gpsimd", lambda e: e.memset(sel_s[:, 1:2], 0.0), tag="sel1", w=["sel"])
    rec.add("gpsimd", lambda e: e.memset(sel_s[64:128, 0:1], 0.0), tag="sel2", w=["sel"])
    rec.add("gpsimd", lambda e: e.memset(sel_s[64:128, 1:2], 1.0), tag="sel3", w=["sel"])
    rec.add("gpsimd", lambda e: e.memset(mask01[:], 1.0), tag="m01a", w=["m01"])
    rec.add("gpsimd", lambda e: e.affine_select(
        out=mask01[:], in_=mask01[:], compare_op=ALU.is_ge, fill=0.0,
        base=0, pattern=[[1, 128]], channel_multiplier=-1), tag="m01b",
        r=["m01"], w=["m01"])
    rec.add("gpsimd", lambda e: e.memset(qmax[:], 0.0), tag="qm0", w=["qmax"])
    rec.add("gpsimd", lambda e: e.memset(kmax[:], 0.0), tag="km0", w=["kmax"])

    rec.dma("sync", cos_s[:], cos_t[:], w=["cos"])
    rec.dma("sync", sin_s[:], sin_t[:], w=["sin"])
    rec.dma("sync", lrow[:], ln1_r[:], w=["lrow"])
    for j in range(2):
        rec.add("tensor", lambda e, j=j: e.matmul(
            p_v2[:, 0:512], ones1[0:1, :], lrow[0:1, 512 * j:512 * (j + 1)],
            start=True, stop=True), tag="lbc", r=["ones1", "lrow"], w=["pv2"])
        rec.add("scalar", lambda e, j=j: e.copy(
            ln1bc[:, 512 * j:512 * (j + 1)], p_v2[:, 0:512]), tag="lbcp",
            r=["pv2"], w=["ln1bc"])
    rec.dma("sync", wstage[0][:, 0:128], pmat[:], w=["wstage0"])
    rec.add("vector", lambda e: e.tensor_copy(p16[:], wstage[0][:, 0:128]),
            tag="p16", r=["wstage0"], w=["p16"])

    def loadw(dst, dst_tok, src, rows, cols, tag):
        nch = rows // 128
        for k in range(nch):
            stg = wstage[k % 2]
            rec.dma("sync", stg[:, 0:cols], src[128 * k:128 * (k + 1), :],
                    tag=f"{tag}ld{k}", w=[f"wstage{k % 2}"])
            rec.add("vector", lambda e, k=k, stg=stg: e.tensor_copy(
                dst[:, cols * k:cols * (k + 1)], stg[:, 0:cols]),
                tag=f"{tag}c{k}", r=[f"wstage{k % 2}"], w=[dst_tok])

    for k in range(8):
        stg = wstage[k % 2]
        rec.dma("sync", stg[:, 0:256], kvdT[128 * k:128 * (k + 1), :],
                tag=f"kvld{k}", w=[f"wstage{k % 2}"])
        rec.dma("sync", stg[:, 256:512], qdT[128 * k:128 * (k + 1), :],
                tag=f"qdld{k}", w=[f"wstage{k % 2}"])
        rec.add("vector", lambda e, k=k, stg=stg: e.tensor_copy(
            kvq_s[:, 512 * k:512 * (k + 1)], stg[:, 0:512]),
            tag=f"kvqc{k}", r=[f"wstage{k % 2}"], w=["kvq"])
    loadw(ropek_s, "ropek", ropekT, 1024, 1024, "rk")
    loadw(kuT_s, "kuT", kuT, 256, 1024, "ku")
    loadw(qmix_s, "qmix", qmixT, 256, 1024, "qm")
    loadw(vuT_s, "vuT", vuT, 256, 1024, "vu")
    for k in range(8):
        stg = wstage[k % 2]
        rec.dma("sync", stg[0:64, 0:2048], owT[0:64, 2048 * k:2048 * (k + 1)],
                tag=f"owld{k}", w=[f"wstage{k % 2}"])
        rec.add("vector", lambda e, k=k, stg=stg: e.tensor_copy(
            owT_s[:, 2048 * k:2048 * (k + 1)], stg[0:64, 0:2048]),
            tag=f"owc{k}", r=[f"wstage{k % 2}"], w=["owT"])

    def rope_chunk(src_psum, src_tok, j, ti, dst_hbm, q2acc):
        par = j % 2
        kx = kx16[par]
        t1 = t1f[par]
        t2 = t2f[par]
        ef = eff16[par]
        pxb, pxt = (p_px, "b2") if par == 0 else (p_n, "b3")
        rec.add("vector", lambda e, kx=kx: e.tensor_copy(
            kx[:], src_psum[:, 0:128]),
            tag="ropeev", r=[src_tok], w=[f"kx16{par}"])
        rec.add("tensor", lambda e, kx=kx, pxb=pxb: e.matmul(
            pxb[:, 0:128], p16[:], kx[:], start=True, stop=True),
            tag="ropepx", r=["p16", f"kx16{par}"], w=[pxt])
        rec.add("vector", lambda e, ti=ti, kx=kx, t1=t1: e.tensor_tensor(
            t1[:], kx[:], cos_s[:, 128 * ti:128 * (ti + 1)], ALU.mult),
            tag="ropet1", r=[f"kx16{par}", "cos"], w=[f"t1f{par}"])
        rec.add("vector", lambda e, ti=ti, t2=t2, pxb=pxb: e.tensor_tensor(
            t2[:], pxb[:, 0:128], sin_s[:, 128 * ti:128 * (ti + 1)], ALU.mult),
            tag="ropet2", r=[pxt, "sin"], w=[f"t2f{par}"])
        rec.add("vector", lambda e, t1=t1, t2=t2, ef=ef: e.tensor_tensor(
            ef[:], t1[:], t2[:], ALU.add),
            tag="ropeadd", r=[f"t1f{par}", f"t2f{par}"], w=[f"eff16{par}"])
        rec.dma("sync", dst_hbm, ef[:], tag="effsp", r=[f"eff16{par}"],
                w=[f"hbm_eff"])
        if q2acc is not None:
            rec.add("scalar", lambda e, ef=ef: e.activation(
                q2f[:], ef[:], AF.Square),
                tag="q2", r=[f"eff16{par}"], w=["q2f"])
            qb, qbt = (p_n, "b3") if par == 0 else (p_px, "b2")
            rec.add("tensor", lambda e, qb=qb: e.matmul(
                qb[:, 0:2], q2f[:], sel_s[:], start=True, stop=True),
                tag="q2mm", r=["q2f", "sel"], w=[qbt])
            rec.add("vector", lambda e, j=j, qb=qb: e.tensor_tensor(
                q2acc[0][:, 2 * j:2 * j + 2], q2acc[0][:, 2 * j:2 * j + 2],
                qb[:, 0:2], ALU.max), tag="q2max",
                r=[q2acc[1], qbt], w=[q2acc[1]])

    def mla_front_a(ti):
        q = ti % 2
        rec.dma("sync", xt[q][:], x[128 * ti:128 * (ti + 1), :],
                tag=f"x{ti}", w=[f"xt{q}"])
        ln.emit(xt[q][:], f"xt{q}", xin_t[:], "xin_t", eps5, const_dep=cp_op)
        rec.dma("gpsimd", xin_hbm[128 * ti:128 * (ti + 1), :], xin_t[:],
                tag=f"xinsp{ti}", r=["xin_t"], w=[f"hbm_xin{ti}"])

    def mla_front_b(ti):
        lnb.emit(xin_t[:], "xin_t", sq_scr[:, 0:D], pfx + "lnsq", eps5,
                 const_dep=cp_op)

    def mla_front_c(ti):
        rec.add("vector", lambda e: e.tensor_tensor(
            ml16[:], sq_scr[:, 0:D], ln1bc[:], ALU.mult),
            tag="ml16", r=[pfx + "lnsq", "ln1bc"], w=["ml16"])

    def mla_front_d(ti):
        q = ti % 2
        transpose_to_fm(rec, ident, o_id, ml16, "ml16", p_tr, "ptr",
                        mlf[q], f"mlf{q}", 0, col_stride=128)

    def mla_mid(ti):
        q = ti % 2
        nxt = ti + 1 if ti + 1 < NT else None
        for b in range(4):
            for k in range(8):
                rec.add("tensor", lambda e, b=b, k=k, q=q: e.matmul(
                    p_lat[:, 128 * b:128 * (b + 1)],
                    kvq_s[:, 512 * k + 128 * b:512 * k + 128 * (b + 1)],
                    mlf[q][:, 128 * k:128 * (k + 1)],
                    start=(k == 0), stop=(k == 7)),
                    tag=f"lat{ti}_{b}", r=["kvq", f"mlf{q}"], w=["b0"])
        rec.add("vector", lambda e: e.tensor_copy(lat16[:], p_lat[:]),
                tag="latev", r=["b0"], w=["lat16"])
        for j in range(8):
            ckb, ckt = (b1, "b1") if j % 2 == 0 else (b0, "b0")
            for k in range(8):
                rec.add("tensor", lambda e, j=j, k=k, ckb=ckb, q=q: e.matmul(
                    ckb[:, 0:128], ropek_s[:, 1024 * k + 128 * j:1024 * k + 128 * (j + 1)],
                    mlf[q][:, 128 * k:128 * (k + 1)],
                    start=(k == 0), stop=False),
                    tag=f"krope{ti}_{j}", r=["ropek", f"mlf{q}"], w=[ckt])
            for l in range(2):
                rec.add("tensor", lambda e, j=j, l=l, ckb=ckb: e.matmul(
                    ckb[:, 0:128], kuT_s[:, 1024 * l + 128 * j:1024 * l + 128 * (j + 1)],
                    lat16[:, 128 * l:128 * (l + 1)],
                    start=False, stop=(l == 1)),
                    tag=f"knr{ti}_{j}", r=["kuT", "lat16"], w=[ckt])
            rope_chunk(ckb, ckt, j, ti,
                       k_hbm[ti, j, :, :], (kmax, "kmax"))
            if nxt is not None:
                if j == 1:
                    mla_front_a(nxt)
                elif j == 3:
                    mla_front_b(nxt)
                elif j == 5:
                    mla_front_c(nxt)
                elif j == 7:
                    mla_front_d(nxt)
        for j in range(8):
            ckb, ckt = (b1, "b1") if j % 2 == 0 else (b0, "b0")
            for l in range(2):
                rec.add("tensor", lambda e, j=j, l=l, ckb=ckb: e.matmul(
                    ckb[:, 0:128], qmix_s[:, 1024 * l + 128 * j:1024 * l + 128 * (j + 1)],
                    lat16[:, 256 + 128 * l:256 + 128 * (l + 1)],
                    start=(l == 0), stop=(l == 1)),
                    tag=f"qmix{ti}_{j}", r=["qmix", "lat16"], w=[ckt])
            rope_chunk(ckb, ckt, j, ti,
                       q_hbm[ti, j, :, :], (qmax, "qmax"))
        for n in range(2):
            for l in range(2):
                rec.add("tensor", lambda e, n=n, l=l: e.matmul(
                    p_v2[:, 512 * n:512 * (n + 1)],
                    lat16[:, 128 * l:128 * (l + 1)],
                    vuT_s[:, 1024 * l + 512 * n:1024 * l + 512 * (n + 1)],
                    start=(l == 0), stop=(l == 1)),
                    tag=f"v{ti}_{n}", r=["lat16", "vuT"], w=["pv2"])
        rec.add("vector", lambda e: e.tensor_copy(
            vst[:].rearrange("p (h c) -> p h c", c=65)[:, :, 0:64],
            p_v2[:].rearrange("p (h c) -> p h c", c=64)),
            tag="vev", r=["pv2"], w=["vst"])
        rec.add("vector", lambda e: e.memset(
            vst[:].rearrange("p (h c) -> p h c", c=65)[:, :, 64:65], 1.0),
            tag="vones", r=[], w=["vst"])
        rec.dma("sync", v_hbm[ti, :, :], vst[:], tag=f"vsp{ti}",
                r=["vst"], w=[f"hbm_v{ti}"])

    mla_front_a(0)
    mla_front_b(0)
    mla_front_c(0)
    mla_front_d(0)
    for ti in range(NT):
        mla_mid(ti)

    def global_max(acc, acc_tok, out_col):
        rec.add("tensor", lambda e, acc=acc: e.transpose(
            p_k[0:16, 0:128], acc[:], ident32[:]), tag="gmtr",
            r=[acc_tok, "id32"], w=["b1"])
        rec.add("vector", lambda e, out_col=out_col: e.reduce_max(
            mh[0:16, out_col:out_col + 1], p_k[0:16, 0:128], axis=AX.X),
            tag="gmred", r=["b1"], w=["mh"])

    global_max(qmax, "qmax", 0)
    global_max(kmax, "kmax", 1)
    rec.add("scalar", lambda e: e.activation(
        mh[0:16, 2:4], mh[0:16, 0:2], AF.Sqrt), tag="gmsqrt",
        r=["mh"], w=["mh"])
    rec.add("vector", lambda e: e.tensor_tensor(
        mh[0:16, 4:5], mh[0:16, 2:3], mh[0:16, 3:4], ALU.mult),
        tag="mhat", r=["mh"], w=["mh"])
    rec.add("vector", lambda e: e.tensor_scalar(
        mh[0:16, 5:6], mh[0:16, 4:5], -0.125, None, ALU.mult),
        tag="mbias", r=["mh"], w=["mh"])
    rec.dma("sync", mrow[0:1, 0:16], mh[0:16, 5:6], tag="mrow",
            r=["mh"], w=["mrow"])
    rec.add("tensor", lambda e: e.matmul(
        p_k[:, 0:16], ones1[0:1, :], mrow[0:1, 0:16],
        start=True, stop=True), tag="mbc", r=["ones1", "mrow"], w=["b1"])
    rec.add("scalar", lambda e: e.copy(bias_t[:], p_k[:, 0:16]), tag="mbcp",
            r=["b1"], w=["bias"])

    for g in range(NP):
        for j in range(8):
            for t in range(4):
                rec.dma("sync", qbuf[:, 512 * j + 128 * t:512 * j + 128 * (t + 1)],
                        q_hbm[4 * g + t, j, :, :],
                        tag=f"ql{g}_{j}_{t}", r=["hbm_eff"], w=[f"qbuf{j}"])
        for j in range(8):
            nkc = 4 * (g + 1)

            def mla_loads(kc):
                rec.dma("sync", kbuf[kc % 4][:], k_hbm[kc, j, :, :],
                        tag=f"kl{g}_{j}_{kc}", r=["hbm_eff"],
                        w=[f"kb{kc % 4}"])
                rec.dma("sync", vbuf[kc % 4][:],
                        v_hbm[kc, :, 130 * j:130 * (j + 1)],
                        tag=f"vl{g}_{j}_{kc}", r=[f"hbm_v{kc}"],
                        w=[f"vb{kc % 4}"])

            def mla_parts(kc):
                q0 = 128 * max(0, kc - 4 * g)
                parts = []
                for h2 in range(2):
                    if h2 == 0:
                        p_s, st = ((p_sA, "b0") if kc % 2 == 0
                                   else (b4, "b4"))
                        px16 = pa16[kc % 2]
                    else:
                        p_s, st = ((p_sB, "b1") if kc % 2 == 0
                                   else (p_v2[:, 0:512], "pv2"))
                        px16 = pb16[kc % 2]
                    parts.append((h2, p_s, st, px16, f"px{h2}_{kc % 2}", q0))
                return parts

            def mla_scores(kc):
                for h2, p_s, st, px16, pt, q0 in mla_parts(kc):
                    rec.add("tensor", lambda e, j=j, h2=h2, kc=kc, q0=q0,
                            p_s=p_s: e.matmul(
                        p_s[:, q0:512],
                        kbuf[kc % 4][64 * h2:64 * (h2 + 1), :],
                        qbuf[64 * h2:64 * (h2 + 1),
                             512 * j + q0:512 * (j + 1)],
                        start=True, stop=True,
                        tile_position=(64 * h2, 0)),
                        tag=f"s{g}_{2 * j + h2}_{kc}",
                        r=[f"kb{kc % 4}", f"qbuf{j}"], w=[st])

            def mla_exp(kc):
                for h2, p_s, st, px16, pt, q0 in mla_parts(kc):
                    h = 2 * j + h2
                    rec.add("scalar", lambda e, h=h, q0=q0, p_s=p_s,
                            px16=px16: e.activation(
                        px16[:, q0:512], p_s[:, q0:512], AF.Exp,
                        bias=bias_t[:, h:h + 1],
                        scale=0.125), tag=f"exp{g}_{h}_{kc}",
                        r=[st, "bias"], w=[pt])
                    if kc - 4 * g >= 0:
                        rec.add("vector", lambda e, q0=q0, px16=px16:
                                e.tensor_tensor(
                            px16[:, q0:q0 + 128],
                            px16[:, q0:q0 + 128],
                            mask01[:], ALU.mult), tag="diag",
                            r=[pt, "m01"], w=[pt])

            def mla_pv(kc):
                for h2, p_s, st, px16, pt, q0 in mla_parts(kc):
                    p_o = p_oA if h2 == 0 else p_oB
                    ot = "b2" if h2 == 0 else "b3"
                    rec.add("tensor", lambda e, h2=h2, kc=kc, q0=q0,
                            p_o=p_o, px16=px16, sp=(kc == nkc - 1): e.matmul(
                        p_o[0:65, q0:512],
                        vbuf[kc % 4][:, 65 * h2:65 * (h2 + 1)],
                        px16[:, q0:512],
                        start=(kc == 0), stop=sp),
                        tag=f"pv{g}_{2 * j + h2}_{kc}",
                        r=[f"vb{kc % 4}", pt], w=[ot])

            for kk in range(min(3, nkc)):
                mla_loads(kk)
            mla_scores(0)
            for kc in range(nkc):
                if kc + 3 < nkc:
                    mla_loads(kc + 3)
                if kc + 1 < nkc:
                    mla_scores(kc + 1)
                mla_exp(kc)
                mla_pv(kc)
            for h2 in range(2):
                h = 2 * j + h2
                p_o = p_oA if h2 == 0 else p_oB
                ot = "b2" if h2 == 0 else "b3"
                rec.add("vector", lambda e, h2=h2, p_o=p_o: e.reciprocal(
                    rs_t[64:65, 512 * h2:512 * (h2 + 1)],
                    p_o[64:65, :]), tag=f"rcp{g}_{h}",
                    r=[ot], w=["rs_t"])
                rec.add("tensor", lambda e, h2=h2: e.matmul(
                    p_bc2[0:64, :], ones64[64:65, 0:64],
                    rs_t[64:65, 512 * h2:512 * (h2 + 1)],
                    start=True, stop=True, tile_position=(64, 0)),
                    tag=f"rbc{g}_{h}", r=["ones64", "rs_t"], w=["b4"])
                rec.add("scalar", lambda e, h2=h2: e.copy(
                    bc_s[:, 512 * h2:512 * (h2 + 1)], p_bc2[0:64, :]),
                    tag=f"rbcp{g}_{h}", r=["b4"], w=["bc_s"])
                rec.add("vector", lambda e, h=h, h2=h2, p_o=p_o:
                        e.tensor_tensor(
                    h_s[0:64, 512 * h:512 * (h + 1)], p_o[0:64, :],
                    bc_s[:, 512 * h2:512 * (h2 + 1)], ALU.mult),
                    tag=f"hnrm{g}_{h}", r=[ot, "bc_s"], w=["h_s"])
        for tl in range(4):
            ti = 4 * g + tl
            q = tl % 2
            rec.dma("scalar", xt[q][:], xin_hbm[128 * ti:128 * (ti + 1), :],
                    tag=f"xinl{ti}", r=[f"hbm_xin{ti}"], w=[f"xt{q}"])
            for nh in range(2):
                for h in range(16):
                    rec.add("tensor", lambda e, h=h, nh=nh, tl=tl: e.matmul(
                        p_v2[:, 512 * nh:512 * (nh + 1)],
                        h_s[0:64, 512 * h + 128 * tl:512 * h + 128 * (tl + 1)],
                        owT_s[0:64, 1024 * h + 512 * nh:1024 * h + 512 * (nh + 1)],
                        start=(h == 0), stop=(h == 15)),
                        tag=f"ow{ti}_{nh}", r=["h_s", "owT"], w=["pv2"])
            rec.add("vector", lambda e, q=q: e.tensor_tensor(
                s1t[q][:], p_v2[:], xt[q][:], ALU.add),
                tag=f"s1{ti}", r=["pv2", f"xt{q}"], w=[f"s1t{q}"])
            rec.dma("sync", s1_out[128 * ti:128 * (ti + 1), :], s1t[q][:],
                    tag=f"s1sp{ti}", r=[f"s1t{q}"])




def build_moe_wconv(nc, al, rec, gateT, upT, downT, gbf, ubf, dbf, pfx="wc_"):
    S = al.sb
    stg = [S(pfx + f"stg{i}", [128, 2048], F32) for i in range(2)]
    ob = [S(pfx + f"ob{i}", [128, 2048], BF16) for i in range(2)]
    jobs = []
    for e in range(8):
        for k in range(8):
            jobs.append((gateT[e, 128 * k:128 * (k + 1), :],
                         gbf[e, 128 * k:128 * (k + 1), :], 2048))
            jobs.append((upT[e, 128 * k:128 * (k + 1), :],
                         ubf[e, 128 * k:128 * (k + 1), :], 2048))
        for m in range(16):
            jobs.append((downT[e, 128 * m:128 * (m + 1), :],
                         dbf[e, 128 * m:128 * (m + 1), :], 1024))
    emits = []
    for i, (src_ap, dst_ap, cols) in enumerate(jobs):
        def emit(i=i, src_ap=src_ap, dst_ap=dst_ap, cols=cols):
            b = i % 2
            rec.dma("gpsimd", stg[b][:, 0:cols], src_ap, tag=f"wcl{i}",
                    w=[pfx + f"s{b}"])
            eng = "vector" if i % 2 == 0 else "scalar"
            if eng == "vector":
                rec.add(eng, lambda e2, b=b, cols=cols: e2.tensor_copy(
                    ob[b][:, 0:cols], stg[b][:, 0:cols]), tag=f"wcc{i}",
                    r=[pfx + f"s{b}"], w=[pfx + f"o{b}"])
            else:
                rec.add(eng, lambda e2, b=b, cols=cols: e2.copy(
                    ob[b][:, 0:cols], stg[b][:, 0:cols]), tag=f"wcc{i}",
                    r=[pfx + f"s{b}"], w=[pfx + f"o{b}"])
            rec.dma("scalar", dst_ap, ob[b][:, 0:cols], tag=f"wcs{i}",
                    r=[pfx + f"o{b}"], w=[pfx + f"h{i}"])
        emits.append(emit)
    return emits



def build_l0_moe(nc, al, rec, s1, x1, x2, alf_hbm, gbf, ubf, dbf,
                 routerT, rbias_r, ln2_r, pfx="f3_"):
    S, P = al.sb, al.ps

    ident = S(pfx + "ident", [128, 128], BF16)
    ones1 = S(pfx + "ones1", [1, 128], F32)
    lrow = S(pfx + "lrow", [1, D], F32)
    ln2bc = S(pfx + "ln2bc", [128, D], F32)
    rbrow = S(pfx + "rbrow", [1, 7], F32)
    rbbc = S(pfx + "rbbc", [128, 7], F32)
    rT_s = S(pfx + "rT", [128, 56], BF16)
    wg_s = S(pfx + "wg", [128, 16384], BF16)
    wu_s = S(pfx + "wu", [128, 16384], BF16)
    wd_s = S(pfx + "wd", [128, 16384], BF16)
    a_s = S(pfx + "a_s", [128, 8192], BF16)
    af_s = [S(pfx + f"af{i}", [128, 4096], BF16) for i in range(2)]
    acc = S(pfx + "acc", [128, 8192], F32)
    xt = [S(pfx + f"xt{i}", [128, D], F32) for i in range(2)]
    sq_scr = S(pfx + "sq", [128, D], F32)
    stats = S(pfx + "stats", [128, 8], F32)
    al16 = S(pfx + "al16", [128, D], BF16)
    alf = S(pfx + "alf", [128, D], BF16)
    prb = S(pfx + "prb", [128, 16], F32)
    wgt_all = S(pfx + "wgt", [128, 256], F32)
    gs = S(pfx + "gs", [128, 512], F32)
    tmp = S(pfx + "tmp", [128, 512], F32)
    outt = [S(pfx + f"outt{i}", [128, D], F32) for i in range(2)]
    cpool = S(pfx + "cpool", [128, 8], F32)

    p_tr = P(pfx + "ptr", [128, 1024], BF16)
    b0 = P(pfx + "b0", [128, 512])
    b1 = P(pfx + "b1", [128, 512])
    b2 = P(pfx + "b2", [128, 512])
    b3 = P(pfx + "b3", [128, 512])

    ln = LNHelper(rec, sq_scr, stats, name=pfx + "ln")
    cp = ConstPool(rec, cpool)
    eps5 = cp.get(1e-5)
    cp_op = cp.op

    o_id = make_identity_ops(rec, ident)
    rec.add("gpsimd", lambda e: e.memset(ones1[:], 1.0), tag="ones", w=["ones1"])
    rec.add("gpsimd", lambda e: e.memset(wgt_all[:], 1.0), tag="wgt1", w=["wgt"])

    rec.dma("sync", lrow[:], ln2_r[:], w=["lrow"])
    rec.dma("sync", rbrow[:], rbias_r[:], w=["rbrow"])
    for j in range(2):
        rec.add("tensor", lambda e, j=j: e.matmul(
            b0[:], ones1[0:1, :], lrow[0:1, 512 * j:512 * (j + 1)],
            start=True, stop=True), tag="lbc", r=["ones1", "lrow"], w=["b0"])
        rec.add("scalar", lambda e, j=j: e.copy(
            ln2bc[:, 512 * j:512 * (j + 1)], b0[:]), tag="lbcp",
            r=["b0"], w=["ln2bc"])
    rec.add("tensor", lambda e: e.matmul(
        b0[:, 0:7], ones1[0:1, :], rbrow[0:1, :],
        start=True, stop=True), tag="rbc", r=["ones1", "rbrow"], w=["b0"])
    rec.add("scalar", lambda e: e.copy(rbbc[:], b0[:, 0:7]), tag="rbcp",
            r=["b0"], w=["rbbc"])
    for c in range(8):
        rec.dma("sync", sq_scr[:, 7 * c:7 * (c + 1)],
                routerT[128 * c:128 * (c + 1), :], w=[pfx + "lnsq"],
                tag=f"rld{c}")
    rec.add("vector", lambda e: e.tensor_copy(rT_s[:], sq_scr[:, 0:56]),
            tag="rcast", r=[pfx + "lnsq"], w=["rT"])

    for ti in range(NT):
        q = ti % 2
        rec.dma("sync", xt[q][:], s1[128 * ti:128 * (ti + 1), :],
                tag=f"s1l{ti}", w=[f"xt{q}"])
        ln.emit(xt[q][:], f"xt{q}", sq_scr[:, 0:D], pfx + "lnsq", eps5,
                const_dep=cp_op)
        rec.add("vector", lambda e: e.tensor_tensor(
            al16[:], sq_scr[:, 0:D], ln2bc[:], ALU.mult),
            tag="al16", r=[pfx + "lnsq", "ln2bc"], w=["al16"])
        transpose_to_fm(rec, ident, o_id, al16, "al16", p_tr, "ptr",
                        alf, "alf", 0, col_stride=128)
        for c in range(8):
            rec.dma("gpsimd", alf_hbm[c, :, 128 * ti:128 * (ti + 1)],
                    alf[:, 128 * c:128 * (c + 1)],
                    tag=f"alfsp{ti}_{c}", r=["alf"], w=[f"half{ti}_{c}"])
        for c in range(8):
            rec.add("tensor", lambda e, c=c: e.matmul(
                b1[:, 0:7], alf[:, 128 * c:128 * (c + 1)],
                rT_s[:, 7 * c:7 * (c + 1)],
                start=(c == 0), stop=(c == 7)),
                tag=f"rt{ti}", r=["alf", "rT"], w=["b1"])
        rec.add("vector", lambda e: e.tensor_tensor(
            prb[:, 8:15], b1[:, 0:7], rbbc[:], ALU.add),
            tag="rta", r=["b1", "rbbc"], w=["prb"])
        rec.add("scalar", lambda e: e.activation(
            prb[:, 0:7], prb[:, 8:15], AF.Sigmoid),
            tag="rsig", r=["prb"], w=["prb"])
        rec.add("vector", lambda e: e.reduce_max(
            prb[:, 8:9], prb[:, 0:7], axis=AX.X), tag="m1",
            r=["prb"], w=["prb"])
        rec.add("vector", lambda e: e.tensor_scalar(
            prb[:, 9:10] if False else tmp[:, 0:7], prb[:, 0:7],
            prb[:, 8:9], None, ALU.is_ge), tag="ge1",
            r=["prb"], w=["tmpa"])
        rec.add("vector", lambda e: e.scalar_tensor_tensor(
            tmp[:, 8:15], tmp[:, 0:7], -1e30, prb[:, 0:7], ALU.mult, ALU.add),
            tag="p2", r=["tmpa", "prb"], w=["tmpa"])
        rec.add("vector", lambda e: e.reduce_max(
            prb[:, 9:10], tmp[:, 8:15], axis=AX.X), tag="m2",
            r=["tmpa", "prb"], w=["prb"])
        rec.add("vector", lambda e: e.tensor_scalar(
            tmp[:, 0:7], prb[:, 0:7], prb[:, 9:10], None, ALU.is_ge),
            tag="keep", r=["prb"], w=["tmpa"])
        rec.add("vector", lambda e: e.tensor_tensor(
            prb[:, 10:11], prb[:, 8:9], prb[:, 9:10], ALU.add), tag="den",
            r=["prb"], w=["prb"])
        rec.add("vector", lambda e: e.reciprocal(
            prb[:, 11:12], prb[:, 10:11]), tag="rden",
            r=["prb"], w=["prb"])
        rec.add("vector", lambda e: e.tensor_tensor(
            tmp[:, 0:7], tmp[:, 0:7], prb[:, 0:7], ALU.mult), tag="wk",
            r=["tmpa", "prb"], w=["tmpa"])
        rec.add("vector", lambda e, ti=ti: e.tensor_scalar(
            wgt_all[:, 8 * ti:8 * ti + 7], tmp[:, 0:7], prb[:, 11:12],
            None, ALU.mult), tag="wnorm", r=["tmpa", "prb"], w=["wgt"])

    for qt in range(4):
        for e_i in range(8):
            for k in range(8):
                rec.dma("sync", wg_s[:, 2048 * k:2048 * (k + 1)],
                        gbf[e_i, 128 * k:128 * (k + 1), :],
                        tag=f"gld{qt}_{e_i}_{k}", w=[f"wg{k}"])
                rec.dma("sync", wu_s[:, 2048 * k:2048 * (k + 1)],
                        ubf[e_i, 128 * k:128 * (k + 1), :],
                        tag=f"uld{qt}_{e_i}_{k}", w=[f"wu{k}"])
            for m in range(16):
                rec.dma("scalar", wd_s[:, 1024 * m:1024 * (m + 1)],
                        dbf[e_i, 128 * m:128 * (m + 1), :],
                        tag=f"dld{qt}_{e_i}_{m}", w=[f"wd{m}"])
            for n in range(2):
                tok0 = 1024 * qt + 512 * n
                afb = af_s[n % 2]
                for c in range(8):
                    ti0 = tok0 // 128
                    deps_toks = [f"half{ti0 + t}_{c}" for t in range(4)]
                    rec.dma("sync", afb[:, 512 * c:512 * (c + 1)],
                            alf_hbm[c, :, tok0:tok0 + 512],
                            tag=f"afl{qt}_{e_i}_{n}_{c}",
                            r=deps_toks, w=[f"af{n % 2}_{c}"])
                for m in range(16):
                    for k in range(8):
                        rec.add("tensor", lambda e, m=m, k=k, afb=afb:
                                e.matmul(
                            b0[:], wg_s[:, 2048 * k + 128 * m:2048 * k + 128 * (m + 1)],
                            afb[:, 512 * k:512 * (k + 1)],
                            start=(k == 0), stop=(k == 7)),
                            tag=f"g{qt}_{e_i}_{n}_{m}",
                            r=[f"wg{k}", f"af{n % 2}_{k}"],
                            w=["b0"])
                        rec.add("tensor", lambda e, m=m, k=k, afb=afb:
                                e.matmul(
                            b1[:], wu_s[:, 2048 * k + 128 * m:2048 * k + 128 * (m + 1)],
                            afb[:, 512 * k:512 * (k + 1)],
                            start=(k == 0), stop=(k == 7)),
                            tag=f"u{qt}_{e_i}_{n}_{m}",
                            r=[f"wu{k}", f"af{n % 2}_{k}"], w=["b1"])
                    rec.add("scalar", lambda e: e.activation(
                        gs[:], b0[:], AF.Silu), tag="silu",
                        r=["b0"], w=["gs"])
                    rec.add("vector", lambda e, m=m: e.tensor_tensor(
                        a_s[:, 512 * m:512 * (m + 1)], b1[:], gs[:], ALU.mult),
                        tag="aact", r=["b1", "gs"], w=[f"a{m}"])
                for tc in range(4):
                    ti = (tok0 + 128 * tc) // 128
                    for nh in range(2):
                        bank = b2 if nh == 0 else b3
                        bt = "b2" if nh == 0 else "b3"
                        for m in range(16):
                            rec.add("tensor", lambda e, m=m, tc=tc, nh=nh,
                                    bank=bank: e.matmul(
                                bank[:],
                                a_s[:, 512 * m + 128 * tc:512 * m + 128 * (tc + 1)],
                                wd_s[:, 1024 * m + 512 * nh:1024 * m + 512 * (nh + 1)],
                                start=(m == 0), stop=(m == 15)),
                                tag=f"d{qt}_{e_i}_{n}_{tc}_{nh}",
                                r=[f"a{m}", f"wd{m}"], w=[bt])
                        accsl = acc[:, 1024 * (4 * n + tc) + 512 * nh:
                                    1024 * (4 * n + tc) + 512 * (nh + 1)]
                        rec.add("scalar", lambda e, ti=ti, e_i=e_i, bank=bank:
                                e.activation(
                            tmp[:], bank[:], AF.Copy,
                            scale=wgt_all[:, 8 * ti + e_i:8 * ti + e_i + 1]),
                            tag="wscale", r=[bt, "wgt"], w=["tmpa"])
                        if e_i == 0:
                            rec.add("vector", lambda e, accsl=accsl:
                                    e.tensor_copy(accsl, tmp[:]),
                                    tag="acc0", r=["tmpa"], w=["acc"])
                        else:
                            rec.add("vector", lambda e, accsl=accsl:
                                    e.tensor_tensor(accsl, accsl, tmp[:],
                                                    ALU.add),
                                    tag="accA", r=["tmpa", "acc"], w=["acc"])
        for t2 in range(8):
            ti = 8 * qt + t2
            q = t2 % 2
            rec.dma("sync", xt[q][:], s1[128 * ti:128 * (ti + 1), :],
                    tag=f"fs1{ti}", w=[f"xt{q}"])
            rec.dma("scalar", outt[q][:], x1[128 * ti:128 * (ti + 1), :],
                    tag=f"fx1{ti}", w=[f"outt{q}"])
            rec.add("vector", lambda e, q=q, t2=t2: e.tensor_tensor(
                outt[q][:], outt[q][:],
                acc[:, 1024 * t2:1024 * (t2 + 1)], ALU.add),
                tag=f"fo1{ti}", r=[f"outt{q}", "acc"], w=[f"outt{q}"])
            rec.add("vector", lambda e, q=q: e.tensor_tensor(
                outt[q][:], outt[q][:], xt[q][:], ALU.add),
                tag=f"fo2{ti}", r=[f"outt{q}", f"xt{q}"], w=[f"outt{q}"])
            rec.dma("sync", x2[128 * ti:128 * (ti + 1), :], outt[q][:],
                    tag=f"x2sp{ti}", r=[f"outt{q}"])


def prep_l0_moe(ins):
    rg = np.asarray(ins["L0_ffn_routed_gate"], np.float32)
    ru = np.asarray(ins["L0_ffn_routed_up"], np.float32)
    rd = np.asarray(ins["L0_ffn_routed_down"], np.float32)
    sg = np.asarray(ins["L0_ffn_shared_gate"], np.float32)
    su = np.asarray(ins["L0_ffn_shared_up"], np.float32)
    sd = np.asarray(ins["L0_ffn_shared_down"], np.float32)
    gateT = np.zeros((8, 1024, 2048), np.float32)
    upT = np.zeros((8, 1024, 2048), np.float32)
    downT = np.zeros((8, 2048, 1024), np.float32)
    for e in range(7):
        gateT[e] = rg[e].T
        upT[e] = ru[e].T
        downT[e] = rd[e].T
    gateT[7] = sg[0].T
    upT[7] = su[0].T
    downT[7] = sd[0].T
    return {
        "gateT": _c(gateT), "upT": _c(upT), "downT": _c(downT),
        "routerT": _c(np.asarray(ins["L0_ffn_router_w"], np.float32).T),
        "rbias_r": _row(ins["L0_ffn_routing_bias"]),
        "ln2_r": _row(ins["L0_ffn_ln2_w"]),
    }


def make_l0_moe_program():
    nc = bass.Bass("TRN2", target_bir_lowering=False)
    al = Alloc(nc)
    rec = Rec()
    def di(name, shape):
        return nc.dram_tensor(name, shape, F32, kind="ExternalInput").ap()
    s1 = di("s1", [T, D])
    x1 = di("x1", [T, D])
    x2 = nc.dram_tensor("x2", [T, D], F32, kind="ExternalOutput").ap()
    alf_hbm = nc.dram_tensor("alf_hbm", [8, 128, T], BF16).ap()
    gbf = nc.dram_tensor("gbf_hbm", [8, 1024, 2048], BF16).ap()
    ubf = nc.dram_tensor("ubf_hbm", [8, 1024, 2048], BF16).ap()
    dbf = nc.dram_tensor("dbf_hbm", [8, 2048, 1024], BF16).ap()
    mk = al.mark()
    build_moe_wconv(nc, al, rec, di("gateT", [8, 1024, 2048]),
                    di("upT", [8, 1024, 2048]), di("downT", [8, 2048, 1024]),
                    gbf, ubf, dbf)
    rec.fence(tag="wc_end")
    al.free_suffix(mk)
    build_l0_moe(nc, al, rec, s1, x1, x2, alf_hbm, gbf, ubf, dbf,
                 di("routerT", [1024, 7]),
                 di("rbias_r", [1, 7]), di("ln2_r", [1, D]))
    finish(nc, rec, al)
    return nc



LN512 = 3.1191592936036462


def build_l1_attn(nc, al, rec, x, y, qf_hbm, kf_hbm, vt_hbm, xcf_hbm, zf_hbm,
                  wupT, qdiag, kdiag, vdiag, convw_c, convb_c, wig_h, big_c,
                  lnw_row, skip_c, wdnT, pfx="f5_"):
    S, P = al.sb, al.ps

    ident = S(pfx + "ident", [128, 128], BF16)
    ident32 = S(pfx + "id32", [128, 128], F32)
    ones1 = S(pfx + "ones1", [1, 128], F32)
    onescol = S(pfx + "onescol", [128, 1], BF16)
    cpool = S(pfx + "cpool", [128, 8], F32)
    xt = [S(pfx + f"xt{i}", [128, D], F32) for i in range(2)]
    sq_scr = S(pfx + "sq", [128, D], F32)
    stats = S(pfx + "stats", [128, 8], F32)
    st2 = S(pfx + "st2", [128, 8], F32)
    cw_s = S(pfx + "cw", [128, 64], F32)
    cb_s = S(pfx + "cb", [128, 16], F32)
    big_s = S(pfx + "big", [128, 2], F32)
    skp_s = S(pfx + "skp", [128, 16], F32)
    wig_s = S(pfx + "wig", [128, 4 * 96], BF16)
    gX = S(pfx + "gX", [4, T], F32)
    gY = S(pfx + "gY", [4, T], F32)
    IGP = gX[0:4, :]
    FGP = gY[0:4, :]

    p_tr = P(pfx + "ptr", [128, 1024], BF16)
    b0 = P(pfx + "b0", [128, 512])
    b1 = P(pfx + "b1", [128, 512])
    b2 = P(pfx + "b2", [128, 512])
    b3 = P(pfx + "b3", [128, 512])
    b4 = P(pfx + "b4", [128, 512])
    b5 = P(pfx + "b5", [128, 512])
    b6 = P(pfx + "b6", [128, 512])

    ln = LNHelper(rec, sq_scr, stats, name=pfx + "ln")
    lnh = LNHelper(rec, sq_scr, st2, name=pfx + "lh", sq_tok=pfx + "lnsq")
    cp = ConstPool(rec, cpool)
    eps5 = cp.get(1e-5)
    one_c = cp.get(1.0)
    cp_op = cp.op

    o_id = make_identity_ops(rec, ident)
    rec.add("gpsimd", lambda e: e.memset(ones1[:], 1.0), tag="ones", w=["ones1"])
    rec.add("gpsimd", lambda e: e.memset(onescol[:], 1.0), tag="onesc",
            w=["onescol"])
    rec.add("gpsimd", lambda e: e.memset(ident32[:], 0.0), tag="i32a",
            w=["id32"])
    rec.add("gpsimd", lambda e: e.affine_select(
        out=ident32[:], in_=ident32[:], compare_op=ALU.not_equal, fill=1.0,
        base=0, pattern=[[-1, 128]], channel_multiplier=1), tag="i32b",
        r=["id32"], w=["id32"])
    rec.dma("sync", cw_s[:], convw_c[:], w=["cw"])
    rec.dma("sync", cb_s[:], convb_c[:], w=["cb"])
    rec.dma("sync", big_s[0:4, :], big_c[:], w=["big"])
    rec.dma("sync", skp_s[:], skip_c[:], w=["skp"])

    wup_s = S(pfx + "wup", [128, 8 * 4096], BF16)
    qd_s = S(pfx + "qd", [128, 64 * 128], BF16)
    kd_s = S(pfx + "kd", [128, 64 * 128], BF16)
    vw_s = S(pfx + "vw", [128, 64], F32)
    wstage = S(pfx + "wstage", [128, 2048], F32)
    xlf = S(pfx + "xlf", [128, 4096], BF16)
    xln16 = S(pfx + "xln16", [128, D], BF16)
    xm_s = [S(pfx + f"xm{m}", [128, 515], BF16) for m in range(16)]
    hl_s = S(pfx + "hl", [128, 3 * 16], BF16)
    cv0 = S(pfx + "cv0", [128, 512], F32)
    cv1 = S(pfx + "cv1", [128, 512], F32)
    xc4 = [S(pfx + f"xc4_{d}", [128, 512], BF16) for d in range(4)]
    qkbuf = [S(pfx + f"qkb{i}", [128, 512], BF16) for i in range(2)]
    vbuf3 = [S(pfx + f"vb3{i}", [128, 512], BF16) for i in range(2)]
    zbuf = [S(pfx + f"zb{i}", [128, 512], BF16) for i in range(2)]
    vt_t = [S(pfx + f"vt{t}", [128, 2048], BF16) for t in range(4)]

    def wload(dst, dst_tok, src_hbm, nch, cols, tag):
        for k in range(nch):
            for piece in range(0, cols, 2048):
                pw = min(2048, cols - piece)
                rec.dma("sync", wstage[:, 0:pw],
                        src_hbm[128 * k:128 * (k + 1), piece:piece + pw],
                        tag=f"{tag}ld{k}_{piece}", w=["wstage"])
                rec.add("vector", lambda e, k=k, piece=piece, pw=pw, dst=dst:
                        e.tensor_copy(
                    dst[:, cols * k + piece:cols * k + piece + pw],
                    wstage[:, 0:pw]),
                    tag=f"{tag}c{k}", r=["wstage"], w=[dst_tok])

    wload(wup_s, "wup", wupT, 8, 4096, "wup")
    wload(qd_s, "qdw", qdiag, 1, 8192, "qd")
    wload(kd_s, "kdw", kdiag, 1, 8192, "kd")
    rec.dma("sync", vw_s[:], vdiag[:], w=["vdw"])
    rec.dma("sync", wstage[:, 0:384], wig_h[:], tag="wigld", w=["wstage"])
    rec.add("vector", lambda e: e.tensor_copy(wig_s[:], wstage[:, 0:384]),
            tag="wigc", r=["wstage"], w=["wig"])
    rec.add("gpsimd", lambda e: e.memset(hl_s[:], 0.0), tag="hl0", w=["hl"])

    for p in range(NP):
        for tl in range(4):
            ti = 4 * p + tl
            q = ti % 2
            rec.dma("sync", xt[q][:], x[128 * ti:128 * (ti + 1), :],
                    tag=f"x{ti}", w=[f"xt{q}"])
            ln.emit(xt[q][:], f"xt{q}", xln16[:], "xln16", eps5,
                    const_dep=cp_op)
            transpose_to_fm(rec, ident, o_id, xln16, "xln16", p_tr, "ptr",
                            xlf, "xlf", tl)
        for m in range(16):
            bank, bt = (b0, "b0") if m % 2 == 0 else (b1, "b1")
            for k in range(8):
                rec.add("tensor", lambda e, m=m, k=k, bank=bank: e.matmul(
                    bank[:], wup_s[:, 4096 * k + 128 * m:4096 * k + 128 * (m + 1)],
                    xlf[:, 512 * k:512 * (k + 1)],
                    start=(k == 0), stop=(k == 7)),
                    tag=f"xm{p}_{m}", r=["wup", "xlf"], w=[bt])
            if p > 0:
                rec.add("vector", lambda e, m=m: e.tensor_copy(
                    hl_s[:, 3 * m:3 * (m + 1)], xm_s[m][:, 512:515]),
                    tag="hlsave", r=[f"xms{m}"], w=["hl"])
            rec.add("vector", lambda e, m=m, bank=bank: e.tensor_copy(
                xm_s[m][:, 3:515], bank[:]),
                tag="xmev", r=[bt, "hl"], w=[f"xms{m}"])
            rec.add("vector", lambda e, m=m: e.tensor_copy(
                xm_s[m][:, 0:3], hl_s[:, 3 * m:3 * (m + 1)]),
                tag="hlrest", r=["hl"], w=[f"xms{m}"])
        for m in range(16):
            bank, bt = (b0, "b0") if m % 2 == 0 else (b1, "b1")
            zb = zbuf[m % 2]
            for k in range(8):
                rec.add("tensor", lambda e, m=m, k=k, bank=bank: e.matmul(
                    bank[:],
                    wup_s[:, 4096 * k + 2048 + 128 * m:4096 * k + 2048 + 128 * (m + 1)],
                    xlf[:, 512 * k:512 * (k + 1)],
                    start=(k == 0), stop=(k == 7)),
                    tag=f"z{p}_{m}", r=["wup", "xlf"], w=[bt])
            rec.add("vector", lambda e, m=m, bank=bank, zb=zb: e.tensor_copy(
                zb[:], bank[:]), tag="zev", r=[bt], w=[f"zbt{m % 2}"])
            rec.dma("gpsimd", zf_hbm[m, :, 512 * p:512 * (p + 1)], zb[:],
                    tag=f"zsp{p}_{m}", r=[f"zbt{m % 2}"], w=[f"hz{p}_{m}"])
        for pc in range(4):
            for d in range(4):
                m = 4 * d + pc
                rec.add("vector", lambda e, m=m: e.tensor_scalar(
                    cv0[:], xm_s[m][:, 3:515], cw_s[:, 4 * m + 3:4 * m + 4],
                    None, ALU.mult), tag="cj3", r=[f"xms{m}", "cw"], w=["cv0"])
                rec.add("vector", lambda e, m=m: e.scalar_tensor_tensor(
                    cv1[:], xm_s[m][:, 2:514], cw_s[:, 4 * m + 2:4 * m + 3],
                    cv0[:], ALU.mult, ALU.add), tag="cj2",
                    r=[f"xms{m}", "cw", "cv0"], w=["cv1"])
                rec.add("vector", lambda e, m=m: e.scalar_tensor_tensor(
                    cv0[:], xm_s[m][:, 1:513], cw_s[:, 4 * m + 1:4 * m + 2],
                    cv1[:], ALU.mult, ALU.add), tag="cj1",
                    r=[f"xms{m}", "cw", "cv1"], w=["cv0"])
                rec.add("vector", lambda e, m=m: e.scalar_tensor_tensor(
                    cv1[:], xm_s[m][:, 0:512], cw_s[:, 4 * m:4 * m + 1],
                    cv0[:], ALU.mult, ALU.add), tag="cj0",
                    r=[f"xms{m}", "cw", "cv0"], w=["cv1"])
                rec.add("scalar", lambda e, m=m, d=d: e.activation(
                    xc4[d][:], cv1[:], AF.Silu, bias=cb_s[:, m:m + 1]),
                    tag="xcact", r=["cv1", "cb"], w=[f"xc4t{d}"])
                rec.dma("gpsimd", xcf_hbm[m, :, 512 * p:512 * (p + 1)],
                        xc4[d][:], tag=f"xcsp{p}_{m}", r=[f"xc4t{d}"],
                        w=[f"hxc{p}_{m}"])
            for o in range(4):
                mo = 4 * o + pc
                for which, dw, dwt, hbm, buf, btok, sec in (
                        (0, qd_s, "qdw", qf_hbm, qkbuf[0], "qkb0", 0),
                        (1, kd_s, "kdw", kf_hbm, qkbuf[1], "qkb1", 1),
                        (2, None, "vdw", None, vbuf3[o % 2], f"vb3{o % 2}", 2)):
                    bank, bt = (b0, "b0") if (2 * o + which) % 2 == 0 \
                        else (b1, "b1")
                    if which == 2:
                        for d in range(4):
                            mi = 4 * d + pc
                            idx = 16 * o + 4 * d + pc
                            if d == 0:
                                rec.add("vector", lambda e, mi=mi, idx=idx:
                                        e.tensor_scalar(
                                    cv0[:], xm_s[mi][:, 3:515],
                                    vw_s[:, idx:idx + 1], None, ALU.mult),
                                    tag="vhw0", r=[f"xms{mi}", "vdw"],
                                    w=["cv0"])
                            else:
                                src_t, dst_t = (cv0, cv1) if d % 2 == 1 \
                                    else (cv1, cv0)
                                stok, dtok2 = ("cv0", "cv1") if d % 2 == 1 \
                                    else ("cv1", "cv0")
                                rec.add("vector", lambda e, mi=mi, idx=idx,
                                        src_t=src_t, dst_t=dst_t:
                                        e.scalar_tensor_tensor(
                                    dst_t[:], xm_s[mi][:, 3:515],
                                    vw_s[:, idx:idx + 1], src_t[:],
                                    ALU.mult, ALU.add),
                                    tag="vhw", r=[f"xms{mi}", "vdw", stok],
                                    w=[dtok2])
                        rec.add("vector", lambda e, buf=buf: e.tensor_copy(
                            buf[:], cv1[:]), tag="vhwev",
                            r=["cv1"], w=[btok])
                    else:
                        for d in range(4):
                            mi = 4 * d + pc
                            idx = 16 * o + 4 * d + pc
                            rhs = lambda d=d: xc4[d][:]
                            rtok = f"xc4t{d}"
                            rec.add("tensor", lambda e, idx=idx, dw=dw,
                                    rhs=rhs, bank=bank, st=(d == 0),
                                    sp=(d == 3): e.matmul(
                                bank[:], dw[:, 128 * idx:128 * (idx + 1)],
                                rhs(), start=st, stop=sp),
                                tag=f"hw{p}_{mo}_{which}", r=[dwt, rtok],
                                w=[bt])
                        rec.add("vector", lambda e, buf=buf, bank=bank:
                                e.tensor_copy(buf[:], bank[:]),
                                tag="hwev", r=[bt], w=[btok])
                    if hbm is not None:
                        rec.dma("gpsimd", hbm[mo, :, 512 * p:512 * (p + 1)],
                                buf[:], tag=f"hwsp{p}_{mo}_{which}",
                                r=[btok],
                                w=[("hqs" if which == 0 else "hks") +
                                   f"{p}_{mo}"])
                    cc = 16 * sec + mo
                    st_ifg = (pc == 0 and o == 0 and which == 0)
                    sp_ifg = (pc == 3 and o == 3 and which == 2)
                    rec.add("tensor", lambda e, cc=cc, buf=buf, st=st_ifg,
                            sp=sp_ifg: e.matmul(
                        b5[0:4, :], wig_s[:, 4 * cc:4 * (cc + 1)], buf[:],
                        start=st, stop=sp),
                        tag=f"ig{p}", r=["wig", btok], w=["b5"])
                    rec.add("tensor", lambda e, cc=cc, buf=buf, st=st_ifg,
                            sp=sp_ifg: e.matmul(
                        b6[0:4, :], wig_s[:, 192 + 4 * cc:192 + 4 * (cc + 1)],
                        buf[:],
                        start=st, stop=sp),
                        tag=f"fg{p}", r=["wig", btok], w=["b6"])
                    if which == 2:
                        vcol = 512 * pc + 128 * o
                        for tl in range(4):
                            rec.add("tensor", lambda e, buf=buf, tl=tl:
                                    e.transpose(
                                p_tr[:, 0:128], buf[:, 128 * tl:128 * (tl + 1)],
                                ident[:]), deps=[o_id], tag="vtr",
                                r=[btok], w=["ptr"])
                            rec.add("vector", lambda e, vcol=vcol, tl=tl:
                                    e.tensor_copy(
                                vt_t[tl][:, vcol:vcol + 128],
                                p_tr[:, 0:128]),
                                tag="vtev", r=["ptr"], w=[f"vtt{tl}"])
        rec.add("vector", lambda e, p=p: e.tensor_scalar(
            IGP[:, 512 * p:512 * (p + 1)], b5[0:4, :],
            big_s[0:4, 0:1], None, ALU.add),
            tag="ifgb_i", r=["b5", "big"], w=["gX"])
        rec.add("vector", lambda e, p=p: e.tensor_scalar(
            FGP[:, 512 * p:512 * (p + 1)], b6[0:4, :],
            big_s[0:4, 1:2], None, ALU.add),
            tag="ifgb_f", r=["b6", "big"], w=["gY"])
        for tl in range(4):
            rec.dma("sync", vt_hbm[4 * p + tl, :, :], vt_t[tl][:],
                    tag=f"vtsp{p}_{tl}", r=[f"vtt{tl}"],
                    w=[f"hvt{4 * p + tl}"])

    rec.fence(tag="fence_gates")
    al.free([pfx + "wup", pfx + "qd", pfx + "kd", pfx + "vw",
             pfx + "wstage", pfx + "xlf", pfx + "xln16", pfx + "hl",
             pfx + "cv0", pfx + "cv1"]
            + [pfx + f"xm{m}" for m in range(16)]
            + [pfx + f"xc4_{d}" for d in range(4)]
            + [pfx + f"qkb{i}" for i in range(2)]
            + [pfx + f"vb3{i}" for i in range(2)]
            + [pfx + f"zb{i}" for i in range(2)]
            + [pfx + f"vt{t}" for t in range(4)])
    gW = S(pfx + "gW", [4, T], F32)
    aT_s = S(pfx + "aT", [128, 128], F32)
    gZ = S(pfx + "gZ", [4, T], F32)

    rec.add("scalar", lambda e: e.activation(gZ[:], FGP, AF.Abs),
            tag="g_abs", r=["gY"], w=["gZ"])
    rec.add("scalar", lambda e: e.activation(gZ[:], gZ[:], AF.Exp, scale=-1.0),
            tag="g_exp", r=["gZ"], w=["gZ"])
    rec.add("scalar", lambda e: e.activation(gZ[:], gZ[:], AF.Ln,
                                             bias=one_c[0:4, 0:1]),
            deps=[cp_op], tag="g_ln", r=["gZ"], w=["gZ"])
    rec.add("vector", lambda e: e.tensor_scalar(gW[:], FGP, 0.0, None,
                                                ALU.max),
            tag="g_relu", r=["gY"], w=["gW"])
    rec.add("vector", lambda e: e.tensor_tensor(gW[:], gW[:], gZ[:], ALU.add),
            tag="g_add", r=["gW", "gZ"], w=["gW"])
    rec.add("vector", lambda e: e.scalar_tensor_tensor(
        gW[:], gW[:], -1.0, FGP, ALU.mult, ALU.add), tag="g_neg",
        r=["gW", "gY"], w=["gW"])
    rec.add("vector", lambda e: e.memset(gY[:], 0.0), tag="g_zero",
            r=["gY"], w=["gY"])
    rec.add("vector", lambda e: e.tensor_tensor_scan(
        gZ[:], gW[:], gY[:], 0.0, ALU.add, ALU.add), tag="g_cum",
        r=["gW", "gY", "gZ"], w=["gZ"])
    rec.add("vector", lambda e: e.tensor_tensor(gW[:], IGP, gZ[:],
                                                ALU.subtract),
            tag="g_a", r=["gX", "gZ", "gW"], w=["gW"])
    for kc in range(NT):
        rec.add("tensor", lambda e, kc=kc: e.transpose(
            b6[0:128, 0:4], gW[:, 128 * kc:128 * (kc + 1)],
            ident32[0:4, 0:4]),
            tag="aTtr", r=["gW", "id32"], w=["b6"])
        rec.add("vector", lambda e, kc=kc: e.tensor_copy(
            aT_s[:, 4 * kc:4 * (kc + 1)], b6[0:128, 0:4]),
            tag="aTev", r=["b6"], w=["aT"])
    rec.add("vector", lambda e: e.tensor_tensor_scan(
        gX[:], gW[:], gY[:], -1e30, ALU.max, ALU.add), tag="g_rm",
        r=["gW", "gY", "gX"], w=["gX"])
    rec.add("vector", lambda e: e.tensor_scalar(
        gY[:], gX[:], -1.0, -LN512, ALU.mult, ALU.add), tag="g_b",
        r=["gX", "gY"], w=["gY"])
    rec.add("vector", lambda e: e.tensor_tensor(gW[:], gZ[:], gX[:], ALU.add),
            tag="g_m", r=["gZ", "gX", "gW"], w=["gW"])
    rec.add("scalar", lambda e: e.activation(gW[:], gW[:], AF.Exp, scale=-1.0),
            tag="g_n2", r=["gW"], w=["gW"])
    BS = gY[0:4, :]
    N2 = gW[0:4, :]
    rec.fence(tag="fence_3b")
    al.free([pfx + "gZ"])

    wdn_s = S(pfx + "wdn", [128, 16 * 1024], BF16)
    wstg2 = S(pfx + "wstg2", [128, 1024], F32)
    lnr_s = S(pfx + "lnr", [1, 2048], F32)
    lnwbc = S(pfx + "lnwbc", [128, 2048], F32)
    qh_s = S(pfx + "qh", [128, 2048], BF16)
    kbb = [S(pfx + f"kbb{i}", [128, 512], BF16) for i in range(4)]
    vbb = [S(pfx + f"vbb{i}", [128, 512], BF16) for i in range(4)]
    bb_s = S(pfx + "bb", [128, 512], F32)
    ld_t = [S(pfx + f"ldt{i}", [128, 512], F32) for i in range(2)]
    ed_t = [S(pfx + f"edt{i}", [128, 512], F32) for i in range(2)]
    c16 = [S(pfx + f"c16{i}", [128, 512], BF16) for i in range(2)]
    maskn = S(pfx + "maskn", [128, 128], F32)
    brow1 = S(pfx + "brow1", [1, 512], F32)
    n2r1 = S(pfx + "n2r1", [1, 512], F32)
    rs1 = S(pfx + "rs1", [1, 512], F32)
    fbc = S(pfx + "fbc", [128, 512], F32)
    h4 = [S(pfx + f"h4_{d}", [128, 512], F32) for d in range(4)]
    ht = S(pfx + "ht", [128, 512], F32)
    hnt16 = S(pfx + "hnt16", [128, 512], BF16)
    xcz = S(pfx + "xcz", [128, 2048], BF16)
    zz = S(pfx + "zz", [128, 2048], BF16)
    szf = S(pfx + "szf", [128, 512], F32)
    skf = S(pfx + "skf", [128, 512], F32)
    hnf = S(pfx + "hnf", [128, 2048], BF16)
    hst_s = S(pfx + "hst", [128, 16 * 512], BF16)
    outt = [S(pfx + f"outt{i}", [128, D], F32) for i in range(2)]

    for m in range(16):
        rec.dma("sync", wstg2[:], wdnT[128 * m:128 * (m + 1), :],
                tag=f"wdnld{m}", w=["wstg2"])
        rec.add("vector", lambda e, m=m: e.tensor_copy(
            wdn_s[:, 1024 * m:1024 * (m + 1)], wstg2[:]),
            tag=f"wdnc{m}", r=["wstg2"], w=["wdn"])
    rec.dma("sync", lnr_s[:], lnw_row[:], w=["lnr"])
    for h in range(4):
        rec.add("tensor", lambda e, h=h: e.matmul(
            b0[:], ones1[0:1, :], lnr_s[0:1, 512 * h:512 * (h + 1)],
            start=True, stop=True), tag="lnwbc", r=["lnr", "ones1"], w=["b0"])
        rec.add("scalar", lambda e, h=h: e.copy(
            lnwbc[:, 512 * h:512 * (h + 1)], b0[:]), tag="lnwbcp",
            r=["b0"], w=["lnwbc"])
    rec.add("gpsimd", lambda e: e.memset(maskn[:], 0.0), tag="mka", w=["mkn"])
    rec.add("gpsimd", lambda e: e.affine_select(
        out=maskn[:], in_=maskn[:], compare_op=ALU.is_ge, fill=-1e30,
        base=0, pattern=[[1, 128]], channel_multiplier=-1), tag="mkb",
        r=["mkn"], w=["mkn"])

    for g in range(NP):
        for h in range(4):
            for d2 in range(4):
                rec.dma("sync", qh_s[:, 512 * d2:512 * (d2 + 1)],
                        qf_hbm[4 * d2 + h, :, 512 * g:512 * (g + 1)],
                        tag=f"qhl{g}_{h}_{d2}", r=[f"hqs{g}_{4 * d2 + h}"],
                        w=["qht"])
            rec.dma("sync", brow1[:], BS[h:h + 1, 512 * g:512 * (g + 1)],
                    tag=f"br{g}_{h}", r=["gY"], w=["br1"])
            rec.dma("sync", n2r1[:], N2[h:h + 1, 512 * g:512 * (g + 1)],
                    tag=f"nr{g}_{h}", r=["gW"], w=["nr1"])
            rec.add("tensor", lambda e: e.matmul(
                b5[:], ones1[0:1, :], brow1[0:1, :],
                start=True, stop=True), tag=f"bbc{g}_{h}",
                r=["br1", "ones1"], w=["b5"])
            rec.add("scalar", lambda e: e.copy(bb_s[:], b5[:]),
                    tag="bbcp", r=["b5"], w=["bbt"])
            nkc = 4 * (g + 1)

            def l1_loads(kc):
                kcb = kc % 4
                for d2 in range(4):
                    rec.dma("sync",
                            kbb[kcb][:, 128 * d2:128 * (d2 + 1)],
                            kf_hbm[4 * d2 + h, :, 128 * kc:128 * (kc + 1)],
                            tag=f"kl{g}_{h}_{kc}_{d2}",
                            r=[f"hks{kc // 4}_{4 * d2 + h}"],
                            w=[f"kbt{kcb}"])
                rec.dma("scalar", vbb[kcb][:],
                        vt_hbm[kc, :, 512 * h:512 * (h + 1)],
                        tag=f"vl{g}_{h}_{kc}", r=[f"hvt{kc}"],
                        w=[f"vbt{kcb}"])

            def l1_qk(kc):
                kcb = kc % 4
                q0 = 128 * max(0, kc - 4 * g)
                qkb, qkt = (b6, "b6") if kc % 2 == 0 else (b5, "b5")
                for d2 in range(4):
                    rec.add("tensor", lambda e, d2=d2, kcb=kcb, q0=q0,
                            qkb=qkb: e.matmul(
                        qkb[:, q0:512], kbb[kcb][:, 128 * d2:128 * (d2 + 1)],
                        qh_s[:, 512 * d2 + q0:512 * (d2 + 1)],
                        start=(d2 == 0), stop=(d2 == 3)),
                        tag=f"qk{g}_{h}_{kc}", r=[f"kbt{kcb}", "qht"],
                        w=[qkt])

            def l1_gate(kc):
                q0 = 128 * max(0, kc - 4 * g)
                ldx, edx = ld_t[kc % 2], ed_t[kc % 2]
                ldtok, edtok = f"ldt{kc % 2}", f"edt{kc % 2}"
                rec.add("vector", lambda e, kc=kc, q0=q0, ldx=ldx, h=h:
                        e.tensor_scalar(
                    ldx[:, q0:512], bb_s[:, q0:512],
                    aT_s[:, 4 * kc + h:4 * kc + h + 1],
                    None, ALU.add), tag=f"ld{g}_{h}_{kc}",
                    r=["bbt", "aT"], w=[ldtok])
                if kc - 4 * g >= 0:
                    rec.add("vector", lambda e, q0=q0, ldx=ldx:
                            e.tensor_tensor(
                        ldx[:, q0:q0 + 128], ldx[:, q0:q0 + 128],
                        maskn[:], ALU.add), tag="lddiag",
                        r=[ldtok, "mkn"], w=[ldtok])
                rec.add("scalar", lambda e, q0=q0, ldx=ldx, edx=edx:
                        e.activation(
                    edx[:, q0:512], ldx[:, q0:512], AF.Exp),
                    tag=f"ed{g}_{h}_{kc}", r=[ldtok], w=[edtok])

            def l1_cmul(kc):
                q0 = 128 * max(0, kc - 4 * g)
                qkb, qkt = (b6, "b6") if kc % 2 == 0 else (b5, "b5")
                edx, c1x = ed_t[kc % 2], c16[kc % 2]
                edtok, ctok = f"edt{kc % 2}", f"c16t{kc % 2}"
                rec.add("vector", lambda e, q0=q0, qkb=qkb, edx=edx,
                        c1x=c1x: e.tensor_tensor(
                    c1x[:, q0:512], qkb[:, q0:512], edx[:, q0:512],
                    ALU.mult),
                    tag=f"c{g}_{h}_{kc}", r=[qkt, edtok], w=[ctok])

            def l1_cv(kc):
                kcb = kc % 4
                q0 = 128 * max(0, kc - 4 * g)
                c1x = c16[kc % 2]
                ctok = f"c16t{kc % 2}"
                sp = (kc == nkc - 1)
                for dv in range(4):
                    bank = (b0, b1, b2, b3)[dv]
                    rec.add("tensor", lambda e, dv=dv, kcb=kcb, kc=kc,
                            q0=q0, bank=bank, c1x=c1x, sp=sp: e.matmul(
                        bank[:, q0:512], vbb[kcb][:, 128 * dv:128 * (dv + 1)],
                        c1x[:, q0:512], start=(kc == 0), stop=sp),
                        tag=f"cv{g}_{h}_{kc}_{dv}",
                        r=[f"vbt{kcb}", ctok], w=[f"b{dv}"])
                rec.add("tensor", lambda e, kc=kc, q0=q0, c1x=c1x, sp=sp:
                        e.matmul(
                    b4[0:1, q0:512], onescol[:], c1x[:, q0:512],
                    start=(kc == 0), stop=sp),
                    tag=f"cs{g}_{h}_{kc}", r=["onescol", ctok], w=["b4"])

            for kk in range(min(3, nkc)):
                l1_loads(kk)
            l1_gate(0)
            l1_qk(0)
            for kc in range(nkc):
                if kc + 3 < nkc:
                    l1_loads(kc + 3)
                if kc + 1 < nkc:
                    l1_gate(kc + 1)
                    l1_qk(kc + 1)
                l1_cmul(kc)
                l1_cv(kc)
            rec.add("scalar", lambda e: e.activation(
                rs1[:], b4[0:1, :], AF.Abs), tag=f"abs{g}_{h}",
                r=["b4"], w=["rst"])
            rec.add("vector", lambda e: e.tensor_tensor(
                rs1[:], rs1[:], n2r1[0:1, :], ALU.max),
                tag="nmax", r=["rst", "nr1"], w=["rst"])
            rec.add("vector", lambda e: e.tensor_scalar(
                rs1[:], rs1[:], 1e-6, None, ALU.add),
                tag="neps", r=["rst"], w=["rst"])
            rec.add("vector", lambda e: e.reciprocal(rs1[:], rs1[:]),
                    tag="nrec", r=["rst"], w=["rst"])
            rec.add("tensor", lambda e: e.matmul(
                b5[:], ones1[0:1, :], rs1[0:1, :],
                start=True, stop=True), tag=f"fbc{g}_{h}",
                r=["rst", "ones1"], w=["b5"])
            rec.add("scalar", lambda e: e.copy(fbc[:], b5[:]),
                    tag="fbcp", r=["b5"], w=["fbct"])
            for dv in range(4):
                bank = (b0, b1, b2, b3)[dv]
                rec.add("vector", lambda e, dv=dv, bank=bank: e.tensor_tensor(
                    h4[dv][:], bank[:], fbc[:], ALU.mult),
                    tag=f"hsc{g}_{h}_{dv}", r=[f"b{dv}", "fbct"],
                    w=[f"h4t{dv}"])
            for ts in range(4):
                for dv in range(4):
                    rec.add("tensor", lambda e, dv=dv, ts=ts: e.transpose(
                        b4[:, 0:128],
                        h4[dv][:, 128 * ts:128 * (ts + 1)], ident32[:]),
                        tag="htr", r=[f"h4t{dv}", "id32"], w=["b4"])
                    rec.add("vector", lambda e, dv=dv: e.tensor_copy(
                        ht[:, 128 * dv:128 * (dv + 1)], b4[:, 0:128]),
                        tag="htev", r=["b4"], w=["htt"])
                lnh.emit(ht[:], "htt", szf[:], "szft", eps5, const_dep=cp_op)
                rec.add("vector", lambda e, h=h: e.tensor_tensor(
                    hnt16[:], szf[:], lnwbc[:, 512 * h:512 * (h + 1)],
                    ALU.mult), tag="hnw", r=["szft", "lnwbc"], w=["hnt"])
                for dv in range(4):
                    rec.add("tensor", lambda e, dv=dv: e.transpose(
                        p_tr[:, 0:128], hnt16[:, 128 * dv:128 * (dv + 1)],
                        ident[:]), deps=[o_id], tag="hntr",
                        r=["hnt"], w=["ptr"])
                    rec.add("vector", lambda e, dv=dv, ts=ts: e.tensor_copy(
                        hnf[:, 512 * dv + 128 * ts:512 * dv + 128 * (ts + 1)],
                        p_tr[:, 0:128]),
                        tag="hnev", r=["ptr"], w=["hnft"])
            for d2 in range(4):
                m = 4 * d2 + h
                rec.dma("sync", xcz[:, 512 * d2:512 * (d2 + 1)],
                        xcf_hbm[m, :, 512 * g:512 * (g + 1)],
                        tag=f"xcl{g}_{h}_{d2}", r=[f"hxc{g}_{m}"], w=["xczt"])
                rec.dma("sync", zz[:, 512 * d2:512 * (d2 + 1)],
                        zf_hbm[m, :, 512 * g:512 * (g + 1)],
                        tag=f"zl{g}_{h}_{d2}", r=[f"hz{g}_{m}"], w=["zzt"])
            for d2 in range(4):
                m = 4 * d2 + h
                rec.add("scalar", lambda e, d2=d2, m=m: e.activation(
                    skf[:], xcz[:, 512 * d2:512 * (d2 + 1)], AF.Copy,
                    scale=skp_s[:, m:m + 1]), tag="hskip",
                    r=["xczt", "skp"], w=["skft"])
                rec.add("vector", lambda e, d2=d2: e.tensor_tensor(
                    skf[:], skf[:],
                    hnf[:, 512 * d2:512 * (d2 + 1)], ALU.add),
                    tag="hskip2", r=["skft", "hnft"], w=["skft"])
                rec.add("scalar", lambda e, d2=d2: e.activation(
                    szf[:], zz[:, 512 * d2:512 * (d2 + 1)], AF.Silu),
                    tag="hsilu", r=["zzt"], w=["szft"])
                rec.add("vector", lambda e, d2=d2, m=m: e.tensor_tensor(
                    hst_s[:, 512 * m:512 * (m + 1)], skf[:], szf[:],
                    ALU.mult), tag="hstate", r=["skft", "szft"],
                    w=[f"hstt{m}"])
        for ts in range(4):
            ti = 4 * g + ts
            q2 = ts % 2
            rec.dma("scalar", xt[q2][:], x[128 * ti:128 * (ti + 1), :],
                    tag=f"xr{ti}", w=[f"xt{q2}"])
            for nh in range(2):
                bank, bt = (b5, "b5") if nh == 0 else (b6, "b6")
                for m in range(16):
                    rec.add("tensor", lambda e, m=m, ts=ts, nh=nh, bank=bank:
                            e.matmul(
                        bank[:],
                        hst_s[:, 512 * m + 128 * ts:512 * m + 128 * (ts + 1)],
                        wdn_s[:, 1024 * m + 512 * nh:1024 * m + 512 * (nh + 1)],
                        start=(m == 0), stop=(m == 15)),
                        tag=f"dn{ti}_{nh}", r=[f"hstt{m}", "wdn"], w=[bt])
                rec.add("vector", lambda e, q2=q2, nh=nh, bank=bank:
                        e.tensor_tensor(
                    outt[q2][:, 512 * nh:512 * (nh + 1)], bank[:],
                    xt[q2][:, 512 * nh:512 * (nh + 1)], ALU.add),
                    tag=f"dres{ti}_{nh}", r=[bt, f"xt{q2}"], w=[f"outt{q2}"])
            rec.dma("sync", y[128 * ti:128 * (ti + 1), :], outt[q2][:],
                    tag=f"y{ti}", r=[f"outt{q2}"])


def prep_l1_attn(ins):
    INNER = 2048
    cc = np.arange(INNER)
    cp_map = (cc % 4) * 512 + cc // 4
    inv = np.empty(INNER, np.int64)
    inv[cp_map] = cc
    w_up = np.asarray(ins["L1_attn_w_up"], np.float32)
    wupT = w_up.T.copy()
    wupT_perm = np.empty_like(wupT)
    wupT_perm[:, cp_map] = wupT[:, 0:2048]
    wupT_perm[:, 2048 + cp_map] = wupT[:, 2048:4096]
    w_down = np.asarray(ins["L1_attn_w_down"], np.float32)
    wdnT_perm = np.empty((2048, 1024), np.float32)
    wdnT_perm[cp_map, :] = w_down.T

    def diag_pack(w):
        out = np.zeros((128, 8192), np.float32)
        pr = np.arange(128)
        for o in range(4):
            for d in range(4):
                for pc in range(4):
                    idx = 16 * o + 4 * d + pc
                    out[pr, 128 * idx + pr] = w[128 * pc + pr, o, d]
        return out

    qdiag = diag_pack(np.asarray(ins["L1_attn_w_q"], np.float32))
    kdiag = diag_pack(np.asarray(ins["L1_attn_w_k"], np.float32))
    w_v = np.asarray(ins["L1_attn_w_v"], np.float32)
    vdiag = np.zeros((128, 64), np.float32)
    pr = np.arange(128)
    for o in range(4):
        for d in range(4):
            for pc in range(4):
                vdiag[:, 16 * o + 4 * d + pc] = w_v[128 * pc + pr, o, d]
    mm, pp = np.meshgrid(np.arange(16), np.arange(128), indexing="ij")
    c_of = 512 * (mm % 4) + 4 * pp + mm // 4
    conv_w = np.asarray(ins["L1_attn_conv_w"], np.float32)[:, 0, :]
    convw_c = np.zeros((128, 64), np.float32)
    for m in range(16):
        for j in range(4):
            convw_c[:, 4 * m + j] = conv_w[c_of[m], j]
    convb_c = np.zeros((128, 16), np.float32)
    skip_c = np.zeros((128, 16), np.float32)
    conv_b = np.asarray(ins["L1_attn_conv_b"], np.float32)
    skip = np.asarray(ins["L1_attn_skip"], np.float32)
    for m in range(16):
        convb_c[:, m] = conv_b[c_of[m]]
        skip_c[:, m] = skip[c_of[m]]
    row_perm = np.concatenate([s * 2048 + cp_map for s in range(3)])
    wig = np.asarray(ins["L1_attn_w_ig"], np.float32)
    wfg = np.asarray(ins["L1_attn_w_fg"], np.float32)
    wig_h = np.zeros((128, 384), np.float32)
    for gi, w in ((0, wig), (1, wfg)):
        wp = np.empty_like(w)
        wp[:, row_perm] = w
        for ccx in range(48):
            wig_h[:, 4 * (48 * gi + ccx):4 * (48 * gi + ccx + 1)] = \
                wp[:, 128 * ccx:128 * (ccx + 1)].T
    big_c = np.stack([np.asarray(ins["L1_attn_b_ig"], np.float32),
                      np.asarray(ins["L1_attn_b_fg"], np.float32)], axis=1)
    ln_w = np.asarray(ins["L1_attn_ln_w"], np.float32)
    lnw_row = np.zeros((1, 2048), np.float32)
    for h in range(4):
        for d in range(4):
            lnw_row[0, 512 * h + 128 * d + np.arange(128)] = \
                ln_w[h, 4 * np.arange(128) + d]
    return {
        "wupT": _c(wupT_perm), "wdnT": _c(wdnT_perm),
        "qdiag": _c(qdiag), "kdiag": _c(kdiag), "vdiag": _c(vdiag),
        "convw_c": _c(convw_c), "convb_c": _c(convb_c),
        "wig_h": _c(wig_h), "big_c": _c(big_c),
        "lnw_row": _c(lnw_row), "skip_c": _c(skip_c),
    }


def make_l1_attn_program():
    nc = bass.Bass("TRN2", target_bir_lowering=False)
    al = Alloc(nc)
    rec = Rec()
    def di(name, shape):
        return nc.dram_tensor(name, shape, F32, kind="ExternalInput").ap()
    x = di("x", [T, D])
    y = nc.dram_tensor("y", [T, D], F32, kind="ExternalOutput").ap()
    qf = nc.dram_tensor("qf_hbm", [16, 128, T], BF16).ap()
    kf = nc.dram_tensor("kf_hbm", [16, 128, T], BF16).ap()
    vt = nc.dram_tensor("vt_hbm", [NT, 128, 2048], BF16).ap()
    xcf = nc.dram_tensor("xcf_hbm", [16, 128, T], BF16).ap()
    zf = nc.dram_tensor("zf_hbm", [16, 128, T], BF16).ap()
    build_l1_attn(nc, al, rec, x, y, qf, kf, vt, xcf, zf,
                  di("wupT", [1024, 4096]), di("qdiag", [128, 8192]),
                  di("kdiag", [128, 8192]), di("vdiag", [128, 64]),
                  di("convw_c", [128, 64]), di("convb_c", [128, 16]),
                  di("wig_h", [128, 384]), di("big_c", [4, 2]),
                  di("lnw_row", [1, 2048]), di("skip_c", [128, 16]),
                  di("wdnT", [2048, 1024]))
    finish(nc, rec, al)
    return nc


def prep_l0_mla(ins):
    kv_d = np.asarray(ins["L0_ffn_kv_d"], np.float32)
    q_d = np.asarray(ins["L0_ffn_q_d"], np.float32)
    k_u = np.asarray(ins["L0_ffn_k_u"], np.float32)
    q_u = np.asarray(ins["L0_ffn_q_u"], np.float32)
    v_u = np.asarray(ins["L0_ffn_v_u"], np.float32)
    rq = np.asarray(ins["L0_ffn_rope_q_w"], np.float32)
    rk = np.asarray(ins["L0_ffn_rope_k_w"], np.float32)
    o_w = np.asarray(ins["L0_ffn_o_w"], np.float32)

    RD = 32
    ropekT = np.zeros((1024, 1024), np.float32)
    kuT = np.zeros((256, 1024), np.float32)
    qmixT = np.zeros((256, 1024), np.float32)
    rkT = rk.T
    rqT = rq.T
    kuTT = k_u.T
    quTT = q_u.T
    for j in range(8):
        for h2 in range(2):
            h = 2 * j + h2
            ropekT[:, 128 * j + 64 * h2:128 * j + 64 * h2 + 32] = \
                rkT[:, 32 * h:32 * (h + 1)]
            kuT[:, 128 * j + 64 * h2 + 32:128 * j + 64 * h2 + 64] = \
                kuTT[:, 64 * h + 32:64 * (h + 1)]
            qmixT[:, 128 * j + 64 * h2:128 * j + 64 * h2 + 32] = \
                rqT[:, 32 * h:32 * (h + 1)]
            qmixT[:, 128 * j + 64 * h2 + 32:128 * j + 64 * h2 + 64] = \
                quTT[:, 64 * h + 32:64 * (h + 1)]
    inv_freq = 1.0 / (10000.0 ** (np.arange(0, RD, 2, dtype=np.float64) / RD))
    tpos = np.arange(T, dtype=np.float64)
    freqs = tpos[:, None] * inv_freq[None, :]
    emb = np.concatenate([freqs, freqs], axis=1)
    cos32 = np.cos(emb).astype(np.float32).T
    sin32 = np.sin(emb).astype(np.float32).T
    sin_fold = sin32.copy()
    sin_fold[0:16] = -sin32[0:16]
    cos_t = np.zeros((128, T), np.float32)
    sin_t = np.zeros((128, T), np.float32)
    for h2 in range(2):
        cos_t[64 * h2:64 * h2 + 32] = cos32
        cos_t[64 * h2 + 32:64 * h2 + 64] = 1.0
        sin_t[64 * h2:64 * h2 + 32] = sin_fold
        sin_t[64 * h2 + 32:64 * h2 + 64] = 0.0
    pm = np.zeros((128, 128), np.float32)
    for i in range(128):
        pm[i ^ 16, i] = 1.0
    owt = o_w.T
    owT_h = np.zeros((64, 16384), np.float32)
    for h in range(16):
        owT_h[:, 1024 * h:1024 * (h + 1)] = owt[64 * h:64 * (h + 1), :]
    return {
        "kvdT": _c(kv_d.T), "qdT": _c(q_d.T), "ropekT": _c(ropekT),
        "kuT": _c(kuT), "qmixT": _c(qmixT), "vuT": _c(v_u.T),
        "owT": _c(owT_h), "ln1_r": _row(ins["L0_ffn_ln1_w"]),
        "cos_t": _c(cos_t), "sin_t": _c(sin_t), "pmat": _c(pm),
    }


def make_l0_mla_program():
    nc = bass.Bass("TRN2", target_bir_lowering=False)
    al = Alloc(nc)
    rec = Rec()
    def di(name, shape):
        return nc.dram_tensor(name, shape, F32, kind="ExternalInput").ap()
    x = di("x", [T, D])
    s1 = nc.dram_tensor("s1", [T, D], F32, kind="ExternalOutput").ap()
    xin_hbm = nc.dram_tensor("xin_hbm", [T, D], F32, kind="ExternalOutput").ap()
    k_hbm = nc.dram_tensor("k_hbm", [NT, 8, 128, 128], BF16).ap()
    q_hbm = nc.dram_tensor("q_hbm", [NT, 8, 128, 128], BF16).ap()
    v_hbm = nc.dram_tensor("v_hbm", [NT, 128, 1040], BF16).ap()
    build_l0_mla(nc, al, rec, x, s1, xin_hbm, k_hbm, q_hbm, v_hbm,
                 di("kvdT", [1024, 256]), di("qdT", [1024, 256]),
                 di("ropekT", [1024, 1024]), di("kuT", [256, 1024]),
                 di("qmixT", [256, 1024]), di("vuT", [256, 1024]),
                 di("owT", [64, 16384]), di("ln1_r", [1, D]),
                 di("cos_t", [128, T]), di("sin_t", [128, T]),
                 di("pmat", [128, 128]))
    finish(nc, rec, al)
    return nc


def prep_l0_attn(ins):
    w_qkv = np.asarray(ins["L0_attn_w_qkv"], np.float32)
    b_qkv = np.asarray(ins["L0_attn_b_qkv"], np.float32)
    return {
        "wqk": _c(w_qkv[:, 0:1024]),
        "wv": _c(w_qkv[:, 1024:1536]),
        "wfc": _c(ins["L0_attn_w_fc"]),
        "bqk_r": _row(b_qkv[0:1024]),
        "bv_c": _col_chunks(b_qkv[1024:1536], 4),
        "bfc_r": _row(ins["L0_attn_b_fc"]),
        "gam_r": _row(ins["L0_attn_gamma"]),
        "bet_r": _row(ins["L0_attn_beta"]),
    }


def make_l0_attn_program():
    nc = bass.Bass("TRN2", target_bir_lowering=False)
    al = Alloc(nc)
    rec = Rec()
    def di(name, shape):
        return nc.dram_tensor(name, shape, F32, kind="ExternalInput").ap()
    x = di("x", [T, D])
    y = nc.dram_tensor("y", [T, D], F32, kind="ExternalOutput").ap()
    build_l0_attn(nc, al, rec, x, y, di("wqk", [1024, 1024]),
                  di("wv", [1024, 512]), di("wfc", [512, 1024]),
                  di("bqk_r", [1, 1024]), di("bv_c", [128, 4]),
                  di("bfc_r", [1, 1024]), di("gam_r", [1, 1024]),
                  di("bet_r", [1, 1024]))
    finish(nc, rec, al)
    return nc


def prep_l1_ffn(ins):
    return {
        "W1": _c(ins["L1_ffn_W1"]),
        "W2": _c(ins["L1_ffn_W2"]),
        "b1c": _col_chunks(ins["L1_ffn_b1"], 32),
        "thc": _col_chunks(ins["L1_ffn_quantum_phases"], 32),
        "wrc": _col_chunks(ins["L1_ffn_quantum_weights_real"], 32),
        "wic": _col_chunks(ins["L1_ffn_quantum_weights_imag"], 32),
        "b2row": _row(ins["L1_ffn_b2"]),
    }


def make_l1_ffn_program():
    nc = bass.Bass("TRN2", target_bir_lowering=False)
    al = Alloc(nc)
    rec = Rec()
    x = nc.dram_tensor("x", [T, D], F32, kind="ExternalInput").ap()
    y = nc.dram_tensor("y", [T, D], F32, kind="ExternalOutput").ap()
    W1 = nc.dram_tensor("W1", [D, 4096], F32, kind="ExternalInput").ap()
    W2 = nc.dram_tensor("W2", [4096, D], F32, kind="ExternalInput").ap()
    b1c = nc.dram_tensor("b1c", [128, 32], F32, kind="ExternalInput").ap()
    thc = nc.dram_tensor("thc", [128, 32], F32, kind="ExternalInput").ap()
    wrc = nc.dram_tensor("wrc", [128, 32], F32, kind="ExternalInput").ap()
    wic = nc.dram_tensor("wic", [128, 32], F32, kind="ExternalInput").ap()
    b2row = nc.dram_tensor("b2row", [1, D], F32, kind="ExternalInput").ap()
    r_hbm = nc.dram_tensor("r_scratch", [32, 128, T], BF16).ap()
    build_l1_ffn(nc, al, rec, x, y, W1, W2, b1c, thc, wrc, wic, b2row, r_hbm)
    finish(nc, rec, al)
    return nc



def make_fused_program():
    nc = bass.Bass("TRN2", target_bir_lowering=False)
    al = Alloc(nc)
    rec = Rec()

    def di(name, shape, dt=F32):
        return nc.dram_tensor(name, shape, dt, kind="ExternalInput").ap()

    def dint(name, shape, dt=F32):
        return nc.dram_tensor(name, shape, dt).ap()

    x0 = di("x", [T, D])
    y = nc.dram_tensor("y", [T, D], F32, kind="ExternalOutput").ap()
    x1 = dint("x1", [T, D])
    s1 = dint("s1", [T, D])
    x2 = dint("x2", [T, D])
    x3 = dint("x3", [T, D])

    gbf = dint("gbf_hbm", [8, 1024, 2048], BF16)
    ubf = dint("ubf_hbm", [8, 1024, 2048], BF16)
    dbf = dint("dbf_hbm", [8, 2048, 1024], BF16)
    mk = al.mark()
    wc_emits = build_moe_wconv(nc, al, rec,
                               di("p3_gateT", [8, 1024, 2048]),
                               di("p3_upT", [8, 1024, 2048]),
                               di("p3_downT", [8, 2048, 1024]), gbf, ubf, dbf)
    wc_it = iter(wc_emits)

    def wc_filler(n):
        for _ in range(n):
            j = next(wc_it, None)
            if j is None:
                return
            j()

    build_l0_attn(nc, al, rec, x0, x1,
                  di("p1_wqk", [1024, 1024]), di("p1_wv", [1024, 512]),
                  di("p1_wfc", [512, 1024]), di("p1_bqk_r", [1, 1024]),
                  di("p1_bv_c", [128, 4]), di("p1_bfc_r", [1, 1024]),
                  di("p1_gam_r", [1, 1024]), di("p1_bet_r", [1, 1024]),
                  filler=wc_filler)
    for j in wc_it:
        j()
    rec.fence(tag="ph1_end")
    al.free_suffix(mk)

    mk = al.mark()
    xin_hbm = dint("xin_hbm", [T, D])
    k_hbm = dint("k_hbm", [NT, 8, 128, 128], BF16)
    q_hbm = dint("q_hbm", [NT, 8, 128, 128], BF16)
    v_hbm = dint("v_hbm", [NT, 128, 1040], BF16)
    build_l0_mla(nc, al, rec, x1, s1, xin_hbm, k_hbm, q_hbm, v_hbm,
                 di("p2_kvdT", [1024, 256]), di("p2_qdT", [1024, 256]),
                 di("p2_ropekT", [1024, 1024]), di("p2_kuT", [256, 1024]),
                 di("p2_qmixT", [256, 1024]), di("p2_vuT", [256, 1024]),
                 di("p2_owT", [64, 16384]), di("p2_ln1_r", [1, D]),
                 di("p2_cos_t", [128, T]), di("p2_sin_t", [128, T]),
                 di("p2_pmat", [128, 128]))
    rec.fence(tag="ph2_end")
    al.free_suffix(mk)

    mk = al.mark()
    alf_hbm = dint("alf_hbm", [8, 128, T], BF16)
    build_l0_moe(nc, al, rec, s1, x1, x2, alf_hbm, gbf, ubf, dbf,
                 di("p3_routerT", [1024, 7]),
                 di("p3_rbias_r", [1, 7]), di("p3_ln2_r", [1, D]))
    rec.fence(tag="ph3_end")
    al.free_suffix(mk)

    mk = al.mark()
    qf = dint("qf_hbm", [16, 128, T], BF16)
    kf = dint("kf_hbm", [16, 128, T], BF16)
    vt = dint("vt_hbm", [NT, 128, 2048], BF16)
    xcf = dint("xcf_hbm", [16, 128, T], BF16)
    zf = dint("zf_hbm", [16, 128, T], BF16)
    build_l1_attn(nc, al, rec, x2, x3, qf, kf, vt, xcf, zf,
                  di("p4_wupT", [1024, 4096]), di("p4_qdiag", [128, 8192]),
                  di("p4_kdiag", [128, 8192]), di("p4_vdiag", [128, 64]),
                  di("p4_convw_c", [128, 64]), di("p4_convb_c", [128, 16]),
                  di("p4_wig_h", [128, 384]), di("p4_big_c", [4, 2]),
                  di("p4_lnw_row", [1, 2048]), di("p4_skip_c", [128, 16]),
                  di("p4_wdnT", [2048, 1024]))
    rec.fence(tag="ph4_end")
    al.free_suffix(mk)

    mk = al.mark()
    r_hbm = dint("r_scratch", [32, 128, T], BF16)
    build_l1_ffn(nc, al, rec, x3, y,
                 di("p5_W1", [D, 4096]), di("p5_W2", [4096, D]),
                 di("p5_b1c", [128, 32]), di("p5_thc", [128, 32]),
                 di("p5_wrc", [128, 32]), di("p5_wic", [128, 32]),
                 di("p5_b2row", [1, D]), r_hbm)
    finish(nc, rec, al)
    return nc



_PROGS = {}


def _get_prog(name, maker):
    if name not in _PROGS:
        _PROGS[name] = maker()
    return _PROGS[name]


def kernel(**inputs):
    x0 = np.asarray(inputs["L0_attn_x"], np.float32)
    B = x0.shape[0]
    cores = list(range(B))

    w = {}
    for pfx, prep in (("p1_", prep_l0_attn), ("p2_", prep_l0_mla),
                      ("p3_", prep_l0_moe), ("p4_", prep_l1_attn),
                      ("p5_", prep_l1_ffn)):
        for k, v in prep(inputs).items():
            w[pfx + k] = v

    nc = _get_prog("fused", make_fused_program)
    in_maps = []
    for b in range(B):
        m = dict(w)
        m["x"] = np.ascontiguousarray(x0[b])
        in_maps.append(m)
    res = run_bass_kernel_spmd(nc, in_maps, core_ids=cores)
    return np.stack([np.asarray(res.results[b]["y"]) for b in range(B)])


# version 16 (final)
# speedup vs baseline: 1.0627x; 1.0213x over previous
import math
import numpy as np
import concourse.bass as bass
import concourse.mybir as mybir
from concourse.bass_utils import run_bass_kernel_spmd

F32 = mybir.dt.float32
BF16 = mybir.dt.bfloat16
AF = mybir.ActivationFunctionType
ALU = mybir.AluOpType
AX = mybir.AxisListType

T = 4096
D = 1024
NT = T // 128
NP = T // 512

ENGINES = ("sync", "tensor", "vector", "scalar", "gpsimd")


NDMA_SEMS = 8


class Op:
    __slots__ = ("eng", "fn", "deps", "is_dma", "ordinal", "waits", "tag",
                 "sem_key", "sem_val")

    def __init__(self, eng, fn, deps, is_dma, tag):
        self.eng = eng
        self.fn = fn
        self.deps = deps
        self.is_dma = is_dma
        self.ordinal = None
        self.waits = []
        self.tag = tag
        self.sem_key = None
        self.sem_val = None


class Rec:

    def __init__(self):
        self.streams = {e: [] for e in ENGINES}
        self.bufs = {}

    def add(self, eng, fn, deps=(), is_dma=False, tag="", r=(), w=()):
        deps = [d for d in deps if d is not None]
        for b in r:
            ent = self.bufs.get(b)
            if ent is not None and ent[0] is not None:
                deps.append(ent[0])
        for b in w:
            ent = self.bufs.get(b)
            if ent is not None:
                if ent[0] is not None:
                    deps.append(ent[0])
                deps.extend(ent[1].values())
                deps.extend(ent[2])
        if eng == "tensor" and not is_dma:
            deps = [d for d in deps
                    if not (d.eng == "tensor" and not d.is_dma)]
        seen = set()
        uniq = []
        for d in deps:
            if id(d) not in seen:
                seen.add(id(d))
                uniq.append(d)
        op = Op(eng, fn, uniq, is_dma, tag)
        self.streams[eng].append(op)
        for b in r:
            ent = self.bufs.setdefault(b, [None, {}, []])
            if is_dma:
                ent[2].append(op)
            else:
                ent[1][eng] = op
        for b in w:
            self.bufs[b] = [op, {}, []]
        return op

    def dma(self, eng, out, in_, deps=(), tag="", r=(), w=()):
        return self.add(
            eng, lambda e, o=out, i=in_: e.dma_start(out=o, in_=i),
            deps, is_dma=True, tag=tag, r=r, w=w)

    def fence(self, tag="fence"):
        lasts = []
        for e in ENGINES:
            if self.streams[e]:
                lasts.append(self.streams[e][-1])
            per_slot = {}
            di = 0
            for op in self.streams[e]:
                if op.is_dma:
                    per_slot[di % NDMA_SEMS] = op
                    di += 1
            lasts.extend(per_slot.values())
        for e in ENGINES:
            self.add(e, lambda en: en.nop(), deps=list(lasts), tag=tag)

    def emit(self, nc, sems):
        pos = {}
        for e in ENGINES:
            for i, op in enumerate(self.streams[e]):
                pos[id(op)] = i
        slot_prev = {}
        for e in ENGINES:
            dma_idx = 0
            dcount = [0] * NDMA_SEMS
            for op in self.streams[e]:
                if not op.is_dma:
                    continue
                j = dma_idx % NDMA_SEMS
                dma_idx += 1
                dcount[j] += 16
                op.ordinal = -1
                op.sem_key = (e, "d", j)
                op.sem_val = dcount[j]
                key = (e, j)
                if key in slot_prev:
                    op.deps = op.deps + [slot_prev[key]]
                slot_prev[key] = op
        for e in ENGINES:
            for op in self.streams[e]:
                for d in op.deps:
                    if d.ordinal is None:
                        d.ordinal = -1
        for e in ENGINES:
            cc = 0
            for op in self.streams[e]:
                if op.ordinal == -1 and not op.is_dma:
                    cc += 1
                    op.sem_key = (e, "c")
                    op.sem_val = cc
        for e in ENGINES:
            water = {}
            for op in self.streams[e]:
                for d in op.deps:
                    if d.sem_key is None:
                        continue
                    if water.get(d.sem_key, 0) < d.sem_val:
                        water[d.sem_key] = d.sem_val
                        op.waits.append((d.sem_key, d.sem_val))
        with nc.Block() as blk:
            for e in ENGINES:
                stream = self.streams[e]
                if not stream:
                    continue
                dec = getattr(blk, e)

                def make(stream=stream):
                    def _(eng):
                        for op in stream:
                            for (key, val) in op.waits:
                                eng.wait_ge(sems[key], val)
                            instr = op.fn(eng)
                            if op.sem_key is not None:
                                instr.then_inc(sems[op.sem_key],
                                               16 if op.is_dma else 1)
                    return _

                dec(make())


class Alloc:

    def __init__(self, nc):
        self.nc = nc
        self.ctx = []

    def sb(self, name, shape, dt):
        cm = self.nc.sbuf_tensor(name, shape, dt)
        h = cm.__enter__()
        self.ctx.append(cm)
        self.named = getattr(self, "named", {})
        self.named[name] = cm
        return h

    def mark(self):
        return len(self.ctx)

    def free_suffix(self, mark):
        for cm in reversed(self.ctx[mark:]):
            cm.__exit__(None, None, None)
        dead = set(id(cm) for cm in self.ctx[mark:])
        del self.ctx[mark:]
        self.named = {k: v for k, v in getattr(self, "named", {}).items()
                      if id(v) not in dead}

    def free(self, names):
        cms = [self.named.pop(n) for n in names]
        cms.sort(key=lambda cm: self.ctx.index(cm), reverse=True)
        for cm in cms:
            self.ctx.remove(cm)
            cm.__exit__(None, None, None)

    def ps(self, name, shape, dt=F32):
        cm = self.nc.psum_tensor(name, shape, dt)
        h = cm.__enter__()
        self.ctx.append(cm)
        return h

    def sem(self, name):
        cm = self.nc.semaphore(name)
        h = cm.__enter__()
        self.ctx.append(cm)
        return h

    def close(self):
        for cm in reversed(self.ctx):
            cm.__exit__(None, None, None)
        self.ctx = []


def finish(nc, rec, al):
    sems = {}
    for e in ENGINES:
        sems[(e, "c")] = al.sem(f"s_{e}_c")
        for j in range(NDMA_SEMS):
            sems[(e, "d", j)] = al.sem(f"s_{e}_d{j}")
    tail_deps = []
    for e in ENGINES:
        if rec.streams[e]:
            tail_deps.append(rec.streams[e][-1])
        last_per_slot = {}
        di = 0
        for op in rec.streams[e]:
            if op.is_dma:
                last_per_slot[di % NDMA_SEMS] = op
                di += 1
        tail_deps.extend(last_per_slot.values())
    rec.add("sync", lambda e: e.nop(), deps=tail_deps, tag="final_wait")
    rec.emit(nc, sems)



def _c(a):
    return np.ascontiguousarray(np.asarray(a), dtype=np.float32)


def _col_chunks(v, nchunk):
    return _c(np.asarray(v).reshape(nchunk, 128).T)


def _row(v):
    return _c(np.asarray(v).reshape(1, -1))



class ConstPool:

    def __init__(self, rec, tile):
        self.rec = rec
        self.tile = tile
        self.vals = {}
        self.n = 0
        self.op = None

    def get(self, v):
        v = float(v)
        if v not in self.vals:
            assert self.n < self.tile.shape[1]
            j = self.n
            self.n += 1
            self.vals[v] = j
            o = self.rec.add("gpsimd", lambda e, j=j, v=v: e.memset(
                self.tile[:, j:j + 1], v), deps=[self.op], tag="const")
            self.op = o
        return self.tile[:, self.vals[v]:self.vals[v] + 1]


def make_identity_ops(rec, ident):
    sq = ident.shape[1]
    o1 = rec.add("gpsimd", lambda e: e.memset(ident[:], 0.0), tag="id0")
    return rec.add("gpsimd", lambda e: e.affine_select(
        out=ident[:], in_=ident[:], compare_op=ALU.not_equal, fill=1.0,
        base=0, pattern=[[-1, sq]], channel_multiplier=1), deps=[o1], tag="id1")


class LNHelper:

    def __init__(self, rec, scratch_sq, stats, name="ln", sq_tok=None):
        self.rec = rec
        self.sq = scratch_sq
        self.st = stats
        self.nm = name
        self.sq_tok = sq_tok or (name + "sq")

    def emit(self, x_ap, x_tok, xln_out, xln_tok, eps_ap, deps=(), const_dep=None):
        rec, sq, st = self.rec, self.sq, self.st
        nm = self.nm
        w = x_ap.shape[1]
        d = float(w)
        o_sum = rec.add("vector", lambda e: e.reduce_sum(st[:, 0:1], x_ap, axis=AX.X),
                        deps=deps, tag="ln_sum", r=[x_tok], w=[nm + "a"])
        o_sq = rec.add("scalar", lambda e: e.activation(
            sq[:, 0:w], x_ap, AF.Square, accum_out=st[:, 1:2]), deps=deps,
            tag="ln_sq", r=[x_tok], w=[nm + "b", self.sq_tok])
        rec.add("vector", lambda e: e.tensor_scalar(
            st[:, 2:3], st[:, 0:1], -1.0 / d, None, ALU.mult),
            tag="ln_nm", r=[nm + "a"], w=[nm + "c"])
        rec.add("vector", lambda e: e.tensor_scalar(
            st[:, 3:4], st[:, 1:2], 1.0 / d, None, ALU.mult),
            tag="ln_ms", r=[nm + "b"], w=[nm + "d"])
        rec.add("vector", lambda e: e.tensor_tensor(
            st[:, 4:5], st[:, 2:3], st[:, 2:3], ALU.mult),
            tag="ln_m2", r=[nm + "c"], w=[nm + "e"])
        rec.add("vector", lambda e: e.tensor_tensor(
            st[:, 5:6], st[:, 3:4], st[:, 4:5], ALU.subtract),
            tag="ln_var", r=[nm + "d", nm + "e"], w=[nm + "f"])
        rec.add("scalar", lambda e: e.activation(
            st[:, 6:7], st[:, 5:6], AF.Ln, bias=eps_ap), deps=[const_dep],
            tag="ln_ln", r=[nm + "f"], w=[nm + "g"])
        rec.add("scalar", lambda e: e.activation(
            st[:, 7:8], st[:, 6:7], AF.Exp, scale=-0.5),
            tag="ln_rstd", r=[nm + "g"], w=[nm + "h"])
        o_out = rec.add("vector", lambda e: e.tensor_scalar(
            xln_out, x_ap, st[:, 2:3], st[:, 7:8], ALU.add, ALU.mult),
            tag="ln_out", r=[x_tok, nm + "c", nm + "h"], w=[xln_tok])
        return o_out


def transpose_to_fm(rec, ident, id_op, xln16, xln_tok, p_tr, ptr_tok,
                    xf, xf_tok, tl, nchunk=8, col_stride=512):
    o_trs = []
    for c in range(nchunk):
        o_t = rec.add("tensor", lambda e, c=c: e.transpose(
            p_tr[:, 128 * c:128 * (c + 1)], xln16[:, 128 * c:128 * (c + 1)],
            ident[:]), deps=[id_op], tag=f"tr{c}",
            r=[xln_tok], w=[ptr_tok])
        o_trs.append(o_t)
    evs = []
    for c in range(nchunk):
        o_ev = rec.add("vector", lambda e, c=c, tl=tl: e.tensor_copy(
            xf[:, col_stride * c + 128 * tl: col_stride * c + 128 * (tl + 1)],
            p_tr[:, 128 * c:128 * (c + 1)]),
            tag=f"trev{c}", r=[ptr_tok], w=[xf_tok])
        evs.append(o_ev)
    return evs



def build_l1_ffn(nc, al, rec, x, y, W1, W2, b1c, thc, wrc, wic, b2row, r_hbm,
                 pfx="f4_"):

    S, P = al.sb, al.ps

    ident = S(pfx + "ident", [128, 128], BF16)
    ones1 = S(pfx + "ones1", [1, 128], F32)
    w_s = S(pfx + "w_s", [128, 32768], BF16)
    wstage = [S(pfx + f"wstage{i}", [128, 4096], F32) for i in range(2)]
    b1_s = S(pfx + "b1", [128, 32], F32)
    th_s = S(pfx + "th", [128, 32], F32)
    wr_s = S(pfx + "wr", [128, 32], F32)
    wi_s = S(pfx + "wi", [128, 32], F32)
    vec_s = S(pfx + "vec", [128, 32 * 8], F32)
    rr_int = S(pfx + "rrint", [128, 32], mybir.dt.int32)
    b2r_s = S(pfx + "b2r", [1, D], F32)
    b2bc = S(pfx + "b2bc", [128, D], F32)
    xt = [S(pfx + f"xt{i}", [128, D], F32) for i in range(2)]
    sq_scr = S(pfx + "sq", [128, D], F32)
    stats = S(pfx + "stats", [128, 8], F32)
    xln16 = S(pfx + "xln16", [128, D], BF16)
    xf = S(pfx + "xf", [128, T], BF16)
    c1 = S(pfx + "c1", [128, 512], F32)
    s1 = S(pfx + "s1", [128, 512], F32)
    hid = S(pfx + "hid", [128, 512], F32)
    t12 = S(pfx + "t12", [128, 512], F32)
    r16 = [S(pfx + f"r16{i}", [128, 512], BF16) for i in range(2)]
    rbuf = S(pfx + "rbuf", [128, 32 * 512], BF16)
    outt = [S(pfx + f"outt{i}", [128, D], F32) for i in range(2)]
    cpool = S(pfx + "cpool", [128, 8], F32)

    p_tr = P(pfx + "ptr", [128, 1024], BF16)
    p_h = [P(pfx + f"ph{i}", [128, 512]) for i in range(2)]
    p_o = [P(pfx + f"po{i}", [128, 512]) for i in range(2)]
    p_bc = P(pfx + "pbc", [128, 512])

    ln = LNHelper(rec, sq_scr, stats, name=pfx + "ln")
    cp = ConstPool(rec, cpool)
    eps5 = cp.get(1e-5)
    cp_op = cp.op

    o_id = make_identity_ops(rec, ident)
    rec.add("gpsimd", lambda e: e.memset(ones1[:], 1.0), tag="ones", w=["ones1"])
    rec.dma("sync", b1_s[:], b1c[:], w=["b1"])
    rec.dma("sync", th_s[:], thc[:], w=["th"])
    rec.dma("sync", wr_s[:], wrc[:], w=["wr"])
    rec.dma("sync", wi_s[:], wic[:], w=["wi"])
    rec.dma("sync", b2r_s[:], b2row[:], w=["b2r"])

    for j in range(2):
        rec.add("tensor", lambda e, j=j: e.matmul(
            p_bc[:], ones1[0:1, :], b2r_s[0:1, 512 * j:512 * (j + 1)],
            start=True, stop=True), tag="b2bc",
            r=["ones1", "b2r"], w=["pbc"])
        rec.add("scalar", lambda e, j=j: e.copy(
            b2bc[:, 512 * j:512 * (j + 1)], p_bc[:]), tag="b2cp",
            r=["pbc"], w=["b2bc"])

    V = lambda i: vec_s[:, 32 * i:32 * (i + 1)]
    TWO_PI = 2.0 * math.pi

    def sin_reduced(dst_i, src_ap, src_tok, add_const, tmp_i=7):
        t_a = V(tmp_i)
        rec.add("vector", lambda e: e.tensor_scalar(
            t_a, src_ap, float(add_const), 1.0 / TWO_PI, ALU.add, ALU.mult),
            tag="rr_scale", r=[src_tok], w=["vec"])
        rec.add("vector", lambda e: e.tensor_copy(rr_int[:], t_a),
                tag="rr_toint", r=["vec"], w=["rrint"])
        rec.add("vector", lambda e: e.tensor_copy(V(dst_i), rr_int[:]),
                tag="rr_tofl", r=["rrint"], w=["vec"])
        rec.add("vector", lambda e: e.tensor_scalar(
            t_a, src_ap, float(add_const), None, ALU.add),
            tag="rr_add", r=[src_tok], w=["vec"])
        rec.add("vector", lambda e: e.scalar_tensor_tensor(
            t_a, V(dst_i), -TWO_PI, t_a, ALU.mult, ALU.add),
            tag="rr_fold", r=["vec"], w=["vec"])
        rec.add("vector", lambda e: e.tensor_scalar(
            V(dst_i), t_a, float(math.pi), None, ALU.is_gt),
            tag="rr_gt", r=["vec"], w=["vec"])
        rec.add("vector", lambda e: e.scalar_tensor_tensor(
            t_a, V(dst_i), -TWO_PI, t_a, ALU.mult, ALU.add),
            tag="rr_f2", r=["vec"], w=["vec"])
        rec.add("vector", lambda e: e.tensor_scalar(
            V(dst_i), t_a, float(-math.pi), None, ALU.is_lt),
            tag="rr_lt", r=["vec"], w=["vec"])
        rec.add("vector", lambda e: e.scalar_tensor_tensor(
            t_a, V(dst_i), TWO_PI, t_a, ALU.mult, ALU.add),
            tag="rr_f3", r=["vec"], w=["vec"])
        return rec.add("scalar", lambda e: e.activation(V(dst_i), t_a, AF.Sin),
                       tag="rr_sin", r=["vec"], w=["vec"])

    sin_reduced(0, th_s[:], "th", 0.0)
    sin_reduced(1, th_s[:], "th", math.pi / 2.0)
    rec.add("vector", lambda e: e.tensor_tensor(V(2), V(1), wr_s[:], ALU.mult),
            tag="qp1", r=["vec", "wr"], w=["vec"])
    rec.add("vector", lambda e: e.tensor_tensor(V(7), V(0), wi_s[:], ALU.mult),
            tag="qp2", r=["vec", "wi"], w=["vec"])
    rec.add("vector", lambda e: e.tensor_tensor(V(2), V(2), V(7), ALU.add),
            tag="qp", r=["vec"], w=["vec"])
    rec.add("vector", lambda e: e.tensor_tensor(V(2), V(2), wi_s[:], ALU.mult),
            tag="A1", r=["vec", "wi"], w=["vec"])
    rec.add("vector", lambda e: e.tensor_scalar(V(2), V(2), 0.1, None, ALU.mult),
            tag="A", r=["vec"], w=["vec"])
    rec.add("vector", lambda e: e.tensor_tensor(V(3), V(2), V(0), ALU.mult),
            tag="Sp", r=["vec"], w=["vec"])
    rec.add("vector", lambda e: e.tensor_tensor(V(4), V(2), V(1), ALU.mult),
            tag="Cp", r=["vec"], w=["vec"])
    rec.add("vector", lambda e: e.tensor_scalar(
        V(5), b1_s[:], 0.1, math.pi / 2.0, ALU.mult, ALU.add),
        tag="b5", r=["b1"], w=["vec"])
    o_vec_done = rec.add("vector", lambda e: e.tensor_scalar(
        V(6), b1_s[:], 0.1, None, ALU.mult), tag="b6", r=["b1"], w=["vec"])

    for k in range(8):
        stg = wstage[k % 2]
        rec.dma("sync", stg[:], W1[128 * k:128 * (k + 1), :],
                tag=f"w1ld{k}", w=[f"wstage{k % 2}"])
        rec.add("vector", lambda e, k=k, stg=stg: e.tensor_copy(
            w_s[:, 4096 * k:4096 * (k + 1)], stg[:]),
            tag=f"w1cast{k}", r=[f"wstage{k % 2}"], w=["w_s"])

    for p in range(NP):
        for tl in range(4):
            ti = 4 * p + tl
            q = ti % 2
            rec.dma("sync", xt[q][:], x[128 * ti:128 * (ti + 1), :],
                    tag=f"x{ti}", w=[f"xt{q}"])
            ln.emit(xt[q][:], f"xt{q}", xln16[:], "xln16", eps5,
                    const_dep=cp_op)
            transpose_to_fm(rec, ident, o_id, xln16, "xln16", p_tr, "ptr",
                            xf, "xf", tl)
        for m in range(32):
            bank = p_h[m % 2]
            bt = f"ph{m % 2}"
            for k in range(8):
                rec.add("tensor", lambda e, k=k, m=m, bank=bank: e.matmul(
                    bank[:], w_s[:, 4096 * k + 128 * m:4096 * k + 128 * (m + 1)],
                    xf[:, 512 * k:512 * (k + 1)],
                    start=(k == 0), stop=(k == 7)),
                    tag=f"hmm{p}_{m}", r=["w_s", "xf"], w=[bt])
            rec.add("scalar", lambda e, m=m, bank=bank: e.activation(
                c1[:], bank[:], AF.Sin, bias=V(5)[:, m:m + 1], scale=0.1),
                tag="qc1", r=[bt, "vec"], w=["c1"])
            rec.add("scalar", lambda e, m=m, bank=bank: e.activation(
                s1[:], bank[:], AF.Sin, bias=V(6)[:, m:m + 1], scale=0.1),
                tag="qs1", r=[bt, "vec"], w=["s1"])
            rec.add("scalar", lambda e, m=m, bank=bank: e.add(
                hid[:], bank[:], b1_s[:, m:m + 1]),
                tag="qh", r=[bt, "b1"], w=["hid"])
            rec.add("vector", lambda e, m=m: e.scalar_tensor_tensor(
                t12[:], c1[:], V(3)[:, m:m + 1], hid[:], ALU.mult, ALU.add),
                tag="qt", r=["c1", "hid", "vec"], w=["t12"])
            rec.add("vector", lambda e, m=m: e.scalar_tensor_tensor(
                c1[:], s1[:], V(4)[:, m:m + 1], t12[:], ALU.mult, ALU.add),
                tag="qq", r=["s1", "t12", "vec", "c1"], w=["c1"])
            rb = r16[m % 2]
            rec.add("scalar", lambda e, rb=rb: e.activation(
                rb[:], c1[:], AF.Relu), tag="qr",
                r=["c1"], w=[f"r16{m % 2}"])
            rec.dma("gpsimd", r_hbm[m, :, 512 * p:512 * (p + 1)], rb[:],
                    tag=f"rs{p}_{m}", r=[f"r16{m % 2}"], w=[f"rhbm{p}_{m}"])

    for m in range(32):
        stg = wstage[m % 2][:, 0:1024]
        rec.dma("sync", stg, W2[128 * m:128 * (m + 1), :],
                tag=f"w2ld{m}", w=[f"wstage{m % 2}"])
        rec.add("vector", lambda e, m=m, stg=stg: e.tensor_copy(
            w_s[:, 1024 * m:1024 * (m + 1)], stg),
            tag=f"w2cast{m}", r=[f"wstage{m % 2}"], w=["w_s"])

    y_done = []
    for p in range(NP):
        for m in range(32):
            rec.dma("sync", rbuf[:, 512 * m:512 * (m + 1)],
                    r_hbm[m, :, 512 * p:512 * (p + 1)],
                    tag=f"rl{p}_{m}", r=[f"rhbm{p}_{m}"], w=[f"rbuf{m}"])
        for tl in range(4):
            ti = 4 * p + tl
            q = tl % 2
            obuf = outt[q]
            rec.dma("scalar", xt[q][:], x[128 * ti:128 * (ti + 1), :],
                    tag=f"xb{ti}", w=[f"xt{q}"])
            for nh in range(2):
                bank = p_o[nh]
                bt = f"po{nh}"
                for m in range(32):
                    rec.add("tensor", lambda e, m=m, tl=tl, nh=nh, bank=bank:
                            e.matmul(
                        bank[:],
                        rbuf[:, 512 * m + 128 * tl:512 * m + 128 * (tl + 1)],
                        w_s[:, 1024 * m + 512 * nh:1024 * m + 512 * (nh + 1)],
                        start=(m == 0), stop=(m == 31)),
                        tag=f"omm{ti}_{nh}", r=[f"rbuf{m}", "w_s"], w=[bt])
                rec.add("vector", lambda e, nh=nh, obuf=obuf, bank=bank:
                        e.tensor_tensor(
                    obuf[:, 512 * nh:512 * (nh + 1)], bank[:],
                    b2bc[:, 512 * nh:512 * (nh + 1)], ALU.add),
                    tag="oadd1", r=[bt, "b2bc"], w=[f"outt{q}"])
                rec.add("vector", lambda e, nh=nh, obuf=obuf, q=q:
                        e.tensor_tensor(
                    obuf[:, 512 * nh:512 * (nh + 1)],
                    obuf[:, 512 * nh:512 * (nh + 1)],
                    xt[q][:, 512 * nh:512 * (nh + 1)], ALU.add),
                    tag="oadd2", r=[f"outt{q}", f"xt{q}"], w=[f"outt{q}"])
            d_y = rec.dma("sync", y[128 * ti:128 * (ti + 1), :], obuf[:],
                          tag=f"y{ti}", r=[f"outt{q}"])
            y_done.append(d_y)
    return y_done



def build_l0_attn(nc, al, rec, x, y, wqk, wv, wfc, bqk_r, bv_c, bfc_r,
                  gam_r, bet_r, pfx="f1_", filler=None):
    S, P = al.sb, al.ps

    ident = S(pfx + "ident", [128, 128], BF16)
    ones1 = S(pfx + "ones1", [1, 128], F32)
    wqk_s = S(pfx + "wqk", [128, 8192], BF16)
    wv_s = S(pfx + "wv", [128, 4096], BF16)
    wfc_s = S(pfx + "wfc", [128, 4096], BF16)
    wstage = [S(pfx + f"wstage{i}", [128, 1024], F32) for i in range(2)]
    brow = S(pfx + "brow", [1, 4096], F32)
    bbc = S(pfx + "bbc", [128, 4096], F32)
    bv_s = S(pfx + "bv", [128, 4], F32)
    xt = [S(pfx + f"xt{i}", [128, D], F32) for i in range(2)]
    sq_scr = S(pfx + "sq", [128, D], F32)
    stats = S(pfx + "stats", [128, 8], F32)
    xln16 = [S(pfx + f"xln16{i}", [128, D], BF16) for i in range(2)]
    xf = [S(pfx + f"xf{i}", [128, D], BF16) for i in range(2)]
    qk16 = S(pfx + "qk16", [128, 1024], BF16)
    vfm = [S(pfx + f"vfm{c}", [128, T], BF16) for c in range(4)]
    mu_all = S(pfx + "mu", [128, 32], F32)
    rs_all = S(pfx + "rs", [128, 32], F32)
    sc32 = S(pfx + "sc32", [64, 512], F32)
    smax = S(pfx + "smax", [64, 8], F32)
    ssum = S(pfx + "ssum", [64, 8], F32)
    srec = S(pfx + "srec", [64, 8], F32)
    attn16a = S(pfx + "attn16a", [64, 512], BF16)
    attn16 = S(pfx + "attn16", [128, 256], BF16)
    o16 = S(pfx + "o16", [128, 2048], BF16)
    ev1 = S(pfx + "ev1", [128, D], F32)
    outt = [S(pfx + f"outt{i}", [128, D], F32) for i in range(2)]
    st2 = S(pfx + "st2", [128, 8], F32)
    cpool = S(pfx + "cpool", [128, 8], F32)

    p_tr = P(pfx + "ptr", [128, 1024], BF16)
    p_qk = P(pfx + "pqk", [128, 1024])
    p_sc = P(pfx + "psc", [64, 512])
    p_v = P(pfx + "pv", [128, 512])
    p_o = P(pfx + "po", [128, 512])
    p_fc = P(pfx + "pfc", [128, 1024])
    p_bc = p_fc

    ln = LNHelper(rec, sq_scr, stats, name=pfx + "ln")
    ln2 = LNHelper(rec, sq_scr, st2, name=pfx + "lm", sq_tok=pfx + "lnsq")
    cp = ConstPool(rec, cpool)
    eps5 = cp.get(1e-5)
    eps6 = cp.get(1e-6)
    cp_op = cp.op

    o_id = make_identity_ops(rec, ident)
    rec.add("gpsimd", lambda e: e.memset(ones1[:], 1.0), tag="ones", w=["ones1"])

    rec.dma("sync", brow[0:1, 0:1024], bqk_r[:], w=["brow"])
    rec.dma("sync", brow[0:1, 1024:2048], bfc_r[:], w=["brow"])
    rec.dma("sync", brow[0:1, 2048:3072], gam_r[:], w=["brow"])
    rec.dma("sync", brow[0:1, 3072:4096], bet_r[:], w=["brow"])
    rec.dma("sync", bv_s[:], bv_c[:], w=["bv"])
    for j in range(8):
        rec.add("tensor", lambda e, j=j: e.matmul(
            p_bc[:, 0:512], ones1[0:1, :], brow[0:1, 512 * j:512 * (j + 1)],
            start=True, stop=True), tag="bbc",
            r=["ones1", "brow"], w=["pfc"])
        rec.add("scalar", lambda e, j=j: e.copy(
            bbc[:, 512 * j:512 * (j + 1)], p_bc[:, 0:512]), tag="bcp",
            r=["pfc"], w=["bbc"])
    BQK = bbc[:, 0:1024]
    BFC = bbc[:, 1024:2048]
    GAM = bbc[:, 2048:3072]
    BET = bbc[:, 3072:4096]

    for k in range(8):
        rec.dma("sync", wstage[k % 2][:], wqk[128 * k:128 * (k + 1), :],
                tag=f"wqkld{k}", w=[f"wstage{k % 2}"])
        rec.add("vector", lambda e, k=k: e.tensor_copy(
            wqk_s[:, 1024 * k:1024 * (k + 1)], wstage[k % 2][:]),
            tag=f"wqkcast{k}", r=[f"wstage{k % 2}"], w=["wqk_s"])
    for k in range(8):
        rec.dma("sync", wstage[k % 2][:, 0:512], wv[128 * k:128 * (k + 1), :],
                tag=f"wvld{k}", w=[f"wstage{k % 2}"])
        rec.add("vector", lambda e, k=k: e.tensor_copy(
            wv_s[:, 512 * k:512 * (k + 1)], wstage[k % 2][:, 0:512]),
            tag=f"wvcast{k}", r=[f"wstage{k % 2}"], w=["wv_s"])
    for j in range(4):
        rec.dma("sync", wstage[j % 2][:], wfc[128 * j:128 * (j + 1), :],
                tag=f"wfcld{j}", w=[f"wstage{j % 2}"])
        rec.add("vector", lambda e, j=j: e.tensor_copy(
            wfc_s[:, 1024 * j:1024 * (j + 1)], wstage[j % 2][:]),
            tag=f"wfccast{j}", r=[f"wstage{j % 2}"], w=["wfc_s"])

    def l0a_front(ti):
        q = ti % 2
        rec.dma("sync", xt[q][:], x[128 * ti:128 * (ti + 1), :],
                tag=f"x{ti}", w=[f"xt{q}"])
        ln.emit(xt[q][:], f"xt{q}", xln16[q][:], f"xln16{q}", eps5,
                const_dep=cp_op)
        rec.add("vector", lambda e, ti=ti: e.tensor_copy(
            mu_all[:, ti:ti + 1], stats[:, 2:3]), tag="stash_mu",
            r=[pfx + "lnc"], w=["mu_all"])
        rec.add("vector", lambda e, ti=ti: e.tensor_copy(
            rs_all[:, ti:ti + 1], stats[:, 7:8]), tag="stash_rs",
            r=[pfx + "lnh"], w=["rs_all"])

    def l0a_mid(ti):
        q = ti % 2
        transpose_to_fm(rec, ident, o_id, xln16[q], f"xln16{q}", p_tr, "ptr",
                        xf[q], f"xf{q}", 0, col_stride=128)
        for n in range(2):
            for c in range(8):
                rec.add("tensor", lambda e, n=n, c=c, q=q: e.matmul(
                    p_qk[:, 512 * n:512 * (n + 1)],
                    xf[q][:, 128 * c:128 * (c + 1)],
                    wqk_s[:, 1024 * c + 512 * n:1024 * c + 512 * (n + 1)],
                    start=(c == 0), stop=(c == 7)),
                    tag=f"qk{ti}_{n}", r=[f"xf{q}", "wqk_s"], w=["pqk"])
        for n in range(2):
            rec.add("vector", lambda e, n=n: e.tensor_tensor(
                qk16[:, 512 * n:512 * (n + 1)], p_qk[:, 512 * n:512 * (n + 1)],
                BQK[:, 512 * n:512 * (n + 1)], ALU.add),
                tag="qkbias", r=["pqk", "bbc"], w=["qk16"])
        for h in range(8):
            rec.add("tensor", lambda e, h=h, ti=ti: e.matmul(
                p_sc[:, 64 * h:64 * (h + 1)],
                qk16[:, 64 * h:64 * (h + 1)],
                qk16[:, 512 + 64 * h:512 + 64 * (h + 1)],
                start=(ti == 0 and h == 0), stop=(ti == NT - 1 and h == 7)),
                tag=f"sc{ti}_{h}", r=["qk16"], w=["psc"])
        if ti + 1 < NT:
            l0a_front(ti + 1)
        for c in range(4):
            for k in range(8):
                rec.add("tensor", lambda e, c=c, k=k, q=q: e.matmul(
                    p_v[:, 128 * c:128 * (c + 1)],
                    wv_s[:, 512 * k + 128 * c:512 * k + 128 * (c + 1)],
                    xf[q][:, 128 * k:128 * (k + 1)],
                    start=(k == 0), stop=(k == 7)),
                    tag=f"v{ti}_{c}", r=[f"xf{q}", "wv_s"], w=["pv"])
        for c in range(4):
            rec.add("vector", lambda e, c=c, ti=ti: e.tensor_scalar(
                vfm[c][:, 128 * ti:128 * (ti + 1)], p_v[:, 128 * c:128 * (c + 1)],
                bv_s[:, c:c + 1], None, ALU.add),
                tag=f"vbias{ti}_{c}", r=["pv", "bv"], w=[f"vfm{c}"])
        if filler is not None:
            filler(5)

    l0a_front(0)
    for ti in range(NT):
        l0a_mid(ti)

    rec.add("vector", lambda e: e.tensor_copy(sc32[:], p_sc[:]),
            tag="scev", r=["psc"], w=["sc32"])
    for h in range(8):
        cs = slice(64 * h, 64 * (h + 1))
        rec.add("vector", lambda e, cs=cs, h=h: e.reduce_max(
            smax[:, h:h + 1], sc32[:, cs], axis=AX.X),
            tag="smx", r=["sc32"], w=["smax"])
        rec.add("vector", lambda e, h=h: e.tensor_scalar(
            smax[:, h:h + 1], smax[:, h:h + 1], -0.125, None, ALU.mult),
            tag="smxn", r=["smax"], w=["smax"])
        rec.add("scalar", lambda e, cs=cs, h=h: e.activation(
            sc32[:, cs], sc32[:, cs], AF.Exp, bias=smax[:, h:h + 1],
            scale=0.125, accum_out=ssum[:, h:h + 1]),
            tag="sexp", r=["sc32", "smax"], w=["sc32", "ssum"])
        rec.add("vector", lambda e, h=h: e.reciprocal(
            srec[:, h:h + 1], ssum[:, h:h + 1]),
            tag="srec", r=["ssum"], w=["srec"])
        rec.add("vector", lambda e, cs=cs, h=h: e.tensor_scalar(
            attn16a[:, cs], sc32[:, cs], srec[:, h:h + 1], None, ALU.mult),
            tag="snorm", r=["sc32", "srec"], w=["attn16a"])
    for c in range(4):
        rec.add("vector", lambda e, c=c: e.tensor_copy(
            attn16[0:64, 64 * c:64 * (c + 1)],
            attn16a[:, 128 * c:128 * c + 64]),
            tag="apack_e", r=["attn16a"], w=["attn16"])
        rec.dma("sync", attn16[64:128, 64 * c:64 * (c + 1)],
                attn16a[:, 128 * c + 64:128 * c + 128],
                tag="apack_o", r=["attn16a"], w=["attn16"])

    y_done = []
    for g in range(NP):
        for c in range(4):
            rec.add("tensor", lambda e, c=c, g=g: e.matmul(
                p_o[0:64, :], attn16[0:64, 64 * c:64 * (c + 1)],
                vfm[c][0:64, 512 * g:512 * (g + 1)],
                start=True, stop=True, tile_position=(0, 0)),
                tag=f"o{g}_{c}a", r=["attn16", f"vfm{c}"], w=["po"])
            rec.add("tensor", lambda e, c=c, g=g: e.matmul(
                p_o[64:128, :], attn16[64:128, 64 * c:64 * (c + 1)],
                vfm[c][64:128, 512 * g:512 * (g + 1)],
                start=True, stop=True, tile_position=(64, 64)),
                tag=f"o{g}_{c}b", r=["attn16", f"vfm{c}"], w=["po"])
            rec.add("vector", lambda e, c=c: e.tensor_copy(
                o16[:, 512 * c:512 * (c + 1)], p_o[:]),
                tag=f"oev{g}_{c}", r=["po"], w=["o16"])
        for tl in range(4):
            ti = 4 * g + tl
            q = tl % 2
            rec.dma("sync", xt[q][:], x[128 * ti:128 * (ti + 1), :],
                    tag=f"xb{ti}", w=[f"xt{q}"])
            for n in range(2):
                for c in range(4):
                    rec.add("tensor", lambda e, n=n, c=c, tl=tl: e.matmul(
                        p_fc[:, 512 * n:512 * (n + 1)],
                        o16[:, 512 * c + 128 * tl:512 * c + 128 * (tl + 1)],
                        wfc_s[:, 1024 * c + 512 * n:1024 * c + 512 * (n + 1)],
                        start=(c == 0), stop=(c == 3)),
                        tag=f"fc{ti}_{n}", r=["o16", "wfc_s"], w=["pfc"])
            rec.add("vector", lambda e: e.tensor_tensor(
                ev1[:], p_fc[:], BFC[:], ALU.add),
                tag="fcb", r=["pfc", "bbc"], w=["ev1"])
            rec.add("vector", lambda e, ti=ti, q=q: e.tensor_scalar(
                sq_scr[:], xt[q][:], mu_all[:, ti:ti + 1], rs_all[:, ti:ti + 1],
                ALU.add, ALU.mult),
                tag="xlnr", r=[f"xt{q}", "mu_all", "rs_all"], w=[pfx + "lnsq"])
            rec.add("vector", lambda e: e.tensor_tensor(
                ev1[:], ev1[:], sq_scr[:], ALU.add),
                tag="fcxln", r=["ev1", pfx + "lnsq"], w=["ev1"])
            ln2.emit(ev1[:], "ev1", outt[q][:], f"outt{q}", eps6, const_dep=cp_op)
            rec.add("vector", lambda e, q=q: e.tensor_tensor(
                outt[q][:], outt[q][:], GAM[:], ALU.mult),
                tag="gmul", r=[f"outt{q}", "bbc"], w=[f"outt{q}"])
            rec.add("vector", lambda e, q=q: e.tensor_tensor(
                outt[q][:], outt[q][:], BET[:], ALU.add),
                tag="badd", r=[f"outt{q}", "bbc"], w=[f"outt{q}"])
            rec.add("vector", lambda e, q=q: e.tensor_tensor(
                outt[q][:], outt[q][:], xt[q][:], ALU.add),
                tag="resid", r=[f"outt{q}", f"xt{q}"], w=[f"outt{q}"])
            d_y = rec.dma("sync", y[128 * ti:128 * (ti + 1), :], outt[q][:],
                          tag=f"y{ti}", r=[f"outt{q}"])
            y_done.append(d_y)
            if filler is not None:
                filler(3)
    return y_done



NH0 = 16
HD0 = 64


def build_l0_mla(nc, al, rec, x, s1_out, xin_hbm, k_hbm, q_hbm, v_hbm,
                 kvdT, qdT, ropekT, kuT, qmixT, vuT, owT, ln1_r,
                 cos_t, sin_t, pmat, pfx="f2_"):
    S, P = al.sb, al.ps

    ident = S(pfx + "ident", [128, 128], BF16)
    ones1 = S(pfx + "ones1", [1, 128], F32)
    ident32 = S(pfx + "ident32", [128, 128], F32)
    kvq_s = S(pfx + "kvq", [128, 4096], BF16)
    ropek_s = S(pfx + "ropek", [128, 8192], BF16)
    kuT_s = S(pfx + "kuT", [128, 2048], BF16)
    qmix_s = S(pfx + "qmix", [128, 2048], BF16)
    vuT_s = S(pfx + "vuT", [128, 2048], BF16)
    owT_s = S(pfx + "owT", [64, 16384], BF16)
    p16 = S(pfx + "p16", [128, 128], BF16)
    cos_s = S(pfx + "cos", [128, T], F32)
    sin_s = S(pfx + "sin", [128, T], F32)
    sel_s = S(pfx + "sel", [128, 2], F32)
    wstage = [S(pfx + f"wstage{i}", [128, 2048], F32) for i in range(2)]
    ln1bc = S(pfx + "ln1bc", [128, D], F32)
    lrow = S(pfx + "lrow", [1, D], F32)
    xt = [S(pfx + f"xt{i}", [128, D], F32) for i in range(2)]
    sq_scr = S(pfx + "sq", [128, D], F32)
    stats = S(pfx + "stats", [128, 8], F32)
    st2 = S(pfx + "st2", [128, 8], F32)
    xin_t = S(pfx + "xin", [128, D], F32)
    ml16 = S(pfx + "ml16", [128, D], BF16)
    mlf = [S(pfx + f"mlf{i}", [128, D], BF16) for i in range(2)]
    lat16 = S(pfx + "lat16", [128, 512], BF16)
    kx16 = [S(pfx + f"kx16{i}", [128, 128], BF16) for i in range(2)]
    t1f = [S(pfx + f"t1f{i}", [128, 128], F32) for i in range(2)]
    t2f = [S(pfx + f"t2f{i}", [128, 128], F32) for i in range(2)]
    eff16 = [S(pfx + f"eff16{i}", [128, 128], BF16) for i in range(2)]
    q2f = S(pfx + "q2f", [128, 128], F32)
    nrm2 = S(pfx + "nrm2", [128, 2], F32)
    qmax = S(pfx + "qmax", [128, 16], F32)
    kmax = S(pfx + "kmax", [128, 16], F32)
    vst = S(pfx + "vst", [128, 1040], BF16)
    mrow = S(pfx + "mrow", [1, 32], F32)
    mh = S(pfx + "mh", [128, 16], F32)
    bias_t = S(pfx + "bias", [128, 16], F32)
    qbuf = S(pfx + "qbuf", [128, 4096], BF16)
    kbuf = [S(pfx + f"kbuf{i}", [128, 128], BF16) for i in range(4)]
    vbuf = [S(pfx + f"vbuf{i}", [128, 130], BF16) for i in range(4)]
    pa16 = [S(pfx + f"pa16{i}", [128, 512], BF16) for i in range(2)]
    pb16 = [S(pfx + f"pb16{i}", [128, 512], BF16) for i in range(2)]
    mask01 = S(pfx + "mask01", [128, 128], BF16)
    rs_t = S(pfx + "rs_t", [128, 1024], F32)
    bc_s = S(pfx + "bc_s", [64, 1024], F32)
    ones64 = S(pfx + "ones64", [128, 64], F32)
    h_s = S(pfx + "h_s", [64, 8192], BF16)
    s1t = [S(pfx + f"s1t{i}", [128, D], F32) for i in range(2)]
    cpool = S(pfx + "cpool", [128, 8], F32)

    p_tr = P(pfx + "ptr", [128, 1024], BF16)
    b0 = P(pfx + "b0", [128, 512])
    b1 = P(pfx + "b1", [128, 512])
    b2 = P(pfx + "b2", [128, 512])
    b3 = P(pfx + "b3", [128, 512])
    b4 = P(pfx + "b4", [128, 512])
    p_v2 = P(pfx + "pv2", [128, 1024])
    p_lat = b0
    p_k = b1
    p_px = b2
    p_n = b3
    p_sA = b0
    p_sB = b1
    p_oA = b2
    p_oB = b3
    p_bc2 = b4

    ln = LNHelper(rec, sq_scr, stats, name=pfx + "ln")
    lnb = LNHelper(rec, sq_scr, st2, name=pfx + "lm", sq_tok=pfx + "lnsq")
    cp = ConstPool(rec, cpool)
    eps5 = cp.get(1e-5)
    cp_op = cp.op

    o_id = make_identity_ops(rec, ident)
    rec.add("gpsimd", lambda e: e.memset(ones1[:], 1.0), tag="ones", w=["ones1"])
    rec.add("gpsimd", lambda e: e.memset(ones64[:], 1.0), tag="ones64", w=["ones64"])
    o_i32a = rec.add("gpsimd", lambda e: e.memset(ident32[:], 0.0), tag="i32a",
                     w=["id32"])
    rec.add("gpsimd", lambda e: e.affine_select(
        out=ident32[:], in_=ident32[:], compare_op=ALU.not_equal, fill=1.0,
        base=0, pattern=[[-1, 128]], channel_multiplier=1), tag="i32b",
        r=["id32"], w=["id32"])
    rec.add("gpsimd", lambda e: e.memset(sel_s[:, 0:1], 1.0), tag="sel0", w=["sel"])
    rec.add("gpsimd", lambda e: e.memset(sel_s[:, 1:2], 0.0), tag="sel1", w=["sel"])
    rec.add("gpsimd", lambda e: e.memset(sel_s[64:128, 0:1], 0.0), tag="sel2", w=["sel"])
    rec.add("gpsimd", lambda e: e.memset(sel_s[64:128, 1:2], 1.0), tag="sel3", w=["sel"])
    rec.add("gpsimd", lambda e: e.memset(mask01[:], 1.0), tag="m01a", w=["m01"])
    rec.add("gpsimd", lambda e: e.affine_select(
        out=mask01[:], in_=mask01[:], compare_op=ALU.is_ge, fill=0.0,
        base=0, pattern=[[1, 128]], channel_multiplier=-1), tag="m01b",
        r=["m01"], w=["m01"])
    rec.add("gpsimd", lambda e: e.memset(qmax[:], 0.0), tag="qm0", w=["qmax"])
    rec.add("gpsimd", lambda e: e.memset(kmax[:], 0.0), tag="km0", w=["kmax"])

    rec.dma("sync", cos_s[:], cos_t[:], w=["cos"])
    rec.dma("sync", sin_s[:], sin_t[:], w=["sin"])
    rec.dma("sync", lrow[:], ln1_r[:], w=["lrow"])
    for j in range(2):
        rec.add("tensor", lambda e, j=j: e.matmul(
            p_v2[:, 0:512], ones1[0:1, :], lrow[0:1, 512 * j:512 * (j + 1)],
            start=True, stop=True), tag="lbc", r=["ones1", "lrow"], w=["pv2"])
        rec.add("scalar", lambda e, j=j: e.copy(
            ln1bc[:, 512 * j:512 * (j + 1)], p_v2[:, 0:512]), tag="lbcp",
            r=["pv2"], w=["ln1bc"])
    rec.dma("sync", wstage[0][:, 0:128], pmat[:], w=["wstage0"])
    rec.add("vector", lambda e: e.tensor_copy(p16[:], wstage[0][:, 0:128]),
            tag="p16", r=["wstage0"], w=["p16"])

    def loadw(dst, dst_tok, src, rows, cols, tag):
        nch = rows // 128
        for k in range(nch):
            stg = wstage[k % 2]
            rec.dma("sync", stg[:, 0:cols], src[128 * k:128 * (k + 1), :],
                    tag=f"{tag}ld{k}", w=[f"wstage{k % 2}"])
            rec.add("vector", lambda e, k=k, stg=stg: e.tensor_copy(
                dst[:, cols * k:cols * (k + 1)], stg[:, 0:cols]),
                tag=f"{tag}c{k}", r=[f"wstage{k % 2}"], w=[dst_tok])

    for k in range(8):
        stg = wstage[k % 2]
        rec.dma("sync", stg[:, 0:256], kvdT[128 * k:128 * (k + 1), :],
                tag=f"kvld{k}", w=[f"wstage{k % 2}"])
        rec.dma("sync", stg[:, 256:512], qdT[128 * k:128 * (k + 1), :],
                tag=f"qdld{k}", w=[f"wstage{k % 2}"])
        rec.add("vector", lambda e, k=k, stg=stg: e.tensor_copy(
            kvq_s[:, 512 * k:512 * (k + 1)], stg[:, 0:512]),
            tag=f"kvqc{k}", r=[f"wstage{k % 2}"], w=["kvq"])
    loadw(ropek_s, "ropek", ropekT, 1024, 1024, "rk")
    loadw(kuT_s, "kuT", kuT, 256, 1024, "ku")
    loadw(qmix_s, "qmix", qmixT, 256, 1024, "qm")
    loadw(vuT_s, "vuT", vuT, 256, 1024, "vu")
    for k in range(8):
        stg = wstage[k % 2]
        rec.dma("sync", stg[0:64, 0:2048], owT[0:64, 2048 * k:2048 * (k + 1)],
                tag=f"owld{k}", w=[f"wstage{k % 2}"])
        rec.add("vector", lambda e, k=k, stg=stg: e.tensor_copy(
            owT_s[:, 2048 * k:2048 * (k + 1)], stg[0:64, 0:2048]),
            tag=f"owc{k}", r=[f"wstage{k % 2}"], w=["owT"])

    def rope_chunk(src_psum, src_tok, j, ti, dst_hbm, q2acc):
        par = j % 2
        kx = kx16[par]
        t1 = t1f[par]
        t2 = t2f[par]
        ef = eff16[par]
        pxb, pxt = (p_px, "b2") if par == 0 else (p_n, "b3")
        rec.add("vector", lambda e, kx=kx: e.tensor_copy(
            kx[:], src_psum[:, 0:128]),
            tag="ropeev", r=[src_tok], w=[f"kx16{par}"])
        rec.add("tensor", lambda e, kx=kx, pxb=pxb: e.matmul(
            pxb[:, 0:128], p16[:], kx[:], start=True, stop=True),
            tag="ropepx", r=["p16", f"kx16{par}"], w=[pxt])
        rec.add("vector", lambda e, ti=ti, kx=kx, t1=t1: e.tensor_tensor(
            t1[:], kx[:], cos_s[:, 128 * ti:128 * (ti + 1)], ALU.mult),
            tag="ropet1", r=[f"kx16{par}", "cos"], w=[f"t1f{par}"])
        rec.add("vector", lambda e, ti=ti, t2=t2, pxb=pxb: e.tensor_tensor(
            t2[:], pxb[:, 0:128], sin_s[:, 128 * ti:128 * (ti + 1)], ALU.mult),
            tag="ropet2", r=[pxt, "sin"], w=[f"t2f{par}"])
        rec.add("vector", lambda e, t1=t1, t2=t2, ef=ef: e.tensor_tensor(
            ef[:], t1[:], t2[:], ALU.add),
            tag="ropeadd", r=[f"t1f{par}", f"t2f{par}"], w=[f"eff16{par}"])
        rec.dma("sync", dst_hbm, ef[:], tag="effsp", r=[f"eff16{par}"],
                w=[f"hbm_eff"])
        if q2acc is not None:
            rec.add("scalar", lambda e, ef=ef: e.activation(
                q2f[:], ef[:], AF.Square),
                tag="q2", r=[f"eff16{par}"], w=["q2f"])
            qb, qbt = (p_n, "b3") if par == 0 else (p_px, "b2")
            rec.add("tensor", lambda e, qb=qb: e.matmul(
                qb[:, 0:2], q2f[:], sel_s[:], start=True, stop=True),
                tag="q2mm", r=["q2f", "sel"], w=[qbt])
            rec.add("vector", lambda e, j=j, qb=qb: e.tensor_tensor(
                q2acc[0][:, 2 * j:2 * j + 2], q2acc[0][:, 2 * j:2 * j + 2],
                qb[:, 0:2], ALU.max), tag="q2max",
                r=[q2acc[1], qbt], w=[q2acc[1]])

    def mla_front_a(ti):
        q = ti % 2
        rec.dma("sync", xt[q][:], x[128 * ti:128 * (ti + 1), :],
                tag=f"x{ti}", w=[f"xt{q}"])
        ln.emit(xt[q][:], f"xt{q}", xin_t[:], "xin_t", eps5, const_dep=cp_op)
        rec.dma("gpsimd", xin_hbm[128 * ti:128 * (ti + 1), :], xin_t[:],
                tag=f"xinsp{ti}", r=["xin_t"], w=[f"hbm_xin{ti}"])

    def mla_front_b(ti):
        lnb.emit(xin_t[:], "xin_t", sq_scr[:, 0:D], pfx + "lnsq", eps5,
                 const_dep=cp_op)

    def mla_front_c(ti):
        rec.add("vector", lambda e: e.tensor_tensor(
            ml16[:], sq_scr[:, 0:D], ln1bc[:], ALU.mult),
            tag="ml16", r=[pfx + "lnsq", "ln1bc"], w=["ml16"])

    def mla_front_d(ti):
        q = ti % 2
        transpose_to_fm(rec, ident, o_id, ml16, "ml16", p_tr, "ptr",
                        mlf[q], f"mlf{q}", 0, col_stride=128)

    def mla_mid(ti):
        q = ti % 2
        nxt = ti + 1 if ti + 1 < NT else None
        for b in range(4):
            for k in range(8):
                rec.add("tensor", lambda e, b=b, k=k, q=q: e.matmul(
                    p_lat[:, 128 * b:128 * (b + 1)],
                    kvq_s[:, 512 * k + 128 * b:512 * k + 128 * (b + 1)],
                    mlf[q][:, 128 * k:128 * (k + 1)],
                    start=(k == 0), stop=(k == 7)),
                    tag=f"lat{ti}_{b}", r=["kvq", f"mlf{q}"], w=["b0"])
        rec.add("vector", lambda e: e.tensor_copy(lat16[:], p_lat[:]),
                tag="latev", r=["b0"], w=["lat16"])
        for j in range(8):
            ckb, ckt = (b1, "b1") if j % 2 == 0 else (b0, "b0")
            for k in range(8):
                rec.add("tensor", lambda e, j=j, k=k, ckb=ckb, q=q: e.matmul(
                    ckb[:, 0:128], ropek_s[:, 1024 * k + 128 * j:1024 * k + 128 * (j + 1)],
                    mlf[q][:, 128 * k:128 * (k + 1)],
                    start=(k == 0), stop=False),
                    tag=f"krope{ti}_{j}", r=["ropek", f"mlf{q}"], w=[ckt])
            for l in range(2):
                rec.add("tensor", lambda e, j=j, l=l, ckb=ckb: e.matmul(
                    ckb[:, 0:128], kuT_s[:, 1024 * l + 128 * j:1024 * l + 128 * (j + 1)],
                    lat16[:, 128 * l:128 * (l + 1)],
                    start=False, stop=(l == 1)),
                    tag=f"knr{ti}_{j}", r=["kuT", "lat16"], w=[ckt])
            rope_chunk(ckb, ckt, j, ti,
                       k_hbm[ti, j, :, :], (kmax, "kmax"))
            if nxt is not None:
                if j == 1:
                    mla_front_a(nxt)
                elif j == 3:
                    mla_front_b(nxt)
                elif j == 5:
                    mla_front_c(nxt)
                elif j == 7:
                    mla_front_d(nxt)
        for j in range(8):
            ckb, ckt = (b1, "b1") if j % 2 == 0 else (b0, "b0")
            for l in range(2):
                rec.add("tensor", lambda e, j=j, l=l, ckb=ckb: e.matmul(
                    ckb[:, 0:128], qmix_s[:, 1024 * l + 128 * j:1024 * l + 128 * (j + 1)],
                    lat16[:, 256 + 128 * l:256 + 128 * (l + 1)],
                    start=(l == 0), stop=(l == 1)),
                    tag=f"qmix{ti}_{j}", r=["qmix", "lat16"], w=[ckt])
            rope_chunk(ckb, ckt, j, ti,
                       q_hbm[ti, j, :, :], (qmax, "qmax"))
        for n in range(2):
            for l in range(2):
                rec.add("tensor", lambda e, n=n, l=l: e.matmul(
                    p_v2[:, 512 * n:512 * (n + 1)],
                    lat16[:, 128 * l:128 * (l + 1)],
                    vuT_s[:, 1024 * l + 512 * n:1024 * l + 512 * (n + 1)],
                    start=(l == 0), stop=(l == 1)),
                    tag=f"v{ti}_{n}", r=["lat16", "vuT"], w=["pv2"])
        rec.add("vector", lambda e: e.tensor_copy(
            vst[:].rearrange("p (h c) -> p h c", c=65)[:, :, 0:64],
            p_v2[:].rearrange("p (h c) -> p h c", c=64)),
            tag="vev", r=["pv2"], w=["vst"])
        rec.add("vector", lambda e: e.memset(
            vst[:].rearrange("p (h c) -> p h c", c=65)[:, :, 64:65], 1.0),
            tag="vones", r=[], w=["vst"])
        rec.dma("sync", v_hbm[ti, :, :], vst[:], tag=f"vsp{ti}",
                r=["vst"], w=[f"hbm_v{ti}"])

    mla_front_a(0)
    mla_front_b(0)
    mla_front_c(0)
    mla_front_d(0)
    for ti in range(NT):
        mla_mid(ti)

    def global_max(acc, acc_tok, out_col):
        rec.add("tensor", lambda e, acc=acc: e.transpose(
            p_k[0:16, 0:128], acc[:], ident32[:]), tag="gmtr",
            r=[acc_tok, "id32"], w=["b1"])
        rec.add("vector", lambda e, out_col=out_col: e.reduce_max(
            mh[0:16, out_col:out_col + 1], p_k[0:16, 0:128], axis=AX.X),
            tag="gmred", r=["b1"], w=["mh"])

    global_max(qmax, "qmax", 0)
    global_max(kmax, "kmax", 1)
    rec.add("scalar", lambda e: e.activation(
        mh[0:16, 2:4], mh[0:16, 0:2], AF.Sqrt), tag="gmsqrt",
        r=["mh"], w=["mh"])
    rec.add("vector", lambda e: e.tensor_tensor(
        mh[0:16, 4:5], mh[0:16, 2:3], mh[0:16, 3:4], ALU.mult),
        tag="mhat", r=["mh"], w=["mh"])
    rec.add("vector", lambda e: e.tensor_scalar(
        mh[0:16, 5:6], mh[0:16, 4:5], -0.125, None, ALU.mult),
        tag="mbias", r=["mh"], w=["mh"])
    rec.dma("sync", mrow[0:1, 0:16], mh[0:16, 5:6], tag="mrow",
            r=["mh"], w=["mrow"])
    rec.add("tensor", lambda e: e.matmul(
        p_k[:, 0:16], ones1[0:1, :], mrow[0:1, 0:16],
        start=True, stop=True), tag="mbc", r=["ones1", "mrow"], w=["b1"])
    rec.add("scalar", lambda e: e.copy(bias_t[:], p_k[:, 0:16]), tag="mbcp",
            r=["b1"], w=["bias"])

    def q_loads(g, j):
        for t in range(4):
            rec.dma("sync", qbuf[:, 512 * j + 128 * t:512 * j + 128 * (t + 1)],
                    q_hbm[4 * g + t, j, :, :],
                    tag=f"ql{g}_{j}_{t}", r=["hbm_eff"], w=[f"qbuf{j}"])

    for g in range(NP):
        if g == 0:
            for j in range(8):
                q_loads(0, j)
        for j in range(8):
            nkc = 4 * (g + 1)

            def mla_loads(kc):
                rec.dma("sync", kbuf[kc % 4][:], k_hbm[kc, j, :, :],
                        tag=f"kl{g}_{j}_{kc}", r=["hbm_eff"],
                        w=[f"kb{kc % 4}"])
                rec.dma("sync", vbuf[kc % 4][:],
                        v_hbm[kc, :, 130 * j:130 * (j + 1)],
                        tag=f"vl{g}_{j}_{kc}", r=[f"hbm_v{kc}"],
                        w=[f"vb{kc % 4}"])

            def mla_parts(kc):
                q0 = 128 * max(0, kc - 4 * g)
                parts = []
                for h2 in range(2):
                    if h2 == 0:
                        p_s, st = ((p_sA, "b0") if kc % 2 == 0
                                   else (b4, "b4"))
                        px16 = pa16[kc % 2]
                    else:
                        p_s, st = ((p_sB, "b1") if kc % 2 == 0
                                   else (p_v2[:, 0:512], "pv2"))
                        px16 = pb16[kc % 2]
                    parts.append((h2, p_s, st, px16, f"px{h2}_{kc % 2}", q0))
                return parts

            def mla_scores(kc):
                for h2, p_s, st, px16, pt, q0 in mla_parts(kc):
                    rec.add("tensor", lambda e, j=j, h2=h2, kc=kc, q0=q0,
                            p_s=p_s: e.matmul(
                        p_s[:, q0:512],
                        kbuf[kc % 4][64 * h2:64 * (h2 + 1), :],
                        qbuf[64 * h2:64 * (h2 + 1),
                             512 * j + q0:512 * (j + 1)],
                        start=True, stop=True,
                        tile_position=(64 * h2, 0)),
                        tag=f"s{g}_{2 * j + h2}_{kc}",
                        r=[f"kb{kc % 4}", f"qbuf{j}"], w=[st])

            def mla_exp(kc):
                for h2, p_s, st, px16, pt, q0 in mla_parts(kc):
                    h = 2 * j + h2
                    rec.add("scalar", lambda e, h=h, q0=q0, p_s=p_s,
                            px16=px16: e.activation(
                        px16[:, q0:512], p_s[:, q0:512], AF.Exp,
                        bias=bias_t[:, h:h + 1],
                        scale=0.125), tag=f"exp{g}_{h}_{kc}",
                        r=[st, "bias"], w=[pt])
                    if kc - 4 * g >= 0:
                        rec.add("vector", lambda e, q0=q0, px16=px16:
                                e.tensor_tensor(
                            px16[:, q0:q0 + 128],
                            px16[:, q0:q0 + 128],
                            mask01[:], ALU.mult), tag="diag",
                            r=[pt, "m01"], w=[pt])

            def mla_pv(kc):
                for h2, p_s, st, px16, pt, q0 in mla_parts(kc):
                    p_o = p_oA if h2 == 0 else p_oB
                    ot = "b2" if h2 == 0 else "b3"
                    rec.add("tensor", lambda e, h2=h2, kc=kc, q0=q0,
                            p_o=p_o, px16=px16, sp=(kc == nkc - 1): e.matmul(
                        p_o[0:65, q0:512],
                        vbuf[kc % 4][:, 65 * h2:65 * (h2 + 1)],
                        px16[:, q0:512],
                        start=(kc == 0), stop=sp),
                        tag=f"pv{g}_{2 * j + h2}_{kc}",
                        r=[f"vb{kc % 4}", pt], w=[ot])

            for kk in range(min(3, nkc)):
                mla_loads(kk)
            mla_scores(0)
            for kc in range(nkc):
                if kc + 3 < nkc:
                    mla_loads(kc + 3)
                if kc + 1 < nkc:
                    mla_scores(kc + 1)
                mla_exp(kc)
                mla_pv(kc)
            for h2 in range(2):
                h = 2 * j + h2
                p_o = p_oA if h2 == 0 else p_oB
                ot = "b2" if h2 == 0 else "b3"
                rec.add("vector", lambda e, h2=h2, p_o=p_o: e.reciprocal(
                    rs_t[64:65, 512 * h2:512 * (h2 + 1)],
                    p_o[64:65, :]), tag=f"rcp{g}_{h}",
                    r=[ot], w=["rs_t"])
                rec.add("tensor", lambda e, h2=h2: e.matmul(
                    p_bc2[0:64, :], ones64[64:65, 0:64],
                    rs_t[64:65, 512 * h2:512 * (h2 + 1)],
                    start=True, stop=True, tile_position=(64, 0)),
                    tag=f"rbc{g}_{h}", r=["ones64", "rs_t"], w=["b4"])
                rec.add("scalar", lambda e, h2=h2: e.copy(
                    bc_s[:, 512 * h2:512 * (h2 + 1)], p_bc2[0:64, :]),
                    tag=f"rbcp{g}_{h}", r=["b4"], w=["bc_s"])
                rec.add("vector", lambda e, h=h, h2=h2, p_o=p_o:
                        e.tensor_tensor(
                    h_s[0:64, 512 * h:512 * (h + 1)], p_o[0:64, :],
                    bc_s[:, 512 * h2:512 * (h2 + 1)], ALU.mult),
                    tag=f"hnrm{g}_{h}", r=[ot, "bc_s"], w=["h_s"])
            if g + 1 < NP:
                q_loads(g + 1, j)
        for tl in range(4):
            ti = 4 * g + tl
            q = tl % 2
            rec.dma("scalar", xt[q][:], xin_hbm[128 * ti:128 * (ti + 1), :],
                    tag=f"xinl{ti}", r=[f"hbm_xin{ti}"], w=[f"xt{q}"])
            for nh in range(2):
                for h in range(16):
                    rec.add("tensor", lambda e, h=h, nh=nh, tl=tl: e.matmul(
                        p_v2[:, 512 * nh:512 * (nh + 1)],
                        h_s[0:64, 512 * h + 128 * tl:512 * h + 128 * (tl + 1)],
                        owT_s[0:64, 1024 * h + 512 * nh:1024 * h + 512 * (nh + 1)],
                        start=(h == 0), stop=(h == 15)),
                        tag=f"ow{ti}_{nh}", r=["h_s", "owT"], w=["pv2"])
            rec.add("vector", lambda e, q=q: e.tensor_tensor(
                s1t[q][:], p_v2[:], xt[q][:], ALU.add),
                tag=f"s1{ti}", r=["pv2", f"xt{q}"], w=[f"s1t{q}"])
            rec.dma("sync", s1_out[128 * ti:128 * (ti + 1), :], s1t[q][:],
                    tag=f"s1sp{ti}", r=[f"s1t{q}"])




def build_moe_wconv(nc, al, rec, gateT, upT, downT, gbf, ubf, dbf, pfx="wc_"):
    S = al.sb
    stg = [S(pfx + f"stg{i}", [128, 2048], F32) for i in range(2)]
    ob = [S(pfx + f"ob{i}", [128, 2048], BF16) for i in range(2)]
    jobs = []
    for e in range(8):
        for k in range(8):
            jobs.append((gateT[e, 128 * k:128 * (k + 1), :],
                         gbf[e, 128 * k:128 * (k + 1), :], 2048))
            jobs.append((upT[e, 128 * k:128 * (k + 1), :],
                         ubf[e, 128 * k:128 * (k + 1), :], 2048))
        for m in range(16):
            jobs.append((downT[e, 128 * m:128 * (m + 1), :],
                         dbf[e, 128 * m:128 * (m + 1), :], 1024))
    emits = []
    for i, (src_ap, dst_ap, cols) in enumerate(jobs):
        def emit(i=i, src_ap=src_ap, dst_ap=dst_ap, cols=cols):
            b = i % 2
            rec.dma("gpsimd", stg[b][:, 0:cols], src_ap, tag=f"wcl{i}",
                    w=[pfx + f"s{b}"])
            eng = "vector" if i % 2 == 0 else "scalar"
            if eng == "vector":
                rec.add(eng, lambda e2, b=b, cols=cols: e2.tensor_copy(
                    ob[b][:, 0:cols], stg[b][:, 0:cols]), tag=f"wcc{i}",
                    r=[pfx + f"s{b}"], w=[pfx + f"o{b}"])
            else:
                rec.add(eng, lambda e2, b=b, cols=cols: e2.copy(
                    ob[b][:, 0:cols], stg[b][:, 0:cols]), tag=f"wcc{i}",
                    r=[pfx + f"s{b}"], w=[pfx + f"o{b}"])
            rec.dma("scalar", dst_ap, ob[b][:, 0:cols], tag=f"wcs{i}",
                    r=[pfx + f"o{b}"], w=[pfx + f"h{i}"])
        emits.append(emit)
    return emits



def build_l0_moe(nc, al, rec, s1, x1, x2, alf_hbm, gbf, ubf, dbf,
                 routerT, rbias_r, ln2_r, pfx="f3_"):
    S, P = al.sb, al.ps

    ident = S(pfx + "ident", [128, 128], BF16)
    ones1 = S(pfx + "ones1", [1, 128], F32)
    lrow = S(pfx + "lrow", [1, D], F32)
    ln2bc = S(pfx + "ln2bc", [128, D], F32)
    rbrow = S(pfx + "rbrow", [1, 7], F32)
    rbbc = S(pfx + "rbbc", [128, 7], F32)
    rT_s = S(pfx + "rT", [128, 56], BF16)
    wg_s = S(pfx + "wg", [128, 16384], BF16)
    wu_s = S(pfx + "wu", [128, 16384], BF16)
    wd_s = S(pfx + "wd", [128, 16384], BF16)
    a_s = S(pfx + "a_s", [128, 8192], BF16)
    af_s = [S(pfx + f"af{i}", [128, 4096], BF16) for i in range(2)]
    acc = S(pfx + "acc", [128, 8192], F32)
    xt = [S(pfx + f"xt{i}", [128, D], F32) for i in range(2)]
    sq_scr = S(pfx + "sq", [128, D], F32)
    stats = S(pfx + "stats", [128, 8], F32)
    al16 = S(pfx + "al16", [128, D], BF16)
    alf = S(pfx + "alf", [128, D], BF16)
    prb = S(pfx + "prb", [128, 16], F32)
    wgt_all = S(pfx + "wgt", [128, 256], F32)
    gs = S(pfx + "gs", [128, 512], F32)
    tmp = S(pfx + "tmp", [128, 512], F32)
    outt = [S(pfx + f"outt{i}", [128, D], F32) for i in range(2)]
    cpool = S(pfx + "cpool", [128, 8], F32)

    p_tr = P(pfx + "ptr", [128, 1024], BF16)
    b0 = P(pfx + "b0", [128, 512])
    b1 = P(pfx + "b1", [128, 512])
    b2 = P(pfx + "b2", [128, 512])
    b3 = P(pfx + "b3", [128, 512])

    ln = LNHelper(rec, sq_scr, stats, name=pfx + "ln")
    cp = ConstPool(rec, cpool)
    eps5 = cp.get(1e-5)
    cp_op = cp.op

    o_id = make_identity_ops(rec, ident)
    rec.add("gpsimd", lambda e: e.memset(ones1[:], 1.0), tag="ones", w=["ones1"])
    rec.add("gpsimd", lambda e: e.memset(wgt_all[:], 1.0), tag="wgt1", w=["wgt"])

    rec.dma("sync", lrow[:], ln2_r[:], w=["lrow"])
    rec.dma("sync", rbrow[:], rbias_r[:], w=["rbrow"])
    for j in range(2):
        rec.add("tensor", lambda e, j=j: e.matmul(
            b0[:], ones1[0:1, :], lrow[0:1, 512 * j:512 * (j + 1)],
            start=True, stop=True), tag="lbc", r=["ones1", "lrow"], w=["b0"])
        rec.add("scalar", lambda e, j=j: e.copy(
            ln2bc[:, 512 * j:512 * (j + 1)], b0[:]), tag="lbcp",
            r=["b0"], w=["ln2bc"])
    rec.add("tensor", lambda e: e.matmul(
        b0[:, 0:7], ones1[0:1, :], rbrow[0:1, :],
        start=True, stop=True), tag="rbc", r=["ones1", "rbrow"], w=["b0"])
    rec.add("scalar", lambda e: e.copy(rbbc[:], b0[:, 0:7]), tag="rbcp",
            r=["b0"], w=["rbbc"])
    for c in range(8):
        rec.dma("sync", sq_scr[:, 7 * c:7 * (c + 1)],
                routerT[128 * c:128 * (c + 1), :], w=[pfx + "lnsq"],
                tag=f"rld{c}")
    rec.add("vector", lambda e: e.tensor_copy(rT_s[:], sq_scr[:, 0:56]),
            tag="rcast", r=[pfx + "lnsq"], w=["rT"])

    for ti in range(NT):
        q = ti % 2
        rec.dma("sync", xt[q][:], s1[128 * ti:128 * (ti + 1), :],
                tag=f"s1l{ti}", w=[f"xt{q}"])
        ln.emit(xt[q][:], f"xt{q}", sq_scr[:, 0:D], pfx + "lnsq", eps5,
                const_dep=cp_op)
        rec.add("vector", lambda e: e.tensor_tensor(
            al16[:], sq_scr[:, 0:D], ln2bc[:], ALU.mult),
            tag="al16", r=[pfx + "lnsq", "ln2bc"], w=["al16"])
        transpose_to_fm(rec, ident, o_id, al16, "al16", p_tr, "ptr",
                        alf, "alf", 0, col_stride=128)
        for c in range(8):
            rec.dma("gpsimd", alf_hbm[c, :, 128 * ti:128 * (ti + 1)],
                    alf[:, 128 * c:128 * (c + 1)],
                    tag=f"alfsp{ti}_{c}", r=["alf"], w=[f"half{ti}_{c}"])
        for c in range(8):
            rec.add("tensor", lambda e, c=c: e.matmul(
                b1[:, 0:7], alf[:, 128 * c:128 * (c + 1)],
                rT_s[:, 7 * c:7 * (c + 1)],
                start=(c == 0), stop=(c == 7)),
                tag=f"rt{ti}", r=["alf", "rT"], w=["b1"])
        rec.add("vector", lambda e: e.tensor_tensor(
            prb[:, 8:15], b1[:, 0:7], rbbc[:], ALU.add),
            tag="rta", r=["b1", "rbbc"], w=["prb"])
        rec.add("scalar", lambda e: e.activation(
            prb[:, 0:7], prb[:, 8:15], AF.Sigmoid),
            tag="rsig", r=["prb"], w=["prb"])
        rec.add("vector", lambda e: e.reduce_max(
            prb[:, 8:9], prb[:, 0:7], axis=AX.X), tag="m1",
            r=["prb"], w=["prb"])
        rec.add("vector", lambda e: e.tensor_scalar(
            prb[:, 9:10] if False else tmp[:, 0:7], prb[:, 0:7],
            prb[:, 8:9], None, ALU.is_ge), tag="ge1",
            r=["prb"], w=["tmpa"])
        rec.add("vector", lambda e: e.scalar_tensor_tensor(
            tmp[:, 8:15], tmp[:, 0:7], -1e30, prb[:, 0:7], ALU.mult, ALU.add),
            tag="p2", r=["tmpa", "prb"], w=["tmpa"])
        rec.add("vector", lambda e: e.reduce_max(
            prb[:, 9:10], tmp[:, 8:15], axis=AX.X), tag="m2",
            r=["tmpa", "prb"], w=["prb"])
        rec.add("vector", lambda e: e.tensor_scalar(
            tmp[:, 0:7], prb[:, 0:7], prb[:, 9:10], None, ALU.is_ge),
            tag="keep", r=["prb"], w=["tmpa"])
        rec.add("vector", lambda e: e.tensor_tensor(
            prb[:, 10:11], prb[:, 8:9], prb[:, 9:10], ALU.add), tag="den",
            r=["prb"], w=["prb"])
        rec.add("vector", lambda e: e.reciprocal(
            prb[:, 11:12], prb[:, 10:11]), tag="rden",
            r=["prb"], w=["prb"])
        rec.add("vector", lambda e: e.tensor_tensor(
            tmp[:, 0:7], tmp[:, 0:7], prb[:, 0:7], ALU.mult), tag="wk",
            r=["tmpa", "prb"], w=["tmpa"])
        rec.add("vector", lambda e, ti=ti: e.tensor_scalar(
            wgt_all[:, 8 * ti:8 * ti + 7], tmp[:, 0:7], prb[:, 11:12],
            None, ALU.mult), tag="wnorm", r=["tmpa", "prb"], w=["wgt"])

    for qt in range(4):
        for e_i in range(8):
            for k in range(8):
                rec.dma("sync", wg_s[:, 2048 * k:2048 * (k + 1)],
                        gbf[e_i, 128 * k:128 * (k + 1), :],
                        tag=f"gld{qt}_{e_i}_{k}", w=[f"wg{k}"])
                rec.dma("sync", wu_s[:, 2048 * k:2048 * (k + 1)],
                        ubf[e_i, 128 * k:128 * (k + 1), :],
                        tag=f"uld{qt}_{e_i}_{k}", w=[f"wu{k}"])
            for m in range(16):
                rec.dma("scalar", wd_s[:, 1024 * m:1024 * (m + 1)],
                        dbf[e_i, 128 * m:128 * (m + 1), :],
                        tag=f"dld{qt}_{e_i}_{m}", w=[f"wd{m}"])
            for n in range(2):
                tok0 = 1024 * qt + 512 * n
                afb = af_s[n % 2]
                for c in range(8):
                    ti0 = tok0 // 128
                    deps_toks = [f"half{ti0 + t}_{c}" for t in range(4)]
                    rec.dma("sync", afb[:, 512 * c:512 * (c + 1)],
                            alf_hbm[c, :, tok0:tok0 + 512],
                            tag=f"afl{qt}_{e_i}_{n}_{c}",
                            r=deps_toks, w=[f"af{n % 2}_{c}"])
                for m in range(16):
                    for k in range(8):
                        rec.add("tensor", lambda e, m=m, k=k, afb=afb:
                                e.matmul(
                            b0[:], wg_s[:, 2048 * k + 128 * m:2048 * k + 128 * (m + 1)],
                            afb[:, 512 * k:512 * (k + 1)],
                            start=(k == 0), stop=(k == 7)),
                            tag=f"g{qt}_{e_i}_{n}_{m}",
                            r=[f"wg{k}", f"af{n % 2}_{k}"],
                            w=["b0"])
                        rec.add("tensor", lambda e, m=m, k=k, afb=afb:
                                e.matmul(
                            b1[:], wu_s[:, 2048 * k + 128 * m:2048 * k + 128 * (m + 1)],
                            afb[:, 512 * k:512 * (k + 1)],
                            start=(k == 0), stop=(k == 7)),
                            tag=f"u{qt}_{e_i}_{n}_{m}",
                            r=[f"wu{k}", f"af{n % 2}_{k}"], w=["b1"])
                    rec.add("scalar", lambda e: e.activation(
                        gs[:], b0[:], AF.Silu), tag="silu",
                        r=["b0"], w=["gs"])
                    rec.add("vector", lambda e, m=m: e.tensor_tensor(
                        a_s[:, 512 * m:512 * (m + 1)], b1[:], gs[:], ALU.mult),
                        tag="aact", r=["b1", "gs"], w=[f"a{m}"])
                for tc in range(4):
                    ti = (tok0 + 128 * tc) // 128
                    for nh in range(2):
                        bank = b2 if nh == 0 else b3
                        bt = "b2" if nh == 0 else "b3"
                        for m in range(16):
                            rec.add("tensor", lambda e, m=m, tc=tc, nh=nh,
                                    bank=bank: e.matmul(
                                bank[:],
                                a_s[:, 512 * m + 128 * tc:512 * m + 128 * (tc + 1)],
                                wd_s[:, 1024 * m + 512 * nh:1024 * m + 512 * (nh + 1)],
                                start=(m == 0), stop=(m == 15)),
                                tag=f"d{qt}_{e_i}_{n}_{tc}_{nh}",
                                r=[f"a{m}", f"wd{m}"], w=[bt])
                        accsl = acc[:, 1024 * (4 * n + tc) + 512 * nh:
                                    1024 * (4 * n + tc) + 512 * (nh + 1)]
                        rec.add("scalar", lambda e, ti=ti, e_i=e_i, bank=bank:
                                e.activation(
                            tmp[:], bank[:], AF.Copy,
                            scale=wgt_all[:, 8 * ti + e_i:8 * ti + e_i + 1]),
                            tag="wscale", r=[bt, "wgt"], w=["tmpa"])
                        if e_i == 0:
                            rec.add("vector", lambda e, accsl=accsl:
                                    e.tensor_copy(accsl, tmp[:]),
                                    tag="acc0", r=["tmpa"], w=["acc"])
                        else:
                            rec.add("vector", lambda e, accsl=accsl:
                                    e.tensor_tensor(accsl, accsl, tmp[:],
                                                    ALU.add),
                                    tag="accA", r=["tmpa", "acc"], w=["acc"])
        for t2 in range(8):
            ti = 8 * qt + t2
            q = t2 % 2
            rec.dma("sync", xt[q][:], s1[128 * ti:128 * (ti + 1), :],
                    tag=f"fs1{ti}", w=[f"xt{q}"])
            rec.dma("scalar", outt[q][:], x1[128 * ti:128 * (ti + 1), :],
                    tag=f"fx1{ti}", w=[f"outt{q}"])
            rec.add("vector", lambda e, q=q, t2=t2: e.tensor_tensor(
                outt[q][:], outt[q][:],
                acc[:, 1024 * t2:1024 * (t2 + 1)], ALU.add),
                tag=f"fo1{ti}", r=[f"outt{q}", "acc"], w=[f"outt{q}"])
            rec.add("vector", lambda e, q=q: e.tensor_tensor(
                outt[q][:], outt[q][:], xt[q][:], ALU.add),
                tag=f"fo2{ti}", r=[f"outt{q}", f"xt{q}"], w=[f"outt{q}"])
            rec.dma("sync", x2[128 * ti:128 * (ti + 1), :], outt[q][:],
                    tag=f"x2sp{ti}", r=[f"outt{q}"])


def prep_l0_moe(ins):
    rg = np.asarray(ins["L0_ffn_routed_gate"], np.float32)
    ru = np.asarray(ins["L0_ffn_routed_up"], np.float32)
    rd = np.asarray(ins["L0_ffn_routed_down"], np.float32)
    sg = np.asarray(ins["L0_ffn_shared_gate"], np.float32)
    su = np.asarray(ins["L0_ffn_shared_up"], np.float32)
    sd = np.asarray(ins["L0_ffn_shared_down"], np.float32)
    gateT = np.zeros((8, 1024, 2048), np.float32)
    upT = np.zeros((8, 1024, 2048), np.float32)
    downT = np.zeros((8, 2048, 1024), np.float32)
    for e in range(7):
        gateT[e] = rg[e].T
        upT[e] = ru[e].T
        downT[e] = rd[e].T
    gateT[7] = sg[0].T
    upT[7] = su[0].T
    downT[7] = sd[0].T
    return {
        "gateT": _c(gateT), "upT": _c(upT), "downT": _c(downT),
        "routerT": _c(np.asarray(ins["L0_ffn_router_w"], np.float32).T),
        "rbias_r": _row(ins["L0_ffn_routing_bias"]),
        "ln2_r": _row(ins["L0_ffn_ln2_w"]),
    }


def make_l0_moe_program():
    nc = bass.Bass("TRN2", target_bir_lowering=False)
    al = Alloc(nc)
    rec = Rec()
    def di(name, shape):
        return nc.dram_tensor(name, shape, F32, kind="ExternalInput").ap()
    s1 = di("s1", [T, D])
    x1 = di("x1", [T, D])
    x2 = nc.dram_tensor("x2", [T, D], F32, kind="ExternalOutput").ap()
    alf_hbm = nc.dram_tensor("alf_hbm", [8, 128, T], BF16).ap()
    gbf = nc.dram_tensor("gbf_hbm", [8, 1024, 2048], BF16).ap()
    ubf = nc.dram_tensor("ubf_hbm", [8, 1024, 2048], BF16).ap()
    dbf = nc.dram_tensor("dbf_hbm", [8, 2048, 1024], BF16).ap()
    mk = al.mark()
    build_moe_wconv(nc, al, rec, di("gateT", [8, 1024, 2048]),
                    di("upT", [8, 1024, 2048]), di("downT", [8, 2048, 1024]),
                    gbf, ubf, dbf)
    rec.fence(tag="wc_end")
    al.free_suffix(mk)
    build_l0_moe(nc, al, rec, s1, x1, x2, alf_hbm, gbf, ubf, dbf,
                 di("routerT", [1024, 7]),
                 di("rbias_r", [1, 7]), di("ln2_r", [1, D]))
    finish(nc, rec, al)
    return nc



LN512 = 3.1191592936036462


def build_l1_attn(nc, al, rec, x, y, qf_hbm, kf_hbm, vt_hbm, xcf_hbm, zf_hbm,
                  wupT, qdiag, kdiag, vdiag, convw_c, convb_c, wig_h, big_c,
                  lnw_row, skip_c, wdnT, pfx="f5_"):
    S, P = al.sb, al.ps

    ident = S(pfx + "ident", [128, 128], BF16)
    ident32 = S(pfx + "id32", [128, 128], F32)
    ones1 = S(pfx + "ones1", [1, 128], F32)
    onescol = S(pfx + "onescol", [128, 1], BF16)
    cpool = S(pfx + "cpool", [128, 8], F32)
    xt = [S(pfx + f"xt{i}", [128, D], F32) for i in range(2)]
    sq_scr = S(pfx + "sq", [128, D], F32)
    stats = S(pfx + "stats", [128, 8], F32)
    st2 = S(pfx + "st2", [128, 8], F32)
    cw_s = S(pfx + "cw", [128, 64], F32)
    cb_s = S(pfx + "cb", [128, 16], F32)
    big_s = S(pfx + "big", [128, 2], F32)
    skp_s = S(pfx + "skp", [128, 16], F32)
    wig_s = S(pfx + "wig", [128, 4 * 96], BF16)
    gX = S(pfx + "gX", [4, T], F32)
    gY = S(pfx + "gY", [4, T], F32)
    IGP = gX[0:4, :]
    FGP = gY[0:4, :]

    p_tr = P(pfx + "ptr", [128, 1024], BF16)
    b0 = P(pfx + "b0", [128, 512])
    b1 = P(pfx + "b1", [128, 512])
    b2 = P(pfx + "b2", [128, 512])
    b3 = P(pfx + "b3", [128, 512])
    b4 = P(pfx + "b4", [128, 512])
    b5 = P(pfx + "b5", [128, 512])
    b6 = P(pfx + "b6", [128, 512])

    ln = LNHelper(rec, sq_scr, stats, name=pfx + "ln")
    lnh = LNHelper(rec, sq_scr, st2, name=pfx + "lh", sq_tok=pfx + "lnsq")
    cp = ConstPool(rec, cpool)
    eps5 = cp.get(1e-5)
    one_c = cp.get(1.0)
    cp_op = cp.op

    o_id = make_identity_ops(rec, ident)
    rec.add("gpsimd", lambda e: e.memset(ones1[:], 1.0), tag="ones", w=["ones1"])
    rec.add("gpsimd", lambda e: e.memset(onescol[:], 1.0), tag="onesc",
            w=["onescol"])
    rec.add("gpsimd", lambda e: e.memset(ident32[:], 0.0), tag="i32a",
            w=["id32"])
    rec.add("gpsimd", lambda e: e.affine_select(
        out=ident32[:], in_=ident32[:], compare_op=ALU.not_equal, fill=1.0,
        base=0, pattern=[[-1, 128]], channel_multiplier=1), tag="i32b",
        r=["id32"], w=["id32"])
    rec.dma("sync", cw_s[:], convw_c[:], w=["cw"])
    rec.dma("sync", cb_s[:], convb_c[:], w=["cb"])
    rec.dma("sync", big_s[0:4, :], big_c[:], w=["big"])
    rec.dma("sync", skp_s[:], skip_c[:], w=["skp"])

    wup_s = S(pfx + "wup", [128, 8 * 4096], BF16)
    qd_s = S(pfx + "qd", [128, 64 * 128], BF16)
    kd_s = S(pfx + "kd", [128, 64 * 128], BF16)
    vw_s = S(pfx + "vw", [128, 64], F32)
    wstage = S(pfx + "wstage", [128, 2048], F32)
    xlf = S(pfx + "xlf", [128, 4096], BF16)
    xln16 = S(pfx + "xln16", [128, D], BF16)
    xm_s = [S(pfx + f"xm{m}", [128, 515], BF16) for m in range(16)]
    hl_s = S(pfx + "hl", [128, 3 * 16], BF16)
    cv0 = S(pfx + "cv0", [128, 512], F32)
    cv1 = S(pfx + "cv1", [128, 512], F32)
    xc4 = [S(pfx + f"xc4_{d}", [128, 512], BF16) for d in range(4)]
    qkbuf = [S(pfx + f"qkb{i}", [128, 512], BF16) for i in range(2)]
    vbuf3 = [S(pfx + f"vb3{i}", [128, 512], BF16) for i in range(2)]
    zbuf = [S(pfx + f"zb{i}", [128, 512], BF16) for i in range(2)]
    vt_t = [S(pfx + f"vt{t}", [128, 2048], BF16) for t in range(4)]

    def wload(dst, dst_tok, src_hbm, nch, cols, tag):
        for k in range(nch):
            for piece in range(0, cols, 2048):
                pw = min(2048, cols - piece)
                rec.dma("sync", wstage[:, 0:pw],
                        src_hbm[128 * k:128 * (k + 1), piece:piece + pw],
                        tag=f"{tag}ld{k}_{piece}", w=["wstage"])
                rec.add("vector", lambda e, k=k, piece=piece, pw=pw, dst=dst:
                        e.tensor_copy(
                    dst[:, cols * k + piece:cols * k + piece + pw],
                    wstage[:, 0:pw]),
                    tag=f"{tag}c{k}", r=["wstage"], w=[dst_tok])

    wload(wup_s, "wup", wupT, 8, 4096, "wup")
    wload(qd_s, "qdw", qdiag, 1, 8192, "qd")
    wload(kd_s, "kdw", kdiag, 1, 8192, "kd")
    rec.dma("sync", vw_s[:], vdiag[:], w=["vdw"])
    rec.dma("sync", wstage[:, 0:384], wig_h[:], tag="wigld", w=["wstage"])
    rec.add("vector", lambda e: e.tensor_copy(wig_s[:], wstage[:, 0:384]),
            tag="wigc", r=["wstage"], w=["wig"])
    rec.add("gpsimd", lambda e: e.memset(hl_s[:], 0.0), tag="hl0", w=["hl"])

    for p in range(NP):
        for tl in range(4):
            ti = 4 * p + tl
            q = ti % 2
            rec.dma("sync", xt[q][:], x[128 * ti:128 * (ti + 1), :],
                    tag=f"x{ti}", w=[f"xt{q}"])
            ln.emit(xt[q][:], f"xt{q}", xln16[:], "xln16", eps5,
                    const_dep=cp_op)
            transpose_to_fm(rec, ident, o_id, xln16, "xln16", p_tr, "ptr",
                            xlf, "xlf", tl)
        for m in range(16):
            bank, bt = (b0, "b0") if m % 2 == 0 else (b1, "b1")
            for k in range(8):
                rec.add("tensor", lambda e, m=m, k=k, bank=bank: e.matmul(
                    bank[:], wup_s[:, 4096 * k + 128 * m:4096 * k + 128 * (m + 1)],
                    xlf[:, 512 * k:512 * (k + 1)],
                    start=(k == 0), stop=(k == 7)),
                    tag=f"xm{p}_{m}", r=["wup", "xlf"], w=[bt])
            if p > 0:
                rec.add("vector", lambda e, m=m: e.tensor_copy(
                    hl_s[:, 3 * m:3 * (m + 1)], xm_s[m][:, 512:515]),
                    tag="hlsave", r=[f"xms{m}"], w=["hl"])
            rec.add("vector", lambda e, m=m, bank=bank: e.tensor_copy(
                xm_s[m][:, 3:515], bank[:]),
                tag="xmev", r=[bt, "hl"], w=[f"xms{m}"])
            rec.add("vector", lambda e, m=m: e.tensor_copy(
                xm_s[m][:, 0:3], hl_s[:, 3 * m:3 * (m + 1)]),
                tag="hlrest", r=["hl"], w=[f"xms{m}"])
        for m in range(16):
            bank, bt = (b0, "b0") if m % 2 == 0 else (b1, "b1")
            zb = zbuf[m % 2]
            for k in range(8):
                rec.add("tensor", lambda e, m=m, k=k, bank=bank: e.matmul(
                    bank[:],
                    wup_s[:, 4096 * k + 2048 + 128 * m:4096 * k + 2048 + 128 * (m + 1)],
                    xlf[:, 512 * k:512 * (k + 1)],
                    start=(k == 0), stop=(k == 7)),
                    tag=f"z{p}_{m}", r=["wup", "xlf"], w=[bt])
            rec.add("vector", lambda e, m=m, bank=bank, zb=zb: e.tensor_copy(
                zb[:], bank[:]), tag="zev", r=[bt], w=[f"zbt{m % 2}"])
            rec.dma("gpsimd", zf_hbm[m, :, 512 * p:512 * (p + 1)], zb[:],
                    tag=f"zsp{p}_{m}", r=[f"zbt{m % 2}"], w=[f"hz{p}_{m}"])
        for pc in range(4):
            for d in range(4):
                m = 4 * d + pc
                rec.add("vector", lambda e, m=m: e.tensor_scalar(
                    cv0[:], xm_s[m][:, 3:515], cw_s[:, 4 * m + 3:4 * m + 4],
                    None, ALU.mult), tag="cj3", r=[f"xms{m}", "cw"], w=["cv0"])
                rec.add("vector", lambda e, m=m: e.scalar_tensor_tensor(
                    cv1[:], xm_s[m][:, 2:514], cw_s[:, 4 * m + 2:4 * m + 3],
                    cv0[:], ALU.mult, ALU.add), tag="cj2",
                    r=[f"xms{m}", "cw", "cv0"], w=["cv1"])
                rec.add("vector", lambda e, m=m: e.scalar_tensor_tensor(
                    cv0[:], xm_s[m][:, 1:513], cw_s[:, 4 * m + 1:4 * m + 2],
                    cv1[:], ALU.mult, ALU.add), tag="cj1",
                    r=[f"xms{m}", "cw", "cv1"], w=["cv0"])
                rec.add("vector", lambda e, m=m: e.scalar_tensor_tensor(
                    cv1[:], xm_s[m][:, 0:512], cw_s[:, 4 * m:4 * m + 1],
                    cv0[:], ALU.mult, ALU.add), tag="cj0",
                    r=[f"xms{m}", "cw", "cv0"], w=["cv1"])
                rec.add("scalar", lambda e, m=m, d=d: e.activation(
                    xc4[d][:], cv1[:], AF.Silu, bias=cb_s[:, m:m + 1]),
                    tag="xcact", r=["cv1", "cb"], w=[f"xc4t{d}"])
                rec.dma("gpsimd", xcf_hbm[m, :, 512 * p:512 * (p + 1)],
                        xc4[d][:], tag=f"xcsp{p}_{m}", r=[f"xc4t{d}"],
                        w=[f"hxc{p}_{m}"])
            for o in range(4):
                mo = 4 * o + pc
                for which, dw, dwt, hbm, buf, btok, sec in (
                        (0, qd_s, "qdw", qf_hbm, qkbuf[0], "qkb0", 0),
                        (1, kd_s, "kdw", kf_hbm, qkbuf[1], "qkb1", 1),
                        (2, None, "vdw", None, vbuf3[o % 2], f"vb3{o % 2}", 2)):
                    bank, bt = (b0, "b0") if (2 * o + which) % 2 == 0 \
                        else (b1, "b1")
                    if which == 2:
                        for d in range(4):
                            mi = 4 * d + pc
                            idx = 16 * o + 4 * d + pc
                            if d == 0:
                                rec.add("vector", lambda e, mi=mi, idx=idx:
                                        e.tensor_scalar(
                                    cv0[:], xm_s[mi][:, 3:515],
                                    vw_s[:, idx:idx + 1], None, ALU.mult),
                                    tag="vhw0", r=[f"xms{mi}", "vdw"],
                                    w=["cv0"])
                            else:
                                src_t, dst_t = (cv0, cv1) if d % 2 == 1 \
                                    else (cv1, cv0)
                                stok, dtok2 = ("cv0", "cv1") if d % 2 == 1 \
                                    else ("cv1", "cv0")
                                rec.add("vector", lambda e, mi=mi, idx=idx,
                                        src_t=src_t, dst_t=dst_t:
                                        e.scalar_tensor_tensor(
                                    dst_t[:], xm_s[mi][:, 3:515],
                                    vw_s[:, idx:idx + 1], src_t[:],
                                    ALU.mult, ALU.add),
                                    tag="vhw", r=[f"xms{mi}", "vdw", stok],
                                    w=[dtok2])
                        rec.add("vector", lambda e, buf=buf: e.tensor_copy(
                            buf[:], cv1[:]), tag="vhwev",
                            r=["cv1"], w=[btok])
                    else:
                        for d in range(4):
                            mi = 4 * d + pc
                            idx = 16 * o + 4 * d + pc
                            rhs = lambda d=d: xc4[d][:]
                            rtok = f"xc4t{d}"
                            rec.add("tensor", lambda e, idx=idx, dw=dw,
                                    rhs=rhs, bank=bank, st=(d == 0),
                                    sp=(d == 3): e.matmul(
                                bank[:], dw[:, 128 * idx:128 * (idx + 1)],
                                rhs(), start=st, stop=sp),
                                tag=f"hw{p}_{mo}_{which}", r=[dwt, rtok],
                                w=[bt])
                        rec.add("vector", lambda e, buf=buf, bank=bank:
                                e.tensor_copy(buf[:], bank[:]),
                                tag="hwev", r=[bt], w=[btok])
                    if hbm is not None:
                        rec.dma("gpsimd", hbm[mo, :, 512 * p:512 * (p + 1)],
                                buf[:], tag=f"hwsp{p}_{mo}_{which}",
                                r=[btok],
                                w=[("hqs" if which == 0 else "hks") +
                                   f"{p}_{mo}"])
                    cc = 16 * sec + mo
                    st_ifg = (pc == 0 and o == 0 and which == 0)
                    sp_ifg = (pc == 3 and o == 3 and which == 2)
                    rec.add("tensor", lambda e, cc=cc, buf=buf, st=st_ifg,
                            sp=sp_ifg: e.matmul(
                        b5[0:4, :], wig_s[:, 4 * cc:4 * (cc + 1)], buf[:],
                        start=st, stop=sp),
                        tag=f"ig{p}", r=["wig", btok], w=["b5"])
                    rec.add("tensor", lambda e, cc=cc, buf=buf, st=st_ifg,
                            sp=sp_ifg: e.matmul(
                        b6[0:4, :], wig_s[:, 192 + 4 * cc:192 + 4 * (cc + 1)],
                        buf[:],
                        start=st, stop=sp),
                        tag=f"fg{p}", r=["wig", btok], w=["b6"])
                    if which == 2:
                        vcol = 512 * pc + 128 * o
                        for tl in range(4):
                            rec.add("tensor", lambda e, buf=buf, tl=tl:
                                    e.transpose(
                                p_tr[:, 0:128], buf[:, 128 * tl:128 * (tl + 1)],
                                ident[:]), deps=[o_id], tag="vtr",
                                r=[btok], w=["ptr"])
                            rec.add("vector", lambda e, vcol=vcol, tl=tl:
                                    e.tensor_copy(
                                vt_t[tl][:, vcol:vcol + 128],
                                p_tr[:, 0:128]),
                                tag="vtev", r=["ptr"], w=[f"vtt{tl}"])
        rec.add("vector", lambda e, p=p: e.tensor_scalar(
            IGP[:, 512 * p:512 * (p + 1)], b5[0:4, :],
            big_s[0:4, 0:1], None, ALU.add),
            tag="ifgb_i", r=["b5", "big"], w=["gX"])
        rec.add("vector", lambda e, p=p: e.tensor_scalar(
            FGP[:, 512 * p:512 * (p + 1)], b6[0:4, :],
            big_s[0:4, 1:2], None, ALU.add),
            tag="ifgb_f", r=["b6", "big"], w=["gY"])
        for tl in range(4):
            rec.dma("sync", vt_hbm[4 * p + tl, :, :], vt_t[tl][:],
                    tag=f"vtsp{p}_{tl}", r=[f"vtt{tl}"],
                    w=[f"hvt{4 * p + tl}"])

    rec.fence(tag="fence_gates")
    al.free([pfx + "wup", pfx + "qd", pfx + "kd", pfx + "vw",
             pfx + "wstage", pfx + "xlf", pfx + "xln16", pfx + "hl",
             pfx + "cv0", pfx + "cv1"]
            + [pfx + f"xm{m}" for m in range(16)]
            + [pfx + f"xc4_{d}" for d in range(4)]
            + [pfx + f"qkb{i}" for i in range(2)]
            + [pfx + f"vb3{i}" for i in range(2)]
            + [pfx + f"zb{i}" for i in range(2)]
            + [pfx + f"vt{t}" for t in range(4)])
    gW = S(pfx + "gW", [4, T], F32)
    aT_s = S(pfx + "aT", [128, 128], F32)
    gZ = S(pfx + "gZ", [4, T], F32)

    rec.add("scalar", lambda e: e.activation(gZ[:], FGP, AF.Abs),
            tag="g_abs", r=["gY"], w=["gZ"])
    rec.add("scalar", lambda e: e.activation(gZ[:], gZ[:], AF.Exp, scale=-1.0),
            tag="g_exp", r=["gZ"], w=["gZ"])
    rec.add("scalar", lambda e: e.activation(gZ[:], gZ[:], AF.Ln,
                                             bias=one_c[0:4, 0:1]),
            deps=[cp_op], tag="g_ln", r=["gZ"], w=["gZ"])
    rec.add("vector", lambda e: e.tensor_scalar(gW[:], FGP, 0.0, None,
                                                ALU.max),
            tag="g_relu", r=["gY"], w=["gW"])
    rec.add("vector", lambda e: e.tensor_tensor(gW[:], gW[:], gZ[:], ALU.add),
            tag="g_add", r=["gW", "gZ"], w=["gW"])
    rec.add("vector", lambda e: e.scalar_tensor_tensor(
        gW[:], gW[:], -1.0, FGP, ALU.mult, ALU.add), tag="g_neg",
        r=["gW", "gY"], w=["gW"])
    rec.add("vector", lambda e: e.memset(gY[:], 0.0), tag="g_zero",
            r=["gY"], w=["gY"])
    rec.add("vector", lambda e: e.tensor_tensor_scan(
        gZ[:], gW[:], gY[:], 0.0, ALU.add, ALU.add), tag="g_cum",
        r=["gW", "gY", "gZ"], w=["gZ"])
    rec.add("vector", lambda e: e.tensor_tensor(gW[:], IGP, gZ[:],
                                                ALU.subtract),
            tag="g_a", r=["gX", "gZ", "gW"], w=["gW"])
    for kc in range(NT):
        rec.add("tensor", lambda e, kc=kc: e.transpose(
            b6[0:128, 0:4], gW[:, 128 * kc:128 * (kc + 1)],
            ident32[0:4, 0:4]),
            tag="aTtr", r=["gW", "id32"], w=["b6"])
        rec.add("vector", lambda e, kc=kc: e.tensor_copy(
            aT_s[:, 4 * kc:4 * (kc + 1)], b6[0:128, 0:4]),
            tag="aTev", r=["b6"], w=["aT"])
    rec.add("vector", lambda e: e.tensor_tensor_scan(
        gX[:], gW[:], gY[:], -1e30, ALU.max, ALU.add), tag="g_rm",
        r=["gW", "gY", "gX"], w=["gX"])
    rec.add("vector", lambda e: e.tensor_scalar(
        gY[:], gX[:], -1.0, -LN512, ALU.mult, ALU.add), tag="g_b",
        r=["gX", "gY"], w=["gY"])
    rec.add("vector", lambda e: e.tensor_tensor(gW[:], gZ[:], gX[:], ALU.add),
            tag="g_m", r=["gZ", "gX", "gW"], w=["gW"])
    rec.add("scalar", lambda e: e.activation(gW[:], gW[:], AF.Exp, scale=-1.0),
            tag="g_n2", r=["gW"], w=["gW"])
    BS = gY[0:4, :]
    N2 = gW[0:4, :]
    rec.fence(tag="fence_3b")
    al.free([pfx + "gZ"])

    wdn_s = S(pfx + "wdn", [128, 16 * 1024], BF16)
    wstg2 = S(pfx + "wstg2", [128, 1024], F32)
    lnr_s = S(pfx + "lnr", [1, 2048], F32)
    lnwbc = S(pfx + "lnwbc", [128, 2048], F32)
    qh_s = S(pfx + "qh", [128, 2048], BF16)
    kbb = [S(pfx + f"kbb{i}", [128, 512], BF16) for i in range(4)]
    vbb = [S(pfx + f"vbb{i}", [128, 512], BF16) for i in range(4)]
    bb_s = S(pfx + "bb", [128, 512], F32)
    ld_t = [S(pfx + f"ldt{i}", [128, 512], F32) for i in range(2)]
    ed_t = [S(pfx + f"edt{i}", [128, 512], F32) for i in range(2)]
    c16 = [S(pfx + f"c16{i}", [128, 512], BF16) for i in range(2)]
    maskn = S(pfx + "maskn", [128, 128], F32)
    brow1 = S(pfx + "brow1", [1, 512], F32)
    n2r1 = S(pfx + "n2r1", [1, 512], F32)
    rs1 = S(pfx + "rs1", [1, 512], F32)
    fbc = S(pfx + "fbc", [128, 512], F32)
    h4 = [S(pfx + f"h4_{d}", [128, 512], F32) for d in range(4)]
    ht = S(pfx + "ht", [128, 512], F32)
    hnt16 = S(pfx + "hnt16", [128, 512], BF16)
    xcz = S(pfx + "xcz", [128, 2048], BF16)
    zz = S(pfx + "zz", [128, 2048], BF16)
    szf = S(pfx + "szf", [128, 512], F32)
    skf = S(pfx + "skf", [128, 512], F32)
    hnf = S(pfx + "hnf", [128, 2048], BF16)
    hst_s = S(pfx + "hst", [128, 16 * 512], BF16)
    outt = [S(pfx + f"outt{i}", [128, D], F32) for i in range(2)]

    for m in range(16):
        rec.dma("sync", wstg2[:], wdnT[128 * m:128 * (m + 1), :],
                tag=f"wdnld{m}", w=["wstg2"])
        rec.add("vector", lambda e, m=m: e.tensor_copy(
            wdn_s[:, 1024 * m:1024 * (m + 1)], wstg2[:]),
            tag=f"wdnc{m}", r=["wstg2"], w=["wdn"])
    rec.dma("sync", lnr_s[:], lnw_row[:], w=["lnr"])
    for h in range(4):
        rec.add("tensor", lambda e, h=h: e.matmul(
            b0[:], ones1[0:1, :], lnr_s[0:1, 512 * h:512 * (h + 1)],
            start=True, stop=True), tag="lnwbc", r=["lnr", "ones1"], w=["b0"])
        rec.add("scalar", lambda e, h=h: e.copy(
            lnwbc[:, 512 * h:512 * (h + 1)], b0[:]), tag="lnwbcp",
            r=["b0"], w=["lnwbc"])
    rec.add("gpsimd", lambda e: e.memset(maskn[:], 0.0), tag="mka", w=["mkn"])
    rec.add("gpsimd", lambda e: e.affine_select(
        out=maskn[:], in_=maskn[:], compare_op=ALU.is_ge, fill=-1e30,
        base=0, pattern=[[1, 128]], channel_multiplier=-1), tag="mkb",
        r=["mkn"], w=["mkn"])

    for g in range(NP):
        for h in range(4):
            for d2 in range(4):
                rec.dma("sync", qh_s[:, 512 * d2:512 * (d2 + 1)],
                        qf_hbm[4 * d2 + h, :, 512 * g:512 * (g + 1)],
                        tag=f"qhl{g}_{h}_{d2}", r=[f"hqs{g}_{4 * d2 + h}"],
                        w=["qht"])
            rec.dma("sync", brow1[:], BS[h:h + 1, 512 * g:512 * (g + 1)],
                    tag=f"br{g}_{h}", r=["gY"], w=["br1"])
            rec.dma("sync", n2r1[:], N2[h:h + 1, 512 * g:512 * (g + 1)],
                    tag=f"nr{g}_{h}", r=["gW"], w=["nr1"])
            rec.add("tensor", lambda e: e.matmul(
                b5[:], ones1[0:1, :], brow1[0:1, :],
                start=True, stop=True), tag=f"bbc{g}_{h}",
                r=["br1", "ones1"], w=["b5"])
            rec.add("scalar", lambda e: e.copy(bb_s[:], b5[:]),
                    tag="bbcp", r=["b5"], w=["bbt"])
            nkc = 4 * (g + 1)

            def l1_loads(kc):
                kcb = kc % 4
                for d2 in range(4):
                    rec.dma("sync",
                            kbb[kcb][:, 128 * d2:128 * (d2 + 1)],
                            kf_hbm[4 * d2 + h, :, 128 * kc:128 * (kc + 1)],
                            tag=f"kl{g}_{h}_{kc}_{d2}",
                            r=[f"hks{kc // 4}_{4 * d2 + h}"],
                            w=[f"kbt{kcb}"])
                rec.dma("scalar", vbb[kcb][:],
                        vt_hbm[kc, :, 512 * h:512 * (h + 1)],
                        tag=f"vl{g}_{h}_{kc}", r=[f"hvt{kc}"],
                        w=[f"vbt{kcb}"])

            def l1_qk(kc):
                kcb = kc % 4
                q0 = 128 * max(0, kc - 4 * g)
                qkb, qkt = (b6, "b6") if kc % 2 == 0 else (b5, "b5")
                for d2 in range(4):
                    rec.add("tensor", lambda e, d2=d2, kcb=kcb, q0=q0,
                            qkb=qkb: e.matmul(
                        qkb[:, q0:512], kbb[kcb][:, 128 * d2:128 * (d2 + 1)],
                        qh_s[:, 512 * d2 + q0:512 * (d2 + 1)],
                        start=(d2 == 0), stop=(d2 == 3)),
                        tag=f"qk{g}_{h}_{kc}", r=[f"kbt{kcb}", "qht"],
                        w=[qkt])

            def l1_gate(kc):
                q0 = 128 * max(0, kc - 4 * g)
                ldx, edx = ld_t[kc % 2], ed_t[kc % 2]
                ldtok, edtok = f"ldt{kc % 2}", f"edt{kc % 2}"
                rec.add("vector", lambda e, kc=kc, q0=q0, ldx=ldx, h=h:
                        e.tensor_scalar(
                    ldx[:, q0:512], bb_s[:, q0:512],
                    aT_s[:, 4 * kc + h:4 * kc + h + 1],
                    None, ALU.add), tag=f"ld{g}_{h}_{kc}",
                    r=["bbt", "aT"], w=[ldtok])
                if kc - 4 * g >= 0:
                    rec.add("vector", lambda e, q0=q0, ldx=ldx:
                            e.tensor_tensor(
                        ldx[:, q0:q0 + 128], ldx[:, q0:q0 + 128],
                        maskn[:], ALU.add), tag="lddiag",
                        r=[ldtok, "mkn"], w=[ldtok])
                rec.add("scalar", lambda e, q0=q0, ldx=ldx, edx=edx:
                        e.activation(
                    edx[:, q0:512], ldx[:, q0:512], AF.Exp),
                    tag=f"ed{g}_{h}_{kc}", r=[ldtok], w=[edtok])

            def l1_cmul(kc):
                q0 = 128 * max(0, kc - 4 * g)
                qkb, qkt = (b6, "b6") if kc % 2 == 0 else (b5, "b5")
                edx, c1x = ed_t[kc % 2], c16[kc % 2]
                edtok, ctok = f"edt{kc % 2}", f"c16t{kc % 2}"
                rec.add("vector", lambda e, q0=q0, qkb=qkb, edx=edx,
                        c1x=c1x: e.tensor_tensor(
                    c1x[:, q0:512], qkb[:, q0:512], edx[:, q0:512],
                    ALU.mult),
                    tag=f"c{g}_{h}_{kc}", r=[qkt, edtok], w=[ctok])

            def l1_cv(kc):
                kcb = kc % 4
                q0 = 128 * max(0, kc - 4 * g)
                c1x = c16[kc % 2]
                ctok = f"c16t{kc % 2}"
                sp = (kc == nkc - 1)
                for dv in range(4):
                    bank = (b0, b1, b2, b3)[dv]
                    rec.add("tensor", lambda e, dv=dv, kcb=kcb, kc=kc,
                            q0=q0, bank=bank, c1x=c1x, sp=sp: e.matmul(
                        bank[:, q0:512], vbb[kcb][:, 128 * dv:128 * (dv + 1)],
                        c1x[:, q0:512], start=(kc == 0), stop=sp),
                        tag=f"cv{g}_{h}_{kc}_{dv}",
                        r=[f"vbt{kcb}", ctok], w=[f"b{dv}"])
                rec.add("tensor", lambda e, kc=kc, q0=q0, c1x=c1x, sp=sp:
                        e.matmul(
                    b4[0:1, q0:512], onescol[:], c1x[:, q0:512],
                    start=(kc == 0), stop=sp),
                    tag=f"cs{g}_{h}_{kc}", r=["onescol", ctok], w=["b4"])

            for kk in range(min(3, nkc)):
                l1_loads(kk)
            l1_gate(0)
            l1_qk(0)
            for kc in range(nkc):
                if kc + 3 < nkc:
                    l1_loads(kc + 3)
                if kc + 1 < nkc:
                    l1_gate(kc + 1)
                    l1_qk(kc + 1)
                l1_cmul(kc)
                l1_cv(kc)
            rec.add("scalar", lambda e: e.activation(
                rs1[:], b4[0:1, :], AF.Abs), tag=f"abs{g}_{h}",
                r=["b4"], w=["rst"])
            rec.add("vector", lambda e: e.tensor_tensor(
                rs1[:], rs1[:], n2r1[0:1, :], ALU.max),
                tag="nmax", r=["rst", "nr1"], w=["rst"])
            rec.add("vector", lambda e: e.tensor_scalar(
                rs1[:], rs1[:], 1e-6, None, ALU.add),
                tag="neps", r=["rst"], w=["rst"])
            rec.add("vector", lambda e: e.reciprocal(rs1[:], rs1[:]),
                    tag="nrec", r=["rst"], w=["rst"])
            rec.add("tensor", lambda e: e.matmul(
                b5[:], ones1[0:1, :], rs1[0:1, :],
                start=True, stop=True), tag=f"fbc{g}_{h}",
                r=["rst", "ones1"], w=["b5"])
            rec.add("scalar", lambda e: e.copy(fbc[:], b5[:]),
                    tag="fbcp", r=["b5"], w=["fbct"])
            for dv in range(4):
                bank = (b0, b1, b2, b3)[dv]
                rec.add("vector", lambda e, dv=dv, bank=bank: e.tensor_tensor(
                    h4[dv][:], bank[:], fbc[:], ALU.mult),
                    tag=f"hsc{g}_{h}_{dv}", r=[f"b{dv}", "fbct"],
                    w=[f"h4t{dv}"])
            for ts in range(4):
                for dv in range(4):
                    rec.add("tensor", lambda e, dv=dv, ts=ts: e.transpose(
                        b4[:, 0:128],
                        h4[dv][:, 128 * ts:128 * (ts + 1)], ident32[:]),
                        tag="htr", r=[f"h4t{dv}", "id32"], w=["b4"])
                    rec.add("vector", lambda e, dv=dv: e.tensor_copy(
                        ht[:, 128 * dv:128 * (dv + 1)], b4[:, 0:128]),
                        tag="htev", r=["b4"], w=["htt"])
                lnh.emit(ht[:], "htt", szf[:], "szft", eps5, const_dep=cp_op)
                rec.add("vector", lambda e, h=h: e.tensor_tensor(
                    hnt16[:], szf[:], lnwbc[:, 512 * h:512 * (h + 1)],
                    ALU.mult), tag="hnw", r=["szft", "lnwbc"], w=["hnt"])
                for dv in range(4):
                    rec.add("tensor", lambda e, dv=dv: e.transpose(
                        p_tr[:, 0:128], hnt16[:, 128 * dv:128 * (dv + 1)],
                        ident[:]), deps=[o_id], tag="hntr",
                        r=["hnt"], w=["ptr"])
                    rec.add("vector", lambda e, dv=dv, ts=ts: e.tensor_copy(
                        hnf[:, 512 * dv + 128 * ts:512 * dv + 128 * (ts + 1)],
                        p_tr[:, 0:128]),
                        tag="hnev", r=["ptr"], w=["hnft"])
            for d2 in range(4):
                m = 4 * d2 + h
                rec.dma("sync", xcz[:, 512 * d2:512 * (d2 + 1)],
                        xcf_hbm[m, :, 512 * g:512 * (g + 1)],
                        tag=f"xcl{g}_{h}_{d2}", r=[f"hxc{g}_{m}"], w=["xczt"])
                rec.dma("sync", zz[:, 512 * d2:512 * (d2 + 1)],
                        zf_hbm[m, :, 512 * g:512 * (g + 1)],
                        tag=f"zl{g}_{h}_{d2}", r=[f"hz{g}_{m}"], w=["zzt"])
            for d2 in range(4):
                m = 4 * d2 + h
                rec.add("scalar", lambda e, d2=d2, m=m: e.activation(
                    skf[:], xcz[:, 512 * d2:512 * (d2 + 1)], AF.Copy,
                    scale=skp_s[:, m:m + 1]), tag="hskip",
                    r=["xczt", "skp"], w=["skft"])
                rec.add("vector", lambda e, d2=d2: e.tensor_tensor(
                    skf[:], skf[:],
                    hnf[:, 512 * d2:512 * (d2 + 1)], ALU.add),
                    tag="hskip2", r=["skft", "hnft"], w=["skft"])
                rec.add("scalar", lambda e, d2=d2: e.activation(
                    szf[:], zz[:, 512 * d2:512 * (d2 + 1)], AF.Silu),
                    tag="hsilu", r=["zzt"], w=["szft"])
                rec.add("vector", lambda e, d2=d2, m=m: e.tensor_tensor(
                    hst_s[:, 512 * m:512 * (m + 1)], skf[:], szf[:],
                    ALU.mult), tag="hstate", r=["skft", "szft"],
                    w=[f"hstt{m}"])
        for ts in range(4):
            ti = 4 * g + ts
            q2 = ts % 2
            rec.dma("scalar", xt[q2][:], x[128 * ti:128 * (ti + 1), :],
                    tag=f"xr{ti}", w=[f"xt{q2}"])
            for nh in range(2):
                bank, bt = (b5, "b5") if nh == 0 else (b6, "b6")
                for m in range(16):
                    rec.add("tensor", lambda e, m=m, ts=ts, nh=nh, bank=bank:
                            e.matmul(
                        bank[:],
                        hst_s[:, 512 * m + 128 * ts:512 * m + 128 * (ts + 1)],
                        wdn_s[:, 1024 * m + 512 * nh:1024 * m + 512 * (nh + 1)],
                        start=(m == 0), stop=(m == 15)),
                        tag=f"dn{ti}_{nh}", r=[f"hstt{m}", "wdn"], w=[bt])
                rec.add("vector", lambda e, q2=q2, nh=nh, bank=bank:
                        e.tensor_tensor(
                    outt[q2][:, 512 * nh:512 * (nh + 1)], bank[:],
                    xt[q2][:, 512 * nh:512 * (nh + 1)], ALU.add),
                    tag=f"dres{ti}_{nh}", r=[bt, f"xt{q2}"], w=[f"outt{q2}"])
            rec.dma("sync", y[128 * ti:128 * (ti + 1), :], outt[q2][:],
                    tag=f"y{ti}", r=[f"outt{q2}"])


def prep_l1_attn(ins):
    INNER = 2048
    cc = np.arange(INNER)
    cp_map = (cc % 4) * 512 + cc // 4
    inv = np.empty(INNER, np.int64)
    inv[cp_map] = cc
    w_up = np.asarray(ins["L1_attn_w_up"], np.float32)
    wupT = w_up.T.copy()
    wupT_perm = np.empty_like(wupT)
    wupT_perm[:, cp_map] = wupT[:, 0:2048]
    wupT_perm[:, 2048 + cp_map] = wupT[:, 2048:4096]
    w_down = np.asarray(ins["L1_attn_w_down"], np.float32)
    wdnT_perm = np.empty((2048, 1024), np.float32)
    wdnT_perm[cp_map, :] = w_down.T

    def diag_pack(w):
        out = np.zeros((128, 8192), np.float32)
        pr = np.arange(128)
        for o in range(4):
            for d in range(4):
                for pc in range(4):
                    idx = 16 * o + 4 * d + pc
                    out[pr, 128 * idx + pr] = w[128 * pc + pr, o, d]
        return out

    qdiag = diag_pack(np.asarray(ins["L1_attn_w_q"], np.float32))
    kdiag = diag_pack(np.asarray(ins["L1_attn_w_k"], np.float32))
    w_v = np.asarray(ins["L1_attn_w_v"], np.float32)
    vdiag = np.zeros((128, 64), np.float32)
    pr = np.arange(128)
    for o in range(4):
        for d in range(4):
            for pc in range(4):
                vdiag[:, 16 * o + 4 * d + pc] = w_v[128 * pc + pr, o, d]
    mm, pp = np.meshgrid(np.arange(16), np.arange(128), indexing="ij")
    c_of = 512 * (mm % 4) + 4 * pp + mm // 4
    conv_w = np.asarray(ins["L1_attn_conv_w"], np.float32)[:, 0, :]
    convw_c = np.zeros((128, 64), np.float32)
    for m in range(16):
        for j in range(4):
            convw_c[:, 4 * m + j] = conv_w[c_of[m], j]
    convb_c = np.zeros((128, 16), np.float32)
    skip_c = np.zeros((128, 16), np.float32)
    conv_b = np.asarray(ins["L1_attn_conv_b"], np.float32)
    skip = np.asarray(ins["L1_attn_skip"], np.float32)
    for m in range(16):
        convb_c[:, m] = conv_b[c_of[m]]
        skip_c[:, m] = skip[c_of[m]]
    row_perm = np.concatenate([s * 2048 + cp_map for s in range(3)])
    wig = np.asarray(ins["L1_attn_w_ig"], np.float32)
    wfg = np.asarray(ins["L1_attn_w_fg"], np.float32)
    wig_h = np.zeros((128, 384), np.float32)
    for gi, w in ((0, wig), (1, wfg)):
        wp = np.empty_like(w)
        wp[:, row_perm] = w
        for ccx in range(48):
            wig_h[:, 4 * (48 * gi + ccx):4 * (48 * gi + ccx + 1)] = \
                wp[:, 128 * ccx:128 * (ccx + 1)].T
    big_c = np.stack([np.asarray(ins["L1_attn_b_ig"], np.float32),
                      np.asarray(ins["L1_attn_b_fg"], np.float32)], axis=1)
    ln_w = np.asarray(ins["L1_attn_ln_w"], np.float32)
    lnw_row = np.zeros((1, 2048), np.float32)
    for h in range(4):
        for d in range(4):
            lnw_row[0, 512 * h + 128 * d + np.arange(128)] = \
                ln_w[h, 4 * np.arange(128) + d]
    return {
        "wupT": _c(wupT_perm), "wdnT": _c(wdnT_perm),
        "qdiag": _c(qdiag), "kdiag": _c(kdiag), "vdiag": _c(vdiag),
        "convw_c": _c(convw_c), "convb_c": _c(convb_c),
        "wig_h": _c(wig_h), "big_c": _c(big_c),
        "lnw_row": _c(lnw_row), "skip_c": _c(skip_c),
    }


def make_l1_attn_program():
    nc = bass.Bass("TRN2", target_bir_lowering=False)
    al = Alloc(nc)
    rec = Rec()
    def di(name, shape):
        return nc.dram_tensor(name, shape, F32, kind="ExternalInput").ap()
    x = di("x", [T, D])
    y = nc.dram_tensor("y", [T, D], F32, kind="ExternalOutput").ap()
    qf = nc.dram_tensor("qf_hbm", [16, 128, T], BF16).ap()
    kf = nc.dram_tensor("kf_hbm", [16, 128, T], BF16).ap()
    vt = nc.dram_tensor("vt_hbm", [NT, 128, 2048], BF16).ap()
    xcf = nc.dram_tensor("xcf_hbm", [16, 128, T], BF16).ap()
    zf = nc.dram_tensor("zf_hbm", [16, 128, T], BF16).ap()
    build_l1_attn(nc, al, rec, x, y, qf, kf, vt, xcf, zf,
                  di("wupT", [1024, 4096]), di("qdiag", [128, 8192]),
                  di("kdiag", [128, 8192]), di("vdiag", [128, 64]),
                  di("convw_c", [128, 64]), di("convb_c", [128, 16]),
                  di("wig_h", [128, 384]), di("big_c", [4, 2]),
                  di("lnw_row", [1, 2048]), di("skip_c", [128, 16]),
                  di("wdnT", [2048, 1024]))
    finish(nc, rec, al)
    return nc


def prep_l0_mla(ins):
    kv_d = np.asarray(ins["L0_ffn_kv_d"], np.float32)
    q_d = np.asarray(ins["L0_ffn_q_d"], np.float32)
    k_u = np.asarray(ins["L0_ffn_k_u"], np.float32)
    q_u = np.asarray(ins["L0_ffn_q_u"], np.float32)
    v_u = np.asarray(ins["L0_ffn_v_u"], np.float32)
    rq = np.asarray(ins["L0_ffn_rope_q_w"], np.float32)
    rk = np.asarray(ins["L0_ffn_rope_k_w"], np.float32)
    o_w = np.asarray(ins["L0_ffn_o_w"], np.float32)

    RD = 32
    ropekT = np.zeros((1024, 1024), np.float32)
    kuT = np.zeros((256, 1024), np.float32)
    qmixT = np.zeros((256, 1024), np.float32)
    rkT = rk.T
    rqT = rq.T
    kuTT = k_u.T
    quTT = q_u.T
    for j in range(8):
        for h2 in range(2):
            h = 2 * j + h2
            ropekT[:, 128 * j + 64 * h2:128 * j + 64 * h2 + 32] = \
                rkT[:, 32 * h:32 * (h + 1)]
            kuT[:, 128 * j + 64 * h2 + 32:128 * j + 64 * h2 + 64] = \
                kuTT[:, 64 * h + 32:64 * (h + 1)]
            qmixT[:, 128 * j + 64 * h2:128 * j + 64 * h2 + 32] = \
                rqT[:, 32 * h:32 * (h + 1)]
            qmixT[:, 128 * j + 64 * h2 + 32:128 * j + 64 * h2 + 64] = \
                quTT[:, 64 * h + 32:64 * (h + 1)]
    inv_freq = 1.0 / (10000.0 ** (np.arange(0, RD, 2, dtype=np.float64) / RD))
    tpos = np.arange(T, dtype=np.float64)
    freqs = tpos[:, None] * inv_freq[None, :]
    emb = np.concatenate([freqs, freqs], axis=1)
    cos32 = np.cos(emb).astype(np.float32).T
    sin32 = np.sin(emb).astype(np.float32).T
    sin_fold = sin32.copy()
    sin_fold[0:16] = -sin32[0:16]
    cos_t = np.zeros((128, T), np.float32)
    sin_t = np.zeros((128, T), np.float32)
    for h2 in range(2):
        cos_t[64 * h2:64 * h2 + 32] = cos32
        cos_t[64 * h2 + 32:64 * h2 + 64] = 1.0
        sin_t[64 * h2:64 * h2 + 32] = sin_fold
        sin_t[64 * h2 + 32:64 * h2 + 64] = 0.0
    pm = np.zeros((128, 128), np.float32)
    for i in range(128):
        pm[i ^ 16, i] = 1.0
    owt = o_w.T
    owT_h = np.zeros((64, 16384), np.float32)
    for h in range(16):
        owT_h[:, 1024 * h:1024 * (h + 1)] = owt[64 * h:64 * (h + 1), :]
    return {
        "kvdT": _c(kv_d.T), "qdT": _c(q_d.T), "ropekT": _c(ropekT),
        "kuT": _c(kuT), "qmixT": _c(qmixT), "vuT": _c(v_u.T),
        "owT": _c(owT_h), "ln1_r": _row(ins["L0_ffn_ln1_w"]),
        "cos_t": _c(cos_t), "sin_t": _c(sin_t), "pmat": _c(pm),
    }


def make_l0_mla_program():
    nc = bass.Bass("TRN2", target_bir_lowering=False)
    al = Alloc(nc)
    rec = Rec()
    def di(name, shape):
        return nc.dram_tensor(name, shape, F32, kind="ExternalInput").ap()
    x = di("x", [T, D])
    s1 = nc.dram_tensor("s1", [T, D], F32, kind="ExternalOutput").ap()
    xin_hbm = nc.dram_tensor("xin_hbm", [T, D], F32, kind="ExternalOutput").ap()
    k_hbm = nc.dram_tensor("k_hbm", [NT, 8, 128, 128], BF16).ap()
    q_hbm = nc.dram_tensor("q_hbm", [NT, 8, 128, 128], BF16).ap()
    v_hbm = nc.dram_tensor("v_hbm", [NT, 128, 1040], BF16).ap()
    build_l0_mla(nc, al, rec, x, s1, xin_hbm, k_hbm, q_hbm, v_hbm,
                 di("kvdT", [1024, 256]), di("qdT", [1024, 256]),
                 di("ropekT", [1024, 1024]), di("kuT", [256, 1024]),
                 di("qmixT", [256, 1024]), di("vuT", [256, 1024]),
                 di("owT", [64, 16384]), di("ln1_r", [1, D]),
                 di("cos_t", [128, T]), di("sin_t", [128, T]),
                 di("pmat", [128, 128]))
    finish(nc, rec, al)
    return nc


def prep_l0_attn(ins):
    w_qkv = np.asarray(ins["L0_attn_w_qkv"], np.float32)
    b_qkv = np.asarray(ins["L0_attn_b_qkv"], np.float32)
    return {
        "wqk": _c(w_qkv[:, 0:1024]),
        "wv": _c(w_qkv[:, 1024:1536]),
        "wfc": _c(ins["L0_attn_w_fc"]),
        "bqk_r": _row(b_qkv[0:1024]),
        "bv_c": _col_chunks(b_qkv[1024:1536], 4),
        "bfc_r": _row(ins["L0_attn_b_fc"]),
        "gam_r": _row(ins["L0_attn_gamma"]),
        "bet_r": _row(ins["L0_attn_beta"]),
    }


def make_l0_attn_program():
    nc = bass.Bass("TRN2", target_bir_lowering=False)
    al = Alloc(nc)
    rec = Rec()
    def di(name, shape):
        return nc.dram_tensor(name, shape, F32, kind="ExternalInput").ap()
    x = di("x", [T, D])
    y = nc.dram_tensor("y", [T, D], F32, kind="ExternalOutput").ap()
    build_l0_attn(nc, al, rec, x, y, di("wqk", [1024, 1024]),
                  di("wv", [1024, 512]), di("wfc", [512, 1024]),
                  di("bqk_r", [1, 1024]), di("bv_c", [128, 4]),
                  di("bfc_r", [1, 1024]), di("gam_r", [1, 1024]),
                  di("bet_r", [1, 1024]))
    finish(nc, rec, al)
    return nc


def prep_l1_ffn(ins):
    return {
        "W1": _c(ins["L1_ffn_W1"]),
        "W2": _c(ins["L1_ffn_W2"]),
        "b1c": _col_chunks(ins["L1_ffn_b1"], 32),
        "thc": _col_chunks(ins["L1_ffn_quantum_phases"], 32),
        "wrc": _col_chunks(ins["L1_ffn_quantum_weights_real"], 32),
        "wic": _col_chunks(ins["L1_ffn_quantum_weights_imag"], 32),
        "b2row": _row(ins["L1_ffn_b2"]),
    }


def make_l1_ffn_program():
    nc = bass.Bass("TRN2", target_bir_lowering=False)
    al = Alloc(nc)
    rec = Rec()
    x = nc.dram_tensor("x", [T, D], F32, kind="ExternalInput").ap()
    y = nc.dram_tensor("y", [T, D], F32, kind="ExternalOutput").ap()
    W1 = nc.dram_tensor("W1", [D, 4096], F32, kind="ExternalInput").ap()
    W2 = nc.dram_tensor("W2", [4096, D], F32, kind="ExternalInput").ap()
    b1c = nc.dram_tensor("b1c", [128, 32], F32, kind="ExternalInput").ap()
    thc = nc.dram_tensor("thc", [128, 32], F32, kind="ExternalInput").ap()
    wrc = nc.dram_tensor("wrc", [128, 32], F32, kind="ExternalInput").ap()
    wic = nc.dram_tensor("wic", [128, 32], F32, kind="ExternalInput").ap()
    b2row = nc.dram_tensor("b2row", [1, D], F32, kind="ExternalInput").ap()
    r_hbm = nc.dram_tensor("r_scratch", [32, 128, T], BF16).ap()
    build_l1_ffn(nc, al, rec, x, y, W1, W2, b1c, thc, wrc, wic, b2row, r_hbm)
    finish(nc, rec, al)
    return nc



def make_fused_program():
    nc = bass.Bass("TRN2", target_bir_lowering=False)
    al = Alloc(nc)
    rec = Rec()

    def di(name, shape, dt=F32):
        return nc.dram_tensor(name, shape, dt, kind="ExternalInput").ap()

    def dint(name, shape, dt=F32):
        return nc.dram_tensor(name, shape, dt).ap()

    x0 = di("x", [T, D])
    y = nc.dram_tensor("y", [T, D], F32, kind="ExternalOutput").ap()
    x1 = dint("x1", [T, D])
    s1 = dint("s1", [T, D])
    x2 = dint("x2", [T, D])
    x3 = dint("x3", [T, D])

    gbf = dint("gbf_hbm", [8, 1024, 2048], BF16)
    ubf = dint("ubf_hbm", [8, 1024, 2048], BF16)
    dbf = dint("dbf_hbm", [8, 2048, 1024], BF16)
    mk = al.mark()
    wc_emits = build_moe_wconv(nc, al, rec,
                               di("p3_gateT", [8, 1024, 2048]),
                               di("p3_upT", [8, 1024, 2048]),
                               di("p3_downT", [8, 2048, 1024]), gbf, ubf, dbf)
    wc_it = iter(wc_emits)

    def wc_filler(n):
        for _ in range(n):
            j = next(wc_it, None)
            if j is None:
                return
            j()

    build_l0_attn(nc, al, rec, x0, x1,
                  di("p1_wqk", [1024, 1024]), di("p1_wv", [1024, 512]),
                  di("p1_wfc", [512, 1024]), di("p1_bqk_r", [1, 1024]),
                  di("p1_bv_c", [128, 4]), di("p1_bfc_r", [1, 1024]),
                  di("p1_gam_r", [1, 1024]), di("p1_bet_r", [1, 1024]),
                  filler=wc_filler)
    for j in wc_it:
        j()
    rec.fence(tag="ph1_end")
    al.free_suffix(mk)

    mk = al.mark()
    xin_hbm = dint("xin_hbm", [T, D])
    k_hbm = dint("k_hbm", [NT, 8, 128, 128], BF16)
    q_hbm = dint("q_hbm", [NT, 8, 128, 128], BF16)
    v_hbm = dint("v_hbm", [NT, 128, 1040], BF16)
    build_l0_mla(nc, al, rec, x1, s1, xin_hbm, k_hbm, q_hbm, v_hbm,
                 di("p2_kvdT", [1024, 256]), di("p2_qdT", [1024, 256]),
                 di("p2_ropekT", [1024, 1024]), di("p2_kuT", [256, 1024]),
                 di("p2_qmixT", [256, 1024]), di("p2_vuT", [256, 1024]),
                 di("p2_owT", [64, 16384]), di("p2_ln1_r", [1, D]),
                 di("p2_cos_t", [128, T]), di("p2_sin_t", [128, T]),
                 di("p2_pmat", [128, 128]))
    rec.fence(tag="ph2_end")
    al.free_suffix(mk)

    mk = al.mark()
    alf_hbm = dint("alf_hbm", [8, 128, T], BF16)
    build_l0_moe(nc, al, rec, s1, x1, x2, alf_hbm, gbf, ubf, dbf,
                 di("p3_routerT", [1024, 7]),
                 di("p3_rbias_r", [1, 7]), di("p3_ln2_r", [1, D]))
    rec.fence(tag="ph3_end")
    al.free_suffix(mk)

    mk = al.mark()
    qf = dint("qf_hbm", [16, 128, T], BF16)
    kf = dint("kf_hbm", [16, 128, T], BF16)
    vt = dint("vt_hbm", [NT, 128, 2048], BF16)
    xcf = dint("xcf_hbm", [16, 128, T], BF16)
    zf = dint("zf_hbm", [16, 128, T], BF16)
    build_l1_attn(nc, al, rec, x2, x3, qf, kf, vt, xcf, zf,
                  di("p4_wupT", [1024, 4096]), di("p4_qdiag", [128, 8192]),
                  di("p4_kdiag", [128, 8192]), di("p4_vdiag", [128, 64]),
                  di("p4_convw_c", [128, 64]), di("p4_convb_c", [128, 16]),
                  di("p4_wig_h", [128, 384]), di("p4_big_c", [4, 2]),
                  di("p4_lnw_row", [1, 2048]), di("p4_skip_c", [128, 16]),
                  di("p4_wdnT", [2048, 1024]))
    rec.fence(tag="ph4_end")
    al.free_suffix(mk)

    mk = al.mark()
    r_hbm = dint("r_scratch", [32, 128, T], BF16)
    build_l1_ffn(nc, al, rec, x3, y,
                 di("p5_W1", [D, 4096]), di("p5_W2", [4096, D]),
                 di("p5_b1c", [128, 32]), di("p5_thc", [128, 32]),
                 di("p5_wrc", [128, 32]), di("p5_wic", [128, 32]),
                 di("p5_b2row", [1, D]), r_hbm)
    finish(nc, rec, al)
    return nc



_PROGS = {}


def _get_prog(name, maker):
    if name not in _PROGS:
        _PROGS[name] = maker()
    return _PROGS[name]


def kernel(**inputs):
    x0 = np.asarray(inputs["L0_attn_x"], np.float32)
    B = x0.shape[0]
    cores = list(range(B))

    w = {}
    for pfx, prep in (("p1_", prep_l0_attn), ("p2_", prep_l0_mla),
                      ("p3_", prep_l0_moe), ("p4_", prep_l1_attn),
                      ("p5_", prep_l1_ffn)):
        for k, v in prep(inputs).items():
            w[pfx + k] = v

    nc = _get_prog("fused", make_fused_program)
    in_maps = []
    for b in range(B):
        m = dict(w)
        m["x"] = np.ascontiguousarray(x0[b])
        in_maps.append(m)
    res = run_bass_kernel_spmd(nc, in_maps, core_ids=cores)
    return np.stack([np.asarray(res.results[b]["y"]) for b in range(B)])


# version 17
# speedup vs baseline: 1.0718x; 1.0085x over previous
import math
import numpy as np
import concourse.bass as bass
import concourse.mybir as mybir
from concourse.bass_utils import run_bass_kernel_spmd

F32 = mybir.dt.float32
BF16 = mybir.dt.bfloat16
AF = mybir.ActivationFunctionType
ALU = mybir.AluOpType
AX = mybir.AxisListType

T = 4096
D = 1024
NT = T // 128
NP = T // 512

ENGINES = ("sync", "tensor", "vector", "scalar", "gpsimd")


NDMA_SEMS = 8


class Op:
    __slots__ = ("eng", "fn", "deps", "is_dma", "ordinal", "waits", "tag",
                 "sem_key", "sem_val")

    def __init__(self, eng, fn, deps, is_dma, tag):
        self.eng = eng
        self.fn = fn
        self.deps = deps
        self.is_dma = is_dma
        self.ordinal = None
        self.waits = []
        self.tag = tag
        self.sem_key = None
        self.sem_val = None


class Rec:

    def __init__(self):
        self.streams = {e: [] for e in ENGINES}
        self.bufs = {}

    def add(self, eng, fn, deps=(), is_dma=False, tag="", r=(), w=()):
        deps = [d for d in deps if d is not None]
        for b in r:
            ent = self.bufs.get(b)
            if ent is not None and ent[0] is not None:
                deps.append(ent[0])
        for b in w:
            ent = self.bufs.get(b)
            if ent is not None:
                if ent[0] is not None:
                    deps.append(ent[0])
                deps.extend(ent[1].values())
                deps.extend(ent[2])
        if eng == "tensor" and not is_dma:
            deps = [d for d in deps
                    if not (d.eng == "tensor" and not d.is_dma)]
        seen = set()
        uniq = []
        for d in deps:
            if id(d) not in seen:
                seen.add(id(d))
                uniq.append(d)
        op = Op(eng, fn, uniq, is_dma, tag)
        self.streams[eng].append(op)
        for b in r:
            ent = self.bufs.setdefault(b, [None, {}, []])
            if is_dma:
                ent[2].append(op)
            else:
                ent[1][eng] = op
        for b in w:
            self.bufs[b] = [op, {}, []]
        return op

    def dma(self, eng, out, in_, deps=(), tag="", r=(), w=()):
        return self.add(
            eng, lambda e, o=out, i=in_: e.dma_start(out=o, in_=i),
            deps, is_dma=True, tag=tag, r=r, w=w)

    def fence(self, tag="fence"):
        lasts = []
        for e in ENGINES:
            if self.streams[e]:
                lasts.append(self.streams[e][-1])
            per_slot = {}
            di = 0
            for op in self.streams[e]:
                if op.is_dma:
                    per_slot[di % NDMA_SEMS] = op
                    di += 1
            lasts.extend(per_slot.values())
        for e in ENGINES:
            self.add(e, lambda en: en.nop(), deps=list(lasts), tag=tag)

    def emit(self, nc, sems):
        pos = {}
        for e in ENGINES:
            for i, op in enumerate(self.streams[e]):
                pos[id(op)] = i
        slot_prev = {}
        for e in ENGINES:
            dma_idx = 0
            dcount = [0] * NDMA_SEMS
            for op in self.streams[e]:
                if not op.is_dma:
                    continue
                j = dma_idx % NDMA_SEMS
                dma_idx += 1
                dcount[j] += 16
                op.ordinal = -1
                op.sem_key = (e, "d", j)
                op.sem_val = dcount[j]
                key = (e, j)
                if key in slot_prev:
                    op.deps = op.deps + [slot_prev[key]]
                slot_prev[key] = op
        for e in ENGINES:
            for op in self.streams[e]:
                for d in op.deps:
                    if d.ordinal is None:
                        d.ordinal = -1
        for e in ENGINES:
            cc = 0
            for op in self.streams[e]:
                if op.ordinal == -1 and not op.is_dma:
                    cc += 1
                    op.sem_key = (e, "c")
                    op.sem_val = cc
        for e in ENGINES:
            water = {}
            for op in self.streams[e]:
                for d in op.deps:
                    if d.sem_key is None:
                        continue
                    if water.get(d.sem_key, 0) < d.sem_val:
                        water[d.sem_key] = d.sem_val
                        op.waits.append((d.sem_key, d.sem_val))
        with nc.Block() as blk:
            for e in ENGINES:
                stream = self.streams[e]
                if not stream:
                    continue
                dec = getattr(blk, e)

                def make(stream=stream):
                    def _(eng):
                        for op in stream:
                            for (key, val) in op.waits:
                                eng.wait_ge(sems[key], val)
                            instr = op.fn(eng)
                            if op.sem_key is not None:
                                instr.then_inc(sems[op.sem_key],
                                               16 if op.is_dma else 1)
                    return _

                dec(make())


class Alloc:

    def __init__(self, nc):
        self.nc = nc
        self.ctx = []

    def sb(self, name, shape, dt):
        cm = self.nc.sbuf_tensor(name, shape, dt)
        h = cm.__enter__()
        self.ctx.append(cm)
        self.named = getattr(self, "named", {})
        self.named[name] = cm
        return h

    def mark(self):
        return len(self.ctx)

    def free_suffix(self, mark):
        for cm in reversed(self.ctx[mark:]):
            cm.__exit__(None, None, None)
        dead = set(id(cm) for cm in self.ctx[mark:])
        del self.ctx[mark:]
        self.named = {k: v for k, v in getattr(self, "named", {}).items()
                      if id(v) not in dead}

    def free(self, names):
        cms = [self.named.pop(n) for n in names]
        cms.sort(key=lambda cm: self.ctx.index(cm), reverse=True)
        for cm in cms:
            self.ctx.remove(cm)
            cm.__exit__(None, None, None)

    def ps(self, name, shape, dt=F32):
        cm = self.nc.psum_tensor(name, shape, dt)
        h = cm.__enter__()
        self.ctx.append(cm)
        return h

    def sem(self, name):
        cm = self.nc.semaphore(name)
        h = cm.__enter__()
        self.ctx.append(cm)
        return h

    def close(self):
        for cm in reversed(self.ctx):
            cm.__exit__(None, None, None)
        self.ctx = []


def finish(nc, rec, al):
    sems = {}
    for e in ENGINES:
        sems[(e, "c")] = al.sem(f"s_{e}_c")
        for j in range(NDMA_SEMS):
            sems[(e, "d", j)] = al.sem(f"s_{e}_d{j}")
    tail_deps = []
    for e in ENGINES:
        if rec.streams[e]:
            tail_deps.append(rec.streams[e][-1])
        last_per_slot = {}
        di = 0
        for op in rec.streams[e]:
            if op.is_dma:
                last_per_slot[di % NDMA_SEMS] = op
                di += 1
        tail_deps.extend(last_per_slot.values())
    rec.add("sync", lambda e: e.nop(), deps=tail_deps, tag="final_wait")
    rec.emit(nc, sems)



def _c(a):
    return np.ascontiguousarray(np.asarray(a), dtype=np.float32)


def _col_chunks(v, nchunk):
    return _c(np.asarray(v).reshape(nchunk, 128).T)


def _row(v):
    return _c(np.asarray(v).reshape(1, -1))



class ConstPool:

    def __init__(self, rec, tile):
        self.rec = rec
        self.tile = tile
        self.vals = {}
        self.n = 0
        self.op = None

    def get(self, v):
        v = float(v)
        if v not in self.vals:
            assert self.n < self.tile.shape[1]
            j = self.n
            self.n += 1
            self.vals[v] = j
            o = self.rec.add("gpsimd", lambda e, j=j, v=v: e.memset(
                self.tile[:, j:j + 1], v), deps=[self.op], tag="const")
            self.op = o
        return self.tile[:, self.vals[v]:self.vals[v] + 1]


def make_identity_ops(rec, ident):
    sq = ident.shape[1]
    o1 = rec.add("gpsimd", lambda e: e.memset(ident[:], 0.0), tag="id0")
    return rec.add("gpsimd", lambda e: e.affine_select(
        out=ident[:], in_=ident[:], compare_op=ALU.not_equal, fill=1.0,
        base=0, pattern=[[-1, sq]], channel_multiplier=1), deps=[o1], tag="id1")


class LNHelper:

    def __init__(self, rec, scratch_sq, stats, name="ln", sq_tok=None):
        self.rec = rec
        self.sq = scratch_sq
        self.st = stats
        self.nm = name
        self.sq_tok = sq_tok or (name + "sq")

    def emit(self, x_ap, x_tok, xln_out, xln_tok, eps_ap, deps=(), const_dep=None):
        rec, sq, st = self.rec, self.sq, self.st
        nm = self.nm
        w = x_ap.shape[1]
        d = float(w)
        o_sum = rec.add("vector", lambda e: e.reduce_sum(st[:, 0:1], x_ap, axis=AX.X),
                        deps=deps, tag="ln_sum", r=[x_tok], w=[nm + "a"])
        o_sq = rec.add("scalar", lambda e: e.activation(
            sq[:, 0:w], x_ap, AF.Square, accum_out=st[:, 1:2]), deps=deps,
            tag="ln_sq", r=[x_tok], w=[nm + "b", self.sq_tok])
        rec.add("vector", lambda e: e.tensor_scalar(
            st[:, 2:3], st[:, 0:1], -1.0 / d, None, ALU.mult),
            tag="ln_nm", r=[nm + "a"], w=[nm + "c"])
        rec.add("vector", lambda e: e.tensor_scalar(
            st[:, 3:4], st[:, 1:2], 1.0 / d, None, ALU.mult),
            tag="ln_ms", r=[nm + "b"], w=[nm + "d"])
        rec.add("vector", lambda e: e.tensor_tensor(
            st[:, 4:5], st[:, 2:3], st[:, 2:3], ALU.mult),
            tag="ln_m2", r=[nm + "c"], w=[nm + "e"])
        rec.add("vector", lambda e: e.tensor_tensor(
            st[:, 5:6], st[:, 3:4], st[:, 4:5], ALU.subtract),
            tag="ln_var", r=[nm + "d", nm + "e"], w=[nm + "f"])
        rec.add("scalar", lambda e: e.activation(
            st[:, 6:7], st[:, 5:6], AF.Ln, bias=eps_ap), deps=[const_dep],
            tag="ln_ln", r=[nm + "f"], w=[nm + "g"])
        rec.add("scalar", lambda e: e.activation(
            st[:, 7:8], st[:, 6:7], AF.Exp, scale=-0.5),
            tag="ln_rstd", r=[nm + "g"], w=[nm + "h"])
        o_out = rec.add("vector", lambda e: e.tensor_scalar(
            xln_out, x_ap, st[:, 2:3], st[:, 7:8], ALU.add, ALU.mult),
            tag="ln_out", r=[x_tok, nm + "c", nm + "h"], w=[xln_tok])
        return o_out


def transpose_to_fm(rec, ident, id_op, xln16, xln_tok, p_tr, ptr_tok,
                    xf, xf_tok, tl, nchunk=8, col_stride=512):
    o_trs = []
    for c in range(nchunk):
        o_t = rec.add("tensor", lambda e, c=c: e.transpose(
            p_tr[:, 128 * c:128 * (c + 1)], xln16[:, 128 * c:128 * (c + 1)],
            ident[:]), deps=[id_op], tag=f"tr{c}",
            r=[xln_tok], w=[ptr_tok])
        o_trs.append(o_t)
    evs = []
    for c in range(nchunk):
        o_ev = rec.add("vector", lambda e, c=c, tl=tl: e.tensor_copy(
            xf[:, col_stride * c + 128 * tl: col_stride * c + 128 * (tl + 1)],
            p_tr[:, 128 * c:128 * (c + 1)]),
            tag=f"trev{c}", r=[ptr_tok], w=[xf_tok])
        evs.append(o_ev)
    return evs



def build_l1_ffn(nc, al, rec, x, y, W1, W2, b1c, thc, wrc, wic, b2row, r_hbm,
                 pfx="f4_"):

    S, P = al.sb, al.ps

    ident = S(pfx + "ident", [128, 128], BF16)
    ones1 = S(pfx + "ones1", [1, 128], F32)
    w_s = S(pfx + "w_s", [128, 32768], BF16)
    wstage = [S(pfx + f"wstage{i}", [128, 4096], F32) for i in range(2)]
    b1_s = S(pfx + "b1", [128, 32], F32)
    th_s = S(pfx + "th", [128, 32], F32)
    wr_s = S(pfx + "wr", [128, 32], F32)
    wi_s = S(pfx + "wi", [128, 32], F32)
    vec_s = S(pfx + "vec", [128, 32 * 8], F32)
    rr_int = S(pfx + "rrint", [128, 32], mybir.dt.int32)
    b2r_s = S(pfx + "b2r", [1, D], F32)
    b2bc = S(pfx + "b2bc", [128, D], F32)
    xt = [S(pfx + f"xt{i}", [128, D], F32) for i in range(2)]
    sq_scr = S(pfx + "sq", [128, D], F32)
    stats = S(pfx + "stats", [128, 8], F32)
    xln16 = S(pfx + "xln16", [128, D], BF16)
    xf = S(pfx + "xf", [128, T], BF16)
    c1 = S(pfx + "c1", [128, 512], F32)
    s1 = S(pfx + "s1", [128, 512], F32)
    hid = S(pfx + "hid", [128, 512], F32)
    t12 = S(pfx + "t12", [128, 512], F32)
    r16 = [S(pfx + f"r16{i}", [128, 512], BF16) for i in range(2)]
    rbuf = S(pfx + "rbuf", [128, 32 * 512], BF16)
    outt = [S(pfx + f"outt{i}", [128, D], F32) for i in range(2)]
    cpool = S(pfx + "cpool", [128, 8], F32)

    p_tr = P(pfx + "ptr", [128, 1024], BF16)
    p_h = [P(pfx + f"ph{i}", [128, 512]) for i in range(2)]
    p_o = [P(pfx + f"po{i}", [128, 512]) for i in range(2)]
    p_bc = P(pfx + "pbc", [128, 512])

    ln = LNHelper(rec, sq_scr, stats, name=pfx + "ln")
    cp = ConstPool(rec, cpool)
    eps5 = cp.get(1e-5)
    cp_op = cp.op

    o_id = make_identity_ops(rec, ident)
    rec.add("gpsimd", lambda e: e.memset(ones1[:], 1.0), tag="ones", w=["ones1"])
    rec.dma("sync", b1_s[:], b1c[:], w=["b1"])
    rec.dma("sync", th_s[:], thc[:], w=["th"])
    rec.dma("sync", wr_s[:], wrc[:], w=["wr"])
    rec.dma("sync", wi_s[:], wic[:], w=["wi"])
    rec.dma("sync", b2r_s[:], b2row[:], w=["b2r"])

    for j in range(2):
        rec.add("tensor", lambda e, j=j: e.matmul(
            p_bc[:], ones1[0:1, :], b2r_s[0:1, 512 * j:512 * (j + 1)],
            start=True, stop=True), tag="b2bc",
            r=["ones1", "b2r"], w=["pbc"])
        rec.add("scalar", lambda e, j=j: e.copy(
            b2bc[:, 512 * j:512 * (j + 1)], p_bc[:]), tag="b2cp",
            r=["pbc"], w=["b2bc"])

    V = lambda i: vec_s[:, 32 * i:32 * (i + 1)]
    TWO_PI = 2.0 * math.pi

    def sin_reduced(dst_i, src_ap, src_tok, add_const, tmp_i=7):
        t_a = V(tmp_i)
        rec.add("vector", lambda e: e.tensor_scalar(
            t_a, src_ap, float(add_const), 1.0 / TWO_PI, ALU.add, ALU.mult),
            tag="rr_scale", r=[src_tok], w=["vec"])
        rec.add("vector", lambda e: e.tensor_copy(rr_int[:], t_a),
                tag="rr_toint", r=["vec"], w=["rrint"])
        rec.add("vector", lambda e: e.tensor_copy(V(dst_i), rr_int[:]),
                tag="rr_tofl", r=["rrint"], w=["vec"])
        rec.add("vector", lambda e: e.tensor_scalar(
            t_a, src_ap, float(add_const), None, ALU.add),
            tag="rr_add", r=[src_tok], w=["vec"])
        rec.add("vector", lambda e: e.scalar_tensor_tensor(
            t_a, V(dst_i), -TWO_PI, t_a, ALU.mult, ALU.add),
            tag="rr_fold", r=["vec"], w=["vec"])
        rec.add("vector", lambda e: e.tensor_scalar(
            V(dst_i), t_a, float(math.pi), None, ALU.is_gt),
            tag="rr_gt", r=["vec"], w=["vec"])
        rec.add("vector", lambda e: e.scalar_tensor_tensor(
            t_a, V(dst_i), -TWO_PI, t_a, ALU.mult, ALU.add),
            tag="rr_f2", r=["vec"], w=["vec"])
        rec.add("vector", lambda e: e.tensor_scalar(
            V(dst_i), t_a, float(-math.pi), None, ALU.is_lt),
            tag="rr_lt", r=["vec"], w=["vec"])
        rec.add("vector", lambda e: e.scalar_tensor_tensor(
            t_a, V(dst_i), TWO_PI, t_a, ALU.mult, ALU.add),
            tag="rr_f3", r=["vec"], w=["vec"])
        return rec.add("scalar", lambda e: e.activation(V(dst_i), t_a, AF.Sin),
                       tag="rr_sin", r=["vec"], w=["vec"])

    sin_reduced(0, th_s[:], "th", 0.0)
    sin_reduced(1, th_s[:], "th", math.pi / 2.0)
    rec.add("vector", lambda e: e.tensor_tensor(V(2), V(1), wr_s[:], ALU.mult),
            tag="qp1", r=["vec", "wr"], w=["vec"])
    rec.add("vector", lambda e: e.tensor_tensor(V(7), V(0), wi_s[:], ALU.mult),
            tag="qp2", r=["vec", "wi"], w=["vec"])
    rec.add("vector", lambda e: e.tensor_tensor(V(2), V(2), V(7), ALU.add),
            tag="qp", r=["vec"], w=["vec"])
    rec.add("vector", lambda e: e.tensor_tensor(V(2), V(2), wi_s[:], ALU.mult),
            tag="A1", r=["vec", "wi"], w=["vec"])
    rec.add("vector", lambda e: e.tensor_scalar(V(2), V(2), 0.1, None, ALU.mult),
            tag="A", r=["vec"], w=["vec"])
    rec.add("vector", lambda e: e.tensor_tensor(V(3), V(2), V(0), ALU.mult),
            tag="Sp", r=["vec"], w=["vec"])
    rec.add("vector", lambda e: e.tensor_tensor(V(4), V(2), V(1), ALU.mult),
            tag="Cp", r=["vec"], w=["vec"])
    rec.add("vector", lambda e: e.tensor_scalar(
        V(5), b1_s[:], 0.1, math.pi / 2.0, ALU.mult, ALU.add),
        tag="b5", r=["b1"], w=["vec"])
    o_vec_done = rec.add("vector", lambda e: e.tensor_scalar(
        V(6), b1_s[:], 0.1, None, ALU.mult), tag="b6", r=["b1"], w=["vec"])

    for k in range(8):
        stg = wstage[k % 2]
        rec.dma("sync", stg[:], W1[128 * k:128 * (k + 1), :],
                tag=f"w1ld{k}", w=[f"wstage{k % 2}"])
        rec.add("vector", lambda e, k=k, stg=stg: e.tensor_copy(
            w_s[:, 4096 * k:4096 * (k + 1)], stg[:]),
            tag=f"w1cast{k}", r=[f"wstage{k % 2}"], w=["w_s"])

    for p in range(NP):
        for tl in range(4):
            ti = 4 * p + tl
            q = ti % 2
            rec.dma("sync", xt[q][:], x[128 * ti:128 * (ti + 1), :],
                    tag=f"x{ti}", w=[f"xt{q}"])
            ln.emit(xt[q][:], f"xt{q}", xln16[:], "xln16", eps5,
                    const_dep=cp_op)
            transpose_to_fm(rec, ident, o_id, xln16, "xln16", p_tr, "ptr",
                            xf, "xf", tl)
        for m in range(32):
            bank = p_h[m % 2]
            bt = f"ph{m % 2}"
            for k in range(8):
                rec.add("tensor", lambda e, k=k, m=m, bank=bank: e.matmul(
                    bank[:], w_s[:, 4096 * k + 128 * m:4096 * k + 128 * (m + 1)],
                    xf[:, 512 * k:512 * (k + 1)],
                    start=(k == 0), stop=(k == 7)),
                    tag=f"hmm{p}_{m}", r=["w_s", "xf"], w=[bt])
            rec.add("scalar", lambda e, m=m, bank=bank: e.activation(
                c1[:], bank[:], AF.Sin, bias=V(5)[:, m:m + 1], scale=0.1),
                tag="qc1", r=[bt, "vec"], w=["c1"])
            rec.add("scalar", lambda e, m=m, bank=bank: e.activation(
                s1[:], bank[:], AF.Sin, bias=V(6)[:, m:m + 1], scale=0.1),
                tag="qs1", r=[bt, "vec"], w=["s1"])
            rec.add("scalar", lambda e, m=m, bank=bank: e.add(
                hid[:], bank[:], b1_s[:, m:m + 1]),
                tag="qh", r=[bt, "b1"], w=["hid"])
            rec.add("vector", lambda e, m=m: e.scalar_tensor_tensor(
                t12[:], c1[:], V(3)[:, m:m + 1], hid[:], ALU.mult, ALU.add),
                tag="qt", r=["c1", "hid", "vec"], w=["t12"])
            rec.add("vector", lambda e, m=m: e.scalar_tensor_tensor(
                c1[:], s1[:], V(4)[:, m:m + 1], t12[:], ALU.mult, ALU.add),
                tag="qq", r=["s1", "t12", "vec", "c1"], w=["c1"])
            rb = r16[m % 2]
            rec.add("scalar", lambda e, rb=rb: e.activation(
                rb[:], c1[:], AF.Relu), tag="qr",
                r=["c1"], w=[f"r16{m % 2}"])
            rec.dma("gpsimd", r_hbm[m, :, 512 * p:512 * (p + 1)], rb[:],
                    tag=f"rs{p}_{m}", r=[f"r16{m % 2}"], w=[f"rhbm{p}_{m}"])

    for m in range(32):
        stg = wstage[m % 2][:, 0:1024]
        rec.dma("sync", stg, W2[128 * m:128 * (m + 1), :],
                tag=f"w2ld{m}", w=[f"wstage{m % 2}"])
        rec.add("vector", lambda e, m=m, stg=stg: e.tensor_copy(
            w_s[:, 1024 * m:1024 * (m + 1)], stg),
            tag=f"w2cast{m}", r=[f"wstage{m % 2}"], w=["w_s"])

    y_done = []
    for p in range(NP):
        for m in range(32):
            rec.dma("sync", rbuf[:, 512 * m:512 * (m + 1)],
                    r_hbm[m, :, 512 * p:512 * (p + 1)],
                    tag=f"rl{p}_{m}", r=[f"rhbm{p}_{m}"], w=[f"rbuf{m}"])
        for tl in range(4):
            ti = 4 * p + tl
            q = tl % 2
            obuf = outt[q]
            rec.dma("scalar", xt[q][:], x[128 * ti:128 * (ti + 1), :],
                    tag=f"xb{ti}", w=[f"xt{q}"])
            for nh in range(2):
                bank = p_o[nh]
                bt = f"po{nh}"
                for m in range(32):
                    rec.add("tensor", lambda e, m=m, tl=tl, nh=nh, bank=bank:
                            e.matmul(
                        bank[:],
                        rbuf[:, 512 * m + 128 * tl:512 * m + 128 * (tl + 1)],
                        w_s[:, 1024 * m + 512 * nh:1024 * m + 512 * (nh + 1)],
                        start=(m == 0), stop=(m == 31)),
                        tag=f"omm{ti}_{nh}", r=[f"rbuf{m}", "w_s"], w=[bt])
                rec.add("vector", lambda e, nh=nh, obuf=obuf, bank=bank:
                        e.tensor_tensor(
                    obuf[:, 512 * nh:512 * (nh + 1)], bank[:],
                    b2bc[:, 512 * nh:512 * (nh + 1)], ALU.add),
                    tag="oadd1", r=[bt, "b2bc"], w=[f"outt{q}"])
                rec.add("vector", lambda e, nh=nh, obuf=obuf, q=q:
                        e.tensor_tensor(
                    obuf[:, 512 * nh:512 * (nh + 1)],
                    obuf[:, 512 * nh:512 * (nh + 1)],
                    xt[q][:, 512 * nh:512 * (nh + 1)], ALU.add),
                    tag="oadd2", r=[f"outt{q}", f"xt{q}"], w=[f"outt{q}"])
            d_y = rec.dma("sync", y[128 * ti:128 * (ti + 1), :], obuf[:],
                          tag=f"y{ti}", r=[f"outt{q}"])
            y_done.append(d_y)
    return y_done



def build_l0_attn(nc, al, rec, x, y, wqk, wv, wfc, bqk_r, bv_c, bfc_r,
                  gam_r, bet_r, pfx="f1_", filler=None):
    S, P = al.sb, al.ps

    ident = S(pfx + "ident", [128, 128], BF16)
    ones1 = S(pfx + "ones1", [1, 128], F32)
    wqk_s = S(pfx + "wqk", [128, 8192], BF16)
    wv_s = S(pfx + "wv", [128, 4096], BF16)
    wfc_s = S(pfx + "wfc", [128, 4096], BF16)
    wstage = [S(pfx + f"wstage{i}", [128, 1024], F32) for i in range(2)]
    brow = S(pfx + "brow", [1, 4096], F32)
    bbc = S(pfx + "bbc", [128, 4096], F32)
    bv_s = S(pfx + "bv", [128, 4], F32)
    xt = [S(pfx + f"xt{i}", [128, D], F32) for i in range(2)]
    sq_scr = S(pfx + "sq", [128, D], F32)
    stats = S(pfx + "stats", [128, 8], F32)
    xln16 = [S(pfx + f"xln16{i}", [128, D], BF16) for i in range(2)]
    xf = [S(pfx + f"xf{i}", [128, D], BF16) for i in range(2)]
    qk16 = S(pfx + "qk16", [128, 1024], BF16)
    vfm = [S(pfx + f"vfm{c}", [128, T], BF16) for c in range(4)]
    mu_all = S(pfx + "mu", [128, 32], F32)
    rs_all = S(pfx + "rs", [128, 32], F32)
    sc32 = S(pfx + "sc32", [64, 512], F32)
    smax = S(pfx + "smax", [64, 8], F32)
    ssum = S(pfx + "ssum", [64, 8], F32)
    srec = S(pfx + "srec", [64, 8], F32)
    attn16a = S(pfx + "attn16a", [64, 512], BF16)
    attn16 = S(pfx + "attn16", [128, 256], BF16)
    o16 = S(pfx + "o16", [128, 2048], BF16)
    ev1 = S(pfx + "ev1", [128, D], F32)
    outt = [S(pfx + f"outt{i}", [128, D], F32) for i in range(2)]
    st2 = S(pfx + "st2", [128, 8], F32)
    cpool = S(pfx + "cpool", [128, 8], F32)

    p_tr = P(pfx + "ptr", [128, 1024], BF16)
    p_qk = P(pfx + "pqk", [128, 1024])
    p_sc = P(pfx + "psc", [64, 512])
    p_v = P(pfx + "pv", [128, 512])
    p_o = P(pfx + "po", [128, 512])
    p_fc = P(pfx + "pfc", [128, 1024])
    p_bc = p_fc

    ln = LNHelper(rec, sq_scr, stats, name=pfx + "ln")
    ln2 = LNHelper(rec, sq_scr, st2, name=pfx + "lm", sq_tok=pfx + "lnsq")
    cp = ConstPool(rec, cpool)
    eps5 = cp.get(1e-5)
    eps6 = cp.get(1e-6)
    cp_op = cp.op

    o_id = make_identity_ops(rec, ident)
    rec.add("gpsimd", lambda e: e.memset(ones1[:], 1.0), tag="ones", w=["ones1"])

    rec.dma("sync", brow[0:1, 0:1024], bqk_r[:], w=["brow"])
    rec.dma("sync", brow[0:1, 1024:2048], bfc_r[:], w=["brow"])
    rec.dma("sync", brow[0:1, 2048:3072], gam_r[:], w=["brow"])
    rec.dma("sync", brow[0:1, 3072:4096], bet_r[:], w=["brow"])
    rec.dma("sync", bv_s[:], bv_c[:], w=["bv"])
    for j in range(8):
        rec.add("tensor", lambda e, j=j: e.matmul(
            p_bc[:, 0:512], ones1[0:1, :], brow[0:1, 512 * j:512 * (j + 1)],
            start=True, stop=True), tag="bbc",
            r=["ones1", "brow"], w=["pfc"])
        rec.add("scalar", lambda e, j=j: e.copy(
            bbc[:, 512 * j:512 * (j + 1)], p_bc[:, 0:512]), tag="bcp",
            r=["pfc"], w=["bbc"])
    BQK = bbc[:, 0:1024]
    BFC = bbc[:, 1024:2048]
    GAM = bbc[:, 2048:3072]
    BET = bbc[:, 3072:4096]

    for k in range(8):
        rec.dma("sync", wstage[k % 2][:], wqk[128 * k:128 * (k + 1), :],
                tag=f"wqkld{k}", w=[f"wstage{k % 2}"])
        rec.add("vector", lambda e, k=k: e.tensor_copy(
            wqk_s[:, 1024 * k:1024 * (k + 1)], wstage[k % 2][:]),
            tag=f"wqkcast{k}", r=[f"wstage{k % 2}"], w=["wqk_s"])
    for k in range(8):
        rec.dma("sync", wstage[k % 2][:, 0:512], wv[128 * k:128 * (k + 1), :],
                tag=f"wvld{k}", w=[f"wstage{k % 2}"])
        rec.add("vector", lambda e, k=k: e.tensor_copy(
            wv_s[:, 512 * k:512 * (k + 1)], wstage[k % 2][:, 0:512]),
            tag=f"wvcast{k}", r=[f"wstage{k % 2}"], w=["wv_s"])
    for j in range(4):
        rec.dma("sync", wstage[j % 2][:], wfc[128 * j:128 * (j + 1), :],
                tag=f"wfcld{j}", w=[f"wstage{j % 2}"])
        rec.add("vector", lambda e, j=j: e.tensor_copy(
            wfc_s[:, 1024 * j:1024 * (j + 1)], wstage[j % 2][:]),
            tag=f"wfccast{j}", r=[f"wstage{j % 2}"], w=["wfc_s"])

    def l0a_front(ti):
        q = ti % 2
        rec.dma("sync", xt[q][:], x[128 * ti:128 * (ti + 1), :],
                tag=f"x{ti}", w=[f"xt{q}"])
        ln.emit(xt[q][:], f"xt{q}", xln16[q][:], f"xln16{q}", eps5,
                const_dep=cp_op)
        rec.add("vector", lambda e, ti=ti: e.tensor_copy(
            mu_all[:, ti:ti + 1], stats[:, 2:3]), tag="stash_mu",
            r=[pfx + "lnc"], w=["mu_all"])
        rec.add("vector", lambda e, ti=ti: e.tensor_copy(
            rs_all[:, ti:ti + 1], stats[:, 7:8]), tag="stash_rs",
            r=[pfx + "lnh"], w=["rs_all"])

    def l0a_mid(ti):
        q = ti % 2
        transpose_to_fm(rec, ident, o_id, xln16[q], f"xln16{q}", p_tr, "ptr",
                        xf[q], f"xf{q}", 0, col_stride=128)
        for n in range(2):
            for c in range(8):
                rec.add("tensor", lambda e, n=n, c=c, q=q: e.matmul(
                    p_qk[:, 512 * n:512 * (n + 1)],
                    xf[q][:, 128 * c:128 * (c + 1)],
                    wqk_s[:, 1024 * c + 512 * n:1024 * c + 512 * (n + 1)],
                    start=(c == 0), stop=(c == 7)),
                    tag=f"qk{ti}_{n}", r=[f"xf{q}", "wqk_s"], w=["pqk"])
        for n in range(2):
            rec.add("vector", lambda e, n=n: e.tensor_tensor(
                qk16[:, 512 * n:512 * (n + 1)], p_qk[:, 512 * n:512 * (n + 1)],
                BQK[:, 512 * n:512 * (n + 1)], ALU.add),
                tag="qkbias", r=["pqk", "bbc"], w=["qk16"])
        for h in range(8):
            rec.add("tensor", lambda e, h=h, ti=ti: e.matmul(
                p_sc[:, 64 * h:64 * (h + 1)],
                qk16[:, 64 * h:64 * (h + 1)],
                qk16[:, 512 + 64 * h:512 + 64 * (h + 1)],
                start=(ti == 0 and h == 0), stop=(ti == NT - 1 and h == 7)),
                tag=f"sc{ti}_{h}", r=["qk16"], w=["psc"])
        if ti + 1 < NT:
            l0a_front(ti + 1)
        for c in range(4):
            for k in range(8):
                rec.add("tensor", lambda e, c=c, k=k, q=q: e.matmul(
                    p_v[:, 128 * c:128 * (c + 1)],
                    wv_s[:, 512 * k + 128 * c:512 * k + 128 * (c + 1)],
                    xf[q][:, 128 * k:128 * (k + 1)],
                    start=(k == 0), stop=(k == 7)),
                    tag=f"v{ti}_{c}", r=[f"xf{q}", "wv_s"], w=["pv"])
        for c in range(4):
            rec.add("vector", lambda e, c=c, ti=ti: e.tensor_scalar(
                vfm[c][:, 128 * ti:128 * (ti + 1)], p_v[:, 128 * c:128 * (c + 1)],
                bv_s[:, c:c + 1], None, ALU.add),
                tag=f"vbias{ti}_{c}", r=["pv", "bv"], w=[f"vfm{c}"])
        if filler is not None:
            filler(5)

    l0a_front(0)
    for ti in range(NT):
        l0a_mid(ti)

    rec.add("vector", lambda e: e.tensor_copy(sc32[:], p_sc[:]),
            tag="scev", r=["psc"], w=["sc32"])
    for h in range(8):
        cs = slice(64 * h, 64 * (h + 1))
        rec.add("vector", lambda e, cs=cs, h=h: e.reduce_max(
            smax[:, h:h + 1], sc32[:, cs], axis=AX.X),
            tag="smx", r=["sc32"], w=["smax"])
        rec.add("vector", lambda e, h=h: e.tensor_scalar(
            smax[:, h:h + 1], smax[:, h:h + 1], -0.125, None, ALU.mult),
            tag="smxn", r=["smax"], w=["smax"])
        rec.add("scalar", lambda e, cs=cs, h=h: e.activation(
            sc32[:, cs], sc32[:, cs], AF.Exp, bias=smax[:, h:h + 1],
            scale=0.125, accum_out=ssum[:, h:h + 1]),
            tag="sexp", r=["sc32", "smax"], w=["sc32", "ssum"])
        rec.add("vector", lambda e, h=h: e.reciprocal(
            srec[:, h:h + 1], ssum[:, h:h + 1]),
            tag="srec", r=["ssum"], w=["srec"])
        rec.add("vector", lambda e, cs=cs, h=h: e.tensor_scalar(
            attn16a[:, cs], sc32[:, cs], srec[:, h:h + 1], None, ALU.mult),
            tag="snorm", r=["sc32", "srec"], w=["attn16a"])
    for c in range(4):
        rec.add("vector", lambda e, c=c: e.tensor_copy(
            attn16[0:64, 64 * c:64 * (c + 1)],
            attn16a[:, 128 * c:128 * c + 64]),
            tag="apack_e", r=["attn16a"], w=["attn16"])
        rec.dma("sync", attn16[64:128, 64 * c:64 * (c + 1)],
                attn16a[:, 128 * c + 64:128 * c + 128],
                tag="apack_o", r=["attn16a"], w=["attn16"])

    y_done = []
    for g in range(NP):
        for c in range(4):
            rec.add("tensor", lambda e, c=c, g=g: e.matmul(
                p_o[0:64, :], attn16[0:64, 64 * c:64 * (c + 1)],
                vfm[c][0:64, 512 * g:512 * (g + 1)],
                start=True, stop=True, tile_position=(0, 0)),
                tag=f"o{g}_{c}a", r=["attn16", f"vfm{c}"], w=["po"])
            rec.add("tensor", lambda e, c=c, g=g: e.matmul(
                p_o[64:128, :], attn16[64:128, 64 * c:64 * (c + 1)],
                vfm[c][64:128, 512 * g:512 * (g + 1)],
                start=True, stop=True, tile_position=(64, 64)),
                tag=f"o{g}_{c}b", r=["attn16", f"vfm{c}"], w=["po"])
            rec.add("vector", lambda e, c=c: e.tensor_copy(
                o16[:, 512 * c:512 * (c + 1)], p_o[:]),
                tag=f"oev{g}_{c}", r=["po"], w=["o16"])
        for tl in range(4):
            ti = 4 * g + tl
            q = tl % 2
            rec.dma("sync", xt[q][:], x[128 * ti:128 * (ti + 1), :],
                    tag=f"xb{ti}", w=[f"xt{q}"])
            for n in range(2):
                for c in range(4):
                    rec.add("tensor", lambda e, n=n, c=c, tl=tl: e.matmul(
                        p_fc[:, 512 * n:512 * (n + 1)],
                        o16[:, 512 * c + 128 * tl:512 * c + 128 * (tl + 1)],
                        wfc_s[:, 1024 * c + 512 * n:1024 * c + 512 * (n + 1)],
                        start=(c == 0), stop=(c == 3)),
                        tag=f"fc{ti}_{n}", r=["o16", "wfc_s"], w=["pfc"])
            rec.add("vector", lambda e: e.tensor_tensor(
                ev1[:], p_fc[:], BFC[:], ALU.add),
                tag="fcb", r=["pfc", "bbc"], w=["ev1"])
            rec.add("vector", lambda e, ti=ti, q=q: e.tensor_scalar(
                sq_scr[:], xt[q][:], mu_all[:, ti:ti + 1], rs_all[:, ti:ti + 1],
                ALU.add, ALU.mult),
                tag="xlnr", r=[f"xt{q}", "mu_all", "rs_all"], w=[pfx + "lnsq"])
            rec.add("vector", lambda e: e.tensor_tensor(
                ev1[:], ev1[:], sq_scr[:], ALU.add),
                tag="fcxln", r=["ev1", pfx + "lnsq"], w=["ev1"])
            ln2.emit(ev1[:], "ev1", outt[q][:], f"outt{q}", eps6, const_dep=cp_op)
            rec.add("vector", lambda e, q=q: e.tensor_tensor(
                outt[q][:], outt[q][:], GAM[:], ALU.mult),
                tag="gmul", r=[f"outt{q}", "bbc"], w=[f"outt{q}"])
            rec.add("vector", lambda e, q=q: e.tensor_tensor(
                outt[q][:], outt[q][:], BET[:], ALU.add),
                tag="badd", r=[f"outt{q}", "bbc"], w=[f"outt{q}"])
            rec.add("vector", lambda e, q=q: e.tensor_tensor(
                outt[q][:], outt[q][:], xt[q][:], ALU.add),
                tag="resid", r=[f"outt{q}", f"xt{q}"], w=[f"outt{q}"])
            d_y = rec.dma("sync", y[128 * ti:128 * (ti + 1), :], outt[q][:],
                          tag=f"y{ti}", r=[f"outt{q}"])
            y_done.append(d_y)
            if filler is not None:
                filler(3)
    return y_done



NH0 = 16
HD0 = 64


def build_l0_mla(nc, al, rec, x, s1_out, xin_hbm, k_hbm, q_hbm, v_hbm,
                 kvdT, qdT, ropekT, kuT, qmixT, vuT, owT, ln1_r,
                 cos_t, sin_t, pmat, pfx="f2_"):
    S, P = al.sb, al.ps

    ident = S(pfx + "ident", [128, 128], BF16)
    ones1 = S(pfx + "ones1", [1, 128], F32)
    ident32 = S(pfx + "ident32", [128, 128], F32)
    kvq_s = S(pfx + "kvq", [128, 4096], BF16)
    ropek_s = S(pfx + "ropek", [128, 8192], BF16)
    kuT_s = S(pfx + "kuT", [128, 2048], BF16)
    qmix_s = S(pfx + "qmix", [128, 2048], BF16)
    vuT_s = S(pfx + "vuT", [128, 2048], BF16)
    owT_s = S(pfx + "owT", [64, 16384], BF16)
    p16 = S(pfx + "p16", [128, 128], BF16)
    cos_s = S(pfx + "cos", [128, T], F32)
    sin_s = S(pfx + "sin", [128, T], F32)
    sel_s = S(pfx + "sel", [128, 2], F32)
    wstage = [S(pfx + f"wstage{i}", [128, 2048], F32) for i in range(2)]
    ln1bc = S(pfx + "ln1bc", [128, D], F32)
    lrow = S(pfx + "lrow", [1, D], F32)
    xt = [S(pfx + f"xt{i}", [128, D], F32) for i in range(2)]
    sq_scr = S(pfx + "sq", [128, D], F32)
    stats = S(pfx + "stats", [128, 8], F32)
    st2 = S(pfx + "st2", [128, 8], F32)
    xin_t = S(pfx + "xin", [128, D], F32)
    ml16 = S(pfx + "ml16", [128, D], BF16)
    mlf = [S(pfx + f"mlf{i}", [128, D], BF16) for i in range(2)]
    lat16 = S(pfx + "lat16", [128, 512], BF16)
    kx16 = [S(pfx + f"kx16{i}", [128, 128], BF16) for i in range(2)]
    t1f = [S(pfx + f"t1f{i}", [128, 128], F32) for i in range(2)]
    t2f = [S(pfx + f"t2f{i}", [128, 128], F32) for i in range(2)]
    eff16 = [S(pfx + f"eff16{i}", [128, 128], BF16) for i in range(2)]
    q2f = S(pfx + "q2f", [128, 128], F32)
    nrm2 = S(pfx + "nrm2", [128, 2], F32)
    qmax = S(pfx + "qmax", [128, 16], F32)
    kmax = S(pfx + "kmax", [128, 16], F32)
    vst = S(pfx + "vst", [128, 1040], BF16)
    mrow = S(pfx + "mrow", [1, 32], F32)
    mh = S(pfx + "mh", [128, 16], F32)
    bias_t = S(pfx + "bias", [128, 16], F32)
    qbuf = S(pfx + "qbuf", [128, 4096], BF16)
    kbuf = [S(pfx + f"kbuf{i}", [128, 128], BF16) for i in range(4)]
    vbuf = [S(pfx + f"vbuf{i}", [128, 130], BF16) for i in range(4)]
    pa16 = [S(pfx + f"pa16{i}", [128, 512], BF16) for i in range(2)]
    pb16 = [S(pfx + f"pb16{i}", [128, 512], BF16) for i in range(2)]
    mask01 = S(pfx + "mask01", [128, 128], BF16)
    rs_t = S(pfx + "rs_t", [128, 1024], F32)
    bc_s = S(pfx + "bc_s", [64, 1024], F32)
    ones64 = S(pfx + "ones64", [128, 64], F32)
    h_s = S(pfx + "h_s", [64, 8192], BF16)
    s1t = [S(pfx + f"s1t{i}", [128, D], F32) for i in range(2)]
    cpool = S(pfx + "cpool", [128, 8], F32)

    p_tr = P(pfx + "ptr", [128, 1024], BF16)
    b0 = P(pfx + "b0", [128, 512])
    b1 = P(pfx + "b1", [128, 512])
    b2 = P(pfx + "b2", [128, 512])
    b3 = P(pfx + "b3", [128, 512])
    b4 = P(pfx + "b4", [128, 512])
    p_v2 = P(pfx + "pv2", [128, 1024])
    p_lat = b0
    p_k = b1
    p_px = b2
    p_n = b3
    p_sA = b0
    p_sB = b1
    p_oA = b2
    p_oB = b3
    p_bc2 = b4

    ln = LNHelper(rec, sq_scr, stats, name=pfx + "ln")
    lnb = LNHelper(rec, sq_scr, st2, name=pfx + "lm", sq_tok=pfx + "lnsq")
    cp = ConstPool(rec, cpool)
    eps5 = cp.get(1e-5)
    cp_op = cp.op

    o_id = make_identity_ops(rec, ident)
    rec.add("gpsimd", lambda e: e.memset(ones1[:], 1.0), tag="ones", w=["ones1"])
    rec.add("gpsimd", lambda e: e.memset(ones64[:], 1.0), tag="ones64", w=["ones64"])
    o_i32a = rec.add("gpsimd", lambda e: e.memset(ident32[:], 0.0), tag="i32a",
                     w=["id32"])
    rec.add("gpsimd", lambda e: e.affine_select(
        out=ident32[:], in_=ident32[:], compare_op=ALU.not_equal, fill=1.0,
        base=0, pattern=[[-1, 128]], channel_multiplier=1), tag="i32b",
        r=["id32"], w=["id32"])
    rec.add("gpsimd", lambda e: e.memset(sel_s[:, 0:1], 1.0), tag="sel0", w=["sel"])
    rec.add("gpsimd", lambda e: e.memset(sel_s[:, 1:2], 0.0), tag="sel1", w=["sel"])
    rec.add("gpsimd", lambda e: e.memset(sel_s[64:128, 0:1], 0.0), tag="sel2", w=["sel"])
    rec.add("gpsimd", lambda e: e.memset(sel_s[64:128, 1:2], 1.0), tag="sel3", w=["sel"])
    rec.add("gpsimd", lambda e: e.memset(mask01[:], 1.0), tag="m01a", w=["m01"])
    rec.add("gpsimd", lambda e: e.affine_select(
        out=mask01[:], in_=mask01[:], compare_op=ALU.is_ge, fill=0.0,
        base=0, pattern=[[1, 128]], channel_multiplier=-1), tag="m01b",
        r=["m01"], w=["m01"])
    rec.add("gpsimd", lambda e: e.memset(qmax[:], 0.0), tag="qm0", w=["qmax"])
    rec.add("gpsimd", lambda e: e.memset(kmax[:], 0.0), tag="km0", w=["kmax"])

    rec.dma("sync", cos_s[:], cos_t[:], w=["cos"])
    rec.dma("sync", sin_s[:], sin_t[:], w=["sin"])
    rec.dma("sync", lrow[:], ln1_r[:], w=["lrow"])
    for j in range(2):
        rec.add("tensor", lambda e, j=j: e.matmul(
            p_v2[:, 0:512], ones1[0:1, :], lrow[0:1, 512 * j:512 * (j + 1)],
            start=True, stop=True), tag="lbc", r=["ones1", "lrow"], w=["pv2"])
        rec.add("scalar", lambda e, j=j: e.copy(
            ln1bc[:, 512 * j:512 * (j + 1)], p_v2[:, 0:512]), tag="lbcp",
            r=["pv2"], w=["ln1bc"])
    rec.dma("sync", wstage[0][:, 0:128], pmat[:], w=["wstage0"])
    rec.add("vector", lambda e: e.tensor_copy(p16[:], wstage[0][:, 0:128]),
            tag="p16", r=["wstage0"], w=["p16"])

    def loadw(dst, dst_tok, src, rows, cols, tag):
        nch = rows // 128
        for k in range(nch):
            stg = wstage[k % 2]
            rec.dma("sync", stg[:, 0:cols], src[128 * k:128 * (k + 1), :],
                    tag=f"{tag}ld{k}", w=[f"wstage{k % 2}"])
            rec.add("vector", lambda e, k=k, stg=stg: e.tensor_copy(
                dst[:, cols * k:cols * (k + 1)], stg[:, 0:cols]),
                tag=f"{tag}c{k}", r=[f"wstage{k % 2}"], w=[dst_tok])

    for k in range(8):
        stg = wstage[k % 2]
        rec.dma("sync", stg[:, 0:256], kvdT[128 * k:128 * (k + 1), :],
                tag=f"kvld{k}", w=[f"wstage{k % 2}"])
        rec.dma("sync", stg[:, 256:512], qdT[128 * k:128 * (k + 1), :],
                tag=f"qdld{k}", w=[f"wstage{k % 2}"])
        rec.add("vector", lambda e, k=k, stg=stg: e.tensor_copy(
            kvq_s[:, 512 * k:512 * (k + 1)], stg[:, 0:512]),
            tag=f"kvqc{k}", r=[f"wstage{k % 2}"], w=["kvq"])
    loadw(ropek_s, "ropek", ropekT, 1024, 1024, "rk")
    loadw(kuT_s, "kuT", kuT, 256, 1024, "ku")
    loadw(qmix_s, "qmix", qmixT, 256, 1024, "qm")
    loadw(vuT_s, "vuT", vuT, 256, 1024, "vu")
    for k in range(8):
        stg = wstage[k % 2]
        rec.dma("sync", stg[0:64, 0:2048], owT[0:64, 2048 * k:2048 * (k + 1)],
                tag=f"owld{k}", w=[f"wstage{k % 2}"])
        rec.add("vector", lambda e, k=k, stg=stg: e.tensor_copy(
            owT_s[:, 2048 * k:2048 * (k + 1)], stg[0:64, 0:2048]),
            tag=f"owc{k}", r=[f"wstage{k % 2}"], w=["owT"])

    def rope_chunk(src_psum, src_tok, j, ti, dst_hbm, q2acc):
        par = j % 2
        kx = kx16[par]
        t1 = t1f[par]
        t2 = t2f[par]
        ef = eff16[par]
        pxb, pxt = (p_px, "b2") if par == 0 else (p_n, "b3")
        rec.add("vector", lambda e, kx=kx: e.tensor_copy(
            kx[:], src_psum[:, 0:128]),
            tag="ropeev", r=[src_tok], w=[f"kx16{par}"])
        rec.add("tensor", lambda e, kx=kx, pxb=pxb: e.matmul(
            pxb[:, 0:128], p16[:], kx[:], start=True, stop=True),
            tag="ropepx", r=["p16", f"kx16{par}"], w=[pxt])
        rec.add("vector", lambda e, ti=ti, kx=kx, t1=t1: e.tensor_tensor(
            t1[:], kx[:], cos_s[:, 128 * ti:128 * (ti + 1)], ALU.mult),
            tag="ropet1", r=[f"kx16{par}", "cos"], w=[f"t1f{par}"])
        rec.add("vector", lambda e, ti=ti, t2=t2, pxb=pxb: e.tensor_tensor(
            t2[:], pxb[:, 0:128], sin_s[:, 128 * ti:128 * (ti + 1)], ALU.mult),
            tag="ropet2", r=[pxt, "sin"], w=[f"t2f{par}"])
        rec.add("vector", lambda e, t1=t1, t2=t2, ef=ef: e.tensor_tensor(
            ef[:], t1[:], t2[:], ALU.add),
            tag="ropeadd", r=[f"t1f{par}", f"t2f{par}"], w=[f"eff16{par}"])
        rec.dma("sync", dst_hbm, ef[:], tag="effsp", r=[f"eff16{par}"],
                w=[f"hbm_eff"])
        if q2acc is not None:
            rec.add("scalar", lambda e, ef=ef: e.activation(
                q2f[:], ef[:], AF.Square),
                tag="q2", r=[f"eff16{par}"], w=["q2f"])
            qb, qbt = (p_n, "b3") if par == 0 else (p_px, "b2")
            rec.add("tensor", lambda e, qb=qb: e.matmul(
                qb[:, 0:2], q2f[:], sel_s[:], start=True, stop=True),
                tag="q2mm", r=["q2f", "sel"], w=[qbt])
            rec.add("vector", lambda e, j=j, qb=qb: e.tensor_tensor(
                q2acc[0][:, 2 * j:2 * j + 2], q2acc[0][:, 2 * j:2 * j + 2],
                qb[:, 0:2], ALU.max), tag="q2max",
                r=[q2acc[1], qbt], w=[q2acc[1]])

    def mla_front_a(ti):
        q = ti % 2
        rec.dma("sync", xt[q][:], x[128 * ti:128 * (ti + 1), :],
                tag=f"x{ti}", w=[f"xt{q}"])
        ln.emit(xt[q][:], f"xt{q}", xin_t[:], "xin_t", eps5, const_dep=cp_op)
        rec.dma("gpsimd", xin_hbm[128 * ti:128 * (ti + 1), :], xin_t[:],
                tag=f"xinsp{ti}", r=["xin_t"], w=[f"hbm_xin{ti}"])

    def mla_front_b(ti):
        lnb.emit(xin_t[:], "xin_t", sq_scr[:, 0:D], pfx + "lnsq", eps5,
                 const_dep=cp_op)

    def mla_front_c(ti):
        rec.add("vector", lambda e: e.tensor_tensor(
            ml16[:], sq_scr[:, 0:D], ln1bc[:], ALU.mult),
            tag="ml16", r=[pfx + "lnsq", "ln1bc"], w=["ml16"])

    def mla_front_d(ti):
        q = ti % 2
        transpose_to_fm(rec, ident, o_id, ml16, "ml16", p_tr, "ptr",
                        mlf[q], f"mlf{q}", 0, col_stride=128)

    def mla_mid(ti):
        q = ti % 2
        nxt = ti + 1 if ti + 1 < NT else None
        for b in range(4):
            for k in range(8):
                rec.add("tensor", lambda e, b=b, k=k, q=q: e.matmul(
                    p_lat[:, 128 * b:128 * (b + 1)],
                    kvq_s[:, 512 * k + 128 * b:512 * k + 128 * (b + 1)],
                    mlf[q][:, 128 * k:128 * (k + 1)],
                    start=(k == 0), stop=(k == 7)),
                    tag=f"lat{ti}_{b}", r=["kvq", f"mlf{q}"], w=["b0"])
        rec.add("vector", lambda e: e.tensor_copy(lat16[:], p_lat[:]),
                tag="latev", r=["b0"], w=["lat16"])
        for j in range(8):
            ckb, ckt = (b1, "b1") if j % 2 == 0 else (b0, "b0")
            for k in range(8):
                rec.add("tensor", lambda e, j=j, k=k, ckb=ckb, q=q: e.matmul(
                    ckb[:, 0:128], ropek_s[:, 1024 * k + 128 * j:1024 * k + 128 * (j + 1)],
                    mlf[q][:, 128 * k:128 * (k + 1)],
                    start=(k == 0), stop=False),
                    tag=f"krope{ti}_{j}", r=["ropek", f"mlf{q}"], w=[ckt])
            for l in range(2):
                rec.add("tensor", lambda e, j=j, l=l, ckb=ckb: e.matmul(
                    ckb[:, 0:128], kuT_s[:, 1024 * l + 128 * j:1024 * l + 128 * (j + 1)],
                    lat16[:, 128 * l:128 * (l + 1)],
                    start=False, stop=(l == 1)),
                    tag=f"knr{ti}_{j}", r=["kuT", "lat16"], w=[ckt])
            rope_chunk(ckb, ckt, j, ti,
                       k_hbm[ti, j, :, :], (kmax, "kmax"))
            if nxt is not None:
                if j == 1:
                    mla_front_a(nxt)
                elif j == 3:
                    mla_front_b(nxt)
                elif j == 5:
                    mla_front_c(nxt)
                elif j == 7:
                    mla_front_d(nxt)
        for j in range(8):
            ckb, ckt = (b1, "b1") if j % 2 == 0 else (b0, "b0")
            for l in range(2):
                rec.add("tensor", lambda e, j=j, l=l, ckb=ckb: e.matmul(
                    ckb[:, 0:128], qmix_s[:, 1024 * l + 128 * j:1024 * l + 128 * (j + 1)],
                    lat16[:, 256 + 128 * l:256 + 128 * (l + 1)],
                    start=(l == 0), stop=(l == 1)),
                    tag=f"qmix{ti}_{j}", r=["qmix", "lat16"], w=[ckt])
            rope_chunk(ckb, ckt, j, ti,
                       q_hbm[ti, j, :, :], (qmax, "qmax"))
        for n in range(2):
            for l in range(2):
                rec.add("tensor", lambda e, n=n, l=l: e.matmul(
                    p_v2[:, 512 * n:512 * (n + 1)],
                    lat16[:, 128 * l:128 * (l + 1)],
                    vuT_s[:, 1024 * l + 512 * n:1024 * l + 512 * (n + 1)],
                    start=(l == 0), stop=(l == 1)),
                    tag=f"v{ti}_{n}", r=["lat16", "vuT"], w=["pv2"])
        rec.add("vector", lambda e: e.tensor_copy(
            vst[:].rearrange("p (h c) -> p h c", c=65)[:, :, 0:64],
            p_v2[:].rearrange("p (h c) -> p h c", c=64)),
            tag="vev", r=["pv2"], w=["vst"])
        rec.add("vector", lambda e: e.memset(
            vst[:].rearrange("p (h c) -> p h c", c=65)[:, :, 64:65], 1.0),
            tag="vones", r=[], w=["vst"])
        rec.dma("sync", v_hbm[ti, :, :], vst[:], tag=f"vsp{ti}",
                r=["vst"], w=[f"hbm_v{ti}"])

    mla_front_a(0)
    mla_front_b(0)
    mla_front_c(0)
    mla_front_d(0)
    for ti in range(NT):
        mla_mid(ti)

    def global_max(acc, acc_tok, out_col):
        rec.add("tensor", lambda e, acc=acc: e.transpose(
            p_k[0:16, 0:128], acc[:], ident32[:]), tag="gmtr",
            r=[acc_tok, "id32"], w=["b1"])
        rec.add("vector", lambda e, out_col=out_col: e.reduce_max(
            mh[0:16, out_col:out_col + 1], p_k[0:16, 0:128], axis=AX.X),
            tag="gmred", r=["b1"], w=["mh"])

    global_max(qmax, "qmax", 0)
    global_max(kmax, "kmax", 1)
    rec.add("scalar", lambda e: e.activation(
        mh[0:16, 2:4], mh[0:16, 0:2], AF.Sqrt), tag="gmsqrt",
        r=["mh"], w=["mh"])
    rec.add("vector", lambda e: e.tensor_tensor(
        mh[0:16, 4:5], mh[0:16, 2:3], mh[0:16, 3:4], ALU.mult),
        tag="mhat", r=["mh"], w=["mh"])
    rec.add("vector", lambda e: e.tensor_scalar(
        mh[0:16, 5:6], mh[0:16, 4:5], -0.125, None, ALU.mult),
        tag="mbias", r=["mh"], w=["mh"])
    rec.dma("sync", mrow[0:1, 0:16], mh[0:16, 5:6], tag="mrow",
            r=["mh"], w=["mrow"])
    rec.add("tensor", lambda e: e.matmul(
        p_k[:, 0:16], ones1[0:1, :], mrow[0:1, 0:16],
        start=True, stop=True), tag="mbc", r=["ones1", "mrow"], w=["b1"])
    rec.add("scalar", lambda e: e.copy(bias_t[:], p_k[:, 0:16]), tag="mbcp",
            r=["b1"], w=["bias"])

    def q_loads(g, j):
        for t in range(4):
            rec.dma("sync", qbuf[:, 512 * j + 128 * t:512 * j + 128 * (t + 1)],
                    q_hbm[4 * g + t, j, :, :],
                    tag=f"ql{g}_{j}_{t}", r=["hbm_eff"], w=[f"qbuf{j}"])

    for g in range(NP):
        if g == 0:
            for j in range(8):
                q_loads(0, j)
        for j in range(8):
            nkc = 4 * (g + 1)

            def mla_loads(kc):
                rec.dma("sync", kbuf[kc % 4][:], k_hbm[kc, j, :, :],
                        tag=f"kl{g}_{j}_{kc}", r=["hbm_eff"],
                        w=[f"kb{kc % 4}"])
                rec.dma("sync", vbuf[kc % 4][:],
                        v_hbm[kc, :, 130 * j:130 * (j + 1)],
                        tag=f"vl{g}_{j}_{kc}", r=[f"hbm_v{kc}"],
                        w=[f"vb{kc % 4}"])

            def mla_parts(kc):
                q0 = 128 * max(0, kc - 4 * g)
                parts = []
                for h2 in range(2):
                    if h2 == 0:
                        p_s, st = ((p_sA, "b0") if kc % 2 == 0
                                   else (b4, "b4"))
                        px16 = pa16[kc % 2]
                    else:
                        p_s, st = ((p_sB, "b1") if kc % 2 == 0
                                   else (p_v2[:, 0:512], "pv2"))
                        px16 = pb16[kc % 2]
                    parts.append((h2, p_s, st, px16, f"px{h2}_{kc % 2}", q0))
                return parts

            def mla_scores(kc):
                for h2, p_s, st, px16, pt, q0 in mla_parts(kc):
                    rec.add("tensor", lambda e, j=j, h2=h2, kc=kc, q0=q0,
                            p_s=p_s: e.matmul(
                        p_s[:, q0:512],
                        kbuf[kc % 4][64 * h2:64 * (h2 + 1), :],
                        qbuf[64 * h2:64 * (h2 + 1),
                             512 * j + q0:512 * (j + 1)],
                        start=True, stop=True,
                        tile_position=(64 * h2, 0)),
                        tag=f"s{g}_{2 * j + h2}_{kc}",
                        r=[f"kb{kc % 4}", f"qbuf{j}"], w=[st])

            def mla_exp(kc):
                for h2, p_s, st, px16, pt, q0 in mla_parts(kc):
                    h = 2 * j + h2
                    rec.add("scalar", lambda e, h=h, q0=q0, p_s=p_s,
                            px16=px16: e.activation(
                        px16[:, q0:512], p_s[:, q0:512], AF.Exp,
                        bias=bias_t[:, h:h + 1],
                        scale=0.125), tag=f"exp{g}_{h}_{kc}",
                        r=[st, "bias"], w=[pt])
                    if kc - 4 * g >= 0:
                        rec.add("vector", lambda e, q0=q0, px16=px16:
                                e.tensor_tensor(
                            px16[:, q0:q0 + 128],
                            px16[:, q0:q0 + 128],
                            mask01[:], ALU.mult), tag="diag",
                            r=[pt, "m01"], w=[pt])

            def mla_pv(kc):
                for h2, p_s, st, px16, pt, q0 in mla_parts(kc):
                    p_o = p_oA if h2 == 0 else p_oB
                    ot = "b2" if h2 == 0 else "b3"
                    rec.add("tensor", lambda e, h2=h2, kc=kc, q0=q0,
                            p_o=p_o, px16=px16, sp=(kc == nkc - 1): e.matmul(
                        p_o[0:65, q0:512],
                        vbuf[kc % 4][:, 65 * h2:65 * (h2 + 1)],
                        px16[:, q0:512],
                        start=(kc == 0), stop=sp),
                        tag=f"pv{g}_{2 * j + h2}_{kc}",
                        r=[f"vb{kc % 4}", pt], w=[ot])

            for kk in range(min(3, nkc)):
                mla_loads(kk)
            mla_scores(0)
            for kc in range(nkc):
                if kc + 3 < nkc:
                    mla_loads(kc + 3)
                if kc + 1 < nkc:
                    mla_scores(kc + 1)
                mla_exp(kc)
                mla_pv(kc)
            for h2 in range(2):
                h = 2 * j + h2
                p_o = p_oA if h2 == 0 else p_oB
                ot = "b2" if h2 == 0 else "b3"
                rec.add("vector", lambda e, h2=h2, p_o=p_o: e.reciprocal(
                    rs_t[64:65, 512 * h2:512 * (h2 + 1)],
                    p_o[64:65, :]), tag=f"rcp{g}_{h}",
                    r=[ot], w=["rs_t"])
                rec.add("tensor", lambda e, h2=h2: e.matmul(
                    p_bc2[0:64, :], ones64[64:65, 0:64],
                    rs_t[64:65, 512 * h2:512 * (h2 + 1)],
                    start=True, stop=True, tile_position=(64, 0)),
                    tag=f"rbc{g}_{h}", r=["ones64", "rs_t"], w=["b4"])
                rec.add("scalar", lambda e, h2=h2: e.copy(
                    bc_s[:, 512 * h2:512 * (h2 + 1)], p_bc2[0:64, :]),
                    tag=f"rbcp{g}_{h}", r=["b4"], w=["bc_s"])
                rec.add("vector", lambda e, h=h, h2=h2, p_o=p_o:
                        e.tensor_tensor(
                    h_s[0:64, 512 * h:512 * (h + 1)], p_o[0:64, :],
                    bc_s[:, 512 * h2:512 * (h2 + 1)], ALU.mult),
                    tag=f"hnrm{g}_{h}", r=[ot, "bc_s"], w=["h_s"])
            if g + 1 < NP:
                q_loads(g + 1, j)
        for tl in range(4):
            ti = 4 * g + tl
            q = tl % 2
            rec.dma("scalar", xt[q][:], xin_hbm[128 * ti:128 * (ti + 1), :],
                    tag=f"xinl{ti}", r=[f"hbm_xin{ti}"], w=[f"xt{q}"])
            for nh in range(2):
                for h in range(16):
                    rec.add("tensor", lambda e, h=h, nh=nh, tl=tl: e.matmul(
                        p_v2[:, 512 * nh:512 * (nh + 1)],
                        h_s[0:64, 512 * h + 128 * tl:512 * h + 128 * (tl + 1)],
                        owT_s[0:64, 1024 * h + 512 * nh:1024 * h + 512 * (nh + 1)],
                        start=(h == 0), stop=(h == 15)),
                        tag=f"ow{ti}_{nh}", r=["h_s", "owT"], w=["pv2"])
            rec.add("vector", lambda e, q=q: e.tensor_tensor(
                s1t[q][:], p_v2[:], xt[q][:], ALU.add),
                tag=f"s1{ti}", r=["pv2", f"xt{q}"], w=[f"s1t{q}"])
            rec.dma("sync", s1_out[128 * ti:128 * (ti + 1), :], s1t[q][:],
                    tag=f"s1sp{ti}", r=[f"s1t{q}"])




def build_moe_wconv(nc, al, rec, gateT, upT, downT, gbf, ubf, dbf, pfx="wc_"):
    S = al.sb
    stg = [S(pfx + f"stg{i}", [128, 2048], F32) for i in range(2)]
    ob = [S(pfx + f"ob{i}", [128, 2048], BF16) for i in range(2)]
    jobs = []
    for e in range(8):
        for k in range(8):
            jobs.append((gateT[e, 128 * k:128 * (k + 1), :],
                         gbf[e, 128 * k:128 * (k + 1), :], 2048))
            jobs.append((upT[e, 128 * k:128 * (k + 1), :],
                         ubf[e, 128 * k:128 * (k + 1), :], 2048))
        for m in range(16):
            jobs.append((downT[e, 128 * m:128 * (m + 1), :],
                         dbf[e, 128 * m:128 * (m + 1), :], 1024))
    emits = []
    for i, (src_ap, dst_ap, cols) in enumerate(jobs):
        def emit(i=i, src_ap=src_ap, dst_ap=dst_ap, cols=cols):
            b = i % 2
            rec.dma("gpsimd", stg[b][:, 0:cols], src_ap, tag=f"wcl{i}",
                    w=[pfx + f"s{b}"])
            eng = "vector" if i % 2 == 0 else "scalar"
            if eng == "vector":
                rec.add(eng, lambda e2, b=b, cols=cols: e2.tensor_copy(
                    ob[b][:, 0:cols], stg[b][:, 0:cols]), tag=f"wcc{i}",
                    r=[pfx + f"s{b}"], w=[pfx + f"o{b}"])
            else:
                rec.add(eng, lambda e2, b=b, cols=cols: e2.copy(
                    ob[b][:, 0:cols], stg[b][:, 0:cols]), tag=f"wcc{i}",
                    r=[pfx + f"s{b}"], w=[pfx + f"o{b}"])
            rec.dma("scalar", dst_ap, ob[b][:, 0:cols], tag=f"wcs{i}",
                    r=[pfx + f"o{b}"], w=[pfx + f"h{i}"])
        emits.append(emit)
    return emits



def build_l0_moe(nc, al, rec, s1, x1, x2, alf_hbm, gbf, ubf, dbf,
                 routerT, rbias_r, ln2_r, pfx="f3_"):
    S, P = al.sb, al.ps

    ident = S(pfx + "ident", [128, 128], BF16)
    ones1 = S(pfx + "ones1", [1, 128], F32)
    lrow = S(pfx + "lrow", [1, D], F32)
    ln2bc = S(pfx + "ln2bc", [128, D], F32)
    rbrow = S(pfx + "rbrow", [1, 7], F32)
    rbbc = S(pfx + "rbbc", [128, 7], F32)
    rT_s = S(pfx + "rT", [128, 56], BF16)
    wg_s = S(pfx + "wg", [128, 16384], BF16)
    wu_s = S(pfx + "wu", [128, 16384], BF16)
    wd_s = S(pfx + "wd", [128, 16384], BF16)
    a_s = S(pfx + "a_s", [128, 8192], BF16)
    af_s = [S(pfx + f"af{i}", [128, 4096], BF16) for i in range(2)]
    acc = S(pfx + "acc", [128, 8192], F32)
    xt = [S(pfx + f"xt{i}", [128, D], F32) for i in range(2)]
    sq_scr = S(pfx + "sq", [128, D], F32)
    stats = S(pfx + "stats", [128, 8], F32)
    al16 = S(pfx + "al16", [128, D], BF16)
    alf = S(pfx + "alf", [128, D], BF16)
    prb = S(pfx + "prb", [128, 16], F32)
    wgt_all = S(pfx + "wgt", [128, 256], F32)
    gs = S(pfx + "gs", [128, 512], F32)
    tmp = S(pfx + "tmp", [128, 512], F32)
    outt = [S(pfx + f"outt{i}", [128, D], F32) for i in range(2)]
    cpool = S(pfx + "cpool", [128, 8], F32)

    p_tr = P(pfx + "ptr", [128, 1024], BF16)
    b0 = P(pfx + "b0", [128, 512])
    b1 = P(pfx + "b1", [128, 512])
    b2 = P(pfx + "b2", [128, 512])
    b3 = P(pfx + "b3", [128, 512])

    ln = LNHelper(rec, sq_scr, stats, name=pfx + "ln")
    cp = ConstPool(rec, cpool)
    eps5 = cp.get(1e-5)
    cp_op = cp.op

    o_id = make_identity_ops(rec, ident)
    rec.add("gpsimd", lambda e: e.memset(ones1[:], 1.0), tag="ones", w=["ones1"])
    rec.add("gpsimd", lambda e: e.memset(wgt_all[:], 1.0), tag="wgt1", w=["wgt"])

    rec.dma("sync", lrow[:], ln2_r[:], w=["lrow"])
    rec.dma("sync", rbrow[:], rbias_r[:], w=["rbrow"])
    for j in range(2):
        rec.add("tensor", lambda e, j=j: e.matmul(
            b0[:], ones1[0:1, :], lrow[0:1, 512 * j:512 * (j + 1)],
            start=True, stop=True), tag="lbc", r=["ones1", "lrow"], w=["b0"])
        rec.add("scalar", lambda e, j=j: e.copy(
            ln2bc[:, 512 * j:512 * (j + 1)], b0[:]), tag="lbcp",
            r=["b0"], w=["ln2bc"])
    rec.add("tensor", lambda e: e.matmul(
        b0[:, 0:7], ones1[0:1, :], rbrow[0:1, :],
        start=True, stop=True), tag="rbc", r=["ones1", "rbrow"], w=["b0"])
    rec.add("scalar", lambda e: e.copy(rbbc[:], b0[:, 0:7]), tag="rbcp",
            r=["b0"], w=["rbbc"])
    for c in range(8):
        rec.dma("sync", sq_scr[:, 7 * c:7 * (c + 1)],
                routerT[128 * c:128 * (c + 1), :], w=[pfx + "lnsq"],
                tag=f"rld{c}")
    rec.add("vector", lambda e: e.tensor_copy(rT_s[:], sq_scr[:, 0:56]),
            tag="rcast", r=[pfx + "lnsq"], w=["rT"])

    for ti in range(NT):
        q = ti % 2
        rec.dma("sync", xt[q][:], s1[128 * ti:128 * (ti + 1), :],
                tag=f"s1l{ti}", w=[f"xt{q}"])
        ln.emit(xt[q][:], f"xt{q}", sq_scr[:, 0:D], pfx + "lnsq", eps5,
                const_dep=cp_op)
        rec.add("vector", lambda e: e.tensor_tensor(
            al16[:], sq_scr[:, 0:D], ln2bc[:], ALU.mult),
            tag="al16", r=[pfx + "lnsq", "ln2bc"], w=["al16"])
        transpose_to_fm(rec, ident, o_id, al16, "al16", p_tr, "ptr",
                        alf, "alf", 0, col_stride=128)
        for c in range(8):
            rec.dma("gpsimd", alf_hbm[c, :, 128 * ti:128 * (ti + 1)],
                    alf[:, 128 * c:128 * (c + 1)],
                    tag=f"alfsp{ti}_{c}", r=["alf"], w=[f"half{ti}_{c}"])
        for c in range(8):
            rec.add("tensor", lambda e, c=c: e.matmul(
                b1[:, 0:7], alf[:, 128 * c:128 * (c + 1)],
                rT_s[:, 7 * c:7 * (c + 1)],
                start=(c == 0), stop=(c == 7)),
                tag=f"rt{ti}", r=["alf", "rT"], w=["b1"])
        rec.add("vector", lambda e: e.tensor_tensor(
            prb[:, 8:15], b1[:, 0:7], rbbc[:], ALU.add),
            tag="rta", r=["b1", "rbbc"], w=["prb"])
        rec.add("scalar", lambda e: e.activation(
            prb[:, 0:7], prb[:, 8:15], AF.Sigmoid),
            tag="rsig", r=["prb"], w=["prb"])
        rec.add("vector", lambda e: e.reduce_max(
            prb[:, 8:9], prb[:, 0:7], axis=AX.X), tag="m1",
            r=["prb"], w=["prb"])
        rec.add("vector", lambda e: e.tensor_scalar(
            prb[:, 9:10] if False else tmp[:, 0:7], prb[:, 0:7],
            prb[:, 8:9], None, ALU.is_ge), tag="ge1",
            r=["prb"], w=["tmpa"])
        rec.add("vector", lambda e: e.scalar_tensor_tensor(
            tmp[:, 8:15], tmp[:, 0:7], -1e30, prb[:, 0:7], ALU.mult, ALU.add),
            tag="p2", r=["tmpa", "prb"], w=["tmpa"])
        rec.add("vector", lambda e: e.reduce_max(
            prb[:, 9:10], tmp[:, 8:15], axis=AX.X), tag="m2",
            r=["tmpa", "prb"], w=["prb"])
        rec.add("vector", lambda e: e.tensor_scalar(
            tmp[:, 0:7], prb[:, 0:7], prb[:, 9:10], None, ALU.is_ge),
            tag="keep", r=["prb"], w=["tmpa"])
        rec.add("vector", lambda e: e.tensor_tensor(
            prb[:, 10:11], prb[:, 8:9], prb[:, 9:10], ALU.add), tag="den",
            r=["prb"], w=["prb"])
        rec.add("vector", lambda e: e.reciprocal(
            prb[:, 11:12], prb[:, 10:11]), tag="rden",
            r=["prb"], w=["prb"])
        rec.add("vector", lambda e: e.tensor_tensor(
            tmp[:, 0:7], tmp[:, 0:7], prb[:, 0:7], ALU.mult), tag="wk",
            r=["tmpa", "prb"], w=["tmpa"])
        rec.add("vector", lambda e, ti=ti: e.tensor_scalar(
            wgt_all[:, 8 * ti:8 * ti + 7], tmp[:, 0:7], prb[:, 11:12],
            None, ALU.mult), tag="wnorm", r=["tmpa", "prb"], w=["wgt"])

    def exp_wloads(e_i, tag):
        for k in range(8):
            rec.dma("sync", wg_s[:, 2048 * k:2048 * (k + 1)],
                    gbf[e_i, 128 * k:128 * (k + 1), :],
                    tag=f"gld{tag}_{k}", w=[f"wg{k}"])
            rec.dma("sync", wu_s[:, 2048 * k:2048 * (k + 1)],
                    ubf[e_i, 128 * k:128 * (k + 1), :],
                    tag=f"uld{tag}_{k}", w=[f"wu{k}"])
        for m in range(16):
            rec.dma("scalar", wd_s[:, 1024 * m:1024 * (m + 1)],
                    dbf[e_i, 128 * m:128 * (m + 1), :],
                    tag=f"dld{tag}_{m}", w=[f"wd{m}"])

    def af_loads(qt, n, tag):
        tok0 = 1024 * qt + 512 * n
        afb = af_s[n % 2]
        for c in range(8):
            ti0 = tok0 // 128
            deps_toks = [f"half{ti0 + t}_{c}" for t in range(4)]
            rec.dma("sync", afb[:, 512 * c:512 * (c + 1)],
                    alf_hbm[c, :, tok0:tok0 + 512],
                    tag=f"afl{tag}_{c}", r=deps_toks, w=[f"af{n % 2}_{c}"])

    for qt in range(4):
        for e_i in range(8):
            if qt == 0 or e_i > 0:
                exp_wloads(e_i, f"{qt}_{e_i}")
            for n in range(2):
                tok0 = 1024 * qt + 512 * n
                afb = af_s[n % 2]
                if e_i == 0 and not (qt > 0 and n == 0):
                    af_loads(qt, n, f"{qt}_{n}")
                for m in range(16):
                    for k in range(8):
                        rec.add("tensor", lambda e, m=m, k=k, afb=afb:
                                e.matmul(
                            b0[:], wg_s[:, 2048 * k + 128 * m:2048 * k + 128 * (m + 1)],
                            afb[:, 512 * k:512 * (k + 1)],
                            start=(k == 0), stop=(k == 7)),
                            tag=f"g{qt}_{e_i}_{n}_{m}",
                            r=[f"wg{k}", f"af{n % 2}_{k}"],
                            w=["b0"])
                        rec.add("tensor", lambda e, m=m, k=k, afb=afb:
                                e.matmul(
                            b1[:], wu_s[:, 2048 * k + 128 * m:2048 * k + 128 * (m + 1)],
                            afb[:, 512 * k:512 * (k + 1)],
                            start=(k == 0), stop=(k == 7)),
                            tag=f"u{qt}_{e_i}_{n}_{m}",
                            r=[f"wu{k}", f"af{n % 2}_{k}"], w=["b1"])
                    rec.add("scalar", lambda e: e.activation(
                        gs[:], b0[:], AF.Silu), tag="silu",
                        r=["b0"], w=["gs"])
                    rec.add("vector", lambda e, m=m: e.tensor_tensor(
                        a_s[:, 512 * m:512 * (m + 1)], b1[:], gs[:], ALU.mult),
                        tag="aact", r=["b1", "gs"], w=[f"a{m}"])
                for tc in range(4):
                    ti = (tok0 + 128 * tc) // 128
                    for nh in range(2):
                        bank = b2 if nh == 0 else b3
                        bt = "b2" if nh == 0 else "b3"
                        for m in range(16):
                            rec.add("tensor", lambda e, m=m, tc=tc, nh=nh,
                                    bank=bank: e.matmul(
                                bank[:],
                                a_s[:, 512 * m + 128 * tc:512 * m + 128 * (tc + 1)],
                                wd_s[:, 1024 * m + 512 * nh:1024 * m + 512 * (nh + 1)],
                                start=(m == 0), stop=(m == 15)),
                                tag=f"d{qt}_{e_i}_{n}_{tc}_{nh}",
                                r=[f"a{m}", f"wd{m}"], w=[bt])
                        accsl = acc[:, 1024 * (4 * n + tc) + 512 * nh:
                                    1024 * (4 * n + tc) + 512 * (nh + 1)]
                        rec.add("scalar", lambda e, ti=ti, e_i=e_i, bank=bank:
                                e.activation(
                            tmp[:], bank[:], AF.Copy,
                            scale=wgt_all[:, 8 * ti + e_i:8 * ti + e_i + 1]),
                            tag="wscale", r=[bt, "wgt"], w=["tmpa"])
                        if e_i == 0:
                            rec.add("vector", lambda e, accsl=accsl:
                                    e.tensor_copy(accsl, tmp[:]),
                                    tag="acc0", r=["tmpa"], w=["acc"])
                        else:
                            rec.add("vector", lambda e, accsl=accsl:
                                    e.tensor_tensor(accsl, accsl, tmp[:],
                                                    ALU.add),
                                    tag="accA", r=["tmpa", "acc"], w=["acc"])
            if e_i == 7 and qt + 1 < 4:
                exp_wloads(0, f"{qt + 1}_0")
                af_loads(qt + 1, 0, f"pf{qt + 1}")
        for t2 in range(8):
            ti = 8 * qt + t2
            q = t2 % 2
            rec.dma("sync", xt[q][:], s1[128 * ti:128 * (ti + 1), :],
                    tag=f"fs1{ti}", w=[f"xt{q}"])
            rec.dma("scalar", outt[q][:], x1[128 * ti:128 * (ti + 1), :],
                    tag=f"fx1{ti}", w=[f"outt{q}"])
            rec.add("vector", lambda e, q=q, t2=t2: e.tensor_tensor(
                outt[q][:], outt[q][:],
                acc[:, 1024 * t2:1024 * (t2 + 1)], ALU.add),
                tag=f"fo1{ti}", r=[f"outt{q}", "acc"], w=[f"outt{q}"])
            rec.add("vector", lambda e, q=q: e.tensor_tensor(
                outt[q][:], outt[q][:], xt[q][:], ALU.add),
                tag=f"fo2{ti}", r=[f"outt{q}", f"xt{q}"], w=[f"outt{q}"])
            rec.dma("sync", x2[128 * ti:128 * (ti + 1), :], outt[q][:],
                    tag=f"x2sp{ti}", r=[f"outt{q}"])


def prep_l0_moe(ins):
    rg = np.asarray(ins["L0_ffn_routed_gate"], np.float32)
    ru = np.asarray(ins["L0_ffn_routed_up"], np.float32)
    rd = np.asarray(ins["L0_ffn_routed_down"], np.float32)
    sg = np.asarray(ins["L0_ffn_shared_gate"], np.float32)
    su = np.asarray(ins["L0_ffn_shared_up"], np.float32)
    sd = np.asarray(ins["L0_ffn_shared_down"], np.float32)
    gateT = np.zeros((8, 1024, 2048), np.float32)
    upT = np.zeros((8, 1024, 2048), np.float32)
    downT = np.zeros((8, 2048, 1024), np.float32)
    for e in range(7):
        gateT[e] = rg[e].T
        upT[e] = ru[e].T
        downT[e] = rd[e].T
    gateT[7] = sg[0].T
    upT[7] = su[0].T
    downT[7] = sd[0].T
    return {
        "gateT": _c(gateT), "upT": _c(upT), "downT": _c(downT),
        "routerT": _c(np.asarray(ins["L0_ffn_router_w"], np.float32).T),
        "rbias_r": _row(ins["L0_ffn_routing_bias"]),
        "ln2_r": _row(ins["L0_ffn_ln2_w"]),
    }


def make_l0_moe_program():
    nc = bass.Bass("TRN2", target_bir_lowering=False)
    al = Alloc(nc)
    rec = Rec()
    def di(name, shape):
        return nc.dram_tensor(name, shape, F32, kind="ExternalInput").ap()
    s1 = di("s1", [T, D])
    x1 = di("x1", [T, D])
    x2 = nc.dram_tensor("x2", [T, D], F32, kind="ExternalOutput").ap()
    alf_hbm = nc.dram_tensor("alf_hbm", [8, 128, T], BF16).ap()
    gbf = nc.dram_tensor("gbf_hbm", [8, 1024, 2048], BF16).ap()
    ubf = nc.dram_tensor("ubf_hbm", [8, 1024, 2048], BF16).ap()
    dbf = nc.dram_tensor("dbf_hbm", [8, 2048, 1024], BF16).ap()
    mk = al.mark()
    build_moe_wconv(nc, al, rec, di("gateT", [8, 1024, 2048]),
                    di("upT", [8, 1024, 2048]), di("downT", [8, 2048, 1024]),
                    gbf, ubf, dbf)
    rec.fence(tag="wc_end")
    al.free_suffix(mk)
    build_l0_moe(nc, al, rec, s1, x1, x2, alf_hbm, gbf, ubf, dbf,
                 di("routerT", [1024, 7]),
                 di("rbias_r", [1, 7]), di("ln2_r", [1, D]))
    finish(nc, rec, al)
    return nc



LN512 = 3.1191592936036462


def build_l1_attn(nc, al, rec, x, y, qf_hbm, kf_hbm, vt_hbm, xcf_hbm, zf_hbm,
                  wupT, qdiag, kdiag, vdiag, convw_c, convb_c, wig_h, big_c,
                  lnw_row, skip_c, wdnT, pfx="f5_"):
    S, P = al.sb, al.ps

    ident = S(pfx + "ident", [128, 128], BF16)
    ident32 = S(pfx + "id32", [128, 128], F32)
    ones1 = S(pfx + "ones1", [1, 128], F32)
    onescol = S(pfx + "onescol", [128, 1], BF16)
    cpool = S(pfx + "cpool", [128, 8], F32)
    xt = [S(pfx + f"xt{i}", [128, D], F32) for i in range(2)]
    sq_scr = S(pfx + "sq", [128, D], F32)
    stats = S(pfx + "stats", [128, 8], F32)
    st2 = S(pfx + "st2", [128, 8], F32)
    cw_s = S(pfx + "cw", [128, 64], F32)
    cb_s = S(pfx + "cb", [128, 16], F32)
    big_s = S(pfx + "big", [128, 2], F32)
    skp_s = S(pfx + "skp", [128, 16], F32)
    wig_s = S(pfx + "wig", [128, 4 * 96], BF16)
    gX = S(pfx + "gX", [4, T], F32)
    gY = S(pfx + "gY", [4, T], F32)
    IGP = gX[0:4, :]
    FGP = gY[0:4, :]

    p_tr = P(pfx + "ptr", [128, 1024], BF16)
    b0 = P(pfx + "b0", [128, 512])
    b1 = P(pfx + "b1", [128, 512])
    b2 = P(pfx + "b2", [128, 512])
    b3 = P(pfx + "b3", [128, 512])
    b4 = P(pfx + "b4", [128, 512])
    b5 = P(pfx + "b5", [128, 512])
    b6 = P(pfx + "b6", [128, 512])

    ln = LNHelper(rec, sq_scr, stats, name=pfx + "ln")
    lnh = LNHelper(rec, sq_scr, st2, name=pfx + "lh", sq_tok=pfx + "lnsq")
    cp = ConstPool(rec, cpool)
    eps5 = cp.get(1e-5)
    one_c = cp.get(1.0)
    cp_op = cp.op

    o_id = make_identity_ops(rec, ident)
    rec.add("gpsimd", lambda e: e.memset(ones1[:], 1.0), tag="ones", w=["ones1"])
    rec.add("gpsimd", lambda e: e.memset(onescol[:], 1.0), tag="onesc",
            w=["onescol"])
    rec.add("gpsimd", lambda e: e.memset(ident32[:], 0.0), tag="i32a",
            w=["id32"])
    rec.add("gpsimd", lambda e: e.affine_select(
        out=ident32[:], in_=ident32[:], compare_op=ALU.not_equal, fill=1.0,
        base=0, pattern=[[-1, 128]], channel_multiplier=1), tag="i32b",
        r=["id32"], w=["id32"])
    rec.dma("sync", cw_s[:], convw_c[:], w=["cw"])
    rec.dma("sync", cb_s[:], convb_c[:], w=["cb"])
    rec.dma("sync", big_s[0:4, :], big_c[:], w=["big"])
    rec.dma("sync", skp_s[:], skip_c[:], w=["skp"])

    wup_s = S(pfx + "wup", [128, 8 * 4096], BF16)
    qd_s = S(pfx + "qd", [128, 64 * 128], BF16)
    kd_s = S(pfx + "kd", [128, 64 * 128], BF16)
    vw_s = S(pfx + "vw", [128, 64], F32)
    wstage = S(pfx + "wstage", [128, 2048], F32)
    xlf = S(pfx + "xlf", [128, 4096], BF16)
    xln16 = S(pfx + "xln16", [128, D], BF16)
    xm_s = [S(pfx + f"xm{m}", [128, 515], BF16) for m in range(16)]
    hl_s = S(pfx + "hl", [128, 3 * 16], BF16)
    cv0 = S(pfx + "cv0", [128, 512], F32)
    cv1 = S(pfx + "cv1", [128, 512], F32)
    xc4 = [S(pfx + f"xc4_{d}", [128, 512], BF16) for d in range(4)]
    qkbuf = [S(pfx + f"qkb{i}", [128, 512], BF16) for i in range(2)]
    vbuf3 = [S(pfx + f"vb3{i}", [128, 512], BF16) for i in range(2)]
    zbuf = [S(pfx + f"zb{i}", [128, 512], BF16) for i in range(2)]
    vt_t = [S(pfx + f"vt{t}", [128, 2048], BF16) for t in range(4)]

    def wload(dst, dst_tok, src_hbm, nch, cols, tag):
        for k in range(nch):
            for piece in range(0, cols, 2048):
                pw = min(2048, cols - piece)
                rec.dma("sync", wstage[:, 0:pw],
                        src_hbm[128 * k:128 * (k + 1), piece:piece + pw],
                        tag=f"{tag}ld{k}_{piece}", w=["wstage"])
                rec.add("vector", lambda e, k=k, piece=piece, pw=pw, dst=dst:
                        e.tensor_copy(
                    dst[:, cols * k + piece:cols * k + piece + pw],
                    wstage[:, 0:pw]),
                    tag=f"{tag}c{k}", r=["wstage"], w=[dst_tok])

    wload(wup_s, "wup", wupT, 8, 4096, "wup")
    wload(qd_s, "qdw", qdiag, 1, 8192, "qd")
    wload(kd_s, "kdw", kdiag, 1, 8192, "kd")
    rec.dma("sync", vw_s[:], vdiag[:], w=["vdw"])
    rec.dma("sync", wstage[:, 0:384], wig_h[:], tag="wigld", w=["wstage"])
    rec.add("vector", lambda e: e.tensor_copy(wig_s[:], wstage[:, 0:384]),
            tag="wigc", r=["wstage"], w=["wig"])
    rec.add("gpsimd", lambda e: e.memset(hl_s[:], 0.0), tag="hl0", w=["hl"])

    for p in range(NP):
        for tl in range(4):
            ti = 4 * p + tl
            q = ti % 2
            rec.dma("sync", xt[q][:], x[128 * ti:128 * (ti + 1), :],
                    tag=f"x{ti}", w=[f"xt{q}"])
            ln.emit(xt[q][:], f"xt{q}", xln16[:], "xln16", eps5,
                    const_dep=cp_op)
            transpose_to_fm(rec, ident, o_id, xln16, "xln16", p_tr, "ptr",
                            xlf, "xlf", tl)
        for m in range(16):
            bank, bt = (b0, "b0") if m % 2 == 0 else (b1, "b1")
            for k in range(8):
                rec.add("tensor", lambda e, m=m, k=k, bank=bank: e.matmul(
                    bank[:], wup_s[:, 4096 * k + 128 * m:4096 * k + 128 * (m + 1)],
                    xlf[:, 512 * k:512 * (k + 1)],
                    start=(k == 0), stop=(k == 7)),
                    tag=f"xm{p}_{m}", r=["wup", "xlf"], w=[bt])
            if p > 0:
                rec.add("vector", lambda e, m=m: e.tensor_copy(
                    hl_s[:, 3 * m:3 * (m + 1)], xm_s[m][:, 512:515]),
                    tag="hlsave", r=[f"xms{m}"], w=["hl"])
            rec.add("vector", lambda e, m=m, bank=bank: e.tensor_copy(
                xm_s[m][:, 3:515], bank[:]),
                tag="xmev", r=[bt, "hl"], w=[f"xms{m}"])
            rec.add("vector", lambda e, m=m: e.tensor_copy(
                xm_s[m][:, 0:3], hl_s[:, 3 * m:3 * (m + 1)]),
                tag="hlrest", r=["hl"], w=[f"xms{m}"])
        for m in range(16):
            bank, bt = (b0, "b0") if m % 2 == 0 else (b1, "b1")
            zb = zbuf[m % 2]
            for k in range(8):
                rec.add("tensor", lambda e, m=m, k=k, bank=bank: e.matmul(
                    bank[:],
                    wup_s[:, 4096 * k + 2048 + 128 * m:4096 * k + 2048 + 128 * (m + 1)],
                    xlf[:, 512 * k:512 * (k + 1)],
                    start=(k == 0), stop=(k == 7)),
                    tag=f"z{p}_{m}", r=["wup", "xlf"], w=[bt])
            rec.add("vector", lambda e, m=m, bank=bank, zb=zb: e.tensor_copy(
                zb[:], bank[:]), tag="zev", r=[bt], w=[f"zbt{m % 2}"])
            rec.dma("gpsimd", zf_hbm[m, :, 512 * p:512 * (p + 1)], zb[:],
                    tag=f"zsp{p}_{m}", r=[f"zbt{m % 2}"], w=[f"hz{p}_{m}"])
        for pc in range(4):
            for d in range(4):
                m = 4 * d + pc
                rec.add("vector", lambda e, m=m: e.tensor_scalar(
                    cv0[:], xm_s[m][:, 3:515], cw_s[:, 4 * m + 3:4 * m + 4],
                    None, ALU.mult), tag="cj3", r=[f"xms{m}", "cw"], w=["cv0"])
                rec.add("vector", lambda e, m=m: e.scalar_tensor_tensor(
                    cv1[:], xm_s[m][:, 2:514], cw_s[:, 4 * m + 2:4 * m + 3],
                    cv0[:], ALU.mult, ALU.add), tag="cj2",
                    r=[f"xms{m}", "cw", "cv0"], w=["cv1"])
                rec.add("vector", lambda e, m=m: e.scalar_tensor_tensor(
                    cv0[:], xm_s[m][:, 1:513], cw_s[:, 4 * m + 1:4 * m + 2],
                    cv1[:], ALU.mult, ALU.add), tag="cj1",
                    r=[f"xms{m}", "cw", "cv1"], w=["cv0"])
                rec.add("vector", lambda e, m=m: e.scalar_tensor_tensor(
                    cv1[:], xm_s[m][:, 0:512], cw_s[:, 4 * m:4 * m + 1],
                    cv0[:], ALU.mult, ALU.add), tag="cj0",
                    r=[f"xms{m}", "cw", "cv0"], w=["cv1"])
                rec.add("scalar", lambda e, m=m, d=d: e.activation(
                    xc4[d][:], cv1[:], AF.Silu, bias=cb_s[:, m:m + 1]),
                    tag="xcact", r=["cv1", "cb"], w=[f"xc4t{d}"])
                rec.dma("gpsimd", xcf_hbm[m, :, 512 * p:512 * (p + 1)],
                        xc4[d][:], tag=f"xcsp{p}_{m}", r=[f"xc4t{d}"],
                        w=[f"hxc{p}_{m}"])
            for o in range(4):
                mo = 4 * o + pc
                for which, dw, dwt, hbm, buf, btok, sec in (
                        (0, qd_s, "qdw", qf_hbm, qkbuf[0], "qkb0", 0),
                        (1, kd_s, "kdw", kf_hbm, qkbuf[1], "qkb1", 1),
                        (2, None, "vdw", None, vbuf3[o % 2], f"vb3{o % 2}", 2)):
                    bank, bt = (b0, "b0") if (2 * o + which) % 2 == 0 \
                        else (b1, "b1")
                    if which == 2:
                        for d in range(4):
                            mi = 4 * d + pc
                            idx = 16 * o + 4 * d + pc
                            if d == 0:
                                rec.add("vector", lambda e, mi=mi, idx=idx:
                                        e.tensor_scalar(
                                    cv0[:], xm_s[mi][:, 3:515],
                                    vw_s[:, idx:idx + 1], None, ALU.mult),
                                    tag="vhw0", r=[f"xms{mi}", "vdw"],
                                    w=["cv0"])
                            else:
                                src_t, dst_t = (cv0, cv1) if d % 2 == 1 \
                                    else (cv1, cv0)
                                stok, dtok2 = ("cv0", "cv1") if d % 2 == 1 \
                                    else ("cv1", "cv0")
                                rec.add("vector", lambda e, mi=mi, idx=idx,
                                        src_t=src_t, dst_t=dst_t:
                                        e.scalar_tensor_tensor(
                                    dst_t[:], xm_s[mi][:, 3:515],
                                    vw_s[:, idx:idx + 1], src_t[:],
                                    ALU.mult, ALU.add),
                                    tag="vhw", r=[f"xms{mi}", "vdw", stok],
                                    w=[dtok2])
                        rec.add("vector", lambda e, buf=buf: e.tensor_copy(
                            buf[:], cv1[:]), tag="vhwev",
                            r=["cv1"], w=[btok])
                    else:
                        for d in range(4):
                            mi = 4 * d + pc
                            idx = 16 * o + 4 * d + pc
                            rhs = lambda d=d: xc4[d][:]
                            rtok = f"xc4t{d}"
                            rec.add("tensor", lambda e, idx=idx, dw=dw,
                                    rhs=rhs, bank=bank, st=(d == 0),
                                    sp=(d == 3): e.matmul(
                                bank[:], dw[:, 128 * idx:128 * (idx + 1)],
                                rhs(), start=st, stop=sp),
                                tag=f"hw{p}_{mo}_{which}", r=[dwt, rtok],
                                w=[bt])
                        rec.add("vector", lambda e, buf=buf, bank=bank:
                                e.tensor_copy(buf[:], bank[:]),
                                tag="hwev", r=[bt], w=[btok])
                    if hbm is not None:
                        rec.dma("gpsimd", hbm[mo, :, 512 * p:512 * (p + 1)],
                                buf[:], tag=f"hwsp{p}_{mo}_{which}",
                                r=[btok],
                                w=[("hqs" if which == 0 else "hks") +
                                   f"{p}_{mo}"])
                    cc = 16 * sec + mo
                    st_ifg = (pc == 0 and o == 0 and which == 0)
                    sp_ifg = (pc == 3 and o == 3 and which == 2)
                    rec.add("tensor", lambda e, cc=cc, buf=buf, st=st_ifg,
                            sp=sp_ifg: e.matmul(
                        b5[0:4, :], wig_s[:, 4 * cc:4 * (cc + 1)], buf[:],
                        start=st, stop=sp),
                        tag=f"ig{p}", r=["wig", btok], w=["b5"])
                    rec.add("tensor", lambda e, cc=cc, buf=buf, st=st_ifg,
                            sp=sp_ifg: e.matmul(
                        b6[0:4, :], wig_s[:, 192 + 4 * cc:192 + 4 * (cc + 1)],
                        buf[:],
                        start=st, stop=sp),
                        tag=f"fg{p}", r=["wig", btok], w=["b6"])
                    if which == 2:
                        vcol = 512 * pc + 128 * o
                        for tl in range(4):
                            rec.add("tensor", lambda e, buf=buf, tl=tl:
                                    e.transpose(
                                p_tr[:, 0:128], buf[:, 128 * tl:128 * (tl + 1)],
                                ident[:]), deps=[o_id], tag="vtr",
                                r=[btok], w=["ptr"])
                            rec.add("vector", lambda e, vcol=vcol, tl=tl:
                                    e.tensor_copy(
                                vt_t[tl][:, vcol:vcol + 128],
                                p_tr[:, 0:128]),
                                tag="vtev", r=["ptr"], w=[f"vtt{tl}"])
        rec.add("vector", lambda e, p=p: e.tensor_scalar(
            IGP[:, 512 * p:512 * (p + 1)], b5[0:4, :],
            big_s[0:4, 0:1], None, ALU.add),
            tag="ifgb_i", r=["b5", "big"], w=["gX"])
        rec.add("vector", lambda e, p=p: e.tensor_scalar(
            FGP[:, 512 * p:512 * (p + 1)], b6[0:4, :],
            big_s[0:4, 1:2], None, ALU.add),
            tag="ifgb_f", r=["b6", "big"], w=["gY"])
        for tl in range(4):
            rec.dma("sync", vt_hbm[4 * p + tl, :, :], vt_t[tl][:],
                    tag=f"vtsp{p}_{tl}", r=[f"vtt{tl}"],
                    w=[f"hvt{4 * p + tl}"])

    rec.fence(tag="fence_gates")
    al.free([pfx + "wup", pfx + "qd", pfx + "kd", pfx + "vw",
             pfx + "wstage", pfx + "xlf", pfx + "xln16", pfx + "hl",
             pfx + "cv0", pfx + "cv1"]
            + [pfx + f"xm{m}" for m in range(16)]
            + [pfx + f"xc4_{d}" for d in range(4)]
            + [pfx + f"qkb{i}" for i in range(2)]
            + [pfx + f"vb3{i}" for i in range(2)]
            + [pfx + f"zb{i}" for i in range(2)]
            + [pfx + f"vt{t}" for t in range(4)])
    gW = S(pfx + "gW", [4, T], F32)
    aT_s = S(pfx + "aT", [128, 128], F32)
    gZ = S(pfx + "gZ", [4, T], F32)

    rec.add("scalar", lambda e: e.activation(gZ[:], FGP, AF.Abs),
            tag="g_abs", r=["gY"], w=["gZ"])
    rec.add("scalar", lambda e: e.activation(gZ[:], gZ[:], AF.Exp, scale=-1.0),
            tag="g_exp", r=["gZ"], w=["gZ"])
    rec.add("scalar", lambda e: e.activation(gZ[:], gZ[:], AF.Ln,
                                             bias=one_c[0:4, 0:1]),
            deps=[cp_op], tag="g_ln", r=["gZ"], w=["gZ"])
    rec.add("vector", lambda e: e.tensor_scalar(gW[:], FGP, 0.0, None,
                                                ALU.max),
            tag="g_relu", r=["gY"], w=["gW"])
    rec.add("vector", lambda e: e.tensor_tensor(gW[:], gW[:], gZ[:], ALU.add),
            tag="g_add", r=["gW", "gZ"], w=["gW"])
    rec.add("vector", lambda e: e.scalar_tensor_tensor(
        gW[:], gW[:], -1.0, FGP, ALU.mult, ALU.add), tag="g_neg",
        r=["gW", "gY"], w=["gW"])
    rec.add("vector", lambda e: e.memset(gY[:], 0.0), tag="g_zero",
            r=["gY"], w=["gY"])
    rec.add("vector", lambda e: e.tensor_tensor_scan(
        gZ[:], gW[:], gY[:], 0.0, ALU.add, ALU.add), tag="g_cum",
        r=["gW", "gY", "gZ"], w=["gZ"])
    rec.add("vector", lambda e: e.tensor_tensor(gW[:], IGP, gZ[:],
                                                ALU.subtract),
            tag="g_a", r=["gX", "gZ", "gW"], w=["gW"])
    for kc in range(NT):
        rec.add("tensor", lambda e, kc=kc: e.transpose(
            b6[0:128, 0:4], gW[:, 128 * kc:128 * (kc + 1)],
            ident32[0:4, 0:4]),
            tag="aTtr", r=["gW", "id32"], w=["b6"])
        rec.add("vector", lambda e, kc=kc: e.tensor_copy(
            aT_s[:, 4 * kc:4 * (kc + 1)], b6[0:128, 0:4]),
            tag="aTev", r=["b6"], w=["aT"])
    rec.add("vector", lambda e: e.tensor_tensor_scan(
        gX[:], gW[:], gY[:], -1e30, ALU.max, ALU.add), tag="g_rm",
        r=["gW", "gY", "gX"], w=["gX"])
    rec.add("vector", lambda e: e.tensor_scalar(
        gY[:], gX[:], -1.0, -LN512, ALU.mult, ALU.add), tag="g_b",
        r=["gX", "gY"], w=["gY"])
    rec.add("vector", lambda e: e.tensor_tensor(gW[:], gZ[:], gX[:], ALU.add),
            tag="g_m", r=["gZ", "gX", "gW"], w=["gW"])
    rec.add("scalar", lambda e: e.activation(gW[:], gW[:], AF.Exp, scale=-1.0),
            tag="g_n2", r=["gW"], w=["gW"])
    BS = gY[0:4, :]
    N2 = gW[0:4, :]
    rec.fence(tag="fence_3b")
    al.free([pfx + "gZ"])

    wdn_s = S(pfx + "wdn", [128, 16 * 1024], BF16)
    wstg2 = S(pfx + "wstg2", [128, 1024], F32)
    lnr_s = S(pfx + "lnr", [1, 2048], F32)
    lnwbc = S(pfx + "lnwbc", [128, 2048], F32)
    qh_s = S(pfx + "qh", [128, 2048], BF16)
    kbb = [S(pfx + f"kbb{i}", [128, 512], BF16) for i in range(4)]
    vbb = [S(pfx + f"vbb{i}", [128, 512], BF16) for i in range(4)]
    bb_s = S(pfx + "bb", [128, 512], F32)
    ld_t = [S(pfx + f"ldt{i}", [128, 512], F32) for i in range(2)]
    ed_t = [S(pfx + f"edt{i}", [128, 512], F32) for i in range(2)]
    c16 = [S(pfx + f"c16{i}", [128, 512], BF16) for i in range(2)]
    maskn = S(pfx + "maskn", [128, 128], F32)
    brow1 = S(pfx + "brow1", [1, 512], F32)
    n2r1 = S(pfx + "n2r1", [1, 512], F32)
    rs1 = S(pfx + "rs1", [1, 512], F32)
    fbc = S(pfx + "fbc", [128, 512], F32)
    h4 = [S(pfx + f"h4_{d}", [128, 512], F32) for d in range(4)]
    ht = S(pfx + "ht", [128, 512], F32)
    hnt16 = S(pfx + "hnt16", [128, 512], BF16)
    xcz = S(pfx + "xcz", [128, 2048], BF16)
    zz = S(pfx + "zz", [128, 2048], BF16)
    szf = S(pfx + "szf", [128, 512], F32)
    skf = S(pfx + "skf", [128, 512], F32)
    hnf = S(pfx + "hnf", [128, 2048], BF16)
    hst_s = S(pfx + "hst", [128, 16 * 512], BF16)
    outt = [S(pfx + f"outt{i}", [128, D], F32) for i in range(2)]

    for m in range(16):
        rec.dma("sync", wstg2[:], wdnT[128 * m:128 * (m + 1), :],
                tag=f"wdnld{m}", w=["wstg2"])
        rec.add("vector", lambda e, m=m: e.tensor_copy(
            wdn_s[:, 1024 * m:1024 * (m + 1)], wstg2[:]),
            tag=f"wdnc{m}", r=["wstg2"], w=["wdn"])
    rec.dma("sync", lnr_s[:], lnw_row[:], w=["lnr"])
    for h in range(4):
        rec.add("tensor", lambda e, h=h: e.matmul(
            b0[:], ones1[0:1, :], lnr_s[0:1, 512 * h:512 * (h + 1)],
            start=True, stop=True), tag="lnwbc", r=["lnr", "ones1"], w=["b0"])
        rec.add("scalar", lambda e, h=h: e.copy(
            lnwbc[:, 512 * h:512 * (h + 1)], b0[:]), tag="lnwbcp",
            r=["b0"], w=["lnwbc"])
    rec.add("gpsimd", lambda e: e.memset(maskn[:], 0.0), tag="mka", w=["mkn"])
    rec.add("gpsimd", lambda e: e.affine_select(
        out=maskn[:], in_=maskn[:], compare_op=ALU.is_ge, fill=-1e30,
        base=0, pattern=[[1, 128]], channel_multiplier=-1), tag="mkb",
        r=["mkn"], w=["mkn"])

    for g in range(NP):
        for h in range(4):
            for d2 in range(4):
                rec.dma("sync", qh_s[:, 512 * d2:512 * (d2 + 1)],
                        qf_hbm[4 * d2 + h, :, 512 * g:512 * (g + 1)],
                        tag=f"qhl{g}_{h}_{d2}", r=[f"hqs{g}_{4 * d2 + h}"],
                        w=["qht"])
            rec.dma("sync", brow1[:], BS[h:h + 1, 512 * g:512 * (g + 1)],
                    tag=f"br{g}_{h}", r=["gY"], w=["br1"])
            rec.dma("sync", n2r1[:], N2[h:h + 1, 512 * g:512 * (g + 1)],
                    tag=f"nr{g}_{h}", r=["gW"], w=["nr1"])
            rec.add("tensor", lambda e: e.matmul(
                b5[:], ones1[0:1, :], brow1[0:1, :],
                start=True, stop=True), tag=f"bbc{g}_{h}",
                r=["br1", "ones1"], w=["b5"])
            rec.add("scalar", lambda e: e.copy(bb_s[:], b5[:]),
                    tag="bbcp", r=["b5"], w=["bbt"])
            nkc = 4 * (g + 1)

            def l1_loads(kc):
                kcb = kc % 4
                for d2 in range(4):
                    rec.dma("sync",
                            kbb[kcb][:, 128 * d2:128 * (d2 + 1)],
                            kf_hbm[4 * d2 + h, :, 128 * kc:128 * (kc + 1)],
                            tag=f"kl{g}_{h}_{kc}_{d2}",
                            r=[f"hks{kc // 4}_{4 * d2 + h}"],
                            w=[f"kbt{kcb}"])
                rec.dma("scalar", vbb[kcb][:],
                        vt_hbm[kc, :, 512 * h:512 * (h + 1)],
                        tag=f"vl{g}_{h}_{kc}", r=[f"hvt{kc}"],
                        w=[f"vbt{kcb}"])

            def l1_qk(kc):
                kcb = kc % 4
                q0 = 128 * max(0, kc - 4 * g)
                qkb, qkt = (b6, "b6") if kc % 2 == 0 else (b5, "b5")
                for d2 in range(4):
                    rec.add("tensor", lambda e, d2=d2, kcb=kcb, q0=q0,
                            qkb=qkb: e.matmul(
                        qkb[:, q0:512], kbb[kcb][:, 128 * d2:128 * (d2 + 1)],
                        qh_s[:, 512 * d2 + q0:512 * (d2 + 1)],
                        start=(d2 == 0), stop=(d2 == 3)),
                        tag=f"qk{g}_{h}_{kc}", r=[f"kbt{kcb}", "qht"],
                        w=[qkt])

            def l1_gate(kc):
                q0 = 128 * max(0, kc - 4 * g)
                ldx, edx = ld_t[kc % 2], ed_t[kc % 2]
                ldtok, edtok = f"ldt{kc % 2}", f"edt{kc % 2}"
                rec.add("vector", lambda e, kc=kc, q0=q0, ldx=ldx, h=h:
                        e.tensor_scalar(
                    ldx[:, q0:512], bb_s[:, q0:512],
                    aT_s[:, 4 * kc + h:4 * kc + h + 1],
                    None, ALU.add), tag=f"ld{g}_{h}_{kc}",
                    r=["bbt", "aT"], w=[ldtok])
                if kc - 4 * g >= 0:
                    rec.add("vector", lambda e, q0=q0, ldx=ldx:
                            e.tensor_tensor(
                        ldx[:, q0:q0 + 128], ldx[:, q0:q0 + 128],
                        maskn[:], ALU.add), tag="lddiag",
                        r=[ldtok, "mkn"], w=[ldtok])
                rec.add("scalar", lambda e, q0=q0, ldx=ldx, edx=edx:
                        e.activation(
                    edx[:, q0:512], ldx[:, q0:512], AF.Exp),
                    tag=f"ed{g}_{h}_{kc}", r=[ldtok], w=[edtok])

            def l1_cmul(kc):
                q0 = 128 * max(0, kc - 4 * g)
                qkb, qkt = (b6, "b6") if kc % 2 == 0 else (b5, "b5")
                edx, c1x = ed_t[kc % 2], c16[kc % 2]
                edtok, ctok = f"edt{kc % 2}", f"c16t{kc % 2}"
                rec.add("vector", lambda e, q0=q0, qkb=qkb, edx=edx,
                        c1x=c1x: e.tensor_tensor(
                    c1x[:, q0:512], qkb[:, q0:512], edx[:, q0:512],
                    ALU.mult),
                    tag=f"c{g}_{h}_{kc}", r=[qkt, edtok], w=[ctok])

            def l1_cv(kc):
                kcb = kc % 4
                q0 = 128 * max(0, kc - 4 * g)
                c1x = c16[kc % 2]
                ctok = f"c16t{kc % 2}"
                sp = (kc == nkc - 1)
                for dv in range(4):
                    bank = (b0, b1, b2, b3)[dv]
                    rec.add("tensor", lambda e, dv=dv, kcb=kcb, kc=kc,
                            q0=q0, bank=bank, c1x=c1x, sp=sp: e.matmul(
                        bank[:, q0:512], vbb[kcb][:, 128 * dv:128 * (dv + 1)],
                        c1x[:, q0:512], start=(kc == 0), stop=sp),
                        tag=f"cv{g}_{h}_{kc}_{dv}",
                        r=[f"vbt{kcb}", ctok], w=[f"b{dv}"])
                rec.add("tensor", lambda e, kc=kc, q0=q0, c1x=c1x, sp=sp:
                        e.matmul(
                    b4[0:1, q0:512], onescol[:], c1x[:, q0:512],
                    start=(kc == 0), stop=sp),
                    tag=f"cs{g}_{h}_{kc}", r=["onescol", ctok], w=["b4"])

            for kk in range(min(3, nkc)):
                l1_loads(kk)
            l1_gate(0)
            l1_qk(0)
            for kc in range(nkc):
                if kc + 3 < nkc:
                    l1_loads(kc + 3)
                if kc + 1 < nkc:
                    l1_gate(kc + 1)
                    l1_qk(kc + 1)
                l1_cmul(kc)
                l1_cv(kc)
            rec.add("scalar", lambda e: e.activation(
                rs1[:], b4[0:1, :], AF.Abs), tag=f"abs{g}_{h}",
                r=["b4"], w=["rst"])
            rec.add("vector", lambda e: e.tensor_tensor(
                rs1[:], rs1[:], n2r1[0:1, :], ALU.max),
                tag="nmax", r=["rst", "nr1"], w=["rst"])
            rec.add("vector", lambda e: e.tensor_scalar(
                rs1[:], rs1[:], 1e-6, None, ALU.add),
                tag="neps", r=["rst"], w=["rst"])
            rec.add("vector", lambda e: e.reciprocal(rs1[:], rs1[:]),
                    tag="nrec", r=["rst"], w=["rst"])
            rec.add("tensor", lambda e: e.matmul(
                b5[:], ones1[0:1, :], rs1[0:1, :],
                start=True, stop=True), tag=f"fbc{g}_{h}",
                r=["rst", "ones1"], w=["b5"])
            rec.add("scalar", lambda e: e.copy(fbc[:], b5[:]),
                    tag="fbcp", r=["b5"], w=["fbct"])
            for dv in range(4):
                bank = (b0, b1, b2, b3)[dv]
                rec.add("vector", lambda e, dv=dv, bank=bank: e.tensor_tensor(
                    h4[dv][:], bank[:], fbc[:], ALU.mult),
                    tag=f"hsc{g}_{h}_{dv}", r=[f"b{dv}", "fbct"],
                    w=[f"h4t{dv}"])
            for ts in range(4):
                for dv in range(4):
                    rec.add("tensor", lambda e, dv=dv, ts=ts: e.transpose(
                        b4[:, 0:128],
                        h4[dv][:, 128 * ts:128 * (ts + 1)], ident32[:]),
                        tag="htr", r=[f"h4t{dv}", "id32"], w=["b4"])
                    rec.add("vector", lambda e, dv=dv: e.tensor_copy(
                        ht[:, 128 * dv:128 * (dv + 1)], b4[:, 0:128]),
                        tag="htev", r=["b4"], w=["htt"])
                lnh.emit(ht[:], "htt", szf[:], "szft", eps5, const_dep=cp_op)
                rec.add("vector", lambda e, h=h: e.tensor_tensor(
                    hnt16[:], szf[:], lnwbc[:, 512 * h:512 * (h + 1)],
                    ALU.mult), tag="hnw", r=["szft", "lnwbc"], w=["hnt"])
                for dv in range(4):
                    rec.add("tensor", lambda e, dv=dv: e.transpose(
                        p_tr[:, 0:128], hnt16[:, 128 * dv:128 * (dv + 1)],
                        ident[:]), deps=[o_id], tag="hntr",
                        r=["hnt"], w=["ptr"])
                    rec.add("vector", lambda e, dv=dv, ts=ts: e.tensor_copy(
                        hnf[:, 512 * dv + 128 * ts:512 * dv + 128 * (ts + 1)],
                        p_tr[:, 0:128]),
                        tag="hnev", r=["ptr"], w=["hnft"])
            for d2 in range(4):
                m = 4 * d2 + h
                rec.dma("sync", xcz[:, 512 * d2:512 * (d2 + 1)],
                        xcf_hbm[m, :, 512 * g:512 * (g + 1)],
                        tag=f"xcl{g}_{h}_{d2}", r=[f"hxc{g}_{m}"], w=["xczt"])
                rec.dma("sync", zz[:, 512 * d2:512 * (d2 + 1)],
                        zf_hbm[m, :, 512 * g:512 * (g + 1)],
                        tag=f"zl{g}_{h}_{d2}", r=[f"hz{g}_{m}"], w=["zzt"])
            for d2 in range(4):
                m = 4 * d2 + h
                rec.add("scalar", lambda e, d2=d2, m=m: e.activation(
                    skf[:], xcz[:, 512 * d2:512 * (d2 + 1)], AF.Copy,
                    scale=skp_s[:, m:m + 1]), tag="hskip",
                    r=["xczt", "skp"], w=["skft"])
                rec.add("vector", lambda e, d2=d2: e.tensor_tensor(
                    skf[:], skf[:],
                    hnf[:, 512 * d2:512 * (d2 + 1)], ALU.add),
                    tag="hskip2", r=["skft", "hnft"], w=["skft"])
                rec.add("scalar", lambda e, d2=d2: e.activation(
                    szf[:], zz[:, 512 * d2:512 * (d2 + 1)], AF.Silu),
                    tag="hsilu", r=["zzt"], w=["szft"])
                rec.add("vector", lambda e, d2=d2, m=m: e.tensor_tensor(
                    hst_s[:, 512 * m:512 * (m + 1)], skf[:], szf[:],
                    ALU.mult), tag="hstate", r=["skft", "szft"],
                    w=[f"hstt{m}"])
        for ts in range(4):
            ti = 4 * g + ts
            q2 = ts % 2
            rec.dma("scalar", xt[q2][:], x[128 * ti:128 * (ti + 1), :],
                    tag=f"xr{ti}", w=[f"xt{q2}"])
            for nh in range(2):
                bank, bt = (b5, "b5") if nh == 0 else (b6, "b6")
                for m in range(16):
                    rec.add("tensor", lambda e, m=m, ts=ts, nh=nh, bank=bank:
                            e.matmul(
                        bank[:],
                        hst_s[:, 512 * m + 128 * ts:512 * m + 128 * (ts + 1)],
                        wdn_s[:, 1024 * m + 512 * nh:1024 * m + 512 * (nh + 1)],
                        start=(m == 0), stop=(m == 15)),
                        tag=f"dn{ti}_{nh}", r=[f"hstt{m}", "wdn"], w=[bt])
                rec.add("vector", lambda e, q2=q2, nh=nh, bank=bank:
                        e.tensor_tensor(
                    outt[q2][:, 512 * nh:512 * (nh + 1)], bank[:],
                    xt[q2][:, 512 * nh:512 * (nh + 1)], ALU.add),
                    tag=f"dres{ti}_{nh}", r=[bt, f"xt{q2}"], w=[f"outt{q2}"])
            rec.dma("sync", y[128 * ti:128 * (ti + 1), :], outt[q2][:],
                    tag=f"y{ti}", r=[f"outt{q2}"])


def prep_l1_attn(ins):
    INNER = 2048
    cc = np.arange(INNER)
    cp_map = (cc % 4) * 512 + cc // 4
    inv = np.empty(INNER, np.int64)
    inv[cp_map] = cc
    w_up = np.asarray(ins["L1_attn_w_up"], np.float32)
    wupT = w_up.T.copy()
    wupT_perm = np.empty_like(wupT)
    wupT_perm[:, cp_map] = wupT[:, 0:2048]
    wupT_perm[:, 2048 + cp_map] = wupT[:, 2048:4096]
    w_down = np.asarray(ins["L1_attn_w_down"], np.float32)
    wdnT_perm = np.empty((2048, 1024), np.float32)
    wdnT_perm[cp_map, :] = w_down.T

    def diag_pack(w):
        out = np.zeros((128, 8192), np.float32)
        pr = np.arange(128)
        for o in range(4):
            for d in range(4):
                for pc in range(4):
                    idx = 16 * o + 4 * d + pc
                    out[pr, 128 * idx + pr] = w[128 * pc + pr, o, d]
        return out

    qdiag = diag_pack(np.asarray(ins["L1_attn_w_q"], np.float32))
    kdiag = diag_pack(np.asarray(ins["L1_attn_w_k"], np.float32))
    w_v = np.asarray(ins["L1_attn_w_v"], np.float32)
    vdiag = np.zeros((128, 64), np.float32)
    pr = np.arange(128)
    for o in range(4):
        for d in range(4):
            for pc in range(4):
                vdiag[:, 16 * o + 4 * d + pc] = w_v[128 * pc + pr, o, d]
    mm, pp = np.meshgrid(np.arange(16), np.arange(128), indexing="ij")
    c_of = 512 * (mm % 4) + 4 * pp + mm // 4
    conv_w = np.asarray(ins["L1_attn_conv_w"], np.float32)[:, 0, :]
    convw_c = np.zeros((128, 64), np.float32)
    for m in range(16):
        for j in range(4):
            convw_c[:, 4 * m + j] = conv_w[c_of[m], j]
    convb_c = np.zeros((128, 16), np.float32)
    skip_c = np.zeros((128, 16), np.float32)
    conv_b = np.asarray(ins["L1_attn_conv_b"], np.float32)
    skip = np.asarray(ins["L1_attn_skip"], np.float32)
    for m in range(16):
        convb_c[:, m] = conv_b[c_of[m]]
        skip_c[:, m] = skip[c_of[m]]
    row_perm = np.concatenate([s * 2048 + cp_map for s in range(3)])
    wig = np.asarray(ins["L1_attn_w_ig"], np.float32)
    wfg = np.asarray(ins["L1_attn_w_fg"], np.float32)
    wig_h = np.zeros((128, 384), np.float32)
    for gi, w in ((0, wig), (1, wfg)):
        wp = np.empty_like(w)
        wp[:, row_perm] = w
        for ccx in range(48):
            wig_h[:, 4 * (48 * gi + ccx):4 * (48 * gi + ccx + 1)] = \
                wp[:, 128 * ccx:128 * (ccx + 1)].T
    big_c = np.stack([np.asarray(ins["L1_attn_b_ig"], np.float32),
                      np.asarray(ins["L1_attn_b_fg"], np.float32)], axis=1)
    ln_w = np.asarray(ins["L1_attn_ln_w"], np.float32)
    lnw_row = np.zeros((1, 2048), np.float32)
    for h in range(4):
        for d in range(4):
            lnw_row[0, 512 * h + 128 * d + np.arange(128)] = \
                ln_w[h, 4 * np.arange(128) + d]
    return {
        "wupT": _c(wupT_perm), "wdnT": _c(wdnT_perm),
        "qdiag": _c(qdiag), "kdiag": _c(kdiag), "vdiag": _c(vdiag),
        "convw_c": _c(convw_c), "convb_c": _c(convb_c),
        "wig_h": _c(wig_h), "big_c": _c(big_c),
        "lnw_row": _c(lnw_row), "skip_c": _c(skip_c),
    }


def make_l1_attn_program():
    nc = bass.Bass("TRN2", target_bir_lowering=False)
    al = Alloc(nc)
    rec = Rec()
    def di(name, shape):
        return nc.dram_tensor(name, shape, F32, kind="ExternalInput").ap()
    x = di("x", [T, D])
    y = nc.dram_tensor("y", [T, D], F32, kind="ExternalOutput").ap()
    qf = nc.dram_tensor("qf_hbm", [16, 128, T], BF16).ap()
    kf = nc.dram_tensor("kf_hbm", [16, 128, T], BF16).ap()
    vt = nc.dram_tensor("vt_hbm", [NT, 128, 2048], BF16).ap()
    xcf = nc.dram_tensor("xcf_hbm", [16, 128, T], BF16).ap()
    zf = nc.dram_tensor("zf_hbm", [16, 128, T], BF16).ap()
    build_l1_attn(nc, al, rec, x, y, qf, kf, vt, xcf, zf,
                  di("wupT", [1024, 4096]), di("qdiag", [128, 8192]),
                  di("kdiag", [128, 8192]), di("vdiag", [128, 64]),
                  di("convw_c", [128, 64]), di("convb_c", [128, 16]),
                  di("wig_h", [128, 384]), di("big_c", [4, 2]),
                  di("lnw_row", [1, 2048]), di("skip_c", [128, 16]),
                  di("wdnT", [2048, 1024]))
    finish(nc, rec, al)
    return nc


def prep_l0_mla(ins):
    kv_d = np.asarray(ins["L0_ffn_kv_d"], np.float32)
    q_d = np.asarray(ins["L0_ffn_q_d"], np.float32)
    k_u = np.asarray(ins["L0_ffn_k_u"], np.float32)
    q_u = np.asarray(ins["L0_ffn_q_u"], np.float32)
    v_u = np.asarray(ins["L0_ffn_v_u"], np.float32)
    rq = np.asarray(ins["L0_ffn_rope_q_w"], np.float32)
    rk = np.asarray(ins["L0_ffn_rope_k_w"], np.float32)
    o_w = np.asarray(ins["L0_ffn_o_w"], np.float32)

    RD = 32
    ropekT = np.zeros((1024, 1024), np.float32)
    kuT = np.zeros((256, 1024), np.float32)
    qmixT = np.zeros((256, 1024), np.float32)
    rkT = rk.T
    rqT = rq.T
    kuTT = k_u.T
    quTT = q_u.T
    for j in range(8):
        for h2 in range(2):
            h = 2 * j + h2
            ropekT[:, 128 * j + 64 * h2:128 * j + 64 * h2 + 32] = \
                rkT[:, 32 * h:32 * (h + 1)]
            kuT[:, 128 * j + 64 * h2 + 32:128 * j + 64 * h2 + 64] = \
                kuTT[:, 64 * h + 32:64 * (h + 1)]
            qmixT[:, 128 * j + 64 * h2:128 * j + 64 * h2 + 32] = \
                rqT[:, 32 * h:32 * (h + 1)]
            qmixT[:, 128 * j + 64 * h2 + 32:128 * j + 64 * h2 + 64] = \
                quTT[:, 64 * h + 32:64 * (h + 1)]
    inv_freq = 1.0 / (10000.0 ** (np.arange(0, RD, 2, dtype=np.float64) / RD))
    tpos = np.arange(T, dtype=np.float64)
    freqs = tpos[:, None] * inv_freq[None, :]
    emb = np.concatenate([freqs, freqs], axis=1)
    cos32 = np.cos(emb).astype(np.float32).T
    sin32 = np.sin(emb).astype(np.float32).T
    sin_fold = sin32.copy()
    sin_fold[0:16] = -sin32[0:16]
    cos_t = np.zeros((128, T), np.float32)
    sin_t = np.zeros((128, T), np.float32)
    for h2 in range(2):
        cos_t[64 * h2:64 * h2 + 32] = cos32
        cos_t[64 * h2 + 32:64 * h2 + 64] = 1.0
        sin_t[64 * h2:64 * h2 + 32] = sin_fold
        sin_t[64 * h2 + 32:64 * h2 + 64] = 0.0
    pm = np.zeros((128, 128), np.float32)
    for i in range(128):
        pm[i ^ 16, i] = 1.0
    owt = o_w.T
    owT_h = np.zeros((64, 16384), np.float32)
    for h in range(16):
        owT_h[:, 1024 * h:1024 * (h + 1)] = owt[64 * h:64 * (h + 1), :]
    return {
        "kvdT": _c(kv_d.T), "qdT": _c(q_d.T), "ropekT": _c(ropekT),
        "kuT": _c(kuT), "qmixT": _c(qmixT), "vuT": _c(v_u.T),
        "owT": _c(owT_h), "ln1_r": _row(ins["L0_ffn_ln1_w"]),
        "cos_t": _c(cos_t), "sin_t": _c(sin_t), "pmat": _c(pm),
    }


def make_l0_mla_program():
    nc = bass.Bass("TRN2", target_bir_lowering=False)
    al = Alloc(nc)
    rec = Rec()
    def di(name, shape):
        return nc.dram_tensor(name, shape, F32, kind="ExternalInput").ap()
    x = di("x", [T, D])
    s1 = nc.dram_tensor("s1", [T, D], F32, kind="ExternalOutput").ap()
    xin_hbm = nc.dram_tensor("xin_hbm", [T, D], F32, kind="ExternalOutput").ap()
    k_hbm = nc.dram_tensor("k_hbm", [NT, 8, 128, 128], BF16).ap()
    q_hbm = nc.dram_tensor("q_hbm", [NT, 8, 128, 128], BF16).ap()
    v_hbm = nc.dram_tensor("v_hbm", [NT, 128, 1040], BF16).ap()
    build_l0_mla(nc, al, rec, x, s1, xin_hbm, k_hbm, q_hbm, v_hbm,
                 di("kvdT", [1024, 256]), di("qdT", [1024, 256]),
                 di("ropekT", [1024, 1024]), di("kuT", [256, 1024]),
                 di("qmixT", [256, 1024]), di("vuT", [256, 1024]),
                 di("owT", [64, 16384]), di("ln1_r", [1, D]),
                 di("cos_t", [128, T]), di("sin_t", [128, T]),
                 di("pmat", [128, 128]))
    finish(nc, rec, al)
    return nc


def prep_l0_attn(ins):
    w_qkv = np.asarray(ins["L0_attn_w_qkv"], np.float32)
    b_qkv = np.asarray(ins["L0_attn_b_qkv"], np.float32)
    return {
        "wqk": _c(w_qkv[:, 0:1024]),
        "wv": _c(w_qkv[:, 1024:1536]),
        "wfc": _c(ins["L0_attn_w_fc"]),
        "bqk_r": _row(b_qkv[0:1024]),
        "bv_c": _col_chunks(b_qkv[1024:1536], 4),
        "bfc_r": _row(ins["L0_attn_b_fc"]),
        "gam_r": _row(ins["L0_attn_gamma"]),
        "bet_r": _row(ins["L0_attn_beta"]),
    }


def make_l0_attn_program():
    nc = bass.Bass("TRN2", target_bir_lowering=False)
    al = Alloc(nc)
    rec = Rec()
    def di(name, shape):
        return nc.dram_tensor(name, shape, F32, kind="ExternalInput").ap()
    x = di("x", [T, D])
    y = nc.dram_tensor("y", [T, D], F32, kind="ExternalOutput").ap()
    build_l0_attn(nc, al, rec, x, y, di("wqk", [1024, 1024]),
                  di("wv", [1024, 512]), di("wfc", [512, 1024]),
                  di("bqk_r", [1, 1024]), di("bv_c", [128, 4]),
                  di("bfc_r", [1, 1024]), di("gam_r", [1, 1024]),
                  di("bet_r", [1, 1024]))
    finish(nc, rec, al)
    return nc


def prep_l1_ffn(ins):
    return {
        "W1": _c(ins["L1_ffn_W1"]),
        "W2": _c(ins["L1_ffn_W2"]),
        "b1c": _col_chunks(ins["L1_ffn_b1"], 32),
        "thc": _col_chunks(ins["L1_ffn_quantum_phases"], 32),
        "wrc": _col_chunks(ins["L1_ffn_quantum_weights_real"], 32),
        "wic": _col_chunks(ins["L1_ffn_quantum_weights_imag"], 32),
        "b2row": _row(ins["L1_ffn_b2"]),
    }


def make_l1_ffn_program():
    nc = bass.Bass("TRN2", target_bir_lowering=False)
    al = Alloc(nc)
    rec = Rec()
    x = nc.dram_tensor("x", [T, D], F32, kind="ExternalInput").ap()
    y = nc.dram_tensor("y", [T, D], F32, kind="ExternalOutput").ap()
    W1 = nc.dram_tensor("W1", [D, 4096], F32, kind="ExternalInput").ap()
    W2 = nc.dram_tensor("W2", [4096, D], F32, kind="ExternalInput").ap()
    b1c = nc.dram_tensor("b1c", [128, 32], F32, kind="ExternalInput").ap()
    thc = nc.dram_tensor("thc", [128, 32], F32, kind="ExternalInput").ap()
    wrc = nc.dram_tensor("wrc", [128, 32], F32, kind="ExternalInput").ap()
    wic = nc.dram_tensor("wic", [128, 32], F32, kind="ExternalInput").ap()
    b2row = nc.dram_tensor("b2row", [1, D], F32, kind="ExternalInput").ap()
    r_hbm = nc.dram_tensor("r_scratch", [32, 128, T], BF16).ap()
    build_l1_ffn(nc, al, rec, x, y, W1, W2, b1c, thc, wrc, wic, b2row, r_hbm)
    finish(nc, rec, al)
    return nc



def make_fused_program():
    nc = bass.Bass("TRN2", target_bir_lowering=False)
    al = Alloc(nc)
    rec = Rec()

    def di(name, shape, dt=F32):
        return nc.dram_tensor(name, shape, dt, kind="ExternalInput").ap()

    def dint(name, shape, dt=F32):
        return nc.dram_tensor(name, shape, dt).ap()

    x0 = di("x", [T, D])
    y = nc.dram_tensor("y", [T, D], F32, kind="ExternalOutput").ap()
    x1 = dint("x1", [T, D])
    s1 = dint("s1", [T, D])
    x2 = dint("x2", [T, D])
    x3 = dint("x3", [T, D])

    gbf = dint("gbf_hbm", [8, 1024, 2048], BF16)
    ubf = dint("ubf_hbm", [8, 1024, 2048], BF16)
    dbf = dint("dbf_hbm", [8, 2048, 1024], BF16)
    mk = al.mark()
    wc_emits = build_moe_wconv(nc, al, rec,
                               di("p3_gateT", [8, 1024, 2048]),
                               di("p3_upT", [8, 1024, 2048]),
                               di("p3_downT", [8, 2048, 1024]), gbf, ubf, dbf)
    wc_it = iter(wc_emits)

    def wc_filler(n):
        for _ in range(n):
            j = next(wc_it, None)
            if j is None:
                return
            j()

    build_l0_attn(nc, al, rec, x0, x1,
                  di("p1_wqk", [1024, 1024]), di("p1_wv", [1024, 512]),
                  di("p1_wfc", [512, 1024]), di("p1_bqk_r", [1, 1024]),
                  di("p1_bv_c", [128, 4]), di("p1_bfc_r", [1, 1024]),
                  di("p1_gam_r", [1, 1024]), di("p1_bet_r", [1, 1024]),
                  filler=wc_filler)
    for j in wc_it:
        j()
    rec.fence(tag="ph1_end")
    al.free_suffix(mk)

    mk = al.mark()
    xin_hbm = dint("xin_hbm", [T, D])
    k_hbm = dint("k_hbm", [NT, 8, 128, 128], BF16)
    q_hbm = dint("q_hbm", [NT, 8, 128, 128], BF16)
    v_hbm = dint("v_hbm", [NT, 128, 1040], BF16)
    build_l0_mla(nc, al, rec, x1, s1, xin_hbm, k_hbm, q_hbm, v_hbm,
                 di("p2_kvdT", [1024, 256]), di("p2_qdT", [1024, 256]),
                 di("p2_ropekT", [1024, 1024]), di("p2_kuT", [256, 1024]),
                 di("p2_qmixT", [256, 1024]), di("p2_vuT", [256, 1024]),
                 di("p2_owT", [64, 16384]), di("p2_ln1_r", [1, D]),
                 di("p2_cos_t", [128, T]), di("p2_sin_t", [128, T]),
                 di("p2_pmat", [128, 128]))
    rec.fence(tag="ph2_end")
    al.free_suffix(mk)

    mk = al.mark()
    alf_hbm = dint("alf_hbm", [8, 128, T], BF16)
    build_l0_moe(nc, al, rec, s1, x1, x2, alf_hbm, gbf, ubf, dbf,
                 di("p3_routerT", [1024, 7]),
                 di("p3_rbias_r", [1, 7]), di("p3_ln2_r", [1, D]))
    rec.fence(tag="ph3_end")
    al.free_suffix(mk)

    mk = al.mark()
    qf = dint("qf_hbm", [16, 128, T], BF16)
    kf = dint("kf_hbm", [16, 128, T], BF16)
    vt = dint("vt_hbm", [NT, 128, 2048], BF16)
    xcf = dint("xcf_hbm", [16, 128, T], BF16)
    zf = dint("zf_hbm", [16, 128, T], BF16)
    build_l1_attn(nc, al, rec, x2, x3, qf, kf, vt, xcf, zf,
                  di("p4_wupT", [1024, 4096]), di("p4_qdiag", [128, 8192]),
                  di("p4_kdiag", [128, 8192]), di("p4_vdiag", [128, 64]),
                  di("p4_convw_c", [128, 64]), di("p4_convb_c", [128, 16]),
                  di("p4_wig_h", [128, 384]), di("p4_big_c", [4, 2]),
                  di("p4_lnw_row", [1, 2048]), di("p4_skip_c", [128, 16]),
                  di("p4_wdnT", [2048, 1024]))
    rec.fence(tag="ph4_end")
    al.free_suffix(mk)

    mk = al.mark()
    r_hbm = dint("r_scratch", [32, 128, T], BF16)
    build_l1_ffn(nc, al, rec, x3, y,
                 di("p5_W1", [D, 4096]), di("p5_W2", [4096, D]),
                 di("p5_b1c", [128, 32]), di("p5_thc", [128, 32]),
                 di("p5_wrc", [128, 32]), di("p5_wic", [128, 32]),
                 di("p5_b2row", [1, D]), r_hbm)
    finish(nc, rec, al)
    return nc



_PROGS = {}


def _get_prog(name, maker):
    if name not in _PROGS:
        _PROGS[name] = maker()
    return _PROGS[name]


def kernel(**inputs):
    x0 = np.asarray(inputs["L0_attn_x"], np.float32)
    B = x0.shape[0]
    cores = list(range(B))

    w = {}
    for pfx, prep in (("p1_", prep_l0_attn), ("p2_", prep_l0_mla),
                      ("p3_", prep_l0_moe), ("p4_", prep_l1_attn),
                      ("p5_", prep_l1_ffn)):
        for k, v in prep(inputs).items():
            w[pfx + k] = v

    nc = _get_prog("fused", make_fused_program)
    in_maps = []
    for b in range(B):
        m = dict(w)
        m["x"] = np.ascontiguousarray(x0[b])
        in_maps.append(m)
    res = run_bass_kernel_spmd(nc, in_maps, core_ids=cores)
    return np.stack([np.asarray(res.results[b]["y"]) for b in range(B)])
